# version 16
# baseline (speedup 1.0000x reference)
import math
import numpy as np
from contextlib import ExitStack
from dataclasses import dataclass

import concourse.bass as bass
import concourse.mybir as mybir
from concourse.bass_utils import run_bass_kernel_spmd

F32 = mybir.dt.float32
BF16 = mybir.dt.bfloat16
I32 = mybir.dt.int32
AF = mybir.ActivationFunctionType
OP = mybir.AluOpType

LUT_N = 16
EPS = 1e-5
MRUN = 8


@dataclass
class Cfg:
    S: int = 2048
    H: int = 2048
    NH: int = 16
    I: int = 5504
    TB: int = 512
    ncores: int = 8

    @property
    def D(self): return self.H // self.NH
    @property
    def F3(self): return 3 * self.H
    @property
    def nHC(self): return self.H // 128
    @property
    def nOC(self): return self.H // 128
    @property
    def nIC(self): return self.I // 128
    @property
    def nTB(self): return self.S // self.TB
    @property
    def nTT(self): return self.S // 128
    @property
    def NSTR(self): return self.TB // 128
    @property
    def DBW(self): return min(512, self.H)
    @property
    def nDB(self): return self.H // self.DBW
    @property
    def G1(self): return self.nHC * self.nIC
    @property
    def NT(self): return 2 * self.G1 + self.nIC * self.nOC
    @property
    def TPC(self): return (self.NT + self.ncores - 1) // self.ncores
    @property
    def qscale(self): return 1.0 / math.sqrt(self.D)


FULL = Cfg()
SMALL = Cfg(S=256, H=256, NH=2, I=384, TB=128, ncores=8)
MED = Cfg(S=1024, H=512, NH=4, I=768, TB=512, ncores=8)

ENGINES = ("sync", "gpsimd", "vector", "scalar", "tensor")


class Emitter:
    def __init__(self, nc, ctx):
        self.nc = nc
        self.ctx = ctx
        self.q = {e: [] for e in ENGINES}
        self.sems = {}

    def _sem(self, name):
        if name not in self.sems:
            h = self.ctx.enter_context(self.nc.semaphore(name))
            self.sems[name] = [h, 0]
        return self.sems[name]

    def count(self, name):
        return self.sems[name][1] if name in self.sems else 0

    def run(self, eng, fn):
        self.q[eng].append(fn)

    def inc(self, eng, sem_name, fn, k=1):
        s = self._sem(sem_name)
        s[1] += k
        self.q[eng].append(lambda e, fn=fn, h=s[0], k=k: fn(e).then_inc(h, k))
        return s[1]

    def dma(self, eng, sem_name, out, in_):
        return self.inc(eng, sem_name, lambda e, o=out, i=in_: e.dma_start(out=o, in_=i), k=16)

    def wait(self, eng, sem_name, val):
        if val is None or val <= 0:
            return
        s = self._sem(sem_name)
        assert val <= s[1], f"wait {sem_name}>={val} but only {s[1]} emitted"
        self.q[eng].append(lambda e, h=s[0], v=val: e.wait_ge(h, v))

    def drain(self, eng):
        self.q[eng].append(lambda e: e.drain())

    def finalize(self, block):
        for eng in ENGINES:
            dec = getattr(block, eng)
            q = self.q[eng]
            def make(qq):
                def body(e):
                    for fn in qq:
                        fn(e)
                return body
            dec(make(q))


def w_tile_src(cfg, gid):
    if gid < cfg.G1:
        return "gidxT", gid // cfg.nIC, gid % cfg.nIC
    if gid < 2 * cfg.G1:
        return "uidxT", (gid - cfg.G1) // cfg.nIC, (gid - cfg.G1) % cfg.nIC
    g2 = gid - 2 * cfg.G1
    return "didxT", g2 // cfg.nOC, g2 % cfg.nOC


def build_program(cfg: Cfg):
    nc = bass.Bass("TRN2", target_bir_lowering=False)
    ctx = ExitStack()
    S, H, I, TB = cfg.S, cfg.H, cfg.I, cfg.TB
    nHC, nOC, nIC, nTT, NSTR = cfg.nHC, cfg.nOC, cfg.nIC, cfg.nTT, cfg.NSTR
    DBW = cfg.DBW

    x_fm = nc.dram_tensor("x_fm", [H, S], F32, kind="ExternalInput")
    wqkv = nc.dram_tensor("wqkv", [H, cfg.F3], F32, kind="ExternalInput")
    wout = nc.dram_tensor("wout", [H, H], F32, kind="ExternalInput")
    myidx = nc.dram_tensor("myidx", [cfg.TPC, 128, 128], I32, kind="ExternalInput")
    lutb = nc.dram_tensor("lutb", [128, LUT_N], F32, kind="ExternalInput")
    cmask = nc.dram_tensor("cmask", [128, (NSTR - 1) * 128 + TB], BF16, kind="ExternalInput")
    onesc = nc.dram_tensor("onesc", [128, 1], BF16, kind="ExternalInput")
    onesr = nc.dram_tensor("onesr", [1, 128], BF16, kind="ExternalInput")
    out_fm = nc.dram_tensor("out_fm", [H, S], F32, kind="ExternalOutput")

    q_scr = nc.dram_tensor("q_scr", [H, S], BF16)
    k_scr = nc.dram_tensor("k_scr", [H, S], BF16)
    v_scr = nc.dram_tensor("v_scr", [S, H], BF16)
    wob_scr = nc.dram_tensor("wob_scr", [H, H], BF16)
    x2f_scr = nc.dram_tensor("x2f_scr", [H, S], F32)
    x2b_scr = nc.dram_tensor("x2b_scr", [H, S], BF16)
    h_scr = nc.dram_tensor("h_scr", [I, S], BF16)
    wcat_my = nc.dram_tensor("wcat_my", [cfg.TPC, 128, 128], BF16)
    wcat = nc.dram_tensor("wcat", [cfg.ncores * cfg.TPC, 128, 128], BF16,
                          addr_space="Shared")

    sb = lambda name, shape, dt: ctx.enter_context(nc.sbuf_tensor(name, shape, dt))

    with ctx:
        xact = sb("xact", [128, nHC * S], BF16)
        p2cols = nTT * TB + 4 * S + 9 * TB
        bigcols = max(p2cols, nIC * TB, 21 * MRUN * 128)
        big = sb("big", [128, bigcols], BF16)
        r1b_sl = sb("r1b_sl", [128, TB], BF16)
        r1sq_sl = sb("r1sq_sl", [128, TB], BF16)
        
        slab16 = sb("slab16", [128, nHC * max(DBW, TB)], BF16)
        arf_cols = max(8 * 128 + 2 * DBW + 2 * S, nOC * TB + 4 * TB)
        arf = sb("arf", [128, arf_cols], F32)

        lut_s = sb("lut_s", [128, LUT_N], F32)
        cm_s = sb("cm_s", [128, (NSTR - 1) * 128 + TB], BF16)
        onec_s = sb("onec_s", [128, 1], BF16)
        oner_s = sb("oner_s", [1, 128], BF16)
        wqk_b = [sb(f"wqk_b{s}", [128, 128], BF16) for s in range(8)]
        qk_e = [sb(f"qk_e{s}", [128, TB], BF16) for s in range(4)]
        ve_t = [sb(f"ve_t{s}", [128, DBW], BF16) for s in range(1)]
        wo_t = [sb(f"wo_t{s}", [128, 2 * 128], BF16) for s in range(4)]
        wg_t = [sb(f"wg_t{s}", [128, 4 * 128], BF16) for s in range(2)]
        wu_t = [sb(f"wu_t{s}", [128, 4 * 128], BF16) for s in range(2)]
        wd_t = [sb(f"wd_t{s}", [128, 2 * 128], BF16) for s in range(6)]
        mn_f = sb("mn_f", [1, TB], F32)
        msq_f = sb("msq_f", [1, TB], F32)
        mn2_f = sb("mn2_f", [1, TB], F32)
        var_f = sb("var_f", [1, TB], F32)
        sd_f = sb("sd_f", [1, TB], F32)
        rstd_f = sb("rstd_f", [1, TB], F32)
        den_f = mn_f
        rden_f = msq_f
        bvec_f = sd_f
        rden_b = sb("rden_b", [1, TB], BF16)
        a_b = sb("a_b", [1, TB], BF16)
        b_b = sb("b_b", [1, TB], BF16)
        

        BKW = 512
        psA = [ctx.enter_context(nc.psum_tensor(f"psA{i}", [128, 2 * BKW], F32)) for i in range(2)]
        psS = [ctx.enter_context(nc.psum_tensor(f"psS{i}", [128, BKW], F32)) for i in range(4)]
        ps = [psA[0][:, 0:BKW], psA[0][:, BKW:2 * BKW],
              psS[0][:, :], psS[1][:, :], psS[2][:, :], psS[3][:, :],
              psA[1][:, 0:BKW], psA[1][:, BKW:2 * BKW]]

        T = dict(
            x_fm=x_fm, wqkv=wqkv, wout=wout, myidx=myidx, lutb=lutb, cmask=cmask,
            onesc=onesc, onesr=onesr, out_fm=out_fm, q_scr=q_scr, k_scr=k_scr,
            v_scr=v_scr, wob_scr=wob_scr, x2f_scr=x2f_scr, x2b_scr=x2b_scr,
            h_scr=h_scr, wcat_my=wcat_my, wcat=wcat,
            xact=xact, big=big, slab16=slab16, arf=arf,
            r1b_sl=r1b_sl, r1sq_sl=r1sq_sl,

            lut_s=lut_s, cm_s=cm_s,
            onec_s=onec_s, oner_s=oner_s, wqk_b=wqk_b, qk_e=qk_e,
            ve_t=ve_t, wo_t=wo_t, wg_t=wg_t, wu_t=wu_t, wd_t=wd_t,
            den_f=den_f, rden_f=rden_f, rden_b=rden_b,
            mn_f=mn_f, msq_f=msq_f, mn2_f=mn2_f, var_f=var_f, sd_f=sd_f,
            rstd_f=rstd_f, bvec_f=bvec_f, a_b=a_b, b_b=b_b, ps=ps, psA=psA,
        )
        with nc.Block() as block:
            E = Emitter(nc, ctx)
            build_phases(cfg, E, T)
            E.finalize(block)
    return nc


def build_phases(cfg, E, T):
    S, H, I, TB = cfg.S, cfg.H, cfg.I, cfg.TB
    nHC, nOC, nIC, nTB, nTT, NSTR = (cfg.nHC, cfg.nOC, cfg.nIC, cfg.nTB,
                                     cfg.nTT, cfg.NSTR)
    NH, DBW, nDB = cfg.NH, cfg.DBW, cfg.nDB
    PSW = 512
    ps = T["ps"]
    big, xact, slab16, arf = T["big"], T["xact"], T["slab16"], T["arf"]
    g = lambda n: T[n]

    et_off = 0
    ksl_off = nTT * TB
    vsl_off = ksl_off + 2 * S
    qsl_off = vsl_off + 2 * S
    et = lambda kt: big[:, et_off + kt * TB: et_off + (kt + 1) * TB]
    k_sl = lambda s, w: big[:, ksl_off + s * S: ksl_off + s * S + w]
    v_sl = lambda s, w: big[:, vsl_off + s * S: vsl_off + s * S + w]
    q_sl = lambda s: big[:, qsl_off + s * TB: qsl_off + (s + 1) * TB]
    r1b_t = lambda s: T["r1b_sl"][:, 0:TB]
    r1sq_t = lambda s: T["r1sq_sl"][:, 0:TB]
    tail0 = qsl_off + 2 * TB
    et_t = lambda s: big[:, tail0 + s * TB: tail0 + (s + 1) * TB]
    h_sl = lambda ic: big[:, ic * TB:(ic + 1) * TB]
    bc_s = big[:, tail0 + 2 * TB: tail0 + 3 * TB]
    sil_t = [big[:, tail0 + (3 + s) * TB: tail0 + (4 + s) * TB] for s in range(2)]
    h_t = [big[:, tail0 + (5 + s) * TB: tail0 + (6 + s) * TB] for s in range(2)]
    attn = lambda dc: slab16[:, dc * TB:(dc + 1) * TB]
    wv_b = lambda hc: slab16[:, hc * DBW:(hc + 1) * DBW]
    wqk_f = [arf[:, s * 128:(s + 1) * 128] for s in range(8)]
    wv_f = [arf[:, 8 * 128 + s * DBW: 8 * 128 + (s + 1) * DBW] for s in range(2)]
    xs_f = [arf[:, 8 * 128 + 2 * DBW + s * S: 8 * 128 + 2 * DBW + (s + 1) * S] for s in range(2)]
    r1 = arf[:, 0:nOC * TB]
    xres = [arf[:, nOC * TB + s * TB: nOC * TB + (s + 1) * TB] for s in range(2)]
    tmp_f = arf[:, nOC * TB + 2 * TB: nOC * TB + 3 * TB]
    x2f_t = [arf[:, nOC * TB + 3 * TB: nOC * TB + 4 * TB]]
    r2, x2r, out_t = r1, xres, x2f_t

    E.dma("sync", "s_lut", g("lut_s")[:, :], g("lutb")[:, :])
    E.dma("sync", "s_cm", g("cm_s")[:, :], g("cmask")[:, :])
    E.dma("sync", "s_onc", g("onec_s")[:, :], g("onesc")[:, :])
    E.dma("sync", "s_onr", g("oner_s")[:, :], g("onesr")[:, :])

    NG = min(4, nHC)
    runs = []
    t = 0
    while t < cfg.TPC:
        w = min(MRUN, cfg.TPC - t)
        runs.append((t, w))
        t += w
    MP = MRUN * 128
    m_part = [big[:, j * MP:(j + 1) * MP] for j in range(16)]
    m_idxb = [big[:, (16 + s) * MP:(17 + s) * MP] for s in range(3)]
    m_out = [big[:, (19 + s) * MP:(20 + s) * MP] for s in range(2)]
    E.wait("vector", "s_lut", 16)
    acc_v = {}

    def mat_dma(ri):
        t0, w = runs[ri]
        slot = ri % 3
        if ri >= 3:
            E.wait("gpsimd", "v_p", acc_v[ri - 3])
        E.dma("gpsimd", f"s_midx{slot}",
              m_idxb[slot][:, 0:w * 128].rearrange("p (t c) -> p t c", c=128),
              g("myidx")[t0:t0 + w, :, :].rearrange("t p c -> p t c"))

    for r0 in range(min(3, len(runs))):
        mat_dma(r0)
    for ri, (t0, w) in enumerate(runs):
        slot = ri % 3
        E.wait("vector", f"s_midx{slot}", 16 * (ri // 3 + 1))
        for j in range(LUT_N):
            E.run("vector", lambda e, w=w, j=j, slot=slot: e.tensor_scalar(
                out=m_part[j][:, 0:w * 128], in0=m_idxb[slot][:, 0:w * 128],
                scalar1=float(j), scalar2=g("lut_s")[:, j:j + 1],
                op0=OP.is_equal, op1=OP.mult))
        E.drain("vector")
        step = 1
        while step < 16:
            for j0 in range(0, 16, 2 * step):
                last = (2 * step >= 16)
                if last:
                    if ri >= 2:
                        E.wait("vector", f"s_wmy{ri % 2}", 16 * ((ri - 2) // 2 + 1))
                    fn = lambda e, w=w, ri=ri: e.tensor_tensor(
                        out=m_out[ri % 2][:, 0:w * 128], in0=m_part[0][:, 0:w * 128],
                        in1=m_part[8][:, 0:w * 128], op=OP.add)
                    acc_v[ri] = E.inc("vector", "v_p", fn)
                else:
                    fn = lambda e, j0=j0, s=step, w=w: e.tensor_tensor(
                        out=m_part[j0][:, 0:w * 128], in0=m_part[j0][:, 0:w * 128],
                        in1=m_part[j0 + s][:, 0:w * 128], op=OP.add)
                    E.run("vector", fn)
            E.drain("vector")
            step *= 2
        if ri + 3 < len(runs):
            mat_dma(ri + 3)
        E.wait("gpsimd", "v_p", acc_v[ri])
        E.dma("gpsimd", f"s_wmy{ri % 2}",
              g("wcat_my")[t0:t0 + w, :, :].rearrange("t p c -> p t c"),
              m_out[ri % 2][:, 0:w * 128].rearrange("p (t c) -> p t c", c=128))
    mat_done_v = acc_v[len(runs) - 1]
    n_wmy = [E.count("s_wmy0"), E.count("s_wmy1")]
    E.wait("gpsimd", "s_wmy0", n_wmy[0])
    E.wait("gpsimd", "s_wmy1", n_wmy[1])
    ag_g = E.inc("gpsimd", "g_p", lambda e: e.collective_compute(
        "AllGather", OP.bypass, replica_groups=[list(range(cfg.ncores))],
        ins=[g("wcat_my").ap().opt()], outs=[g("wcat").ap().opt()]))
    for i4 in range(NG):
        E.dma("gpsimd", "s_wob", g("wob_scr")[i4 * (H // NG):(i4 + 1) * (H // NG), :],
              g("wout")[i4 * (H // NG):(i4 + 1) * (H // NG), :])
    n_wob = E.count("s_wob")

    qkcast_s, lastmm_t, stopmm_t, qkevac_s = {}, {}, {}, {}
    xcast_s = {}
    for ft in range(2 * nHC):
        isq = ft < nHC
        for hc in range(nHC):
            it = ft * nHC + hc
            slot = it % 8
            if ft == 0:
                xsl = hc % 2
                if hc >= 2:
                    E.wait("sync", "sc_p", xcast_s[hc - 2])
                E.dma("sync", f"s_xs{xsl}", xs_f[xsl],
                      g("x_fm")[hc * 128:(hc + 1) * 128, :])
                E.wait("scalar", f"s_xs{xsl}", 16 * (hc // 2 + 1))
                xcast_s[hc] = E.inc("scalar", "sc_p",
                                    lambda e, s=xsl, k=hc: e.copy(
                                        out=xact[:, k * S:(k + 1) * S], in_=xs_f[s]))
            if it >= 8:
                E.wait("sync", "sc_p", qkcast_s[it - 8])
            E.dma("sync", f"s_wqk{slot}", wqk_f[slot],
                  g("wqkv")[hc * 128:(hc + 1) * 128, ft * 128:(ft + 1) * 128])
            if it >= 8:
                E.wait("scalar", "t_p", lastmm_t[it - 8])
            E.wait("scalar", f"s_wqk{slot}", 16 * (it // 8 + 1))
            qkcast_s[it] = E.inc("scalar", "sc_p",
                                 lambda e, s=slot: e.copy(out=g("wqk_b")[s][:, :],
                                                          in_=wqk_f[s]))
            E.wait("tensor", "sc_p", qkcast_s[it])
            if ft == 0:
                E.wait("tensor", "sc_p", xcast_s[hc])
            if hc == 0 and ft >= 2:
                E.wait("tensor", "sc_p", qkevac_s[(ft - 2, nTB - 1)])
            bk0 = (ft % 2) * nTB
            for tb in range(nTB):
                fn = lambda e, s=slot, hc=hc, tb=tb, bk0=bk0, st=(hc == 0), sp=(hc == nHC - 1): e.matmul(
                    ps[bk0 + tb][:, 0:TB], g("wqk_b")[s][:, :],
                    xact[:, hc * S + tb * TB: hc * S + (tb + 1) * TB],
                    start=st, stop=sp, skip_group_check=True)
                if hc == nHC - 1:
                    stopmm_t[(ft, tb)] = E.inc("tensor", "t_p", fn)
                    lastmm_t[it] = stopmm_t[(ft, tb)]
                elif tb == nTB - 1:
                    lastmm_t[it] = E.inc("tensor", "t_p", fn)
                else:
                    E.run("tensor", fn)
        for tb in range(nTB):
            es = (ft * nTB + tb) % 4
            idx = ft * nTB + tb
            E.wait("scalar", "t_p", stopmm_t[(ft, tb)])
            if idx >= 4:
                E.wait("scalar", f"s_qkst{es}", 16 * (idx // 4))
            sc = cfg.qscale if isq else 1.0
            qkevac_s[(ft, tb)] = E.inc("scalar", "sc_p",
                                       lambda e, tb=tb, es=es, sc=sc, bk0=(ft % 2) * nTB:
                                       e.activation(
                                           out=g("qk_e")[es][:, :], in_=ps[bk0 + tb][:, 0:TB],
                                           func=AF.Copy, scale=sc))
            E.wait("scalar", "sc_p", qkevac_s[(ft, tb)])
            dst = g("q_scr") if isq else g("k_scr")
            r0 = (ft if isq else ft - nHC) * 128
            E.dma("scalar", f"s_qkst{es}", dst[r0:r0 + 128, tb * TB:(tb + 1) * TB],
                  g("qk_e")[es][:, :])
    n_qkst = [E.count(f"s_qkst{e_}") for e_ in range(4)]


    wvc_s, vstop_t, vevac_s = {}, {}, {}
    for db in range(nDB):
        for hc in range(nHC):
            it = db * nHC + hc
            slot = hc % 2
            if it >= 2:
                E.wait("sync", "sc_p", wvc_s[it - 2])
            E.dma("sync", f"s_wv{slot}", wv_f[slot],
                  g("wqkv")[hc * 128:(hc + 1) * 128, 2 * H + db * DBW: 2 * H + (db + 1) * DBW])
            if db >= 1:
                E.wait("scalar", "t_p", vstop_t[(db - 1, nTT - 1)])
            E.wait("scalar", f"s_wv{slot}", 16 * (it // 2 + 1))
            wvc_s[it] = E.inc("scalar", "sc_p",
                              lambda e, hc=hc, s=slot: e.copy(out=wv_b(hc), in_=wv_f[s]))
        for tt in range(nTT):
            bank = 4 + (tt % 2)
            it = db * nTT + tt
            if it == 0:
                E.wait("tensor", "sc_p", qkevac_s[(2 * nHC - 1, nTB - 1)])
            if it >= 2:
                E.wait("tensor", "sc_p", vevac_s[it - 2])
            E.wait("tensor", "sc_p", wvc_s[db * nHC + nHC - 1])
            for hc in range(nHC):
                fn = lambda e, hc=hc, tt=tt, bank=bank, st=(hc == 0), sp=(hc == nHC - 1): e.matmul(
                    ps[bank][:, 0:DBW],
                    xact[:, hc * S + tt * 128: hc * S + (tt + 1) * 128],
                    wv_b(hc), start=st, stop=sp, skip_group_check=True)
                if hc == nHC - 1:
                    vstop_t[(db, tt)] = E.inc("tensor", "t_p", fn)
                else:
                    E.run("tensor", fn)
            E.wait("scalar", "t_p", vstop_t[(db, tt)])
            if it >= 1:
                E.wait("scalar", "s_vst", 16 * it)
            es = 0
            vevac_s[it] = E.inc("scalar", "sc_p",
                                lambda e, bank=bank, es=es: e.copy(out=g("ve_t")[es][:, :],
                                                                   in_=ps[bank][:, 0:DBW]))
            E.wait("scalar", "sc_p", vevac_s[it])
            E.dma("scalar", "s_vst",
                  g("v_scr")[tt * 128:(tt + 1) * 128, db * DBW:(db + 1) * DBW],
                  g("ve_t")[es][:, :])
    n_vst = E.count("s_vst")
    p1_sc = E.count("sc_p")

    E.wait("vector", "s_cm", 16)
    E.wait("scalar", "v_p", mat_done_v)
    E.wait("scalar", "s_wmy0", n_wmy[0])
    E.wait("scalar", "s_wmy1", n_wmy[1])
    E.wait("sync", "v_p", mat_done_v)
    E.wait("sync", "s_wmy0", n_wmy[0])
    E.wait("sync", "s_wmy1", n_wmy[1])
    E.wait("gpsimd", "s_wob", n_wob)
    E.wait("sync", "s_wob", n_wob)
    n_strad = 0
    msk_v, avden_t, attn_v, denev_v, bcs_v = {}, {}, {}, {}, {}
    wo_use_t, r1_v, x2f_v = {}, {}, {}
    wo_qctr = {0: 0, 1: 0}
    vsl_fill = {}
    prev_exp_last = 0
    n_x2f = 0
    ln_prev_stat = {'t': 0, 'sc': 0}
    wo_last_t = 0
    avkt_t = {}
    msk_last = {}
    for qb in range(nTB):
        nkt = (qb + 1) * NSTR
        nkt_prev_qb = qb * NSTR

        def den_tail(iq2):
            head2 = iq2 % NH
            bav = 2 + (head2 % 2)
            bden = 4 + (head2 % 2)
            E.wait("scalar", "t_p", avden_t[iq2])
            E.run("scalar", lambda e, bden=bden: e.activation(
                out=g("rden_f")[:, :], in_=ps[bden][0:1, 0:TB], func=AF.Ln))
            E.drain("scalar")
            denev_v[iq2] = E.inc("scalar", "sc_p", lambda e: e.activation(
                out=g("rden_b")[:, :], in_=g("rden_f")[:, :], func=AF.Exp, scale=-1.0))
            E.wait("tensor", "sc_p", denev_v[iq2])
            E.wait("tensor", "s_onr", 16)
            if iq2 > 0:
                E.wait("tensor", "v_p", bcs_v[iq2 - 1])
            bcb = 2 + ((head2 + 1) % 2)
            if iq2 >= 1:
                E.wait("tensor", "v_p", attn_v[iq2 - 1])
            tbc = E.inc("tensor", "t_p", lambda e, bcb=bcb: e.matmul(
                ps[bcb][:, 0:TB], g("oner_s")[:, :], g("rden_b")[:, :],
                start=True, stop=True, skip_group_check=True))
            E.wait("vector", "t_p", tbc)
            bcs_v[iq2] = E.inc("vector", "v_p",
                               lambda e, bcb=bcb: e.tensor_copy(bc_s, ps[bcb][:, 0:TB]))
            E.drain("vector")
            if head2 == 0 and iq2 >= NH:
                E.wait("vector", "t_p", wo_last_t)
            attn_v[iq2] = E.inc("vector", "v_p",
                                lambda e, head2=head2, bav=bav: e.tensor_tensor(
                                    out=attn(head2), in0=ps[bav][:, 0:TB], in1=bc_s,
                                    op=OP.mult))
            E.drain("vector")

        def emit_av_kt(iq2, kt, nkt2, last):
            head2 = iq2 % NH
            bav = 2 + (head2 % 2)
            bden = 4 + (head2 % 2)
            sl2 = head2 % 2
            if kt == 0:
                if head2 <= 1 and iq2 >= NH and (iq2 // NH) == qb:
                    E.wait("tensor", "v_p", x2f_v[(qb - 1) * nOC + nOC - 1])
                if iq2 >= 2:
                    E.wait("tensor", "v_p", attn_v[iq2 - 2])
                    E.wait("tensor", "sc_p", denev_v[iq2 - 2])
                if iq2 >= 1:
                    E.wait("tensor", "v_p", bcs_v[iq2 - 1])
                if iq2 in msk_last:
                    E.wait("tensor", "v_p", msk_last[iq2])
                E.wait("tensor", f"s_vsl{sl2}", vsl_fill[iq2])
                E.wait("tensor", "s_onc", 16)
            E.run("tensor", lambda e, sl2=sl2, kt=kt, nkt2=nkt2, bav=bav,
                  st=(kt == 0), sp=last: e.matmul(
                      ps[bav][:, 0:TB], v_sl(sl2, nkt2 * 128)[:, kt * 128:(kt + 1) * 128],
                      et(kt), start=st, stop=sp, skip_group_check=True))
            avkt_t[(iq2, kt)] = E.inc("tensor", "t_p",
                                      lambda e, kt=kt, bden=bden, st=(kt == 0), sp=last: e.matmul(
                                          ps[bden][0:1, 0:TB], g("onec_s")[:, :], et(kt),
                                          start=st, stop=sp, skip_group_check=True))
            if last:
                avden_t[iq2] = avkt_t[(iq2, kt)]

        for head in range(NH):
            sl = head % 2
            iq = qb * NH + head
            if iq >= 2:
                E.wait("sync", "t_p", avden_t[iq - 2])
            idxq = head * nTB + qb
            idxk = (nHC + head) * nTB + qb
            E.wait("sync", f"s_qkst{idxq % 4}", 16 * (idxq // 4 + 1))
            E.wait("sync", f"s_qkst{idxk % 4}", 16 * (idxk // 4 + 1))
            idxv = (nDB - 1) * nTT + (NSTR * (qb + 1) - 1)
            E.wait("sync", "s_vst", 16 * (idxv + 1))
            E.dma("sync", f"s_qsl{sl}", q_sl(sl),
                  g("q_scr")[head * 128:(head + 1) * 128, qb * TB:(qb + 1) * TB])
            E.dma("sync", f"s_ksl{sl}", k_sl(sl, nkt * 128),
                  g("k_scr")[head * 128:(head + 1) * 128, 0:nkt * 128])
            E.dma("sync", f"s_vsl{sl}",
                  v_sl(sl, nkt * 128).rearrange("p (t c) -> p t c", c=128),
                  g("v_scr")[0:nkt * 128, head * 128:(head + 1) * 128]
                  .rearrange("(t p) c -> p t c", p=128))
            nfill = 16 * (iq // 2 + 1)
            vsl_fill[iq] = nfill
            E.wait("tensor", f"s_qsl{sl}", nfill)
            E.wait("tensor", f"s_ksl{sl}", nfill)
            exp_slot = {}
            PAIR_OK = (TB == 512)
            SBK4 = (0, 1, 6, 7)
            ns_end = nkt - NSTR
            last_exp = 0
            kt = 0
            while kt < nkt:
                pair = (kt + 1 < ns_end and kt % 2 == 0 and PAIR_OK)
                subkts = (kt, kt + 1) if pair else (kt,)
                for k2 in subkts:
                    c = k2
                    bank = SBK4[c % 4]
                    if c >= 4:
                        E.wait("tensor", "sc_p", exp_slot[c - 4])
                    elif iq > 0:
                        E.wait("tensor", "sc_p", prev_exp_last)
                    else:
                        E.wait("tensor", "sc_p", p1_sc)
                    if c == 0 and head == 0 and qb > 0:
                        E.wait("tensor", "v_p", r1_v[(qb - 1) * nOC + nOC - 1])
                    tsc = E.inc("tensor", "t_p", lambda e, sl=sl, k2=k2, bank=bank, nkt=nkt: e.matmul(
                        ps[bank][:, 0:TB], k_sl(sl, nkt * 128)[:, k2 * 128:(k2 + 1) * 128],
                        q_sl(sl), start=True, stop=True, skip_group_check=True))
                    if head > 0:
                        emit_av_kt(iq - 1, k2, nkt, last=(k2 == nkt - 1))
                kt1 = subkts[-1]
                E.wait("scalar", "t_p", tsc)
                if head > 0:
                    E.wait("scalar", "t_p", avkt_t[(iq - 1, kt1)])
                elif qb > 0:
                    pl = qb * NH - 1
                    kk = min(kt1, nkt_prev_qb - 1)
                    if kk >= 0:
                        E.wait("scalar", "t_p", avkt_t[(pl, kk)])
                if pair:
                    pb = (kt // 2) % 2
                    v = E.inc("scalar", "sc_p",
                              lambda e, pb=pb, kt=kt: e.activation(
                                  out=big[:, et_off + kt * TB: et_off + (kt + 2) * TB],
                                  in_=T["psA"][pb][:, 0:2 * TB], func=AF.Exp))
                    exp_slot[kt] = exp_slot[kt + 1] = v
                    last_exp = v
                elif kt1 >= nkt - NSTR:
                    es = n_strad % 2
                    n_strad += 1
                    if n_strad > 2:
                        E.wait("scalar", "v_p", msk_v[n_strad - 2])
                    v = E.inc("scalar", "sc_p",
                              lambda e, bank=SBK4[kt1 % 4], es=es: e.activation(
                                  out=et_t(es), in_=ps[bank][:, 0:TB], func=AF.Exp))
                    exp_slot[kt1] = v
                    last_exp = v
                    E.wait("vector", "sc_p", v)
                    j = kt1 - (nkt - NSTR)
                    msk_v[n_strad] = E.inc("vector", "v_p",
                                           lambda e, kt1=kt1, es=es, j=j: e.tensor_tensor(
                                               out=et(kt1), in0=et_t(es),
                                               in1=g("cm_s")[:, (NSTR - 1 - j) * 128:
                                                             (NSTR - 1 - j) * 128 + TB],
                                               op=OP.mult))
                    msk_last[iq] = msk_v[n_strad]
                else:
                    v = E.inc("scalar", "sc_p",
                              lambda e, bank=SBK4[kt1 % 4], kt1=kt1: e.activation(
                                  out=et(kt1), in_=ps[bank][:, 0:TB], func=AF.Exp))
                    exp_slot[kt1] = v
                    last_exp = v
                kt += 2 if pair else 1
            prev_exp_last = last_exp
            if head > 0:
                den_tail(iq - 1)
        liq = qb * NH + NH - 1
        E.wait("tensor", "sc_p", prev_exp_last)
        for kt in range(nkt):
            emit_av_kt(liq, kt, nkt, last=(kt == nkt - 1))
        den_tail(liq)
        n_attn_qb = attn_v[qb * NH + NH - 1]
        first_wo = True
        for ot in range(nOC):
            bank = ot % 2
            io = qb * nOC + ot
            if io >= 2:
                E.wait("tensor", "v_p", r1_v[io - 2])
            for dc2 in range(nHC // 2):
                dc = 2 * dc2
                qsel = dc2 % 2
                eng = "sync" if qsel == 0 else "gpsimd"
                jq = wo_qctr[qsel]
                wo_qctr[qsel] += 1
                ws = 2 * qsel + (jq % 2)
                if jq >= 2:
                    E.wait(eng, "t_p", wo_use_t[(qsel, jq - 2)])
                E.dma(eng, f"s_wot{ws}",
                      g("wo_t")[ws][:, :].rearrange("p (t c) -> p t c", c=128),
                      g("wob_scr")[dc * 128:(dc + 2) * 128,
                                   ot * 128:(ot + 1) * 128].rearrange("(t p) c -> p t c", p=128))
                E.wait("tensor", f"s_wot{ws}", 16 * (jq // 2 + 1))
                E.wait("tensor", "v_p", n_attn_qb)
                if first_wo:
                    E.wait("tensor", "sc_p", prev_exp_last)
                    first_wo = False
                E.run("tensor", lambda e, ws=ws, dc=dc, bank=bank, st=(dc == 0): e.matmul(
                    ps[bank][:, 0:TB], g("wo_t")[ws][:, 0:128], attn(dc),
                    start=st, stop=False, skip_group_check=True))
                wo_use_t[(qsel, jq)] = E.inc("tensor", "t_p",
                                             lambda e, ws=ws, dc=dc, bank=bank,
                                             sp=(dc == nHC - 2): e.matmul(
                                                 ps[bank][:, 0:TB], g("wo_t")[ws][:, 128:256],
                                                 attn(dc + 1),
                                                 start=False, stop=sp, skip_group_check=True))
            wo_last_t = wo_use_t[(qsel, jq)]
            xs = ot % 2
            if io >= 2:
                E.wait("sync", "v_p", r1_v[io - 2])
            if qb == 0 and ot == 0:
                E.wait("sync", "sc_p", p1_sc)
            E.dma("sync", f"s_xres{xs}", xres[xs],
                  g("x_fm")[ot * 128:(ot + 1) * 128, qb * TB:(qb + 1) * TB])
            if qb == 0 and ot == 0:
                E.wait("vector", "sc_p", p1_sc)
            if qb > 0 and ot == 0:
                E.wait("vector", "sc_p", ln_prev_stat['sc'])
            E.wait("vector", "t_p", wo_last_t)
            E.wait("vector", f"s_xres{xs}", 16 * (io // 2 + 1))
            r1_v[io] = E.inc("vector", "v_p",
                             lambda e, ot=ot, bank=bank, xs=xs: e.tensor_tensor(
                                 out=r1[:, ot * TB:(ot + 1) * TB], in0=ps[bank][:, 0:TB],
                                 in1=xres[xs], op=OP.add))
        rv = {oc: r1_v[qb * nOC + oc] for oc in range(nOC)}
        emit_ln(cfg, E, T, r1, rv, r1b_t, r1sq_t, (2, 3, 4, 5), ln_prev_stat)
        for oc in range(nOC):
            s = 0
            io = qb * nOC + oc
            if io >= 1:
                E.wait("vector", "s_x2f", 16 * io)
            E.run("vector", lambda e, oc=oc: e.tensor_tensor(
                out=tmp_f, in0=r1[:, oc * TB:(oc + 1) * TB], in1=ps[4][:, 0:TB], op=OP.mult))
            E.drain("vector")
            x2f_v[io] = E.inc("vector", "v_p", lambda e, s=s: e.tensor_tensor(
                out=x2f_t[s], in0=tmp_f, in1=ps[5][:, 0:TB], op=OP.subtract))
            E.wait("gpsimd", "v_p", x2f_v[io])
            n_x2f = E.dma("gpsimd", "s_x2f",
                          g("x2f_scr")[oc * 128:(oc + 1) * 128, qb * TB:(qb + 1) * TB],
                          x2f_t[s])

    E.wait("gpsimd", "s_x2f", n_x2f)
    E.wait("gpsimd", "t_p", E.count("t_p"))
    for hc in range(nHC):
        E.dma("gpsimd", f"s_x2g{hc * NG // nHC}", xact[:, hc * S:(hc + 1) * S],
              g("x2f_scr")[hc * 128:(hc + 1) * 128, :])
    n_x2g = [E.count(f"s_x2g{i}") for i in range(NG)]
    E.wait("gpsimd", "g_p", ag_g)
    E.wait("sync", "g_p", ag_g)
    sil_sc, h_v, gu_t = {}, {}, {}
    h_fill = {0: 0, 1: 0}
    QW = min(4, nHC)
    nHQ = (nHC + QW - 1) // QW
    wcat_g = g("wcat")[0:cfg.G1, :, :].rearrange("(h a) p c -> h a p c", a=nIC)
    wcat_u = g("wcat")[cfg.G1:2 * cfg.G1, :, :].rearrange("(h a) p c -> h a p c", a=nIC)
    wgu_fill = {0: 0, 1: 0}
    for ic in range(nIC):
        for hc in range(nHC):
            iw = ic * nHC + hc
            hq, hsub = divmod(hc, QW)
            jq = ic * nHQ + hq
            sg = jq % 2
            if hsub == 0:
                w = min(QW, nHC - hc)
                if jq >= 2:
                    pic2, phq2 = divmod(jq - 2, nHQ)
                    iw2 = pic2 * nHC + min((phq2 + 1) * QW, nHC) - 1
                    E.wait("gpsimd", "t_p", gu_t[iw2])
                    E.wait("sync", "t_p", gu_t[iw2])
                E.dma("gpsimd", f"s_wg{sg}",
                      g("wg_t")[sg][:, 0:w * 128].rearrange("p (t c) -> p t c", c=128),
                      wcat_g[hc:hc + w, ic:ic + 1, :, :].rearrange("t a p c -> p (t a) c"))
                E.dma("sync", f"s_wu{sg}",
                      g("wu_t")[sg][:, 0:w * 128].rearrange("p (t c) -> p t c", c=128),
                      wcat_u[hc:hc + w, ic:ic + 1, :, :].rearrange("t a p c -> p (t a) c"))
                wgu_fill[sg] += 16
                E.wait("tensor", f"s_wg{sg}", wgu_fill[sg])
                E.wait("tensor", f"s_wu{sg}", wgu_fill[sg])
            if ic == 0 and hc % (nHC // NG) == 0:
                E.wait("tensor", f"s_x2g{hc * NG // nHC}", n_x2g[hc * NG // nHC])
            if iw == 0:
                E.wait("tensor", "v_p", E.count("v_p"))
                E.wait("tensor", "sc_p", E.count("sc_p"))
            for qb in range(nTB):
                if ic >= 1 and hc == 0:
                    E.wait("tensor", "sc_p", sil_sc[(ic - 1, qb)])
                    E.wait("tensor", "v_p", h_v[(ic - 1, qb)])
                fng = lambda e, sg=sg, hc=hc, qb=qb, hs=hsub, st=(hc == 0), sp=(hc == nHC - 1): e.matmul(
                    ps[qb][:, 0:TB], g("wg_t")[sg][:, hs * 128:(hs + 1) * 128],
                    xact[:, hc * S + qb * TB: hc * S + (qb + 1) * TB],
                    start=st, stop=sp, skip_group_check=True)
                fnu = lambda e, sg=sg, hc=hc, qb=qb, hs=hsub, st=(hc == 0), sp=(hc == nHC - 1): e.matmul(
                    ps[nTB + qb][:, 0:TB], g("wu_t")[sg][:, hs * 128:(hs + 1) * 128],
                    xact[:, hc * S + qb * TB: hc * S + (qb + 1) * TB],
                    start=st, stop=sp, skip_group_check=True)
                E.run("tensor", fng)
                if qb == nTB - 1:
                    gu_t[iw] = E.inc("tensor", "t_p", fnu)
                else:
                    E.run("tensor", fnu)
        for qb in range(nTB):
            s = qb % 2
            iq = ic * nTB + qb
            E.wait("scalar", "t_p", gu_t[ic * nHC + nHC - 1])
            if iq >= 2:
                E.wait("scalar", "v_p", h_v[((iq - 2) // nTB, (iq - 2) % nTB)])
            sil_sc[(ic, qb)] = E.inc("scalar", "sc_p",
                                     lambda e, qb=qb, s=s: e.activation(
                                         out=sil_t[s], in_=ps[qb][:, 0:TB],
                                         func=AF.Silu))
            E.wait("vector", "sc_p", sil_sc[(ic, qb)])
            if h_fill[s] >= 1:
                E.wait("vector", f"s_h{s}", 16 * h_fill[s])
            h_v[(ic, qb)] = E.inc("vector", "v_p",
                                  lambda e, qb=qb, s=s: e.tensor_tensor(
                                      out=h_t[s], in0=sil_t[s],
                                      in1=ps[nTB + qb][:, 0:TB], op=OP.mult))
            E.wait("scalar", "v_p", h_v[(ic, qb)])
            E.dma("scalar", f"s_h{s}",
                  g("h_scr")[ic * 128:(ic + 1) * 128, qb * TB:(qb + 1) * TB],
                  h_t[s])
            h_fill[s] += 1
    n_h = [E.count(f"s_h{s_}") for s_ in range(2)]

    for s_ in range(2):
        E.wait("sync", f"s_h{s_}", n_h[s_])
    E.wait("sync", "s_x2f", n_x2f)
    E.wait("sync", "t_p", E.count("t_p"))
    do_t, r2_v, wd_use_t = {}, {}, {}
    wd_qctr = {0: 0, 1: 0}
    n_out = 0
    first_p4 = True
    nICP = (nIC + 1) // 2
    h_free_t, n_hl_q = {}, {}
    for s_ in range(2):
        E.wait("gpsimd", f"s_h{s_}", n_h[s_])
    E.wait("gpsimd", "t_p", E.count("t_p"))
    for icp in range(nICP):
        ic0, icw = 2 * icp, min(2, nIC - 2 * icp)
        eng2 = "sync" if icp % 2 == 0 else "gpsimd"
        for icsub in range(icw):
            ic = ic0 + icsub
            E.dma(eng2, f"s_hl{icp % 2}", h_sl(ic),
                  g("h_scr")[ic * 128:(ic + 1) * 128, 0:TB])
    n_hl_q[0] = (E.count("s_hl0"), E.count("s_hl1"))
    for qb in range(nTB):
        n_hl = n_hl_q[qb]
        for op_ in range(nOC // 2):
            ot0, ot1 = 2 * op_, 2 * op_ + 1
            gp_pair = qb * (nOC // 2) + op_
            b0, b1 = 2 * (gp_pair % 2), 2 * (gp_pair % 2) + 1
            if gp_pair >= 2:
                qb_p, op_p = divmod(gp_pair - 2, nOC // 2)
                E.wait("tensor", "v_p", r2_v[qb_p * nOC + 2 * op_p + 1])
            for ic in range(nIC):
                qsel = ic % 2
                eng = "sync" if qsel == 0 else "gpsimd"
                jq = wd_qctr[qsel]
                wd_qctr[qsel] += 1
                ws = 3 * qsel + (jq % 3)
                if jq >= 3:
                    E.wait(eng, "t_p", wd_use_t[(qsel, jq - 3)])
                gid_d = 2 * cfg.G1 + ic * nOC + ot0
                E.dma(eng, f"s_wd{ws}",
                      g("wd_t")[ws][:, :].rearrange("p (t c) -> p t c", c=128),
                      g("wcat")[gid_d:gid_d + 2, :, :].rearrange("t p c -> p t c"))
                E.wait("tensor", f"s_wd{ws}", 16 * (jq // 3 + 1))
                if first_p4:
                    E.wait("tensor", "v_p", E.count("v_p"))
                    E.wait("tensor", "sc_p", E.count("sc_p"))
                    first_p4 = False
                if ic == 0:
                    E.wait("tensor", "s_hl0", n_hl[0])
                    E.wait("tensor", "s_hl1", n_hl[1])
                E.run("tensor", lambda e, ws=ws, ic=ic, b=b0, st=(ic == 0),
                      sp=(ic == nIC - 1): e.matmul(
                          ps[b][:, 0:TB], g("wd_t")[ws][:, 0:128], h_sl(ic),
                          start=st, stop=sp, skip_group_check=True))
                tmm = E.inc("tensor", "t_p",
                            lambda e, ws=ws, ic=ic, b=b1, st=(ic == 0),
                            sp=(ic == nIC - 1): e.matmul(
                                ps[b][:, 0:TB], g("wd_t")[ws][:, 128:256],
                                h_sl(ic),
                                start=st, stop=sp, skip_group_check=True))
                wd_use_t[(qsel, jq)] = tmm
                if op_ == nOC // 2 - 1:
                    h_free_t[ic] = tmm
            tdo_pair = tmm
            for i, (ot, bank) in enumerate(((ot0, b0), (ot1, b1))):
                io = qb * nOC + ot
                do_t[(qb, ot)] = tdo_pair
                xs = ot % 2
                if io >= 2:
                    E.wait("sync", "v_p", r2_v[io - 2])
                E.dma("sync", f"s_x2r{xs}", x2r[xs],
                      g("x2f_scr")[ot * 128:(ot + 1) * 128, qb * TB:(qb + 1) * TB])
                if qb == 0 and ot == 0:
                    E.wait("vector", "s_x2f", n_x2f)
                    E.wait("vector", "sc_p", ln_prev_stat['sc'])
                E.wait("vector", "t_p", tdo_pair)
                E.wait("vector", f"s_x2r{xs}", 16 * (io // 2 + 1))
                r2_v[io] = E.inc("vector", "v_p",
                                 lambda e, ot=ot, bank=bank, xs=xs: e.tensor_tensor(
                                     out=r2[:, ot * TB:(ot + 1) * TB], in0=ps[bank][:, 0:TB],
                                     in1=x2r[xs], op=OP.add))
        if qb + 1 < nTB:
            for icp in range(nICP):
                ic0, icw = 2 * icp, min(2, nIC - 2 * icp)
                eng2 = "sync" if icp % 2 == 0 else "gpsimd"
                E.wait(eng2, "t_p", h_free_t[ic0 + icw - 1])
                for icsub in range(icw):
                    ic = ic0 + icsub
                    E.dma(eng2, f"s_hl{icp % 2}", h_sl(ic),
                          g("h_scr")[ic * 128:(ic + 1) * 128,
                                     (qb + 1) * TB:(qb + 2) * TB])
            n_hl_q[qb + 1] = (E.count("s_hl0"), E.count("s_hl1"))
        rv = {oc: r2_v[qb * nOC + oc] for oc in range(nOC)}
        emit_ln(cfg, E, T, r2, rv, r1b_t, r1sq_t, (6, 7, 4, 5), ln_prev_stat)
        for oc in range(nOC):
            s = 0
            io = qb * nOC + oc
            if io >= 1:
                E.wait("vector", "s_out", 16 * io)
            E.run("vector", lambda e, oc=oc: e.tensor_tensor(
                out=tmp_f, in0=r2[:, oc * TB:(oc + 1) * TB], in1=ps[4][:, 0:TB], op=OP.mult))
            E.drain("vector")
            co = E.inc("vector", "v_p", lambda e, s=s: e.tensor_tensor(
                out=out_t[s], in0=tmp_f, in1=ps[5][:, 0:TB], op=OP.subtract))
            E.wait("gpsimd", "v_p", co)
            n_out = E.dma("gpsimd", "s_out",
                          g("out_fm")[oc * 128:(oc + 1) * 128, qb * TB:(qb + 1) * TB],
                          out_t[s])
    E.wait("gpsimd", "s_out", n_out)


def emit_ln(cfg, E, T, r_slab, r_v, r1b_t, r1sq_t, banks, ln_prev_stat):
    H, TB, nOC = cfg.H, cfg.TB, cfg.nOC
    g = lambda n: T[n]
    ps = T["ps"]
    bsum, bsq, ba, bb = banks
    stat_t = {}
    crs_last = 0
    for oc in range(nOC):
        s = oc % 2
        E.wait("scalar", "v_p", r_v[oc])
        if oc >= 1:
            E.wait("scalar", "t_p", stat_t[oc - 1])
        else:
            E.wait("scalar", "t_p", ln_prev_stat['t'])
        crb = E.inc("scalar", "sc_p", lambda e, oc=oc, s=s: e.copy(
            out=r1b_t(s), in_=r_slab[:, oc * TB:(oc + 1) * TB]))
        crs = E.inc("scalar", "sc_p", lambda e, oc=oc, s=s: e.activation(
            out=r1sq_t(s), in_=r_slab[:, oc * TB:(oc + 1) * TB], func=AF.Square))
        crs_last = crs
        E.wait("tensor", "sc_p", crs)
        E.run("tensor", lambda e, s=s, st=(oc == 0), sp=(oc == nOC - 1): e.matmul(
            ps[bsum][0:1, 0:TB], g("onec_s")[:, :], r1b_t(s),
            start=st, stop=sp, skip_group_check=True))
        stat_t[oc] = E.inc("tensor", "t_p",
                           lambda e, s=s, st=(oc == 0), sp=(oc == nOC - 1): e.matmul(
                               ps[bsq][0:1, 0:TB], g("onec_s")[:, :], r1sq_t(s),
                               start=st, stop=sp, skip_group_check=True))
    ln_prev_stat['t'] = stat_t[nOC - 1]
    ln_prev_stat['sc'] = crs_last
    E.wait("vector", "t_p", stat_t[nOC - 1])
    E.run("vector", lambda e: e.tensor_scalar(out=g("mn_f")[:, :], in0=ps[bsum][0:1, 0:TB],
                                              scalar1=1.0 / H, scalar2=None, op0=OP.mult))
    E.run("vector", lambda e: e.tensor_scalar(out=g("msq_f")[:, :], in0=ps[bsq][0:1, 0:TB],
                                              scalar1=1.0 / H, scalar2=None, op0=OP.mult))
    E.drain("vector")
    E.run("vector", lambda e: e.tensor_tensor(out=g("mn2_f")[:, :], in0=g("mn_f")[:, :],
                                              in1=g("mn_f")[:, :], op=OP.mult))
    E.drain("vector")
    E.run("vector", lambda e: e.tensor_tensor(
        out=g("var_f")[:, :], in0=g("msq_f")[:, :], in1=g("mn2_f")[:, :], op=OP.subtract))
    E.drain("vector")
    cvar = E.inc("vector", "v_p", lambda e: e.tensor_scalar(
        out=g("msq_f")[:, :], in0=g("var_f")[:, :], scalar1=EPS, scalar2=None, op0=OP.add))
    E.wait("scalar", "v_p", cvar)
    E.run("scalar", lambda e: e.activation(
        out=g("sd_f")[:, :], in_=g("msq_f")[:, :], func=AF.Ln))
    E.drain("scalar")
    csd = E.inc("scalar", "sc_p", lambda e: e.activation(
        out=g("rstd_f")[:, :], in_=g("sd_f")[:, :], func=AF.Exp, scale=-0.5))
    E.wait("vector", "sc_p", csd)
    E.run("vector", lambda e: e.tensor_tensor(out=g("bvec_f")[:, :], in0=g("mn_f")[:, :],
                                              in1=g("rstd_f")[:, :], op=OP.mult))
    E.drain("vector")
    E.run("vector", lambda e: e.tensor_copy(g("a_b")[:, :], g("rstd_f")[:, :]))
    E.drain("vector")
    cab = E.inc("vector", "v_p", lambda e: e.tensor_copy(g("b_b")[:, :], g("bvec_f")[:, :]))
    E.wait("tensor", "v_p", cab)
    E.run("tensor", lambda e: e.matmul(ps[ba][:, 0:TB], g("oner_s")[:, :], g("a_b")[:, :],
                                       start=True, stop=True, skip_group_check=True))
    tab = E.inc("tensor", "t_p", lambda e: e.matmul(
        ps[bb][:, 0:TB], g("oner_s")[:, :], g("b_b")[:, :], start=True, stop=True,
        skip_group_check=True))
    E.wait("vector", "t_p", tab)



def make_host_inputs(cfg: Cfg, x, w_qkv, w_out, lut, gate_idx, up_idx, down_idx):
    import ml_dtypes
    BF = ml_dtypes.bfloat16
    lutb = np.broadcast_to(np.asarray(lut, np.float32), (128, LUT_N)).copy()
    kp = np.arange(128)[:, None]
    uu = np.arange((cfg.NSTR - 1) * 128 + cfg.TB)[None, :]
    cm = (uu >= kp + (cfg.NSTR - 1) * 128).astype(BF)
    onesc = np.ones((128, 1), dtype=BF)
    onesr = np.ones((1, 128), dtype=BF)
    srcs = {
        "gidxT": np.ascontiguousarray(np.asarray(gate_idx).T.astype(np.int32)),
        "uidxT": np.ascontiguousarray(np.asarray(up_idx).T.astype(np.int32)),
        "didxT": np.ascontiguousarray(np.asarray(down_idx).T.astype(np.int32)),
    }
    in_maps = []
    for c in range(cfg.ncores):
        myidx = np.zeros((cfg.TPC, 128, 128), np.int32)
        for t in range(cfg.TPC):
            gid = c * cfg.TPC + t
            if gid >= cfg.NT:
                break
            src, rc, ct = w_tile_src(cfg, gid)
            myidx[t] = srcs[src][rc * 128:(rc + 1) * 128, ct * 128:(ct + 1) * 128]
        in_maps.append({
            "x_fm": np.ascontiguousarray(np.asarray(x[c]).T).astype(np.float32, copy=False),
            "wqkv": np.asarray(w_qkv, np.float32),
            "wout": np.asarray(w_out, np.float32),
            "myidx": myidx,
            "lutb": lutb,
            "cmask": cm,
            "onesc": onesc,
            "onesr": onesr,
        })
    return in_maps


def gather_output(cfg, results):
    return np.stack([np.ascontiguousarray(r["out_fm"].T) for r in results], axis=0)


def numpy_reference(cfg: Cfg, x, w_qkv, w_out, lut, gate_idx, up_idx, down_idx):
    def layernorm(v):
        m = v.mean(-1, keepdims=True)
        s = v.var(-1, keepdims=True)
        return (v - m) / np.sqrt(s + EPS)
    def softmax(s):
        m = s.max(-1, keepdims=True)
        e = np.exp(s - m)
        return e / e.sum(-1, keepdims=True)
    x = np.asarray(x, np.float64)
    w_qkv = np.asarray(w_qkv, np.float64)
    w_out = np.asarray(w_out, np.float64)
    lut = np.asarray(lut, np.float64)
    B, S, H = x.shape
    NH, D = cfg.NH, cfg.D
    qkv = x @ w_qkv
    qkv = qkv.reshape(B, S, 3, NH, D)
    q, k, v = qkv[:, :, 0], qkv[:, :, 1], qkv[:, :, 2]
    q = q / np.sqrt(np.float64(D))
    scores = np.einsum("bqhd,bkhd->bhqk", q, k)
    mask = np.arange(S)[:, None] >= np.arange(S)[None, :]
    scores = np.where(mask[None, None], scores, -1e10)
    a = softmax(scores)
    o = np.einsum("bhqk,bkhd->bqhd", a, v).reshape(B, S, H)
    x1 = layernorm(x + o @ w_out)
    gate = x1 @ lut[gate_idx].T
    up = x1 @ lut[up_idx].T
    hh = (gate / (1 + np.exp(-gate))) * up
    x2 = layernorm(x1 + hh @ lut[down_idx].T)
    return x2.astype(np.float32)



def kernel(L0_attn_x, L0_attn_w_qkv, L0_attn_w_out, L0_ffn_lut,
           L0_ffn_gate_idx, L0_ffn_up_idx, L0_ffn_down_idx):
    cfg = FULL
    in_maps = make_host_inputs(cfg, L0_attn_x, L0_attn_w_qkv, L0_attn_w_out,
                               L0_ffn_lut, L0_ffn_gate_idx, L0_ffn_up_idx,
                               L0_ffn_down_idx)
    nc = build_program(cfg)
    res = run_bass_kernel_spmd(nc, in_maps, core_ids=list(range(cfg.ncores)))
    return gather_output(cfg, res.results)


# version 17
# speedup vs baseline: 1.0171x; 1.0171x over previous
import math
import numpy as np
from contextlib import ExitStack
from dataclasses import dataclass

import concourse.bass as bass
import concourse.mybir as mybir
from concourse.bass_utils import run_bass_kernel_spmd

F32 = mybir.dt.float32
BF16 = mybir.dt.bfloat16
I32 = mybir.dt.int32
AF = mybir.ActivationFunctionType
OP = mybir.AluOpType

LUT_N = 16
EPS = 1e-5
MRUN = 8


@dataclass
class Cfg:
    S: int = 2048
    H: int = 2048
    NH: int = 16
    I: int = 5504
    TB: int = 512
    ncores: int = 8

    @property
    def D(self): return self.H // self.NH
    @property
    def F3(self): return 3 * self.H
    @property
    def nHC(self): return self.H // 128
    @property
    def nOC(self): return self.H // 128
    @property
    def nIC(self): return self.I // 128
    @property
    def nTB(self): return self.S // self.TB
    @property
    def nTT(self): return self.S // 128
    @property
    def NSTR(self): return self.TB // 128
    @property
    def DBW(self): return min(512, self.H)
    @property
    def nDB(self): return self.H // self.DBW
    @property
    def G1(self): return self.nHC * self.nIC
    @property
    def NT(self): return 2 * self.G1 + self.nIC * self.nOC
    @property
    def TPC0(self): return (self.NT + self.ncores - 1) // self.ncores
    @property
    def CSZ(self): return self.nIC
    @property
    def KAG(self): return (self.TPC0 + self.CSZ - 1) // self.CSZ
    @property
    def TPC(self): return self.KAG * self.CSZ
    @property
    def qscale(self): return 1.0 / math.sqrt(self.D)


FULL = Cfg()
SMALL = Cfg(S=256, H=256, NH=2, I=384, TB=128, ncores=8)
MED = Cfg(S=1024, H=512, NH=4, I=768, TB=512, ncores=8)

ENGINES = ("sync", "gpsimd", "vector", "scalar", "tensor")


class Emitter:
    def __init__(self, nc, ctx):
        self.nc = nc
        self.ctx = ctx
        self.q = {e: [] for e in ENGINES}
        self.sems = {}

    def _sem(self, name):
        if name not in self.sems:
            h = self.ctx.enter_context(self.nc.semaphore(name))
            self.sems[name] = [h, 0]
        return self.sems[name]

    def count(self, name):
        return self.sems[name][1] if name in self.sems else 0

    def run(self, eng, fn):
        self.q[eng].append(fn)

    def inc(self, eng, sem_name, fn, k=1):
        s = self._sem(sem_name)
        s[1] += k
        self.q[eng].append(lambda e, fn=fn, h=s[0], k=k: fn(e).then_inc(h, k))
        return s[1]

    def dma(self, eng, sem_name, out, in_):
        return self.inc(eng, sem_name, lambda e, o=out, i=in_: e.dma_start(out=o, in_=i), k=16)

    def wait(self, eng, sem_name, val):
        if val is None or val <= 0:
            return
        s = self._sem(sem_name)
        assert val <= s[1], f"wait {sem_name}>={val} but only {s[1]} emitted"
        self.q[eng].append(lambda e, h=s[0], v=val: e.wait_ge(h, v))

    def drain(self, eng):
        self.q[eng].append(lambda e: e.drain())

    def finalize(self, block):
        for eng in ENGINES:
            dec = getattr(block, eng)
            q = self.q[eng]
            def make(qq):
                def body(e):
                    for fn in qq:
                        fn(e)
                return body
            dec(make(q))


def w_tile_src(cfg, gid):
    if gid < cfg.G1:
        return "gidxT", gid // cfg.nIC, gid % cfg.nIC
    if gid < 2 * cfg.G1:
        return "uidxT", (gid - cfg.G1) // cfg.nIC, (gid - cfg.G1) % cfg.nIC
    g2 = gid - 2 * cfg.G1
    return "didxT", g2 // cfg.nOC, g2 % cfg.nOC


def build_program(cfg: Cfg):
    nc = bass.Bass("TRN2", target_bir_lowering=False)
    ctx = ExitStack()
    S, H, I, TB = cfg.S, cfg.H, cfg.I, cfg.TB
    nHC, nOC, nIC, nTT, NSTR = cfg.nHC, cfg.nOC, cfg.nIC, cfg.nTT, cfg.NSTR
    DBW = cfg.DBW

    x_fm = nc.dram_tensor("x_fm", [H, S], F32, kind="ExternalInput")
    wqkv = nc.dram_tensor("wqkv", [H, cfg.F3], F32, kind="ExternalInput")
    wout = nc.dram_tensor("wout", [H, H], F32, kind="ExternalInput")
    myidx = nc.dram_tensor("myidx", [cfg.TPC, 128, 128], I32, kind="ExternalInput")
    lutb = nc.dram_tensor("lutb", [128, LUT_N], F32, kind="ExternalInput")
    cmask = nc.dram_tensor("cmask", [128, (NSTR - 1) * 128 + TB], BF16, kind="ExternalInput")
    onesc = nc.dram_tensor("onesc", [128, 1], BF16, kind="ExternalInput")
    onesr = nc.dram_tensor("onesr", [1, 128], BF16, kind="ExternalInput")
    out_fm = nc.dram_tensor("out_fm", [H, S], F32, kind="ExternalOutput")

    q_scr = nc.dram_tensor("q_scr", [H, S], BF16)
    k_scr = nc.dram_tensor("k_scr", [H, S], BF16)
    v_scr = nc.dram_tensor("v_scr", [S, H], BF16)
    wob_scr = nc.dram_tensor("wob_scr", [H, H], BF16)
    x2f_scr = nc.dram_tensor("x2f_scr", [H, S], F32)
    x2b_scr = nc.dram_tensor("x2b_scr", [H, S], BF16)
    h_scr = nc.dram_tensor("h_scr", [I, S], BF16)
    wcat_my = nc.dram_tensor("wcat_my", [cfg.TPC, 128, 128], BF16)
    if cfg.KAG > 1:
        assert nHC % 4 == 0 and nOC % 2 == 0
    wc = [nc.dram_tensor(f"wcat{k}", [cfg.ncores * cfg.CSZ, 128, 128], BF16,
                         addr_space="Shared") for k in range(cfg.KAG)]

    sb = lambda name, shape, dt: ctx.enter_context(nc.sbuf_tensor(name, shape, dt))

    with ctx:
        xact = sb("xact", [128, nHC * S], BF16)
        p2cols = nTT * TB + 4 * S + 9 * TB
        bigcols = max(p2cols, nIC * TB, 21 * MRUN * 128)
        big = sb("big", [128, bigcols], BF16)
        r1b_sl = sb("r1b_sl", [128, TB], BF16)
        r1sq_sl = sb("r1sq_sl", [128, TB], BF16)
        
        slab16 = sb("slab16", [128, nHC * max(DBW, TB)], BF16)
        arf_cols = max(8 * 128 + 2 * DBW + 2 * S, nOC * TB + 4 * TB)
        arf = sb("arf", [128, arf_cols], F32)

        lut_s = sb("lut_s", [128, LUT_N], F32)
        cm_s = sb("cm_s", [128, (NSTR - 1) * 128 + TB], BF16)
        onec_s = sb("onec_s", [128, 1], BF16)
        oner_s = sb("oner_s", [1, 128], BF16)
        wqk_b = [sb(f"wqk_b{s}", [128, 128], BF16) for s in range(8)]
        qk_e = [sb(f"qk_e{s}", [128, TB], BF16) for s in range(4)]
        ve_t = [sb(f"ve_t{s}", [128, DBW], BF16) for s in range(1)]
        wo_t = [sb(f"wo_t{s}", [128, 2 * 128], BF16) for s in range(4)]
        wg_t = [sb(f"wg_t{s}", [128, 4 * 128], BF16) for s in range(2)]
        wu_t = [sb(f"wu_t{s}", [128, 4 * 128], BF16) for s in range(2)]
        wd_t = [sb(f"wd_t{s}", [128, 2 * 128], BF16) for s in range(6)]
        mn_f = sb("mn_f", [1, TB], F32)
        msq_f = sb("msq_f", [1, TB], F32)
        mn2_f = sb("mn2_f", [1, TB], F32)
        var_f = sb("var_f", [1, TB], F32)
        sd_f = sb("sd_f", [1, TB], F32)
        rstd_f = sb("rstd_f", [1, TB], F32)
        den_f = mn_f
        rden_f = msq_f
        bvec_f = sd_f
        rden_b = sb("rden_b", [1, TB], BF16)
        a_b = sb("a_b", [1, TB], BF16)
        b_b = sb("b_b", [1, TB], BF16)
        

        BKW = 512
        psA = [ctx.enter_context(nc.psum_tensor(f"psA{i}", [128, 2 * BKW], F32)) for i in range(2)]
        psS = [ctx.enter_context(nc.psum_tensor(f"psS{i}", [128, BKW], F32)) for i in range(4)]
        ps = [psA[0][:, 0:BKW], psA[0][:, BKW:2 * BKW],
              psS[0][:, :], psS[1][:, :], psS[2][:, :], psS[3][:, :],
              psA[1][:, 0:BKW], psA[1][:, BKW:2 * BKW]]

        T = dict(
            x_fm=x_fm, wqkv=wqkv, wout=wout, myidx=myidx, lutb=lutb, cmask=cmask,
            onesc=onesc, onesr=onesr, out_fm=out_fm, q_scr=q_scr, k_scr=k_scr,
            v_scr=v_scr, wob_scr=wob_scr, x2f_scr=x2f_scr, x2b_scr=x2b_scr,
            h_scr=h_scr, wcat_my=wcat_my, wc=wc,
            xact=xact, big=big, slab16=slab16, arf=arf,
            r1b_sl=r1b_sl, r1sq_sl=r1sq_sl,

            lut_s=lut_s, cm_s=cm_s,
            onec_s=onec_s, oner_s=oner_s, wqk_b=wqk_b, qk_e=qk_e,
            ve_t=ve_t, wo_t=wo_t, wg_t=wg_t, wu_t=wu_t, wd_t=wd_t,
            den_f=den_f, rden_f=rden_f, rden_b=rden_b,
            mn_f=mn_f, msq_f=msq_f, mn2_f=mn2_f, var_f=var_f, sd_f=sd_f,
            rstd_f=rstd_f, bvec_f=bvec_f, a_b=a_b, b_b=b_b, ps=ps, psA=psA,
        )
        with nc.Block() as block:
            E = Emitter(nc, ctx)
            build_phases(cfg, E, T)
            E.finalize(block)
    return nc


def build_phases(cfg, E, T):
    S, H, I, TB = cfg.S, cfg.H, cfg.I, cfg.TB
    nHC, nOC, nIC, nTB, nTT, NSTR = (cfg.nHC, cfg.nOC, cfg.nIC, cfg.nTB,
                                     cfg.nTT, cfg.NSTR)
    NH, DBW, nDB = cfg.NH, cfg.DBW, cfg.nDB
    PSW = 512
    ps = T["ps"]
    big, xact, slab16, arf = T["big"], T["xact"], T["slab16"], T["arf"]
    g = lambda n: T[n]

    et_off = 0
    ksl_off = nTT * TB
    vsl_off = ksl_off + 2 * S
    qsl_off = vsl_off + 2 * S
    et = lambda kt: big[:, et_off + kt * TB: et_off + (kt + 1) * TB]
    k_sl = lambda s, w: big[:, ksl_off + s * S: ksl_off + s * S + w]
    v_sl = lambda s, w: big[:, vsl_off + s * S: vsl_off + s * S + w]
    q_sl = lambda s: big[:, qsl_off + s * TB: qsl_off + (s + 1) * TB]
    r1b_t = lambda s: T["r1b_sl"][:, 0:TB]
    r1sq_t = lambda s: T["r1sq_sl"][:, 0:TB]
    tail0 = qsl_off + 2 * TB
    et_t = lambda s: big[:, tail0 + s * TB: tail0 + (s + 1) * TB]
    h_sl = lambda ic: big[:, ic * TB:(ic + 1) * TB]
    bc_s = big[:, tail0 + 2 * TB: tail0 + 3 * TB]
    sil_t = [big[:, tail0 + (3 + s) * TB: tail0 + (4 + s) * TB] for s in range(2)]
    h_t = [big[:, tail0 + (5 + s) * TB: tail0 + (6 + s) * TB] for s in range(2)]
    attn = lambda dc: slab16[:, dc * TB:(dc + 1) * TB]
    wv_b = lambda hc: slab16[:, hc * DBW:(hc + 1) * DBW]
    wqk_f = [arf[:, s * 128:(s + 1) * 128] for s in range(8)]
    wv_f = [arf[:, 8 * 128 + s * DBW: 8 * 128 + (s + 1) * DBW] for s in range(2)]
    xs_f = [arf[:, 8 * 128 + 2 * DBW + s * S: 8 * 128 + 2 * DBW + (s + 1) * S] for s in range(2)]
    r1 = arf[:, 0:nOC * TB]
    xres = [arf[:, nOC * TB + s * TB: nOC * TB + (s + 1) * TB] for s in range(2)]
    tmp_f = arf[:, nOC * TB + 2 * TB: nOC * TB + 3 * TB]
    x2f_t = [arf[:, nOC * TB + 3 * TB: nOC * TB + 4 * TB]]
    r2, x2r, out_t = r1, xres, x2f_t

    E.dma("sync", "s_lut", g("lut_s")[:, :], g("lutb")[:, :])
    E.dma("sync", "s_cm", g("cm_s")[:, :], g("cmask")[:, :])
    E.dma("sync", "s_onc", g("onec_s")[:, :], g("onesc")[:, :])
    E.dma("sync", "s_onr", g("oner_s")[:, :], g("onesr")[:, :])

    NG = min(4, nHC)
    runs = []
    t = 0
    while t < cfg.TPC:
        w = min(MRUN, cfg.TPC - t)
        runs.append((t, w))
        t += w
    MP = MRUN * 128
    m_part = [big[:, j * MP:(j + 1) * MP] for j in range(16)]
    m_idxb = [big[:, (16 + s) * MP:(17 + s) * MP] for s in range(3)]
    m_out = [big[:, (19 + s) * MP:(20 + s) * MP] for s in range(2)]
    E.wait("vector", "s_lut", 16)
    acc_v = {}

    def mat_dma(ri):
        t0, w = runs[ri]
        slot = ri % 3
        if ri >= 3:
            E.wait("gpsimd", "v_p", acc_v[ri - 3])
        E.dma("gpsimd", f"s_midx{slot}",
              m_idxb[slot][:, 0:w * 128].rearrange("p (t c) -> p t c", c=128),
              g("myidx")[t0:t0 + w, :, :].rearrange("t p c -> p t c"))

    for r0 in range(min(3, len(runs))):
        mat_dma(r0)
    for ri, (t0, w) in enumerate(runs):
        slot = ri % 3
        E.wait("vector", f"s_midx{slot}", 16 * (ri // 3 + 1))
        for j in range(LUT_N):
            E.run("vector", lambda e, w=w, j=j, slot=slot: e.tensor_scalar(
                out=m_part[j][:, 0:w * 128], in0=m_idxb[slot][:, 0:w * 128],
                scalar1=float(j), scalar2=g("lut_s")[:, j:j + 1],
                op0=OP.is_equal, op1=OP.mult))
        E.drain("vector")
        step = 1
        while step < 16:
            for j0 in range(0, 16, 2 * step):
                last = (2 * step >= 16)
                if last:
                    if ri >= 2:
                        E.wait("vector", f"s_wmy{ri % 2}", 16 * ((ri - 2) // 2 + 1))
                    fn = lambda e, w=w, ri=ri: e.tensor_tensor(
                        out=m_out[ri % 2][:, 0:w * 128], in0=m_part[0][:, 0:w * 128],
                        in1=m_part[8][:, 0:w * 128], op=OP.add)
                    acc_v[ri] = E.inc("vector", "v_p", fn)
                else:
                    fn = lambda e, j0=j0, s=step, w=w: e.tensor_tensor(
                        out=m_part[j0][:, 0:w * 128], in0=m_part[j0][:, 0:w * 128],
                        in1=m_part[j0 + s][:, 0:w * 128], op=OP.add)
                    E.run("vector", fn)
            E.drain("vector")
            step *= 2
        if ri + 3 < len(runs):
            mat_dma(ri + 3)
        E.wait("gpsimd", "v_p", acc_v[ri])
        E.dma("gpsimd", f"s_wmy{ri % 2}",
              g("wcat_my")[t0:t0 + w, :, :].rearrange("t p c -> p t c"),
              m_out[ri % 2][:, 0:w * 128].rearrange("p (t c) -> p t c", c=128))
        for k in range(cfg.KAG):
            if ((k + 1) * cfg.CSZ - 1) // MRUN == ri:
                E.wait("gpsimd", "s_wmy0", 16 * (ri // 2 + 1))
                if ri >= 1:
                    E.wait("gpsimd", "s_wmy1", 16 * ((ri - 1) // 2 + 1))
                ag_g = E.inc("gpsimd", "g_p", lambda e, k=k: e.collective_compute(
                    "AllGather", OP.bypass, replica_groups=[list(range(cfg.ncores))],
                    ins=[g("wcat_my")[k * cfg.CSZ:(k + 1) * cfg.CSZ, :, :].opt()],
                    outs=[g("wc")[k].ap().opt()]))
    mat_done_v = acc_v[len(runs) - 1]
    n_wmy = [E.count("s_wmy0"), E.count("s_wmy1")]
    for i4 in range(NG):
        E.dma("gpsimd", "s_wob", g("wob_scr")[i4 * (H // NG):(i4 + 1) * (H // NG), :],
              g("wout")[i4 * (H // NG):(i4 + 1) * (H // NG), :])
    n_wob = E.count("s_wob")

    qkcast_s, lastmm_t, stopmm_t, qkevac_s = {}, {}, {}, {}
    xcast_s = {}
    for ft in range(2 * nHC):
        isq = ft < nHC
        for hc in range(nHC):
            it = ft * nHC + hc
            slot = it % 8
            if ft == 0:
                xsl = hc % 2
                if hc >= 2:
                    E.wait("sync", "sc_p", xcast_s[hc - 2])
                E.dma("sync", f"s_xs{xsl}", xs_f[xsl],
                      g("x_fm")[hc * 128:(hc + 1) * 128, :])
                E.wait("scalar", f"s_xs{xsl}", 16 * (hc // 2 + 1))
                xcast_s[hc] = E.inc("scalar", "sc_p",
                                    lambda e, s=xsl, k=hc: e.copy(
                                        out=xact[:, k * S:(k + 1) * S], in_=xs_f[s]))
            if it >= 8:
                E.wait("sync", "sc_p", qkcast_s[it - 8])
            E.dma("sync", f"s_wqk{slot}", wqk_f[slot],
                  g("wqkv")[hc * 128:(hc + 1) * 128, ft * 128:(ft + 1) * 128])
            if it >= 8:
                E.wait("scalar", "t_p", lastmm_t[it - 8])
            E.wait("scalar", f"s_wqk{slot}", 16 * (it // 8 + 1))
            qkcast_s[it] = E.inc("scalar", "sc_p",
                                 lambda e, s=slot: e.copy(out=g("wqk_b")[s][:, :],
                                                          in_=wqk_f[s]))
            E.wait("tensor", "sc_p", qkcast_s[it])
            if ft == 0:
                E.wait("tensor", "sc_p", xcast_s[hc])
            if hc == 0 and ft >= 2:
                E.wait("tensor", "sc_p", qkevac_s[(ft - 2, nTB - 1)])
            bk0 = (ft % 2) * nTB
            for tb in range(nTB):
                fn = lambda e, s=slot, hc=hc, tb=tb, bk0=bk0, st=(hc == 0), sp=(hc == nHC - 1): e.matmul(
                    ps[bk0 + tb][:, 0:TB], g("wqk_b")[s][:, :],
                    xact[:, hc * S + tb * TB: hc * S + (tb + 1) * TB],
                    start=st, stop=sp, skip_group_check=True)
                if hc == nHC - 1:
                    stopmm_t[(ft, tb)] = E.inc("tensor", "t_p", fn)
                    lastmm_t[it] = stopmm_t[(ft, tb)]
                elif tb == nTB - 1:
                    lastmm_t[it] = E.inc("tensor", "t_p", fn)
                else:
                    E.run("tensor", fn)
        for tb in range(nTB):
            es = (ft * nTB + tb) % 4
            idx = ft * nTB + tb
            E.wait("scalar", "t_p", stopmm_t[(ft, tb)])
            if idx >= 4:
                E.wait("scalar", f"s_qkst{es}", 16 * (idx // 4))
            sc = cfg.qscale if isq else 1.0
            qkevac_s[(ft, tb)] = E.inc("scalar", "sc_p",
                                       lambda e, tb=tb, es=es, sc=sc, bk0=(ft % 2) * nTB:
                                       e.activation(
                                           out=g("qk_e")[es][:, :], in_=ps[bk0 + tb][:, 0:TB],
                                           func=AF.Copy, scale=sc))
            E.wait("scalar", "sc_p", qkevac_s[(ft, tb)])
            dst = g("q_scr") if isq else g("k_scr")
            r0 = (ft if isq else ft - nHC) * 128
            E.dma("scalar", f"s_qkst{es}", dst[r0:r0 + 128, tb * TB:(tb + 1) * TB],
                  g("qk_e")[es][:, :])
    n_qkst = [E.count(f"s_qkst{e_}") for e_ in range(4)]


    wvc_s, vstop_t, vevac_s = {}, {}, {}
    for db in range(nDB):
        for hc in range(nHC):
            it = db * nHC + hc
            slot = hc % 2
            if it >= 2:
                E.wait("sync", "sc_p", wvc_s[it - 2])
            E.dma("sync", f"s_wv{slot}", wv_f[slot],
                  g("wqkv")[hc * 128:(hc + 1) * 128, 2 * H + db * DBW: 2 * H + (db + 1) * DBW])
            if db >= 1:
                E.wait("scalar", "t_p", vstop_t[(db - 1, nTT - 1)])
            E.wait("scalar", f"s_wv{slot}", 16 * (it // 2 + 1))
            wvc_s[it] = E.inc("scalar", "sc_p",
                              lambda e, hc=hc, s=slot: e.copy(out=wv_b(hc), in_=wv_f[s]))
        for tt in range(nTT):
            bank = 4 + (tt % 2)
            it = db * nTT + tt
            if it == 0:
                E.wait("tensor", "sc_p", qkevac_s[(2 * nHC - 1, nTB - 1)])
            if it >= 2:
                E.wait("tensor", "sc_p", vevac_s[it - 2])
            E.wait("tensor", "sc_p", wvc_s[db * nHC + nHC - 1])
            for hc in range(nHC):
                fn = lambda e, hc=hc, tt=tt, bank=bank, st=(hc == 0), sp=(hc == nHC - 1): e.matmul(
                    ps[bank][:, 0:DBW],
                    xact[:, hc * S + tt * 128: hc * S + (tt + 1) * 128],
                    wv_b(hc), start=st, stop=sp, skip_group_check=True)
                if hc == nHC - 1:
                    vstop_t[(db, tt)] = E.inc("tensor", "t_p", fn)
                else:
                    E.run("tensor", fn)
            E.wait("scalar", "t_p", vstop_t[(db, tt)])
            if it >= 1:
                E.wait("scalar", "s_vst", 16 * it)
            es = 0
            vevac_s[it] = E.inc("scalar", "sc_p",
                                lambda e, bank=bank, es=es: e.copy(out=g("ve_t")[es][:, :],
                                                                   in_=ps[bank][:, 0:DBW]))
            E.wait("scalar", "sc_p", vevac_s[it])
            E.dma("scalar", "s_vst",
                  g("v_scr")[tt * 128:(tt + 1) * 128, db * DBW:(db + 1) * DBW],
                  g("ve_t")[es][:, :])
    n_vst = E.count("s_vst")
    p1_sc = E.count("sc_p")

    E.wait("vector", "s_cm", 16)
    E.wait("scalar", "v_p", mat_done_v)
    E.wait("scalar", "s_wmy0", n_wmy[0])
    E.wait("scalar", "s_wmy1", n_wmy[1])
    E.wait("sync", "v_p", mat_done_v)
    E.wait("sync", "s_wmy0", n_wmy[0])
    E.wait("sync", "s_wmy1", n_wmy[1])
    E.wait("gpsimd", "s_wob", n_wob)
    E.wait("sync", "s_wob", n_wob)
    n_strad = 0
    msk_v, avden_t, attn_v, denev_v, bcs_v = {}, {}, {}, {}, {}
    wo_use_t, r1_v, x2f_v = {}, {}, {}
    wo_qctr = {0: 0, 1: 0}
    vsl_fill = {}
    prev_exp_last = 0
    n_x2f = 0
    ln_prev_stat = {'t': 0, 'sc': 0}
    wo_last_t = 0
    avkt_t = {}
    msk_last = {}
    for qb in range(nTB):
        nkt = (qb + 1) * NSTR
        nkt_prev_qb = qb * NSTR

        def den_tail(iq2):
            head2 = iq2 % NH
            bav = 2 + (head2 % 2)
            bden = 4 + (head2 % 2)
            E.wait("scalar", "t_p", avden_t[iq2])
            E.run("scalar", lambda e, bden=bden: e.activation(
                out=g("rden_f")[:, :], in_=ps[bden][0:1, 0:TB], func=AF.Ln))
            E.drain("scalar")
            denev_v[iq2] = E.inc("scalar", "sc_p", lambda e: e.activation(
                out=g("rden_b")[:, :], in_=g("rden_f")[:, :], func=AF.Exp, scale=-1.0))
            E.wait("tensor", "sc_p", denev_v[iq2])
            E.wait("tensor", "s_onr", 16)
            if iq2 > 0:
                E.wait("tensor", "v_p", bcs_v[iq2 - 1])
            bcb = 2 + ((head2 + 1) % 2)
            if iq2 >= 1:
                E.wait("tensor", "v_p", attn_v[iq2 - 1])
            tbc = E.inc("tensor", "t_p", lambda e, bcb=bcb: e.matmul(
                ps[bcb][:, 0:TB], g("oner_s")[:, :], g("rden_b")[:, :],
                start=True, stop=True, skip_group_check=True))
            E.wait("vector", "t_p", tbc)
            bcs_v[iq2] = E.inc("vector", "v_p",
                               lambda e, bcb=bcb: e.tensor_copy(bc_s, ps[bcb][:, 0:TB]))
            E.drain("vector")
            if head2 == 0 and iq2 >= NH:
                E.wait("vector", "t_p", wo_last_t)
            attn_v[iq2] = E.inc("vector", "v_p",
                                lambda e, head2=head2, bav=bav: e.tensor_tensor(
                                    out=attn(head2), in0=ps[bav][:, 0:TB], in1=bc_s,
                                    op=OP.mult))
            E.drain("vector")

        def emit_av_kt(iq2, kt, nkt2, last):
            head2 = iq2 % NH
            bav = 2 + (head2 % 2)
            bden = 4 + (head2 % 2)
            sl2 = head2 % 2
            if kt == 0:
                if head2 <= 1 and iq2 >= NH and (iq2 // NH) == qb:
                    E.wait("tensor", "v_p", x2f_v[(qb - 1) * nOC + nOC - 1])
                if iq2 >= 2:
                    E.wait("tensor", "v_p", attn_v[iq2 - 2])
                    E.wait("tensor", "sc_p", denev_v[iq2 - 2])
                if iq2 >= 1:
                    E.wait("tensor", "v_p", bcs_v[iq2 - 1])
                if iq2 in msk_last:
                    E.wait("tensor", "v_p", msk_last[iq2])
                E.wait("tensor", f"s_vsl{sl2}", vsl_fill[iq2])
                E.wait("tensor", "s_onc", 16)
            E.run("tensor", lambda e, sl2=sl2, kt=kt, nkt2=nkt2, bav=bav,
                  st=(kt == 0), sp=last: e.matmul(
                      ps[bav][:, 0:TB], v_sl(sl2, nkt2 * 128)[:, kt * 128:(kt + 1) * 128],
                      et(kt), start=st, stop=sp, skip_group_check=True))
            avkt_t[(iq2, kt)] = E.inc("tensor", "t_p",
                                      lambda e, kt=kt, bden=bden, st=(kt == 0), sp=last: e.matmul(
                                          ps[bden][0:1, 0:TB], g("onec_s")[:, :], et(kt),
                                          start=st, stop=sp, skip_group_check=True))
            if last:
                avden_t[iq2] = avkt_t[(iq2, kt)]

        for head in range(NH):
            sl = head % 2
            iq = qb * NH + head
            if iq >= 2:
                E.wait("sync", "t_p", avden_t[iq - 2])
            idxq = head * nTB + qb
            idxk = (nHC + head) * nTB + qb
            E.wait("sync", f"s_qkst{idxq % 4}", 16 * (idxq // 4 + 1))
            E.wait("sync", f"s_qkst{idxk % 4}", 16 * (idxk // 4 + 1))
            idxv = (nDB - 1) * nTT + (NSTR * (qb + 1) - 1)
            E.wait("sync", "s_vst", 16 * (idxv + 1))
            E.dma("sync", f"s_qsl{sl}", q_sl(sl),
                  g("q_scr")[head * 128:(head + 1) * 128, qb * TB:(qb + 1) * TB])
            E.dma("sync", f"s_ksl{sl}", k_sl(sl, nkt * 128),
                  g("k_scr")[head * 128:(head + 1) * 128, 0:nkt * 128])
            E.dma("sync", f"s_vsl{sl}",
                  v_sl(sl, nkt * 128).rearrange("p (t c) -> p t c", c=128),
                  g("v_scr")[0:nkt * 128, head * 128:(head + 1) * 128]
                  .rearrange("(t p) c -> p t c", p=128))
            nfill = 16 * (iq // 2 + 1)
            vsl_fill[iq] = nfill
            E.wait("tensor", f"s_qsl{sl}", nfill)
            E.wait("tensor", f"s_ksl{sl}", nfill)
            exp_slot = {}
            PAIR_OK = (TB == 512)
            SBK4 = (0, 1, 6, 7)
            ns_end = nkt - NSTR
            last_exp = 0
            kt = 0
            while kt < nkt:
                pair = (kt + 1 < ns_end and kt % 2 == 0 and PAIR_OK)
                subkts = (kt, kt + 1) if pair else (kt,)
                for k2 in subkts:
                    c = k2
                    bank = SBK4[c % 4]
                    if c >= 4:
                        E.wait("tensor", "sc_p", exp_slot[c - 4])
                    elif iq > 0:
                        E.wait("tensor", "sc_p", prev_exp_last)
                    else:
                        E.wait("tensor", "sc_p", p1_sc)
                    if c == 0 and head == 0 and qb > 0:
                        E.wait("tensor", "v_p", r1_v[(qb - 1) * nOC + nOC - 1])
                    tsc = E.inc("tensor", "t_p", lambda e, sl=sl, k2=k2, bank=bank, nkt=nkt: e.matmul(
                        ps[bank][:, 0:TB], k_sl(sl, nkt * 128)[:, k2 * 128:(k2 + 1) * 128],
                        q_sl(sl), start=True, stop=True, skip_group_check=True))
                    if head > 0:
                        emit_av_kt(iq - 1, k2, nkt, last=(k2 == nkt - 1))
                kt1 = subkts[-1]
                E.wait("scalar", "t_p", tsc)
                if head > 0:
                    E.wait("scalar", "t_p", avkt_t[(iq - 1, kt1)])
                elif qb > 0:
                    pl = qb * NH - 1
                    kk = min(kt1, nkt_prev_qb - 1)
                    if kk >= 0:
                        E.wait("scalar", "t_p", avkt_t[(pl, kk)])
                if pair:
                    pb = (kt // 2) % 2
                    v = E.inc("scalar", "sc_p",
                              lambda e, pb=pb, kt=kt: e.activation(
                                  out=big[:, et_off + kt * TB: et_off + (kt + 2) * TB],
                                  in_=T["psA"][pb][:, 0:2 * TB], func=AF.Exp))
                    exp_slot[kt] = exp_slot[kt + 1] = v
                    last_exp = v
                elif kt1 >= nkt - NSTR:
                    es = n_strad % 2
                    n_strad += 1
                    if n_strad > 2:
                        E.wait("scalar", "v_p", msk_v[n_strad - 2])
                    v = E.inc("scalar", "sc_p",
                              lambda e, bank=SBK4[kt1 % 4], es=es: e.activation(
                                  out=et_t(es), in_=ps[bank][:, 0:TB], func=AF.Exp))
                    exp_slot[kt1] = v
                    last_exp = v
                    E.wait("vector", "sc_p", v)
                    j = kt1 - (nkt - NSTR)
                    msk_v[n_strad] = E.inc("vector", "v_p",
                                           lambda e, kt1=kt1, es=es, j=j: e.tensor_tensor(
                                               out=et(kt1), in0=et_t(es),
                                               in1=g("cm_s")[:, (NSTR - 1 - j) * 128:
                                                             (NSTR - 1 - j) * 128 + TB],
                                               op=OP.mult))
                    msk_last[iq] = msk_v[n_strad]
                else:
                    v = E.inc("scalar", "sc_p",
                              lambda e, bank=SBK4[kt1 % 4], kt1=kt1: e.activation(
                                  out=et(kt1), in_=ps[bank][:, 0:TB], func=AF.Exp))
                    exp_slot[kt1] = v
                    last_exp = v
                kt += 2 if pair else 1
            prev_exp_last = last_exp
            if head > 0:
                den_tail(iq - 1)
        liq = qb * NH + NH - 1
        E.wait("tensor", "sc_p", prev_exp_last)
        for kt in range(nkt):
            emit_av_kt(liq, kt, nkt, last=(kt == nkt - 1))
        den_tail(liq)
        n_attn_qb = attn_v[qb * NH + NH - 1]
        first_wo = True
        for ot in range(nOC):
            bank = ot % 2
            io = qb * nOC + ot
            if io >= 2:
                E.wait("tensor", "v_p", r1_v[io - 2])
            for dc2 in range(nHC // 2):
                dc = 2 * dc2
                qsel = dc2 % 2
                eng = "sync" if qsel == 0 else "gpsimd"
                jq = wo_qctr[qsel]
                wo_qctr[qsel] += 1
                ws = 2 * qsel + (jq % 2)
                if jq >= 2:
                    E.wait(eng, "t_p", wo_use_t[(qsel, jq - 2)])
                E.dma(eng, f"s_wot{ws}",
                      g("wo_t")[ws][:, :].rearrange("p (t c) -> p t c", c=128),
                      g("wob_scr")[dc * 128:(dc + 2) * 128,
                                   ot * 128:(ot + 1) * 128].rearrange("(t p) c -> p t c", p=128))
                E.wait("tensor", f"s_wot{ws}", 16 * (jq // 2 + 1))
                E.wait("tensor", "v_p", n_attn_qb)
                if first_wo:
                    E.wait("tensor", "sc_p", prev_exp_last)
                    first_wo = False
                E.run("tensor", lambda e, ws=ws, dc=dc, bank=bank, st=(dc == 0): e.matmul(
                    ps[bank][:, 0:TB], g("wo_t")[ws][:, 0:128], attn(dc),
                    start=st, stop=False, skip_group_check=True))
                wo_use_t[(qsel, jq)] = E.inc("tensor", "t_p",
                                             lambda e, ws=ws, dc=dc, bank=bank,
                                             sp=(dc == nHC - 2): e.matmul(
                                                 ps[bank][:, 0:TB], g("wo_t")[ws][:, 128:256],
                                                 attn(dc + 1),
                                                 start=False, stop=sp, skip_group_check=True))
            wo_last_t = wo_use_t[(qsel, jq)]
            xs = ot % 2
            if io >= 2:
                E.wait("sync", "v_p", r1_v[io - 2])
            if qb == 0 and ot == 0:
                E.wait("sync", "sc_p", p1_sc)
            E.dma("sync", f"s_xres{xs}", xres[xs],
                  g("x_fm")[ot * 128:(ot + 1) * 128, qb * TB:(qb + 1) * TB])
            if qb == 0 and ot == 0:
                E.wait("vector", "sc_p", p1_sc)
            if qb > 0 and ot == 0:
                E.wait("vector", "sc_p", ln_prev_stat['sc'])
            E.wait("vector", "t_p", wo_last_t)
            E.wait("vector", f"s_xres{xs}", 16 * (io // 2 + 1))
            r1_v[io] = E.inc("vector", "v_p",
                             lambda e, ot=ot, bank=bank, xs=xs: e.tensor_tensor(
                                 out=r1[:, ot * TB:(ot + 1) * TB], in0=ps[bank][:, 0:TB],
                                 in1=xres[xs], op=OP.add))
        rv = {oc: r1_v[qb * nOC + oc] for oc in range(nOC)}
        emit_ln(cfg, E, T, r1, rv, r1b_t, r1sq_t, (2, 3, 4, 5), ln_prev_stat)
        for oc in range(nOC):
            s = 0
            io = qb * nOC + oc
            if io >= 1:
                E.wait("vector", "s_x2f", 16 * io)
            E.run("vector", lambda e, oc=oc: e.tensor_tensor(
                out=tmp_f, in0=r1[:, oc * TB:(oc + 1) * TB], in1=ps[4][:, 0:TB], op=OP.mult))
            E.drain("vector")
            x2f_v[io] = E.inc("vector", "v_p", lambda e, s=s: e.tensor_tensor(
                out=x2f_t[s], in0=tmp_f, in1=ps[5][:, 0:TB], op=OP.subtract))
            E.wait("gpsimd", "v_p", x2f_v[io])
            n_x2f = E.dma("gpsimd", "s_x2f",
                          g("x2f_scr")[oc * 128:(oc + 1) * 128, qb * TB:(qb + 1) * TB],
                          x2f_t[s])

    E.wait("gpsimd", "s_x2f", n_x2f)
    E.wait("gpsimd", "t_p", E.count("t_p"))
    for hc in range(nHC):
        E.dma("gpsimd", f"s_x2g{hc * NG // nHC}", xact[:, hc * S:(hc + 1) * S],
              g("x2f_scr")[hc * 128:(hc + 1) * 128, :])
    n_x2g = [E.count(f"s_x2g{i}") for i in range(NG)]
    E.wait("gpsimd", "g_p", ag_g)
    E.wait("sync", "g_p", ag_g)
    sil_sc, h_v, gu_t = {}, {}, {}
    h_fill = {0: 0, 1: 0}
    QW = min(4, nHC)
    nHQ = (nHC + QW - 1) // QW
    wcv = [g("wc")[k][:, :, :].rearrange("(h a) p c -> h a p c", a=nIC)
           for k in range(cfg.KAG)]
    BL = cfg.ncores * nIC
    wgu_fill = {0: 0, 1: 0}
    for ic in range(nIC):
        for hc in range(nHC):
            iw = ic * nHC + hc
            hq, hsub = divmod(hc, QW)
            jq = ic * nHQ + hq
            sg = jq % 2
            if hsub == 0:
                w = min(QW, nHC - hc)
                if jq >= 2:
                    pic2, phq2 = divmod(jq - 2, nHQ)
                    iw2 = pic2 * nHC + min((phq2 + 1) * QW, nHC) - 1
                    E.wait("gpsimd", "t_p", gu_t[iw2])
                    E.wait("sync", "t_p", gu_t[iw2])
                kg, hg = divmod(hc, cfg.ncores)
                ku, hu = divmod(nHC + hc, cfg.ncores)
                E.dma("gpsimd", f"s_wg{sg}",
                      g("wg_t")[sg][:, 0:w * 128].rearrange("p (t c) -> p t c", c=128),
                      wcv[kg][hg:hg + w, ic:ic + 1, :, :].rearrange("t a p c -> p (t a) c"))
                E.dma("sync", f"s_wu{sg}",
                      g("wu_t")[sg][:, 0:w * 128].rearrange("p (t c) -> p t c", c=128),
                      wcv[ku][hu:hu + w, ic:ic + 1, :, :].rearrange("t a p c -> p (t a) c"))
                wgu_fill[sg] += 16
                E.wait("tensor", f"s_wg{sg}", wgu_fill[sg])
                E.wait("tensor", f"s_wu{sg}", wgu_fill[sg])
            if ic == 0 and hc % (nHC // NG) == 0:
                E.wait("tensor", f"s_x2g{hc * NG // nHC}", n_x2g[hc * NG // nHC])
            if iw == 0:
                E.wait("tensor", "v_p", E.count("v_p"))
                E.wait("tensor", "sc_p", E.count("sc_p"))
            for qb in range(nTB):
                if ic >= 1 and hc == 0:
                    E.wait("tensor", "sc_p", sil_sc[(ic - 1, qb)])
                    E.wait("tensor", "v_p", h_v[(ic - 1, qb)])
                fng = lambda e, sg=sg, hc=hc, qb=qb, hs=hsub, st=(hc == 0), sp=(hc == nHC - 1): e.matmul(
                    ps[qb][:, 0:TB], g("wg_t")[sg][:, hs * 128:(hs + 1) * 128],
                    xact[:, hc * S + qb * TB: hc * S + (qb + 1) * TB],
                    start=st, stop=sp, skip_group_check=True)
                fnu = lambda e, sg=sg, hc=hc, qb=qb, hs=hsub, st=(hc == 0), sp=(hc == nHC - 1): e.matmul(
                    ps[nTB + qb][:, 0:TB], g("wu_t")[sg][:, hs * 128:(hs + 1) * 128],
                    xact[:, hc * S + qb * TB: hc * S + (qb + 1) * TB],
                    start=st, stop=sp, skip_group_check=True)
                E.run("tensor", fng)
                if qb == nTB - 1:
                    gu_t[iw] = E.inc("tensor", "t_p", fnu)
                else:
                    E.run("tensor", fnu)
        for qb in range(nTB):
            s = qb % 2
            iq = ic * nTB + qb
            E.wait("scalar", "t_p", gu_t[ic * nHC + nHC - 1])
            if iq >= 2:
                E.wait("scalar", "v_p", h_v[((iq - 2) // nTB, (iq - 2) % nTB)])
            sil_sc[(ic, qb)] = E.inc("scalar", "sc_p",
                                     lambda e, qb=qb, s=s: e.activation(
                                         out=sil_t[s], in_=ps[qb][:, 0:TB],
                                         func=AF.Silu))
            E.wait("vector", "sc_p", sil_sc[(ic, qb)])
            if h_fill[s] >= 1:
                E.wait("vector", f"s_h{s}", 16 * h_fill[s])
            h_v[(ic, qb)] = E.inc("vector", "v_p",
                                  lambda e, qb=qb, s=s: e.tensor_tensor(
                                      out=h_t[s], in0=sil_t[s],
                                      in1=ps[nTB + qb][:, 0:TB], op=OP.mult))
            E.wait("scalar", "v_p", h_v[(ic, qb)])
            E.dma("scalar", f"s_h{s}",
                  g("h_scr")[ic * 128:(ic + 1) * 128, qb * TB:(qb + 1) * TB],
                  h_t[s])
            h_fill[s] += 1
    n_h = [E.count(f"s_h{s_}") for s_ in range(2)]

    for s_ in range(2):
        E.wait("sync", f"s_h{s_}", n_h[s_])
    E.wait("sync", "s_x2f", n_x2f)
    E.wait("sync", "t_p", E.count("t_p"))
    do_t, r2_v, wd_use_t = {}, {}, {}
    wd_qctr = {0: 0, 1: 0}
    n_out = 0
    first_p4 = True
    nICP = (nIC + 1) // 2
    h_free_t, n_hl_q = {}, {}
    for s_ in range(2):
        E.wait("gpsimd", f"s_h{s_}", n_h[s_])
    E.wait("gpsimd", "t_p", E.count("t_p"))
    for icp in range(nICP):
        ic0, icw = 2 * icp, min(2, nIC - 2 * icp)
        eng2 = "sync" if icp % 2 == 0 else "gpsimd"
        for icsub in range(icw):
            ic = ic0 + icsub
            E.dma(eng2, f"s_hl{icp % 2}", h_sl(ic),
                  g("h_scr")[ic * 128:(ic + 1) * 128, 0:TB])
    n_hl_q[0] = (E.count("s_hl0"), E.count("s_hl1"))
    for qb in range(nTB):
        n_hl = n_hl_q[qb]
        for op_ in range(nOC // 2):
            ot0, ot1 = 2 * op_, 2 * op_ + 1
            gp_pair = qb * (nOC // 2) + op_
            b0, b1 = 2 * (gp_pair % 2), 2 * (gp_pair % 2) + 1
            if gp_pair >= 2:
                qb_p, op_p = divmod(gp_pair - 2, nOC // 2)
                E.wait("tensor", "v_p", r2_v[qb_p * nOC + 2 * op_p + 1])
            for ic in range(nIC):
                qsel = ic % 2
                eng = "sync" if qsel == 0 else "gpsimd"
                jq = wd_qctr[qsel]
                wd_qctr[qsel] += 1
                ws = 3 * qsel + (jq % 3)
                if jq >= 3:
                    E.wait(eng, "t_p", wd_use_t[(qsel, jq - 3)])
                gid_d = 2 * cfg.G1 + ic * nOC + ot0
                kd, ld = divmod(gid_d, BL)
                E.dma(eng, f"s_wd{ws}",
                      g("wd_t")[ws][:, :].rearrange("p (t c) -> p t c", c=128),
                      g("wc")[kd][ld:ld + 2, :, :].rearrange("t p c -> p t c"))
                E.wait("tensor", f"s_wd{ws}", 16 * (jq // 3 + 1))
                if first_p4:
                    E.wait("tensor", "v_p", E.count("v_p"))
                    E.wait("tensor", "sc_p", E.count("sc_p"))
                    first_p4 = False
                if ic == 0:
                    E.wait("tensor", "s_hl0", n_hl[0])
                    E.wait("tensor", "s_hl1", n_hl[1])
                E.run("tensor", lambda e, ws=ws, ic=ic, b=b0, st=(ic == 0),
                      sp=(ic == nIC - 1): e.matmul(
                          ps[b][:, 0:TB], g("wd_t")[ws][:, 0:128], h_sl(ic),
                          start=st, stop=sp, skip_group_check=True))
                tmm = E.inc("tensor", "t_p",
                            lambda e, ws=ws, ic=ic, b=b1, st=(ic == 0),
                            sp=(ic == nIC - 1): e.matmul(
                                ps[b][:, 0:TB], g("wd_t")[ws][:, 128:256],
                                h_sl(ic),
                                start=st, stop=sp, skip_group_check=True))
                wd_use_t[(qsel, jq)] = tmm
                if op_ == nOC // 2 - 1:
                    h_free_t[ic] = tmm
            tdo_pair = tmm
            for i, (ot, bank) in enumerate(((ot0, b0), (ot1, b1))):
                io = qb * nOC + ot
                do_t[(qb, ot)] = tdo_pair
                xs = ot % 2
                if io >= 2:
                    E.wait("sync", "v_p", r2_v[io - 2])
                E.dma("sync", f"s_x2r{xs}", x2r[xs],
                      g("x2f_scr")[ot * 128:(ot + 1) * 128, qb * TB:(qb + 1) * TB])
                if qb == 0 and ot == 0:
                    E.wait("vector", "s_x2f", n_x2f)
                    E.wait("vector", "sc_p", ln_prev_stat['sc'])
                E.wait("vector", "t_p", tdo_pair)
                E.wait("vector", f"s_x2r{xs}", 16 * (io // 2 + 1))
                r2_v[io] = E.inc("vector", "v_p",
                                 lambda e, ot=ot, bank=bank, xs=xs: e.tensor_tensor(
                                     out=r2[:, ot * TB:(ot + 1) * TB], in0=ps[bank][:, 0:TB],
                                     in1=x2r[xs], op=OP.add))
        if qb + 1 < nTB:
            for icp in range(nICP):
                ic0, icw = 2 * icp, min(2, nIC - 2 * icp)
                eng2 = "sync" if icp % 2 == 0 else "gpsimd"
                E.wait(eng2, "t_p", h_free_t[ic0 + icw - 1])
                for icsub in range(icw):
                    ic = ic0 + icsub
                    E.dma(eng2, f"s_hl{icp % 2}", h_sl(ic),
                          g("h_scr")[ic * 128:(ic + 1) * 128,
                                     (qb + 1) * TB:(qb + 2) * TB])
            n_hl_q[qb + 1] = (E.count("s_hl0"), E.count("s_hl1"))
        rv = {oc: r2_v[qb * nOC + oc] for oc in range(nOC)}
        emit_ln(cfg, E, T, r2, rv, r1b_t, r1sq_t, (6, 7, 4, 5), ln_prev_stat)
        for oc in range(nOC):
            s = 0
            io = qb * nOC + oc
            if io >= 1:
                E.wait("vector", "s_out", 16 * io)
            E.run("vector", lambda e, oc=oc: e.tensor_tensor(
                out=tmp_f, in0=r2[:, oc * TB:(oc + 1) * TB], in1=ps[4][:, 0:TB], op=OP.mult))
            E.drain("vector")
            co = E.inc("vector", "v_p", lambda e, s=s: e.tensor_tensor(
                out=out_t[s], in0=tmp_f, in1=ps[5][:, 0:TB], op=OP.subtract))
            E.wait("gpsimd", "v_p", co)
            n_out = E.dma("gpsimd", "s_out",
                          g("out_fm")[oc * 128:(oc + 1) * 128, qb * TB:(qb + 1) * TB],
                          out_t[s])
    E.wait("gpsimd", "s_out", n_out)


def emit_ln(cfg, E, T, r_slab, r_v, r1b_t, r1sq_t, banks, ln_prev_stat):
    H, TB, nOC = cfg.H, cfg.TB, cfg.nOC
    g = lambda n: T[n]
    ps = T["ps"]
    bsum, bsq, ba, bb = banks
    stat_t = {}
    crs_last = 0
    for oc in range(nOC):
        s = oc % 2
        E.wait("scalar", "v_p", r_v[oc])
        if oc >= 1:
            E.wait("scalar", "t_p", stat_t[oc - 1])
        else:
            E.wait("scalar", "t_p", ln_prev_stat['t'])
        crb = E.inc("scalar", "sc_p", lambda e, oc=oc, s=s: e.copy(
            out=r1b_t(s), in_=r_slab[:, oc * TB:(oc + 1) * TB]))
        crs = E.inc("scalar", "sc_p", lambda e, oc=oc, s=s: e.activation(
            out=r1sq_t(s), in_=r_slab[:, oc * TB:(oc + 1) * TB], func=AF.Square))
        crs_last = crs
        E.wait("tensor", "sc_p", crs)
        E.run("tensor", lambda e, s=s, st=(oc == 0), sp=(oc == nOC - 1): e.matmul(
            ps[bsum][0:1, 0:TB], g("onec_s")[:, :], r1b_t(s),
            start=st, stop=sp, skip_group_check=True))
        stat_t[oc] = E.inc("tensor", "t_p",
                           lambda e, s=s, st=(oc == 0), sp=(oc == nOC - 1): e.matmul(
                               ps[bsq][0:1, 0:TB], g("onec_s")[:, :], r1sq_t(s),
                               start=st, stop=sp, skip_group_check=True))
    ln_prev_stat['t'] = stat_t[nOC - 1]
    ln_prev_stat['sc'] = crs_last
    E.wait("vector", "t_p", stat_t[nOC - 1])
    E.run("vector", lambda e: e.tensor_scalar(out=g("mn_f")[:, :], in0=ps[bsum][0:1, 0:TB],
                                              scalar1=1.0 / H, scalar2=None, op0=OP.mult))
    E.run("vector", lambda e: e.tensor_scalar(out=g("msq_f")[:, :], in0=ps[bsq][0:1, 0:TB],
                                              scalar1=1.0 / H, scalar2=None, op0=OP.mult))
    E.drain("vector")
    E.run("vector", lambda e: e.tensor_tensor(out=g("mn2_f")[:, :], in0=g("mn_f")[:, :],
                                              in1=g("mn_f")[:, :], op=OP.mult))
    E.drain("vector")
    E.run("vector", lambda e: e.tensor_tensor(
        out=g("var_f")[:, :], in0=g("msq_f")[:, :], in1=g("mn2_f")[:, :], op=OP.subtract))
    E.drain("vector")
    cvar = E.inc("vector", "v_p", lambda e: e.tensor_scalar(
        out=g("msq_f")[:, :], in0=g("var_f")[:, :], scalar1=EPS, scalar2=None, op0=OP.add))
    E.wait("scalar", "v_p", cvar)
    E.run("scalar", lambda e: e.activation(
        out=g("sd_f")[:, :], in_=g("msq_f")[:, :], func=AF.Ln))
    E.drain("scalar")
    csd = E.inc("scalar", "sc_p", lambda e: e.activation(
        out=g("rstd_f")[:, :], in_=g("sd_f")[:, :], func=AF.Exp, scale=-0.5))
    E.wait("vector", "sc_p", csd)
    E.run("vector", lambda e: e.tensor_tensor(out=g("bvec_f")[:, :], in0=g("mn_f")[:, :],
                                              in1=g("rstd_f")[:, :], op=OP.mult))
    E.drain("vector")
    E.run("vector", lambda e: e.tensor_copy(g("a_b")[:, :], g("rstd_f")[:, :]))
    E.drain("vector")
    cab = E.inc("vector", "v_p", lambda e: e.tensor_copy(g("b_b")[:, :], g("bvec_f")[:, :]))
    E.wait("tensor", "v_p", cab)
    E.run("tensor", lambda e: e.matmul(ps[ba][:, 0:TB], g("oner_s")[:, :], g("a_b")[:, :],
                                       start=True, stop=True, skip_group_check=True))
    tab = E.inc("tensor", "t_p", lambda e: e.matmul(
        ps[bb][:, 0:TB], g("oner_s")[:, :], g("b_b")[:, :], start=True, stop=True,
        skip_group_check=True))
    E.wait("vector", "t_p", tab)



def make_host_inputs(cfg: Cfg, x, w_qkv, w_out, lut, gate_idx, up_idx, down_idx):
    import ml_dtypes
    BF = ml_dtypes.bfloat16
    lutb = np.broadcast_to(np.asarray(lut, np.float32), (128, LUT_N)).copy()
    kp = np.arange(128)[:, None]
    uu = np.arange((cfg.NSTR - 1) * 128 + cfg.TB)[None, :]
    cm = (uu >= kp + (cfg.NSTR - 1) * 128).astype(BF)
    onesc = np.ones((128, 1), dtype=BF)
    onesr = np.ones((1, 128), dtype=BF)
    srcs = {
        "gidxT": np.ascontiguousarray(np.asarray(gate_idx).T.astype(np.int32)),
        "uidxT": np.ascontiguousarray(np.asarray(up_idx).T.astype(np.int32)),
        "didxT": np.ascontiguousarray(np.asarray(down_idx).T.astype(np.int32)),
    }
    in_maps = []
    for c in range(cfg.ncores):
        myidx = np.zeros((cfg.TPC, 128, 128), np.int32)
        for t in range(cfg.TPC):
            gid = (t // cfg.CSZ) * (cfg.ncores * cfg.CSZ) + c * cfg.CSZ + (t % cfg.CSZ)
            if gid >= cfg.NT:
                continue
            src, rc, ct = w_tile_src(cfg, gid)
            myidx[t] = srcs[src][rc * 128:(rc + 1) * 128, ct * 128:(ct + 1) * 128]
        in_maps.append({
            "x_fm": np.ascontiguousarray(np.asarray(x[c]).T).astype(np.float32, copy=False),
            "wqkv": np.asarray(w_qkv, np.float32),
            "wout": np.asarray(w_out, np.float32),
            "myidx": myidx,
            "lutb": lutb,
            "cmask": cm,
            "onesc": onesc,
            "onesr": onesr,
        })
    return in_maps


def gather_output(cfg, results):
    return np.stack([np.ascontiguousarray(r["out_fm"].T) for r in results], axis=0)


def numpy_reference(cfg: Cfg, x, w_qkv, w_out, lut, gate_idx, up_idx, down_idx):
    def layernorm(v):
        m = v.mean(-1, keepdims=True)
        s = v.var(-1, keepdims=True)
        return (v - m) / np.sqrt(s + EPS)
    def softmax(s):
        m = s.max(-1, keepdims=True)
        e = np.exp(s - m)
        return e / e.sum(-1, keepdims=True)
    x = np.asarray(x, np.float64)
    w_qkv = np.asarray(w_qkv, np.float64)
    w_out = np.asarray(w_out, np.float64)
    lut = np.asarray(lut, np.float64)
    B, S, H = x.shape
    NH, D = cfg.NH, cfg.D
    qkv = x @ w_qkv
    qkv = qkv.reshape(B, S, 3, NH, D)
    q, k, v = qkv[:, :, 0], qkv[:, :, 1], qkv[:, :, 2]
    q = q / np.sqrt(np.float64(D))
    scores = np.einsum("bqhd,bkhd->bhqk", q, k)
    mask = np.arange(S)[:, None] >= np.arange(S)[None, :]
    scores = np.where(mask[None, None], scores, -1e10)
    a = softmax(scores)
    o = np.einsum("bhqk,bkhd->bqhd", a, v).reshape(B, S, H)
    x1 = layernorm(x + o @ w_out)
    gate = x1 @ lut[gate_idx].T
    up = x1 @ lut[up_idx].T
    hh = (gate / (1 + np.exp(-gate))) * up
    x2 = layernorm(x1 + hh @ lut[down_idx].T)
    return x2.astype(np.float32)



def kernel(L0_attn_x, L0_attn_w_qkv, L0_attn_w_out, L0_ffn_lut,
           L0_ffn_gate_idx, L0_ffn_up_idx, L0_ffn_down_idx):
    cfg = FULL
    in_maps = make_host_inputs(cfg, L0_attn_x, L0_attn_w_qkv, L0_attn_w_out,
                               L0_ffn_lut, L0_ffn_gate_idx, L0_ffn_up_idx,
                               L0_ffn_down_idx)
    nc = build_program(cfg)
    res = run_bass_kernel_spmd(nc, in_maps, core_ids=list(range(cfg.ncores)))
    return gather_output(cfg, res.results)


# version 18
# speedup vs baseline: 1.0204x; 1.0032x over previous
import math
import numpy as np
from contextlib import ExitStack
from dataclasses import dataclass

import concourse.bass as bass
import concourse.mybir as mybir
from concourse.bass_utils import run_bass_kernel_spmd

F32 = mybir.dt.float32
BF16 = mybir.dt.bfloat16
I32 = mybir.dt.int32
AF = mybir.ActivationFunctionType
OP = mybir.AluOpType

LUT_N = 16
EPS = 1e-5
MRUN = 8


@dataclass
class Cfg:
    S: int = 2048
    H: int = 2048
    NH: int = 16
    I: int = 5504
    TB: int = 512
    ncores: int = 8

    @property
    def D(self): return self.H // self.NH
    @property
    def F3(self): return 3 * self.H
    @property
    def nHC(self): return self.H // 128
    @property
    def nOC(self): return self.H // 128
    @property
    def nIC(self): return self.I // 128
    @property
    def nTB(self): return self.S // self.TB
    @property
    def nTT(self): return self.S // 128
    @property
    def NSTR(self): return self.TB // 128
    @property
    def DBW(self): return min(512, self.H)
    @property
    def nDB(self): return self.H // self.DBW
    @property
    def G1(self): return self.nHC * self.nIC
    @property
    def NT(self): return 2 * self.G1 + self.nIC * self.nOC
    @property
    def TPC0(self): return (self.NT + self.ncores - 1) // self.ncores
    @property
    def CSZ(self): return self.nIC
    @property
    def KAG(self): return (self.TPC0 + self.CSZ - 1) // self.CSZ
    @property
    def TPC(self): return self.KAG * self.CSZ
    @property
    def qscale(self): return 1.0 / math.sqrt(self.D)


FULL = Cfg()
SMALL = Cfg(S=256, H=256, NH=2, I=384, TB=128, ncores=8)
MED = Cfg(S=1024, H=512, NH=4, I=768, TB=512, ncores=8)

ENGINES = ("sync", "gpsimd", "vector", "scalar", "tensor")


class Emitter:
    def __init__(self, nc, ctx):
        self.nc = nc
        self.ctx = ctx
        self.q = {e: [] for e in ENGINES}
        self.sems = {}

    def _sem(self, name):
        if name not in self.sems:
            h = self.ctx.enter_context(self.nc.semaphore(name))
            self.sems[name] = [h, 0]
        return self.sems[name]

    def count(self, name):
        return self.sems[name][1] if name in self.sems else 0

    def run(self, eng, fn):
        self.q[eng].append(fn)

    def inc(self, eng, sem_name, fn, k=1):
        s = self._sem(sem_name)
        s[1] += k
        self.q[eng].append(lambda e, fn=fn, h=s[0], k=k: fn(e).then_inc(h, k))
        return s[1]

    def dma(self, eng, sem_name, out, in_):
        return self.inc(eng, sem_name, lambda e, o=out, i=in_: e.dma_start(out=o, in_=i), k=16)

    def wait(self, eng, sem_name, val):
        if val is None or val <= 0:
            return
        s = self._sem(sem_name)
        assert val <= s[1], f"wait {sem_name}>={val} but only {s[1]} emitted"
        self.q[eng].append(lambda e, h=s[0], v=val: e.wait_ge(h, v))

    def drain(self, eng):
        self.q[eng].append(lambda e: e.drain())

    def finalize(self, block):
        for eng in ENGINES:
            dec = getattr(block, eng)
            q = self.q[eng]
            def make(qq):
                def body(e):
                    for fn in qq:
                        fn(e)
                return body
            dec(make(q))


def w_tile_src(cfg, gid):
    if gid < cfg.G1:
        return "gidxT", gid // cfg.nIC, gid % cfg.nIC
    if gid < 2 * cfg.G1:
        return "uidxT", (gid - cfg.G1) // cfg.nIC, (gid - cfg.G1) % cfg.nIC
    g2 = gid - 2 * cfg.G1
    return "didxT", g2 // cfg.nOC, g2 % cfg.nOC


def build_program(cfg: Cfg):
    nc = bass.Bass("TRN2", target_bir_lowering=False)
    ctx = ExitStack()
    S, H, I, TB = cfg.S, cfg.H, cfg.I, cfg.TB
    nHC, nOC, nIC, nTT, NSTR = cfg.nHC, cfg.nOC, cfg.nIC, cfg.nTT, cfg.NSTR
    DBW = cfg.DBW

    x_fm = nc.dram_tensor("x_fm", [H, S], F32, kind="ExternalInput")
    wqkv = nc.dram_tensor("wqkv", [H, cfg.F3], F32, kind="ExternalInput")
    wout = nc.dram_tensor("wout", [H, H], F32, kind="ExternalInput")
    myidx = nc.dram_tensor("myidx", [cfg.TPC, 128, 128], I32, kind="ExternalInput")
    lutb = nc.dram_tensor("lutb", [128, LUT_N], F32, kind="ExternalInput")
    cmask = nc.dram_tensor("cmask", [128, (NSTR - 1) * 128 + TB], BF16, kind="ExternalInput")
    onesc = nc.dram_tensor("onesc", [128, 1], BF16, kind="ExternalInput")
    onesr = nc.dram_tensor("onesr", [1, 128], BF16, kind="ExternalInput")
    out_fm = nc.dram_tensor("out_fm", [H, S], F32, kind="ExternalOutput")

    q_scr = nc.dram_tensor("q_scr", [H, S], BF16)
    k_scr = nc.dram_tensor("k_scr", [H, S], BF16)
    v_scr = nc.dram_tensor("v_scr", [S, H], BF16)
    wob_scr = nc.dram_tensor("wob_scr", [H, H], BF16)
    x2f_scr = nc.dram_tensor("x2f_scr", [H, S], F32)
    x2b_scr = nc.dram_tensor("x2b_scr", [H, S], BF16)
    h_scr = nc.dram_tensor("h_scr", [I, S], BF16)
    wcat_my = nc.dram_tensor("wcat_my", [cfg.TPC, 128, 128], BF16)
    if cfg.KAG > 1:
        assert nHC % 4 == 0 and nOC % 2 == 0
    wc = [nc.dram_tensor(f"wcat{k}", [cfg.ncores * cfg.CSZ, 128, 128], BF16,
                         addr_space="Shared") for k in range(cfg.KAG)]

    sb = lambda name, shape, dt: ctx.enter_context(nc.sbuf_tensor(name, shape, dt))

    with ctx:
        xact = sb("xact", [128, nHC * S], BF16)
        p2cols = nTT * TB + 4 * S + 9 * TB
        bigcols = max(p2cols, nIC * TB, 21 * MRUN * 128)
        big = sb("big", [128, bigcols], BF16)
        r1b_sl = sb("r1b_sl", [128, TB], BF16)
        r1sq_sl = sb("r1sq_sl", [128, TB], BF16)
        
        slab16 = sb("slab16", [128, nHC * max(DBW, TB)], BF16)
        arf_cols = max(8 * 128 + 2 * DBW + 2 * S, nOC * TB + 4 * TB)
        arf = sb("arf", [128, arf_cols], F32)

        lut_s = sb("lut_s", [128, LUT_N], F32)
        cm_s = sb("cm_s", [128, (NSTR - 1) * 128 + TB], BF16)
        onec_s = sb("onec_s", [128, 1], BF16)
        oner_s = sb("oner_s", [1, 128], BF16)
        wqk_b = [sb(f"wqk_b{s}", [128, 128], BF16) for s in range(8)]
        qk_e = [sb(f"qk_e{s}", [128, TB], BF16) for s in range(4)]
        ve_t = [sb(f"ve_t{s}", [128, DBW], BF16) for s in range(1)]
        wo_t = [sb(f"wo_t{s}", [128, 2 * 128], BF16) for s in range(4)]
        wg_t = [sb(f"wg_t{s}", [128, 4 * 128], BF16) for s in range(2)]
        wu_t = [sb(f"wu_t{s}", [128, 4 * 128], BF16) for s in range(2)]
        wd_t = [sb(f"wd_t{s}", [128, 2 * 128], BF16) for s in range(6)]
        mn_f = sb("mn_f", [1, TB], F32)
        msq_f = sb("msq_f", [1, TB], F32)
        mn2_f = sb("mn2_f", [1, TB], F32)
        var_f = sb("var_f", [1, TB], F32)
        sd_f = sb("sd_f", [1, TB], F32)
        rstd_f = sb("rstd_f", [1, TB], F32)
        den_f = mn_f
        rden_f = msq_f
        bvec_f = sd_f
        rden_b = sb("rden_b", [1, TB], BF16)
        a_b = sb("a_b", [1, TB], BF16)
        b_b = sb("b_b", [1, TB], BF16)
        

        BKW = 512
        psA = [ctx.enter_context(nc.psum_tensor(f"psA{i}", [128, 2 * BKW], F32)) for i in range(2)]
        psS = [ctx.enter_context(nc.psum_tensor(f"psS{i}", [128, BKW], F32)) for i in range(4)]
        ps = [psA[0][:, 0:BKW], psA[0][:, BKW:2 * BKW],
              psS[0][:, :], psS[1][:, :], psS[2][:, :], psS[3][:, :],
              psA[1][:, 0:BKW], psA[1][:, BKW:2 * BKW]]

        T = dict(
            x_fm=x_fm, wqkv=wqkv, wout=wout, myidx=myidx, lutb=lutb, cmask=cmask,
            onesc=onesc, onesr=onesr, out_fm=out_fm, q_scr=q_scr, k_scr=k_scr,
            v_scr=v_scr, wob_scr=wob_scr, x2f_scr=x2f_scr, x2b_scr=x2b_scr,
            h_scr=h_scr, wcat_my=wcat_my, wc=wc,
            xact=xact, big=big, slab16=slab16, arf=arf,
            r1b_sl=r1b_sl, r1sq_sl=r1sq_sl,

            lut_s=lut_s, cm_s=cm_s,
            onec_s=onec_s, oner_s=oner_s, wqk_b=wqk_b, qk_e=qk_e,
            ve_t=ve_t, wo_t=wo_t, wg_t=wg_t, wu_t=wu_t, wd_t=wd_t,
            den_f=den_f, rden_f=rden_f, rden_b=rden_b,
            mn_f=mn_f, msq_f=msq_f, mn2_f=mn2_f, var_f=var_f, sd_f=sd_f,
            rstd_f=rstd_f, bvec_f=bvec_f, a_b=a_b, b_b=b_b, ps=ps, psA=psA,
        )
        with nc.Block() as block:
            E = Emitter(nc, ctx)
            build_phases(cfg, E, T)
            E.finalize(block)
    return nc


def build_phases(cfg, E, T):
    S, H, I, TB = cfg.S, cfg.H, cfg.I, cfg.TB
    nHC, nOC, nIC, nTB, nTT, NSTR = (cfg.nHC, cfg.nOC, cfg.nIC, cfg.nTB,
                                     cfg.nTT, cfg.NSTR)
    NH, DBW, nDB = cfg.NH, cfg.DBW, cfg.nDB
    PSW = 512
    ps = T["ps"]
    big, xact, slab16, arf = T["big"], T["xact"], T["slab16"], T["arf"]
    g = lambda n: T[n]

    et_off = 0
    ksl_off = nTT * TB
    vsl_off = ksl_off + 2 * S
    qsl_off = vsl_off + 2 * S
    et = lambda kt: big[:, et_off + kt * TB: et_off + (kt + 1) * TB]
    k_sl = lambda s, w: big[:, ksl_off + s * S: ksl_off + s * S + w]
    v_sl = lambda s, w: big[:, vsl_off + s * S: vsl_off + s * S + w]
    q_sl = lambda s: big[:, qsl_off + s * TB: qsl_off + (s + 1) * TB]
    r1b_t = lambda s: T["r1b_sl"][:, 0:TB]
    r1sq_t = lambda s: T["r1sq_sl"][:, 0:TB]
    tail0 = qsl_off + 2 * TB
    et_t = lambda s: big[:, tail0 + s * TB: tail0 + (s + 1) * TB]
    h_sl = lambda ic: big[:, ic * TB:(ic + 1) * TB]
    bc_s = big[:, tail0 + 2 * TB: tail0 + 3 * TB]
    sil_t = [big[:, tail0 + (3 + s) * TB: tail0 + (4 + s) * TB] for s in range(2)]
    h_t = [big[:, tail0 + (5 + s) * TB: tail0 + (6 + s) * TB] for s in range(2)]
    attn = lambda dc: slab16[:, dc * TB:(dc + 1) * TB]
    wv_b = lambda hc: slab16[:, hc * DBW:(hc + 1) * DBW]
    wqk_f = [arf[:, s * 128:(s + 1) * 128] for s in range(8)]
    wv_f = [arf[:, 8 * 128 + s * DBW: 8 * 128 + (s + 1) * DBW] for s in range(2)]
    xs_f = [arf[:, 8 * 128 + 2 * DBW + s * S: 8 * 128 + 2 * DBW + (s + 1) * S] for s in range(2)]
    r1 = arf[:, 0:nOC * TB]
    xres = [arf[:, nOC * TB + s * TB: nOC * TB + (s + 1) * TB] for s in range(2)]
    tmp_f = arf[:, nOC * TB + 2 * TB: nOC * TB + 3 * TB]
    x2f_t = [arf[:, nOC * TB + 3 * TB: nOC * TB + 4 * TB]]
    r2, x2r, out_t = r1, xres, x2f_t

    E.dma("sync", "s_lut", g("lut_s")[:, :], g("lutb")[:, :])
    E.dma("sync", "s_cm", g("cm_s")[:, :], g("cmask")[:, :])
    E.dma("sync", "s_onc", g("onec_s")[:, :], g("onesc")[:, :])
    E.dma("sync", "s_onr", g("oner_s")[:, :], g("onesr")[:, :])

    NG = min(4, nHC)
    runs = []
    t = 0
    while t < cfg.TPC:
        w = min(MRUN, cfg.TPC - t)
        runs.append((t, w))
        t += w
    MP = MRUN * 128
    m_part = [big[:, j * MP:(j + 1) * MP] for j in range(16)]
    m_idxb = [big[:, (16 + s) * MP:(17 + s) * MP] for s in range(3)]
    m_out = [big[:, (19 + s) * MP:(20 + s) * MP] for s in range(2)]
    E.wait("vector", "s_lut", 16)
    acc_v = {}

    def mat_dma(ri):
        t0, w = runs[ri]
        slot = ri % 3
        if ri >= 3:
            E.wait("gpsimd", "v_p", acc_v[ri - 3])
        E.dma("gpsimd", f"s_midx{slot}",
              m_idxb[slot][:, 0:w * 128].rearrange("p (t c) -> p t c", c=128),
              g("myidx")[t0:t0 + w, :, :].rearrange("t p c -> p t c"))

    for r0 in range(min(3, len(runs))):
        mat_dma(r0)
    for ri, (t0, w) in enumerate(runs):
        slot = ri % 3
        E.wait("vector", f"s_midx{slot}", 16 * (ri // 3 + 1))
        for j in range(LUT_N):
            E.run("vector", lambda e, w=w, j=j, slot=slot: e.tensor_scalar(
                out=m_part[j][:, 0:w * 128], in0=m_idxb[slot][:, 0:w * 128],
                scalar1=float(j), scalar2=g("lut_s")[:, j:j + 1],
                op0=OP.is_equal, op1=OP.mult))
        E.drain("vector")
        step = 1
        while step < 16:
            for j0 in range(0, 16, 2 * step):
                last = (2 * step >= 16)
                if last:
                    if ri >= 2:
                        E.wait("vector", f"s_wmy{ri % 2}", 16 * ((ri - 2) // 2 + 1))
                    fn = lambda e, w=w, ri=ri: e.tensor_tensor(
                        out=m_out[ri % 2][:, 0:w * 128], in0=m_part[0][:, 0:w * 128],
                        in1=m_part[8][:, 0:w * 128], op=OP.add)
                    acc_v[ri] = E.inc("vector", "v_p", fn)
                else:
                    fn = lambda e, j0=j0, s=step, w=w: e.tensor_tensor(
                        out=m_part[j0][:, 0:w * 128], in0=m_part[j0][:, 0:w * 128],
                        in1=m_part[j0 + s][:, 0:w * 128], op=OP.add)
                    E.run("vector", fn)
            E.drain("vector")
            step *= 2
        if ri + 3 < len(runs):
            mat_dma(ri + 3)
        E.wait("gpsimd", "v_p", acc_v[ri])
        E.dma("gpsimd", f"s_wmy{ri % 2}",
              g("wcat_my")[t0:t0 + w, :, :].rearrange("t p c -> p t c"),
              m_out[ri % 2][:, 0:w * 128].rearrange("p (t c) -> p t c", c=128))
        for k in range(cfg.KAG):
            if ((k + 1) * cfg.CSZ - 1) // MRUN == ri:
                E.wait("gpsimd", "s_wmy0", 16 * (ri // 2 + 1))
                if ri >= 1:
                    E.wait("gpsimd", "s_wmy1", 16 * ((ri - 1) // 2 + 1))
                ag_g = E.inc("gpsimd", "g_p", lambda e, k=k: e.collective_compute(
                    "AllGather", OP.bypass, replica_groups=[list(range(cfg.ncores))],
                    ins=[g("wcat_my")[k * cfg.CSZ:(k + 1) * cfg.CSZ, :, :].opt()],
                    outs=[g("wc")[k].ap().opt()]))
    mat_done_v = acc_v[len(runs) - 1]
    n_wmy = [E.count("s_wmy0"), E.count("s_wmy1")]
    for i4 in range(NG):
        E.dma("gpsimd", "s_wob", g("wob_scr")[i4 * (H // NG):(i4 + 1) * (H // NG), :],
              g("wout")[i4 * (H // NG):(i4 + 1) * (H // NG), :])
    n_wob = E.count("s_wob")

    qkcast_s, lastmm_t, stopmm_t, qkevac_s = {}, {}, {}, {}
    xcast_s = {}
    for ft in range(2 * nHC):
        isq = ft < nHC
        for hc in range(nHC):
            it = ft * nHC + hc
            slot = it % 8
            if ft == 0:
                xsl = hc % 2
                if hc >= 2:
                    E.wait("sync", "sc_p", xcast_s[hc - 2])
                E.dma("sync", f"s_xs{xsl}", xs_f[xsl],
                      g("x_fm")[hc * 128:(hc + 1) * 128, :])
                E.wait("scalar", f"s_xs{xsl}", 16 * (hc // 2 + 1))
                xcast_s[hc] = E.inc("scalar", "sc_p",
                                    lambda e, s=xsl, k=hc: e.copy(
                                        out=xact[:, k * S:(k + 1) * S], in_=xs_f[s]))
            if it >= 8:
                E.wait("sync", "sc_p", qkcast_s[it - 8])
            E.dma("sync", f"s_wqk{slot}", wqk_f[slot],
                  g("wqkv")[hc * 128:(hc + 1) * 128, ft * 128:(ft + 1) * 128])
            if it >= 8:
                E.wait("scalar", "t_p", lastmm_t[it - 8])
            E.wait("scalar", f"s_wqk{slot}", 16 * (it // 8 + 1))
            qkcast_s[it] = E.inc("scalar", "sc_p",
                                 lambda e, s=slot: e.copy(out=g("wqk_b")[s][:, :],
                                                          in_=wqk_f[s]))
            E.wait("tensor", "sc_p", qkcast_s[it])
            if ft == 0:
                E.wait("tensor", "sc_p", xcast_s[hc])
            if hc == 0 and ft >= 2:
                E.wait("tensor", "sc_p", qkevac_s[(ft - 2, nTB - 1)])
            bk0 = (ft % 2) * nTB
            for tb in range(nTB):
                fn = lambda e, s=slot, hc=hc, tb=tb, bk0=bk0, st=(hc == 0), sp=(hc == nHC - 1): e.matmul(
                    ps[bk0 + tb][:, 0:TB], g("wqk_b")[s][:, :],
                    xact[:, hc * S + tb * TB: hc * S + (tb + 1) * TB],
                    start=st, stop=sp, skip_group_check=True)
                if hc == nHC - 1:
                    stopmm_t[(ft, tb)] = E.inc("tensor", "t_p", fn)
                    lastmm_t[it] = stopmm_t[(ft, tb)]
                elif tb == nTB - 1:
                    lastmm_t[it] = E.inc("tensor", "t_p", fn)
                else:
                    E.run("tensor", fn)
        for tb in range(nTB):
            es = (ft * nTB + tb) % 4
            idx = ft * nTB + tb
            E.wait("scalar", "t_p", stopmm_t[(ft, tb)])
            if idx >= 4:
                E.wait("scalar", f"s_qkst{es}", 16 * (idx // 4))
            sc = cfg.qscale if isq else 1.0
            qkevac_s[(ft, tb)] = E.inc("scalar", "sc_p",
                                       lambda e, tb=tb, es=es, sc=sc, bk0=(ft % 2) * nTB:
                                       e.activation(
                                           out=g("qk_e")[es][:, :], in_=ps[bk0 + tb][:, 0:TB],
                                           func=AF.Copy, scale=sc))
            E.wait("scalar", "sc_p", qkevac_s[(ft, tb)])
            dst = g("q_scr") if isq else g("k_scr")
            r0 = (ft if isq else ft - nHC) * 128
            E.dma("scalar", f"s_qkst{es}", dst[r0:r0 + 128, tb * TB:(tb + 1) * TB],
                  g("qk_e")[es][:, :])
    n_qkst = [E.count(f"s_qkst{e_}") for e_ in range(4)]


    wvc_s, vstop_t, vevac_s = {}, {}, {}
    for db in range(nDB):
        for hc in range(nHC):
            it = db * nHC + hc
            slot = hc % 2
            if it >= 2:
                E.wait("sync", "sc_p", wvc_s[it - 2])
            E.dma("sync", f"s_wv{slot}", wv_f[slot],
                  g("wqkv")[hc * 128:(hc + 1) * 128, 2 * H + db * DBW: 2 * H + (db + 1) * DBW])
            if db >= 1:
                E.wait("scalar", "t_p", vstop_t[(db - 1, nTT - 1)])
            E.wait("scalar", f"s_wv{slot}", 16 * (it // 2 + 1))
            wvc_s[it] = E.inc("scalar", "sc_p",
                              lambda e, hc=hc, s=slot: e.copy(out=wv_b(hc), in_=wv_f[s]))
        for tt in range(nTT):
            bank = 4 + (tt % 2)
            it = db * nTT + tt
            if it == 0:
                E.wait("tensor", "sc_p", qkevac_s[(2 * nHC - 1, nTB - 1)])
            if it >= 2:
                E.wait("tensor", "sc_p", vevac_s[it - 2])
            E.wait("tensor", "sc_p", wvc_s[db * nHC + nHC - 1])
            for hc in range(nHC):
                fn = lambda e, hc=hc, tt=tt, bank=bank, st=(hc == 0), sp=(hc == nHC - 1): e.matmul(
                    ps[bank][:, 0:DBW],
                    xact[:, hc * S + tt * 128: hc * S + (tt + 1) * 128],
                    wv_b(hc), start=st, stop=sp, skip_group_check=True)
                if hc == nHC - 1:
                    vstop_t[(db, tt)] = E.inc("tensor", "t_p", fn)
                else:
                    E.run("tensor", fn)
            E.wait("scalar", "t_p", vstop_t[(db, tt)])
            if it >= 1:
                E.wait("scalar", "s_vst", 16 * it)
            es = 0
            vevac_s[it] = E.inc("scalar", "sc_p",
                                lambda e, bank=bank, es=es: e.copy(out=g("ve_t")[es][:, :],
                                                                   in_=ps[bank][:, 0:DBW]))
            E.wait("scalar", "sc_p", vevac_s[it])
            E.dma("scalar", "s_vst",
                  g("v_scr")[tt * 128:(tt + 1) * 128, db * DBW:(db + 1) * DBW],
                  g("ve_t")[es][:, :])
    n_vst = E.count("s_vst")
    p1_sc = E.count("sc_p")
    tp_p1v = E.count("t_p")

    E.wait("vector", "s_cm", 16)
    E.wait("scalar", "v_p", mat_done_v)
    E.wait("scalar", "s_wmy0", n_wmy[0])
    E.wait("scalar", "s_wmy1", n_wmy[1])
    E.wait("sync", "v_p", mat_done_v)
    E.wait("sync", "s_wmy0", n_wmy[0])
    E.wait("sync", "s_wmy1", n_wmy[1])
    E.wait("gpsimd", "s_wob", n_wob)
    E.wait("sync", "s_wob", n_wob)
    n_strad = 0
    msk_v, avden_t, attn_v, denev_v, bcs_v = {}, {}, {}, {}, {}
    wo_use_t, r1_v, x2f_v = {}, {}, {}
    wo_qctr = {0: 0, 1: 0}
    vsl_fill = {}
    prev_exp_last = 0
    n_x2f = 0
    ln_prev_stat = {'t': 0, 'sc': 0}
    wo_last_t = 0
    x2f_cnt = {}
    avkt_t = {}
    msk_last = {}
    for qb in range(nTB):
        nkt = (qb + 1) * NSTR
        nkt_prev_qb = qb * NSTR

        def den_tail(iq2):
            head2 = iq2 % NH
            bav = 2 + (head2 % 2)
            bden = 4 + (head2 % 2)
            E.wait("scalar", "t_p", avden_t[iq2])
            E.run("scalar", lambda e, bden=bden: e.activation(
                out=g("rden_f")[:, :], in_=ps[bden][0:1, 0:TB], func=AF.Ln))
            E.drain("scalar")
            denev_v[iq2] = E.inc("scalar", "sc_p", lambda e: e.activation(
                out=g("rden_b")[:, :], in_=g("rden_f")[:, :], func=AF.Exp, scale=-1.0))
            E.wait("tensor", "sc_p", denev_v[iq2])
            E.wait("tensor", "s_onr", 16)
            if iq2 > 0:
                E.wait("tensor", "v_p", bcs_v[iq2 - 1])
            bcb = 2 + ((head2 + 1) % 2)
            if iq2 >= 1:
                E.wait("tensor", "v_p", attn_v[iq2 - 1])
            tbc = E.inc("tensor", "t_p", lambda e, bcb=bcb: e.matmul(
                ps[bcb][:, 0:TB], g("oner_s")[:, :], g("rden_b")[:, :],
                start=True, stop=True, skip_group_check=True))
            E.wait("vector", "t_p", tbc)
            bcs_v[iq2] = E.inc("vector", "v_p",
                               lambda e, bcb=bcb: e.tensor_copy(bc_s, ps[bcb][:, 0:TB]))
            E.drain("vector")
            if head2 == 0 and iq2 >= NH:
                E.wait("vector", "t_p", wo_last_t)
            attn_v[iq2] = E.inc("vector", "v_p",
                                lambda e, head2=head2, bav=bav: e.tensor_tensor(
                                    out=attn(head2), in0=ps[bav][:, 0:TB], in1=bc_s,
                                    op=OP.mult))
            E.drain("vector")

        def emit_av_kt(iq2, kt, nkt2, last):
            head2 = iq2 % NH
            bav = 2 + (head2 % 2)
            bden = 4 + (head2 % 2)
            sl2 = head2 % 2
            if kt == 0:
                if head2 <= 1 and iq2 >= NH and (iq2 // NH) == qb:
                    E.wait("tensor", "v_p", x2f_v[(qb - 1) * nOC + nOC - 1])
                if iq2 >= 2:
                    E.wait("tensor", "v_p", attn_v[iq2 - 2])
                    E.wait("tensor", "sc_p", denev_v[iq2 - 2])
                if iq2 >= 1:
                    E.wait("tensor", "v_p", bcs_v[iq2 - 1])
                if iq2 in msk_last:
                    E.wait("tensor", "v_p", msk_last[iq2])
                E.wait("tensor", f"s_vsl{sl2}", vsl_fill[iq2])
                E.wait("tensor", "s_onc", 16)
            E.run("tensor", lambda e, sl2=sl2, kt=kt, nkt2=nkt2, bav=bav,
                  st=(kt == 0), sp=last: e.matmul(
                      ps[bav][:, 0:TB], v_sl(sl2, nkt2 * 128)[:, kt * 128:(kt + 1) * 128],
                      et(kt), start=st, stop=sp, skip_group_check=True))
            avkt_t[(iq2, kt)] = E.inc("tensor", "t_p",
                                      lambda e, kt=kt, bden=bden, st=(kt == 0), sp=last: e.matmul(
                                          ps[bden][0:1, 0:TB], g("onec_s")[:, :], et(kt),
                                          start=st, stop=sp, skip_group_check=True))
            if last:
                avden_t[iq2] = avkt_t[(iq2, kt)]

        for head in range(NH):
            sl = head % 2
            iq = qb * NH + head
            if iq >= 2:
                E.wait("sync", "t_p", avden_t[iq - 2])
            idxq = head * nTB + qb
            idxk = (nHC + head) * nTB + qb
            E.wait("sync", f"s_qkst{idxq % 4}", 16 * (idxq // 4 + 1))
            E.wait("sync", f"s_qkst{idxk % 4}", 16 * (idxk // 4 + 1))
            idxv = (nDB - 1) * nTT + (NSTR * (qb + 1) - 1)
            E.wait("sync", "s_vst", 16 * (idxv + 1))
            E.dma("sync", f"s_qsl{sl}", q_sl(sl),
                  g("q_scr")[head * 128:(head + 1) * 128, qb * TB:(qb + 1) * TB])
            E.dma("sync", f"s_ksl{sl}", k_sl(sl, nkt * 128),
                  g("k_scr")[head * 128:(head + 1) * 128, 0:nkt * 128])
            E.dma("sync", f"s_vsl{sl}",
                  v_sl(sl, nkt * 128).rearrange("p (t c) -> p t c", c=128),
                  g("v_scr")[0:nkt * 128, head * 128:(head + 1) * 128]
                  .rearrange("(t p) c -> p t c", p=128))
            nfill = 16 * (iq // 2 + 1)
            vsl_fill[iq] = nfill
            E.wait("tensor", f"s_qsl{sl}", nfill)
            E.wait("tensor", f"s_ksl{sl}", nfill)
            exp_slot = {}
            PAIR_OK = (TB == 512)
            SBK4 = (0, 1, 6, 7)
            ns_end = nkt - NSTR
            last_exp = 0
            kt = 0
            while kt < nkt:
                pair = (kt + 1 < ns_end and kt % 2 == 0 and PAIR_OK)
                subkts = (kt, kt + 1) if pair else (kt,)
                for k2 in subkts:
                    c = k2
                    bank = SBK4[c % 4]
                    if c >= 4:
                        E.wait("tensor", "sc_p", exp_slot[c - 4])
                    elif iq > 0:
                        E.wait("tensor", "sc_p", prev_exp_last)
                    else:
                        E.wait("tensor", "sc_p", p1_sc)
                    if c == 0 and head == 0 and qb > 0:
                        E.wait("tensor", "v_p", r1_v[(qb - 1) * nOC + nOC - 1])
                    tsc = E.inc("tensor", "t_p", lambda e, sl=sl, k2=k2, bank=bank, nkt=nkt: e.matmul(
                        ps[bank][:, 0:TB], k_sl(sl, nkt * 128)[:, k2 * 128:(k2 + 1) * 128],
                        q_sl(sl), start=True, stop=True, skip_group_check=True))
                    if head > 0:
                        emit_av_kt(iq - 1, k2, nkt, last=(k2 == nkt - 1))
                kt1 = subkts[-1]
                E.wait("scalar", "t_p", tsc)
                if head > 0:
                    E.wait("scalar", "t_p", avkt_t[(iq - 1, kt1)])
                elif qb > 0:
                    pl = qb * NH - 1
                    kk = min(kt1, nkt_prev_qb - 1)
                    if kk >= 0:
                        E.wait("scalar", "t_p", avkt_t[(pl, kk)])
                if pair:
                    pb = (kt // 2) % 2
                    v = E.inc("scalar", "sc_p",
                              lambda e, pb=pb, kt=kt: e.activation(
                                  out=big[:, et_off + kt * TB: et_off + (kt + 2) * TB],
                                  in_=T["psA"][pb][:, 0:2 * TB], func=AF.Exp))
                    exp_slot[kt] = exp_slot[kt + 1] = v
                    last_exp = v
                elif kt1 >= nkt - NSTR:
                    es = n_strad % 2
                    n_strad += 1
                    if n_strad > 2:
                        E.wait("scalar", "v_p", msk_v[n_strad - 2])
                    v = E.inc("scalar", "sc_p",
                              lambda e, bank=SBK4[kt1 % 4], es=es: e.activation(
                                  out=et_t(es), in_=ps[bank][:, 0:TB], func=AF.Exp))
                    exp_slot[kt1] = v
                    last_exp = v
                    E.wait("vector", "sc_p", v)
                    j = kt1 - (nkt - NSTR)
                    msk_v[n_strad] = E.inc("vector", "v_p",
                                           lambda e, kt1=kt1, es=es, j=j: e.tensor_tensor(
                                               out=et(kt1), in0=et_t(es),
                                               in1=g("cm_s")[:, (NSTR - 1 - j) * 128:
                                                             (NSTR - 1 - j) * 128 + TB],
                                               op=OP.mult))
                    msk_last[iq] = msk_v[n_strad]
                else:
                    v = E.inc("scalar", "sc_p",
                              lambda e, bank=SBK4[kt1 % 4], kt1=kt1: e.activation(
                                  out=et(kt1), in_=ps[bank][:, 0:TB], func=AF.Exp))
                    exp_slot[kt1] = v
                    last_exp = v
                kt += 2 if pair else 1
            prev_exp_last = last_exp
            if head > 0:
                den_tail(iq - 1)
        liq = qb * NH + NH - 1
        E.wait("tensor", "sc_p", prev_exp_last)
        for kt in range(nkt):
            emit_av_kt(liq, kt, nkt, last=(kt == nkt - 1))
        den_tail(liq)
        n_attn_qb = attn_v[qb * NH + NH - 1]
        first_wo = True
        for ot in range(nOC):
            bank = ot % 2
            io = qb * nOC + ot
            if io >= 2:
                E.wait("tensor", "v_p", r1_v[io - 2])
            for dc2 in range(nHC // 2):
                dc = 2 * dc2
                qsel = dc2 % 2
                eng = "sync" if qsel == 0 else "gpsimd"
                jq = wo_qctr[qsel]
                wo_qctr[qsel] += 1
                ws = 2 * qsel + (jq % 2)
                if jq >= 2:
                    E.wait(eng, "t_p", wo_use_t[(qsel, jq - 2)])
                E.dma(eng, f"s_wot{ws}",
                      g("wo_t")[ws][:, :].rearrange("p (t c) -> p t c", c=128),
                      g("wob_scr")[dc * 128:(dc + 2) * 128,
                                   ot * 128:(ot + 1) * 128].rearrange("(t p) c -> p t c", p=128))
                E.wait("tensor", f"s_wot{ws}", 16 * (jq // 2 + 1))
                E.wait("tensor", "v_p", n_attn_qb)
                if first_wo:
                    E.wait("tensor", "sc_p", prev_exp_last)
                    first_wo = False
                E.run("tensor", lambda e, ws=ws, dc=dc, bank=bank, st=(dc == 0): e.matmul(
                    ps[bank][:, 0:TB], g("wo_t")[ws][:, 0:128], attn(dc),
                    start=st, stop=False, skip_group_check=True))
                wo_use_t[(qsel, jq)] = E.inc("tensor", "t_p",
                                             lambda e, ws=ws, dc=dc, bank=bank,
                                             sp=(dc == nHC - 2): e.matmul(
                                                 ps[bank][:, 0:TB], g("wo_t")[ws][:, 128:256],
                                                 attn(dc + 1),
                                                 start=False, stop=sp, skip_group_check=True))
            wo_last_t = wo_use_t[(qsel, jq)]
            xs = ot % 2
            if io >= 2:
                E.wait("sync", "v_p", r1_v[io - 2])
            if qb == 0 and ot == 0:
                E.wait("sync", "sc_p", p1_sc)
            E.dma("sync", f"s_xres{xs}", xres[xs],
                  g("x_fm")[ot * 128:(ot + 1) * 128, qb * TB:(qb + 1) * TB])
            if qb == 0 and ot == 0:
                E.wait("vector", "sc_p", p1_sc)
            if qb > 0 and ot == 0:
                E.wait("vector", "sc_p", ln_prev_stat['sc'])
            E.wait("vector", "t_p", wo_last_t)
            E.wait("vector", f"s_xres{xs}", 16 * (io // 2 + 1))
            r1_v[io] = E.inc("vector", "v_p",
                             lambda e, ot=ot, bank=bank, xs=xs: e.tensor_tensor(
                                 out=r1[:, ot * TB:(ot + 1) * TB], in0=ps[bank][:, 0:TB],
                                 in1=xres[xs], op=OP.add))
        rv = {oc: r1_v[qb * nOC + oc] for oc in range(nOC)}
        emit_ln(cfg, E, T, r1, rv, r1b_t, r1sq_t, (2, 3, 4, 5), ln_prev_stat)
        for oc in range(nOC):
            s = 0
            io = qb * nOC + oc
            if io >= 1:
                E.wait("vector", "s_x2f", 16 * io)
            E.run("vector", lambda e, oc=oc: e.tensor_tensor(
                out=tmp_f, in0=r1[:, oc * TB:(oc + 1) * TB], in1=ps[4][:, 0:TB], op=OP.mult))
            E.drain("vector")
            x2f_v[io] = E.inc("vector", "v_p", lambda e, s=s: e.tensor_tensor(
                out=x2f_t[s], in0=tmp_f, in1=ps[5][:, 0:TB], op=OP.subtract))
            E.wait("gpsimd", "v_p", x2f_v[io])
            n_x2f = E.dma("gpsimd", "s_x2f",
                          g("x2f_scr")[oc * 128:(oc + 1) * 128, qb * TB:(qb + 1) * TB],
                          x2f_t[s])
            x2f_cnt[(qb, oc)] = n_x2f

    E.wait("gpsimd", "t_p", tp_p1v)
    for hc in range(nHC):
        E.wait("gpsimd", "s_x2f", x2f_cnt[(nTB - 1, hc)])
        E.dma("gpsimd", f"s_x2g{hc * NG // nHC}", xact[:, hc * S:(hc + 1) * S],
              g("x2f_scr")[hc * 128:(hc + 1) * 128, :])
    n_x2g = [E.count(f"s_x2g{i}") for i in range(NG)]
    E.wait("gpsimd", "g_p", ag_g)
    E.wait("sync", "g_p", ag_g)
    sil_sc, h_v, gu_t = {}, {}, {}
    h_fill = {0: 0, 1: 0}
    QW = min(4, nHC)
    nHQ = (nHC + QW - 1) // QW
    wcv = [g("wc")[k][:, :, :].rearrange("(h a) p c -> h a p c", a=nIC)
           for k in range(cfg.KAG)]
    BL = cfg.ncores * nIC
    wgu_fill = {0: 0, 1: 0}
    for ic in range(nIC):
        for hc in range(nHC):
            iw = ic * nHC + hc
            hq, hsub = divmod(hc, QW)
            jq = ic * nHQ + hq
            sg = jq % 2
            if hsub == 0:
                w = min(QW, nHC - hc)
                if jq >= 2:
                    pic2, phq2 = divmod(jq - 2, nHQ)
                    iw2 = pic2 * nHC + min((phq2 + 1) * QW, nHC) - 1
                    E.wait("gpsimd", "t_p", gu_t[iw2])
                    E.wait("sync", "t_p", gu_t[iw2])
                kg, hg = divmod(hc, cfg.ncores)
                ku, hu = divmod(nHC + hc, cfg.ncores)
                E.dma("gpsimd", f"s_wg{sg}",
                      g("wg_t")[sg][:, 0:w * 128].rearrange("p (t c) -> p t c", c=128),
                      wcv[kg][hg:hg + w, ic:ic + 1, :, :].rearrange("t a p c -> p (t a) c"))
                E.dma("sync", f"s_wu{sg}",
                      g("wu_t")[sg][:, 0:w * 128].rearrange("p (t c) -> p t c", c=128),
                      wcv[ku][hu:hu + w, ic:ic + 1, :, :].rearrange("t a p c -> p (t a) c"))
                wgu_fill[sg] += 16
                E.wait("tensor", f"s_wg{sg}", wgu_fill[sg])
                E.wait("tensor", f"s_wu{sg}", wgu_fill[sg])
            if ic == 0 and hc % (nHC // NG) == 0:
                E.wait("tensor", f"s_x2g{hc * NG // nHC}", n_x2g[hc * NG // nHC])
            if iw == 0:
                E.wait("tensor", "v_p", E.count("v_p"))
                E.wait("tensor", "sc_p", E.count("sc_p"))
            for qb in range(nTB):
                if ic >= 1 and hc == 0:
                    E.wait("tensor", "sc_p", sil_sc[(ic - 1, qb)])
                    E.wait("tensor", "v_p", h_v[(ic - 1, qb)])
                fng = lambda e, sg=sg, hc=hc, qb=qb, hs=hsub, st=(hc == 0), sp=(hc == nHC - 1): e.matmul(
                    ps[qb][:, 0:TB], g("wg_t")[sg][:, hs * 128:(hs + 1) * 128],
                    xact[:, hc * S + qb * TB: hc * S + (qb + 1) * TB],
                    start=st, stop=sp, skip_group_check=True)
                fnu = lambda e, sg=sg, hc=hc, qb=qb, hs=hsub, st=(hc == 0), sp=(hc == nHC - 1): e.matmul(
                    ps[nTB + qb][:, 0:TB], g("wu_t")[sg][:, hs * 128:(hs + 1) * 128],
                    xact[:, hc * S + qb * TB: hc * S + (qb + 1) * TB],
                    start=st, stop=sp, skip_group_check=True)
                E.run("tensor", fng)
                if qb == nTB - 1:
                    gu_t[iw] = E.inc("tensor", "t_p", fnu)
                else:
                    E.run("tensor", fnu)
        for qb in range(nTB):
            s = qb % 2
            iq = ic * nTB + qb
            E.wait("scalar", "t_p", gu_t[ic * nHC + nHC - 1])
            if iq >= 2:
                E.wait("scalar", "v_p", h_v[((iq - 2) // nTB, (iq - 2) % nTB)])
            sil_sc[(ic, qb)] = E.inc("scalar", "sc_p",
                                     lambda e, qb=qb, s=s: e.activation(
                                         out=sil_t[s], in_=ps[qb][:, 0:TB],
                                         func=AF.Silu))
            E.wait("vector", "sc_p", sil_sc[(ic, qb)])
            if h_fill[s] >= 1:
                E.wait("vector", f"s_h{s}", 16 * h_fill[s])
            h_v[(ic, qb)] = E.inc("vector", "v_p",
                                  lambda e, qb=qb, s=s: e.tensor_tensor(
                                      out=h_t[s], in0=sil_t[s],
                                      in1=ps[nTB + qb][:, 0:TB], op=OP.mult))
            E.wait("scalar", "v_p", h_v[(ic, qb)])
            E.dma("scalar", f"s_h{s}",
                  g("h_scr")[ic * 128:(ic + 1) * 128, qb * TB:(qb + 1) * TB],
                  h_t[s])
            h_fill[s] += 1
    n_h = [E.count(f"s_h{s_}") for s_ in range(2)]

    E.wait("sync", "s_x2f", n_x2f)
    E.wait("sync", "t_p", E.count("t_p"))
    do_t, r2_v, wd_use_t = {}, {}, {}
    wd_qctr = {0: 0, 1: 0}
    n_out = 0
    first_p4 = True
    nICP = (nIC + 1) // 2
    h_free_t = {}
    hl_cnt, hl_ctr, hlf = {}, [0], [0, 0, 0, 0]

    def h_load(qb_, ic_):
        j = hl_ctr[0] % 4
        hl_ctr[0] += 1
        if hlf[j] >= 1:
            E.wait("scalar", f"s_hlq{j}", 16 * hlf[j])
        hlf[j] += 1
        E.dma("scalar", f"s_hlq{j}", h_sl(ic_),
              g("h_scr")[ic_ * 128:(ic_ + 1) * 128, qb_ * TB:(qb_ + 1) * TB])
        hl_cnt[(qb_, ic_)] = (j, 16 * hlf[j])
    for s_ in range(2):
        E.wait("scalar", f"s_h{s_}", n_h[s_])
    E.wait("scalar", "t_p", E.count("t_p"))
    for ic in range(nIC):
        h_load(0, ic)
    for qb in range(nTB):
        for op_ in range(nOC // 2):
            ot0, ot1 = 2 * op_, 2 * op_ + 1
            gp_pair = qb * (nOC // 2) + op_
            b0, b1 = 2 * (gp_pair % 2), 2 * (gp_pair % 2) + 1
            if gp_pair >= 2:
                qb_p, op_p = divmod(gp_pair - 2, nOC // 2)
                E.wait("tensor", "v_p", r2_v[qb_p * nOC + 2 * op_p + 1])
            for ic in range(nIC):
                qsel = ic % 2
                eng = "sync" if qsel == 0 else "gpsimd"
                jq = wd_qctr[qsel]
                wd_qctr[qsel] += 1
                ws = 3 * qsel + (jq % 3)
                if jq >= 3:
                    E.wait(eng, "t_p", wd_use_t[(qsel, jq - 3)])
                gid_d = 2 * cfg.G1 + ic * nOC + ot0
                kd, ld = divmod(gid_d, BL)
                E.dma(eng, f"s_wd{ws}",
                      g("wd_t")[ws][:, :].rearrange("p (t c) -> p t c", c=128),
                      g("wc")[kd][ld:ld + 2, :, :].rearrange("t p c -> p t c"))
                E.wait("tensor", f"s_wd{ws}", 16 * (jq // 3 + 1))
                if first_p4:
                    E.wait("tensor", "v_p", E.count("v_p"))
                    E.wait("tensor", "sc_p", E.count("sc_p"))
                    first_p4 = False
                if op_ == 0:
                    hj, hv = hl_cnt[(qb, ic)]
                    E.wait("tensor", f"s_hlq{hj}", hv)
                E.run("tensor", lambda e, ws=ws, ic=ic, b=b0, st=(ic == 0),
                      sp=(ic == nIC - 1): e.matmul(
                          ps[b][:, 0:TB], g("wd_t")[ws][:, 0:128], h_sl(ic),
                          start=st, stop=sp, skip_group_check=True))
                tmm = E.inc("tensor", "t_p",
                            lambda e, ws=ws, ic=ic, b=b1, st=(ic == 0),
                            sp=(ic == nIC - 1): e.matmul(
                                ps[b][:, 0:TB], g("wd_t")[ws][:, 128:256],
                                h_sl(ic),
                                start=st, stop=sp, skip_group_check=True))
                wd_use_t[(qsel, jq)] = tmm
                if op_ == nOC // 2 - 1:
                    h_free_t[ic] = tmm
            tdo_pair = tmm
            for i, (ot, bank) in enumerate(((ot0, b0), (ot1, b1))):
                io = qb * nOC + ot
                do_t[(qb, ot)] = tdo_pair
                xs = ot % 2
                if io >= 2:
                    E.wait("sync", "v_p", r2_v[io - 2])
                E.dma("sync", f"s_x2r{xs}", x2r[xs],
                      g("x2f_scr")[ot * 128:(ot + 1) * 128, qb * TB:(qb + 1) * TB])
                if qb == 0 and ot == 0:
                    E.wait("vector", "s_x2f", n_x2f)
                    E.wait("vector", "sc_p", ln_prev_stat['sc'])
                E.wait("vector", "t_p", tdo_pair)
                E.wait("vector", f"s_x2r{xs}", 16 * (io // 2 + 1))
                r2_v[io] = E.inc("vector", "v_p",
                                 lambda e, ot=ot, bank=bank, xs=xs: e.tensor_tensor(
                                     out=r2[:, ot * TB:(ot + 1) * TB], in0=ps[bank][:, 0:TB],
                                     in1=x2r[xs], op=OP.add))
        if qb + 1 < nTB:
            for icp in range(nICP):
                ic0, icw = 2 * icp, min(2, nIC - 2 * icp)
                E.wait("scalar", "t_p", h_free_t[ic0 + icw - 1])
                for icsub in range(icw):
                    h_load(qb + 1, ic0 + icsub)
        rv = {oc: r2_v[qb * nOC + oc] for oc in range(nOC)}
        emit_ln(cfg, E, T, r2, rv, r1b_t, r1sq_t, (6, 7, 4, 5), ln_prev_stat)
        for oc in range(nOC):
            s = 0
            io = qb * nOC + oc
            if io >= 1:
                E.wait("vector", "s_out", 16 * io)
            E.run("vector", lambda e, oc=oc: e.tensor_tensor(
                out=tmp_f, in0=r2[:, oc * TB:(oc + 1) * TB], in1=ps[4][:, 0:TB], op=OP.mult))
            E.drain("vector")
            co = E.inc("vector", "v_p", lambda e, s=s: e.tensor_tensor(
                out=out_t[s], in0=tmp_f, in1=ps[5][:, 0:TB], op=OP.subtract))
            E.wait("gpsimd", "v_p", co)
            n_out = E.dma("gpsimd", "s_out",
                          g("out_fm")[oc * 128:(oc + 1) * 128, qb * TB:(qb + 1) * TB],
                          out_t[s])
    E.wait("gpsimd", "s_out", n_out)


def emit_ln(cfg, E, T, r_slab, r_v, r1b_t, r1sq_t, banks, ln_prev_stat):
    H, TB, nOC = cfg.H, cfg.TB, cfg.nOC
    g = lambda n: T[n]
    ps = T["ps"]
    bsum, bsq, ba, bb = banks
    stat_t = {}
    crs_last = 0
    for oc in range(nOC):
        s = oc % 2
        E.wait("scalar", "v_p", r_v[oc])
        if oc >= 1:
            E.wait("scalar", "t_p", stat_t[oc - 1])
        else:
            E.wait("scalar", "t_p", ln_prev_stat['t'])
        crb = E.inc("scalar", "sc_p", lambda e, oc=oc, s=s: e.copy(
            out=r1b_t(s), in_=r_slab[:, oc * TB:(oc + 1) * TB]))
        crs = E.inc("scalar", "sc_p", lambda e, oc=oc, s=s: e.activation(
            out=r1sq_t(s), in_=r_slab[:, oc * TB:(oc + 1) * TB], func=AF.Square))
        crs_last = crs
        E.wait("tensor", "sc_p", crs)
        E.run("tensor", lambda e, s=s, st=(oc == 0), sp=(oc == nOC - 1): e.matmul(
            ps[bsum][0:1, 0:TB], g("onec_s")[:, :], r1b_t(s),
            start=st, stop=sp, skip_group_check=True))
        stat_t[oc] = E.inc("tensor", "t_p",
                           lambda e, s=s, st=(oc == 0), sp=(oc == nOC - 1): e.matmul(
                               ps[bsq][0:1, 0:TB], g("onec_s")[:, :], r1sq_t(s),
                               start=st, stop=sp, skip_group_check=True))
    ln_prev_stat['t'] = stat_t[nOC - 1]
    ln_prev_stat['sc'] = crs_last
    E.wait("vector", "t_p", stat_t[nOC - 1])
    E.run("vector", lambda e: e.tensor_scalar(out=g("mn_f")[:, :], in0=ps[bsum][0:1, 0:TB],
                                              scalar1=1.0 / H, scalar2=None, op0=OP.mult))
    E.run("vector", lambda e: e.tensor_scalar(out=g("msq_f")[:, :], in0=ps[bsq][0:1, 0:TB],
                                              scalar1=1.0 / H, scalar2=None, op0=OP.mult))
    E.drain("vector")
    E.run("vector", lambda e: e.tensor_tensor(out=g("mn2_f")[:, :], in0=g("mn_f")[:, :],
                                              in1=g("mn_f")[:, :], op=OP.mult))
    E.drain("vector")
    E.run("vector", lambda e: e.tensor_tensor(
        out=g("var_f")[:, :], in0=g("msq_f")[:, :], in1=g("mn2_f")[:, :], op=OP.subtract))
    E.drain("vector")
    cvar = E.inc("vector", "v_p", lambda e: e.tensor_scalar(
        out=g("msq_f")[:, :], in0=g("var_f")[:, :], scalar1=EPS, scalar2=None, op0=OP.add))
    E.wait("scalar", "v_p", cvar)
    E.run("scalar", lambda e: e.activation(
        out=g("sd_f")[:, :], in_=g("msq_f")[:, :], func=AF.Ln))
    E.drain("scalar")
    csd = E.inc("scalar", "sc_p", lambda e: e.activation(
        out=g("rstd_f")[:, :], in_=g("sd_f")[:, :], func=AF.Exp, scale=-0.5))
    E.wait("vector", "sc_p", csd)
    E.run("vector", lambda e: e.tensor_tensor(out=g("bvec_f")[:, :], in0=g("mn_f")[:, :],
                                              in1=g("rstd_f")[:, :], op=OP.mult))
    E.drain("vector")
    E.run("vector", lambda e: e.tensor_copy(g("a_b")[:, :], g("rstd_f")[:, :]))
    E.drain("vector")
    cab = E.inc("vector", "v_p", lambda e: e.tensor_copy(g("b_b")[:, :], g("bvec_f")[:, :]))
    E.wait("tensor", "v_p", cab)
    E.run("tensor", lambda e: e.matmul(ps[ba][:, 0:TB], g("oner_s")[:, :], g("a_b")[:, :],
                                       start=True, stop=True, skip_group_check=True))
    tab = E.inc("tensor", "t_p", lambda e: e.matmul(
        ps[bb][:, 0:TB], g("oner_s")[:, :], g("b_b")[:, :], start=True, stop=True,
        skip_group_check=True))
    E.wait("vector", "t_p", tab)



def make_host_inputs(cfg: Cfg, x, w_qkv, w_out, lut, gate_idx, up_idx, down_idx):
    import ml_dtypes
    BF = ml_dtypes.bfloat16
    lutb = np.broadcast_to(np.asarray(lut, np.float32), (128, LUT_N)).copy()
    kp = np.arange(128)[:, None]
    uu = np.arange((cfg.NSTR - 1) * 128 + cfg.TB)[None, :]
    cm = (uu >= kp + (cfg.NSTR - 1) * 128).astype(BF)
    onesc = np.ones((128, 1), dtype=BF)
    onesr = np.ones((1, 128), dtype=BF)
    srcs = {
        "gidxT": np.ascontiguousarray(np.asarray(gate_idx).T.astype(np.int32)),
        "uidxT": np.ascontiguousarray(np.asarray(up_idx).T.astype(np.int32)),
        "didxT": np.ascontiguousarray(np.asarray(down_idx).T.astype(np.int32)),
    }
    in_maps = []
    for c in range(cfg.ncores):
        myidx = np.zeros((cfg.TPC, 128, 128), np.int32)
        for t in range(cfg.TPC):
            gid = (t // cfg.CSZ) * (cfg.ncores * cfg.CSZ) + c * cfg.CSZ + (t % cfg.CSZ)
            if gid >= cfg.NT:
                continue
            src, rc, ct = w_tile_src(cfg, gid)
            myidx[t] = srcs[src][rc * 128:(rc + 1) * 128, ct * 128:(ct + 1) * 128]
        in_maps.append({
            "x_fm": np.ascontiguousarray(np.asarray(x[c]).T).astype(np.float32, copy=False),
            "wqkv": np.asarray(w_qkv, np.float32),
            "wout": np.asarray(w_out, np.float32),
            "myidx": myidx,
            "lutb": lutb,
            "cmask": cm,
            "onesc": onesc,
            "onesr": onesr,
        })
    return in_maps


def gather_output(cfg, results):
    return np.stack([np.ascontiguousarray(r["out_fm"].T) for r in results], axis=0)


def numpy_reference(cfg: Cfg, x, w_qkv, w_out, lut, gate_idx, up_idx, down_idx):
    def layernorm(v):
        m = v.mean(-1, keepdims=True)
        s = v.var(-1, keepdims=True)
        return (v - m) / np.sqrt(s + EPS)
    def softmax(s):
        m = s.max(-1, keepdims=True)
        e = np.exp(s - m)
        return e / e.sum(-1, keepdims=True)
    x = np.asarray(x, np.float64)
    w_qkv = np.asarray(w_qkv, np.float64)
    w_out = np.asarray(w_out, np.float64)
    lut = np.asarray(lut, np.float64)
    B, S, H = x.shape
    NH, D = cfg.NH, cfg.D
    qkv = x @ w_qkv
    qkv = qkv.reshape(B, S, 3, NH, D)
    q, k, v = qkv[:, :, 0], qkv[:, :, 1], qkv[:, :, 2]
    q = q / np.sqrt(np.float64(D))
    scores = np.einsum("bqhd,bkhd->bhqk", q, k)
    mask = np.arange(S)[:, None] >= np.arange(S)[None, :]
    scores = np.where(mask[None, None], scores, -1e10)
    a = softmax(scores)
    o = np.einsum("bhqk,bkhd->bqhd", a, v).reshape(B, S, H)
    x1 = layernorm(x + o @ w_out)
    gate = x1 @ lut[gate_idx].T
    up = x1 @ lut[up_idx].T
    hh = (gate / (1 + np.exp(-gate))) * up
    x2 = layernorm(x1 + hh @ lut[down_idx].T)
    return x2.astype(np.float32)



def kernel(L0_attn_x, L0_attn_w_qkv, L0_attn_w_out, L0_ffn_lut,
           L0_ffn_gate_idx, L0_ffn_up_idx, L0_ffn_down_idx):
    cfg = FULL
    in_maps = make_host_inputs(cfg, L0_attn_x, L0_attn_w_qkv, L0_attn_w_out,
                               L0_ffn_lut, L0_ffn_gate_idx, L0_ffn_up_idx,
                               L0_ffn_down_idx)
    nc = build_program(cfg)
    res = run_bass_kernel_spmd(nc, in_maps, core_ids=list(range(cfg.ncores)))
    return gather_output(cfg, res.results)


# version 19
# speedup vs baseline: 1.0216x; 1.0012x over previous
import math
import numpy as np
from contextlib import ExitStack
from dataclasses import dataclass

import concourse.bass as bass
import concourse.mybir as mybir
from concourse.bass_utils import run_bass_kernel_spmd

F32 = mybir.dt.float32
BF16 = mybir.dt.bfloat16
I32 = mybir.dt.int32
AF = mybir.ActivationFunctionType
OP = mybir.AluOpType

LUT_N = 16
EPS = 1e-5
MRUN = 8


@dataclass
class Cfg:
    S: int = 2048
    H: int = 2048
    NH: int = 16
    I: int = 5504
    TB: int = 512
    ncores: int = 8

    @property
    def D(self): return self.H // self.NH
    @property
    def F3(self): return 3 * self.H
    @property
    def nHC(self): return self.H // 128
    @property
    def nOC(self): return self.H // 128
    @property
    def nIC(self): return self.I // 128
    @property
    def nTB(self): return self.S // self.TB
    @property
    def nTT(self): return self.S // 128
    @property
    def NSTR(self): return self.TB // 128
    @property
    def DBW(self): return min(512, self.H)
    @property
    def nDB(self): return self.H // self.DBW
    @property
    def G1(self): return self.nHC * self.nIC
    @property
    def NT(self): return 2 * self.G1 + self.nIC * self.nOC
    @property
    def TPC0(self): return (self.NT + self.ncores - 1) // self.ncores
    @property
    def CSZ(self): return self.nIC
    @property
    def KAG(self): return (self.TPC0 + self.CSZ - 1) // self.CSZ
    @property
    def TPC(self): return self.KAG * self.CSZ
    @property
    def qscale(self): return 1.0 / math.sqrt(self.D)


FULL = Cfg()
SMALL = Cfg(S=256, H=256, NH=2, I=384, TB=128, ncores=8)
MED = Cfg(S=1024, H=512, NH=4, I=768, TB=512, ncores=8)

ENGINES = ("sync", "gpsimd", "vector", "scalar", "tensor")


class Emitter:
    def __init__(self, nc, ctx):
        self.nc = nc
        self.ctx = ctx
        self.q = {e: [] for e in ENGINES}
        self.sems = {}

    def _sem(self, name):
        if name not in self.sems:
            h = self.ctx.enter_context(self.nc.semaphore(name))
            self.sems[name] = [h, 0]
        return self.sems[name]

    def count(self, name):
        return self.sems[name][1] if name in self.sems else 0

    def run(self, eng, fn):
        self.q[eng].append(fn)

    def inc(self, eng, sem_name, fn, k=1):
        s = self._sem(sem_name)
        s[1] += k
        self.q[eng].append(lambda e, fn=fn, h=s[0], k=k: fn(e).then_inc(h, k))
        return s[1]

    def dma(self, eng, sem_name, out, in_):
        return self.inc(eng, sem_name, lambda e, o=out, i=in_: e.dma_start(out=o, in_=i), k=16)

    def wait(self, eng, sem_name, val):
        if val is None or val <= 0:
            return
        s = self._sem(sem_name)
        assert val <= s[1], f"wait {sem_name}>={val} but only {s[1]} emitted"
        self.q[eng].append(lambda e, h=s[0], v=val: e.wait_ge(h, v))

    def drain(self, eng):
        self.q[eng].append(lambda e: e.drain())

    def finalize(self, block):
        for eng in ENGINES:
            dec = getattr(block, eng)
            q = self.q[eng]
            def make(qq):
                def body(e):
                    for fn in qq:
                        fn(e)
                return body
            dec(make(q))


def w_tile_src(cfg, gid):
    if gid < cfg.G1:
        return "gidxT", gid // cfg.nIC, gid % cfg.nIC
    if gid < 2 * cfg.G1:
        return "uidxT", (gid - cfg.G1) // cfg.nIC, (gid - cfg.G1) % cfg.nIC
    g2 = gid - 2 * cfg.G1
    return "didxT", g2 // cfg.nOC, g2 % cfg.nOC


def build_program(cfg: Cfg):
    nc = bass.Bass("TRN2", target_bir_lowering=False)
    ctx = ExitStack()
    S, H, I, TB = cfg.S, cfg.H, cfg.I, cfg.TB
    nHC, nOC, nIC, nTT, NSTR = cfg.nHC, cfg.nOC, cfg.nIC, cfg.nTT, cfg.NSTR
    DBW = cfg.DBW

    x_fm = nc.dram_tensor("x_fm", [H, S], F32, kind="ExternalInput")
    wqkv = nc.dram_tensor("wqkv", [H, cfg.F3], F32, kind="ExternalInput")
    wout = nc.dram_tensor("wout", [H, H], F32, kind="ExternalInput")
    myidx = nc.dram_tensor("myidx", [cfg.TPC, 128, 128], I32, kind="ExternalInput")
    lutb = nc.dram_tensor("lutb", [128, LUT_N], F32, kind="ExternalInput")
    cmask = nc.dram_tensor("cmask", [128, (NSTR - 1) * 128 + TB], BF16, kind="ExternalInput")
    onesc = nc.dram_tensor("onesc", [128, 1], BF16, kind="ExternalInput")
    onesr = nc.dram_tensor("onesr", [1, 128], BF16, kind="ExternalInput")
    out_fm = nc.dram_tensor("out_fm", [H, S], F32, kind="ExternalOutput")

    q_scr = nc.dram_tensor("q_scr", [H, S], BF16)
    k_scr = nc.dram_tensor("k_scr", [H, S], BF16)
    v_scr = nc.dram_tensor("v_scr", [S, H], BF16)
    wob_scr = nc.dram_tensor("wob_scr", [H, H], BF16)
    x2f_scr = nc.dram_tensor("x2f_scr", [H, S], F32)
    x2b_scr = nc.dram_tensor("x2b_scr", [H, S], BF16)
    h_scr = nc.dram_tensor("h_scr", [I, S], BF16)
    wcat_my = nc.dram_tensor("wcat_my", [cfg.TPC, 128, 128], BF16)
    if cfg.KAG > 1:
        assert nHC % 4 == 0 and nOC % 2 == 0
    wc = [nc.dram_tensor(f"wcat{k}", [cfg.ncores * cfg.CSZ, 128, 128], BF16,
                         addr_space="Shared") for k in range(cfg.KAG)]

    sb = lambda name, shape, dt: ctx.enter_context(nc.sbuf_tensor(name, shape, dt))

    with ctx:
        xact = sb("xact", [128, nHC * S], BF16)
        p2cols = nTT * TB + 4 * S + 9 * TB
        bigcols = max(p2cols, nIC * TB, 21 * MRUN * 128)
        big = sb("big", [128, bigcols], BF16)
        r1b_sl = sb("r1b_sl", [128, TB], BF16)
        r1sq_sl = sb("r1sq_sl", [128, TB], BF16)
        
        slab16 = sb("slab16", [128, nHC * max(DBW, TB)], BF16)
        arf_cols = max(8 * 128 + 2 * DBW + 2 * S, nOC * TB + 4 * TB)
        arf = sb("arf", [128, arf_cols], F32)

        lut_s = sb("lut_s", [128, LUT_N], F32)
        cm_s = sb("cm_s", [128, (NSTR - 1) * 128 + TB], BF16)
        onec_s = sb("onec_s", [128, 1], BF16)
        oner_s = sb("oner_s", [1, 128], BF16)
        wqk_b = [sb(f"wqk_b{s}", [128, 128], BF16) for s in range(8)]
        qk_e = [sb(f"qk_e{s}", [128, TB], BF16) for s in range(4)]
        ve_t = [sb(f"ve_t{s}", [128, DBW], BF16) for s in range(1)]
        wo_t = [sb(f"wo_t{s}", [128, 2 * 128], BF16) for s in range(4)]
        wg_t = [sb(f"wg_t{s}", [128, 4 * 128], BF16) for s in range(2)]
        wu_t = [sb(f"wu_t{s}", [128, 4 * 128], BF16) for s in range(2)]
        wd_t = [sb(f"wd_t{s}", [128, 2 * 128], BF16) for s in range(6)]
        mn_f = sb("mn_f", [1, TB], F32)
        msq_f = sb("msq_f", [1, TB], F32)
        mn2_f = sb("mn2_f", [1, TB], F32)
        var_f = sb("var_f", [1, TB], F32)
        sd_f = sb("sd_f", [1, TB], F32)
        rstd_f = sb("rstd_f", [1, TB], F32)
        den_f = mn_f
        rden_f = msq_f
        bvec_f = sd_f
        rden_b = sb("rden_b", [1, TB], BF16)
        a_b = sb("a_b", [1, TB], BF16)
        b_b = sb("b_b", [1, TB], BF16)
        

        BKW = 512
        psA = [ctx.enter_context(nc.psum_tensor(f"psA{i}", [128, 2 * BKW], F32)) for i in range(2)]
        psS = [ctx.enter_context(nc.psum_tensor(f"psS{i}", [128, BKW], F32)) for i in range(4)]
        ps = [psA[0][:, 0:BKW], psA[0][:, BKW:2 * BKW],
              psS[0][:, :], psS[1][:, :], psS[2][:, :], psS[3][:, :],
              psA[1][:, 0:BKW], psA[1][:, BKW:2 * BKW]]

        T = dict(
            x_fm=x_fm, wqkv=wqkv, wout=wout, myidx=myidx, lutb=lutb, cmask=cmask,
            onesc=onesc, onesr=onesr, out_fm=out_fm, q_scr=q_scr, k_scr=k_scr,
            v_scr=v_scr, wob_scr=wob_scr, x2f_scr=x2f_scr, x2b_scr=x2b_scr,
            h_scr=h_scr, wcat_my=wcat_my, wc=wc,
            xact=xact, big=big, slab16=slab16, arf=arf,
            r1b_sl=r1b_sl, r1sq_sl=r1sq_sl,

            lut_s=lut_s, cm_s=cm_s,
            onec_s=onec_s, oner_s=oner_s, wqk_b=wqk_b, qk_e=qk_e,
            ve_t=ve_t, wo_t=wo_t, wg_t=wg_t, wu_t=wu_t, wd_t=wd_t,
            den_f=den_f, rden_f=rden_f, rden_b=rden_b,
            mn_f=mn_f, msq_f=msq_f, mn2_f=mn2_f, var_f=var_f, sd_f=sd_f,
            rstd_f=rstd_f, bvec_f=bvec_f, a_b=a_b, b_b=b_b, ps=ps, psA=psA,
        )
        with nc.Block() as block:
            E = Emitter(nc, ctx)
            build_phases(cfg, E, T)
            E.finalize(block)
    return nc


def build_phases(cfg, E, T):
    S, H, I, TB = cfg.S, cfg.H, cfg.I, cfg.TB
    nHC, nOC, nIC, nTB, nTT, NSTR = (cfg.nHC, cfg.nOC, cfg.nIC, cfg.nTB,
                                     cfg.nTT, cfg.NSTR)
    NH, DBW, nDB = cfg.NH, cfg.DBW, cfg.nDB
    PSW = 512
    ps = T["ps"]
    big, xact, slab16, arf = T["big"], T["xact"], T["slab16"], T["arf"]
    g = lambda n: T[n]

    et_off = 0
    ksl_off = nTT * TB
    vsl_off = ksl_off + 2 * S
    qsl_off = vsl_off + 2 * S
    et = lambda kt: big[:, et_off + kt * TB: et_off + (kt + 1) * TB]
    k_sl = lambda s, w: big[:, ksl_off + s * S: ksl_off + s * S + w]
    v_sl = lambda s, w: big[:, vsl_off + s * S: vsl_off + s * S + w]
    q_sl = lambda s: big[:, qsl_off + s * TB: qsl_off + (s + 1) * TB]
    r1b_t = lambda s: T["r1b_sl"][:, 0:TB]
    r1sq_t = lambda s: T["r1sq_sl"][:, 0:TB]
    tail0 = qsl_off + 2 * TB
    et_t = lambda s: big[:, tail0 + s * TB: tail0 + (s + 1) * TB]
    h_sl = lambda ic: big[:, ic * TB:(ic + 1) * TB]
    bc_s = big[:, tail0 + 2 * TB: tail0 + 3 * TB]
    sil_t = [big[:, tail0 + (3 + s) * TB: tail0 + (4 + s) * TB] for s in range(2)]
    h_t = [big[:, tail0 + (5 + s) * TB: tail0 + (6 + s) * TB] for s in range(2)]
    attn = lambda dc: slab16[:, dc * TB:(dc + 1) * TB]
    wv_b = lambda hc: slab16[:, hc * DBW:(hc + 1) * DBW]
    wqk_f = [arf[:, s * 128:(s + 1) * 128] for s in range(8)]
    wv_f = [arf[:, 8 * 128 + s * DBW: 8 * 128 + (s + 1) * DBW] for s in range(2)]
    xs_f = [arf[:, 8 * 128 + 2 * DBW + s * S: 8 * 128 + 2 * DBW + (s + 1) * S] for s in range(2)]
    r1 = arf[:, 0:nOC * TB]
    xres = [arf[:, nOC * TB + s * TB: nOC * TB + (s + 1) * TB] for s in range(2)]
    tmp_f = arf[:, nOC * TB + 2 * TB: nOC * TB + 3 * TB]
    x2f_t = [arf[:, nOC * TB + 3 * TB: nOC * TB + 4 * TB]]
    r2, x2r, out_t = r1, xres, x2f_t

    E.dma("sync", "s_lut", g("lut_s")[:, :], g("lutb")[:, :])
    E.dma("sync", "s_cm", g("cm_s")[:, :], g("cmask")[:, :])
    E.dma("sync", "s_onc", g("onec_s")[:, :], g("onesc")[:, :])
    E.dma("sync", "s_onr", g("oner_s")[:, :], g("onesr")[:, :])

    NG = min(4, nHC)
    runs = []
    t = 0
    while t < cfg.TPC:
        w = min(MRUN, cfg.TPC - t)
        runs.append((t, w))
        t += w
    MP = MRUN * 128
    m_part = [big[:, j * MP:(j + 1) * MP] for j in range(16)]
    m_idxb = [big[:, (16 + s) * MP:(17 + s) * MP] for s in range(3)]
    m_out = [big[:, (19 + s) * MP:(20 + s) * MP] for s in range(2)]
    E.wait("vector", "s_lut", 16)
    acc_v = {}

    def mat_dma(ri):
        t0, w = runs[ri]
        slot = ri % 3
        if ri >= 3:
            E.wait("gpsimd", "v_p", acc_v[ri - 3])
        E.dma("gpsimd", f"s_midx{slot}",
              m_idxb[slot][:, 0:w * 128].rearrange("p (t c) -> p t c", c=128),
              g("myidx")[t0:t0 + w, :, :].rearrange("t p c -> p t c"))

    for r0 in range(min(3, len(runs))):
        mat_dma(r0)
    for ri, (t0, w) in enumerate(runs):
        slot = ri % 3
        E.wait("vector", f"s_midx{slot}", 16 * (ri // 3 + 1))
        for j in range(LUT_N):
            E.run("vector", lambda e, w=w, j=j, slot=slot: e.tensor_scalar(
                out=m_part[j][:, 0:w * 128], in0=m_idxb[slot][:, 0:w * 128],
                scalar1=float(j), scalar2=g("lut_s")[:, j:j + 1],
                op0=OP.is_equal, op1=OP.mult))
        E.drain("vector")
        step = 1
        while step < 16:
            for j0 in range(0, 16, 2 * step):
                last = (2 * step >= 16)
                if last:
                    if ri >= 2:
                        E.wait("vector", f"s_wmy{ri % 2}", 16 * ((ri - 2) // 2 + 1))
                    fn = lambda e, w=w, ri=ri: e.tensor_tensor(
                        out=m_out[ri % 2][:, 0:w * 128], in0=m_part[0][:, 0:w * 128],
                        in1=m_part[8][:, 0:w * 128], op=OP.add)
                    acc_v[ri] = E.inc("vector", "v_p", fn)
                else:
                    fn = lambda e, j0=j0, s=step, w=w: e.tensor_tensor(
                        out=m_part[j0][:, 0:w * 128], in0=m_part[j0][:, 0:w * 128],
                        in1=m_part[j0 + s][:, 0:w * 128], op=OP.add)
                    E.run("vector", fn)
            E.drain("vector")
            step *= 2
        if ri + 3 < len(runs):
            mat_dma(ri + 3)
        E.wait("gpsimd", "v_p", acc_v[ri])
        E.dma("gpsimd", f"s_wmy{ri % 2}",
              g("wcat_my")[t0:t0 + w, :, :].rearrange("t p c -> p t c"),
              m_out[ri % 2][:, 0:w * 128].rearrange("p (t c) -> p t c", c=128))
        for k in range(cfg.KAG):
            if ((k + 1) * cfg.CSZ - 1) // MRUN == ri:
                E.wait("gpsimd", "s_wmy0", 16 * (ri // 2 + 1))
                if ri >= 1:
                    E.wait("gpsimd", "s_wmy1", 16 * ((ri - 1) // 2 + 1))
                ag_g = E.inc("gpsimd", "g_p", lambda e, k=k: e.collective_compute(
                    "AllGather", OP.bypass, replica_groups=[list(range(cfg.ncores))],
                    ins=[g("wcat_my")[k * cfg.CSZ:(k + 1) * cfg.CSZ, :, :].opt()],
                    outs=[g("wc")[k].ap().opt()]))
    mat_done_v = acc_v[len(runs) - 1]
    n_wmy = [E.count("s_wmy0"), E.count("s_wmy1")]
    for i4 in range(NG):
        E.dma("gpsimd", "s_wob", g("wob_scr")[i4 * (H // NG):(i4 + 1) * (H // NG), :],
              g("wout")[i4 * (H // NG):(i4 + 1) * (H // NG), :])
    n_wob = E.count("s_wob")

    qkcast_s, lastmm_t, stopmm_t, qkevac_s = {}, {}, {}, {}
    xcast_s = {}
    for ft in range(2 * nHC):
        isq = ft < nHC
        for hc in range(nHC):
            it = ft * nHC + hc
            slot = it % 8
            if ft == 0:
                xsl = hc % 2
                if hc >= 2:
                    E.wait("sync", "sc_p", xcast_s[hc - 2])
                E.dma("sync", f"s_xs{xsl}", xs_f[xsl],
                      g("x_fm")[hc * 128:(hc + 1) * 128, :])
                E.wait("scalar", f"s_xs{xsl}", 16 * (hc // 2 + 1))
                xcast_s[hc] = E.inc("scalar", "sc_p",
                                    lambda e, s=xsl, k=hc: e.copy(
                                        out=xact[:, k * S:(k + 1) * S], in_=xs_f[s]))
            if it >= 8:
                E.wait("sync", "sc_p", qkcast_s[it - 8])
            E.dma("sync", f"s_wqk{slot}", wqk_f[slot],
                  g("wqkv")[hc * 128:(hc + 1) * 128, ft * 128:(ft + 1) * 128])
            if it >= 8:
                E.wait("scalar", "t_p", lastmm_t[it - 8])
            E.wait("scalar", f"s_wqk{slot}", 16 * (it // 8 + 1))
            qkcast_s[it] = E.inc("scalar", "sc_p",
                                 lambda e, s=slot: e.copy(out=g("wqk_b")[s][:, :],
                                                          in_=wqk_f[s]))
            E.wait("tensor", "sc_p", qkcast_s[it])
            if ft == 0:
                E.wait("tensor", "sc_p", xcast_s[hc])
            if hc == 0 and ft >= 2:
                E.wait("tensor", "sc_p", qkevac_s[(ft - 2, nTB - 1)])
            bk0 = (ft % 2) * nTB
            for tb in range(nTB):
                fn = lambda e, s=slot, hc=hc, tb=tb, bk0=bk0, st=(hc == 0), sp=(hc == nHC - 1): e.matmul(
                    ps[bk0 + tb][:, 0:TB], g("wqk_b")[s][:, :],
                    xact[:, hc * S + tb * TB: hc * S + (tb + 1) * TB],
                    start=st, stop=sp, skip_group_check=True)
                if hc == nHC - 1:
                    stopmm_t[(ft, tb)] = E.inc("tensor", "t_p", fn)
                    lastmm_t[it] = stopmm_t[(ft, tb)]
                elif tb == nTB - 1:
                    lastmm_t[it] = E.inc("tensor", "t_p", fn)
                else:
                    E.run("tensor", fn)
        for tb in range(nTB):
            es = (ft * nTB + tb) % 4
            idx = ft * nTB + tb
            E.wait("scalar", "t_p", stopmm_t[(ft, tb)])
            if idx >= 4:
                E.wait("scalar", f"s_qkst{es}", 16 * (idx // 4))
            sc = cfg.qscale if isq else 1.0
            qkevac_s[(ft, tb)] = E.inc("scalar", "sc_p",
                                       lambda e, tb=tb, es=es, sc=sc, bk0=(ft % 2) * nTB:
                                       e.activation(
                                           out=g("qk_e")[es][:, :], in_=ps[bk0 + tb][:, 0:TB],
                                           func=AF.Copy, scale=sc))
            E.wait("scalar", "sc_p", qkevac_s[(ft, tb)])
            dst = g("q_scr") if isq else g("k_scr")
            r0 = (ft if isq else ft - nHC) * 128
            E.dma("scalar", f"s_qkst{es}", dst[r0:r0 + 128, tb * TB:(tb + 1) * TB],
                  g("qk_e")[es][:, :])
    n_qkst = [E.count(f"s_qkst{e_}") for e_ in range(4)]


    wvc_s, vstop_t, vevac_s = {}, {}, {}
    for db in range(nDB):
        for hc in range(nHC):
            it = db * nHC + hc
            slot = hc % 2
            if it >= 2:
                E.wait("sync", "sc_p", wvc_s[it - 2])
            E.dma("sync", f"s_wv{slot}", wv_f[slot],
                  g("wqkv")[hc * 128:(hc + 1) * 128, 2 * H + db * DBW: 2 * H + (db + 1) * DBW])
            if db >= 1:
                E.wait("scalar", "t_p", vstop_t[(db - 1, nTT - 1)])
            E.wait("scalar", f"s_wv{slot}", 16 * (it // 2 + 1))
            wvc_s[it] = E.inc("scalar", "sc_p",
                              lambda e, hc=hc, s=slot: e.copy(out=wv_b(hc), in_=wv_f[s]))
        for tt in range(nTT):
            bank = 4 + (tt % 2)
            it = db * nTT + tt
            if it == 0:
                E.wait("tensor", "sc_p", qkevac_s[(2 * nHC - 1, nTB - 1)])
            if it >= 2:
                E.wait("tensor", "sc_p", vevac_s[it - 2])
            E.wait("tensor", "sc_p", wvc_s[db * nHC + nHC - 1])
            for hc in range(nHC):
                fn = lambda e, hc=hc, tt=tt, bank=bank, st=(hc == 0), sp=(hc == nHC - 1): e.matmul(
                    ps[bank][:, 0:DBW],
                    xact[:, hc * S + tt * 128: hc * S + (tt + 1) * 128],
                    wv_b(hc), start=st, stop=sp, skip_group_check=True)
                if hc == nHC - 1:
                    vstop_t[(db, tt)] = E.inc("tensor", "t_p", fn)
                else:
                    E.run("tensor", fn)
            E.wait("scalar", "t_p", vstop_t[(db, tt)])
            if it >= 1:
                E.wait("scalar", "s_vst", 16 * it)
            es = 0
            vevac_s[it] = E.inc("scalar", "sc_p",
                                lambda e, bank=bank, es=es: e.copy(out=g("ve_t")[es][:, :],
                                                                   in_=ps[bank][:, 0:DBW]))
            E.wait("scalar", "sc_p", vevac_s[it])
            E.dma("scalar", "s_vst",
                  g("v_scr")[tt * 128:(tt + 1) * 128, db * DBW:(db + 1) * DBW],
                  g("ve_t")[es][:, :])
    n_vst = E.count("s_vst")
    p1_sc = E.count("sc_p")
    tp_p1v = E.count("t_p")

    E.wait("vector", "s_cm", 16)
    E.wait("scalar", "v_p", mat_done_v)
    E.wait("scalar", "s_wmy0", n_wmy[0])
    E.wait("scalar", "s_wmy1", n_wmy[1])
    E.wait("sync", "v_p", mat_done_v)
    E.wait("sync", "s_wmy0", n_wmy[0])
    E.wait("sync", "s_wmy1", n_wmy[1])
    E.wait("gpsimd", "s_wob", n_wob)
    E.wait("sync", "s_wob", n_wob)
    n_strad = 0
    msk_v, avden_t, attn_v, denev_v, bcs_v = {}, {}, {}, {}, {}
    wo_use_t, r1_v, x2f_v = {}, {}, {}
    wo_qctr = {0: 0, 1: 0}
    vsl_fill = {}
    prev_exp_last = 0
    n_x2f = 0
    ln_prev_stat = {'t': 0, 'sc': 0}
    wo_last_t = 0
    x2f_cnt = {}
    avkt_t = {}
    msk_last = {}
    for qb in range(nTB):
        nkt = (qb + 1) * NSTR
        nkt_prev_qb = qb * NSTR

        def den_tail(iq2):
            head2 = iq2 % NH
            bav = 2 + (head2 % 2)
            bden = 4 + (head2 % 2)
            E.wait("scalar", "t_p", avden_t[iq2])
            E.run("scalar", lambda e, bden=bden: e.activation(
                out=g("rden_f")[:, :], in_=ps[bden][0:1, 0:TB], func=AF.Ln))
            E.drain("scalar")
            denev_v[iq2] = E.inc("scalar", "sc_p", lambda e: e.activation(
                out=g("rden_b")[:, :], in_=g("rden_f")[:, :], func=AF.Exp, scale=-1.0))
            E.wait("tensor", "sc_p", denev_v[iq2])
            E.wait("tensor", "s_onr", 16)
            if iq2 > 0:
                E.wait("tensor", "v_p", bcs_v[iq2 - 1])
            bcb = 2 + ((head2 + 1) % 2)
            if iq2 >= 1:
                E.wait("tensor", "v_p", attn_v[iq2 - 1])
            tbc = E.inc("tensor", "t_p", lambda e, bcb=bcb: e.matmul(
                ps[bcb][:, 0:TB], g("oner_s")[:, :], g("rden_b")[:, :],
                start=True, stop=True, skip_group_check=True))
            E.wait("vector", "t_p", tbc)
            bcs_v[iq2] = E.inc("vector", "v_p",
                               lambda e, bcb=bcb: e.tensor_copy(bc_s, ps[bcb][:, 0:TB]))
            E.drain("vector")
            if head2 == 0 and iq2 >= NH:
                E.wait("vector", "t_p", wo_last_t)
            attn_v[iq2] = E.inc("vector", "v_p",
                                lambda e, head2=head2, bav=bav: e.tensor_tensor(
                                    out=attn(head2), in0=ps[bav][:, 0:TB], in1=bc_s,
                                    op=OP.mult))
            E.drain("vector")

        def emit_av_kt(iq2, kt, nkt2, last):
            head2 = iq2 % NH
            bav = 2 + (head2 % 2)
            bden = 4 + (head2 % 2)
            sl2 = head2 % 2
            if kt == 0:
                if head2 <= 1 and iq2 >= NH and (iq2 // NH) == qb:
                    E.wait("tensor", "v_p", x2f_v[(qb - 1) * nOC + nOC - 1])
                if iq2 >= 2:
                    E.wait("tensor", "v_p", attn_v[iq2 - 2])
                    E.wait("tensor", "sc_p", denev_v[iq2 - 2])
                if iq2 >= 1:
                    E.wait("tensor", "v_p", bcs_v[iq2 - 1])
                if iq2 in msk_last:
                    E.wait("tensor", "v_p", msk_last[iq2])
                E.wait("tensor", f"s_vsl{sl2}", vsl_fill[iq2])
                E.wait("tensor", "s_onc", 16)
            E.run("tensor", lambda e, sl2=sl2, kt=kt, nkt2=nkt2, bav=bav,
                  st=(kt == 0), sp=last: e.matmul(
                      ps[bav][:, 0:TB], v_sl(sl2, nkt2 * 128)[:, kt * 128:(kt + 1) * 128],
                      et(kt), start=st, stop=sp, skip_group_check=True))
            avkt_t[(iq2, kt)] = E.inc("tensor", "t_p",
                                      lambda e, kt=kt, bden=bden, st=(kt == 0), sp=last: e.matmul(
                                          ps[bden][0:1, 0:TB], g("onec_s")[:, :], et(kt),
                                          start=st, stop=sp, skip_group_check=True))
            if last:
                avden_t[iq2] = avkt_t[(iq2, kt)]

        for head in range(NH):
            sl = head % 2
            iq = qb * NH + head
            if iq >= 2:
                E.wait("sync", "t_p", avden_t[iq - 2])
            idxq = head * nTB + qb
            idxk = (nHC + head) * nTB + qb
            E.wait("sync", f"s_qkst{idxq % 4}", 16 * (idxq // 4 + 1))
            E.wait("sync", f"s_qkst{idxk % 4}", 16 * (idxk // 4 + 1))
            idxv = (nDB - 1) * nTT + (NSTR * (qb + 1) - 1)
            E.wait("sync", "s_vst", 16 * (idxv + 1))
            E.dma("sync", f"s_qsl{sl}", q_sl(sl),
                  g("q_scr")[head * 128:(head + 1) * 128, qb * TB:(qb + 1) * TB])
            E.dma("sync", f"s_ksl{sl}", k_sl(sl, nkt * 128),
                  g("k_scr")[head * 128:(head + 1) * 128, 0:nkt * 128])
            E.dma("sync", f"s_vsl{sl}",
                  v_sl(sl, nkt * 128).rearrange("p (t c) -> p t c", c=128),
                  g("v_scr")[0:nkt * 128, head * 128:(head + 1) * 128]
                  .rearrange("(t p) c -> p t c", p=128))
            nfill = 16 * (iq // 2 + 1)
            vsl_fill[iq] = nfill
            E.wait("tensor", f"s_qsl{sl}", nfill)
            E.wait("tensor", f"s_ksl{sl}", nfill)
            exp_slot = {}
            PAIR_OK = (TB == 512)
            SBK4 = (0, 1, 6, 7)
            ns_end = nkt - NSTR
            last_exp = 0
            kt = 0
            while kt < nkt:
                pair = (kt + 1 < ns_end and kt % 2 == 0 and PAIR_OK)
                subkts = (kt, kt + 1) if pair else (kt,)
                for k2 in subkts:
                    c = k2
                    bank = SBK4[c % 4]
                    if c >= 4:
                        E.wait("tensor", "sc_p", exp_slot[c - 4])
                    elif iq > 0:
                        E.wait("tensor", "sc_p", prev_exp_last)
                    else:
                        E.wait("tensor", "sc_p", p1_sc)
                    if c == 0 and head == 0 and qb > 0:
                        E.wait("tensor", "v_p", r1_v[(qb - 1) * nOC + nOC - 1])
                    tsc = E.inc("tensor", "t_p", lambda e, sl=sl, k2=k2, bank=bank, nkt=nkt: e.matmul(
                        ps[bank][:, 0:TB], k_sl(sl, nkt * 128)[:, k2 * 128:(k2 + 1) * 128],
                        q_sl(sl), start=True, stop=True, skip_group_check=True))
                    if head > 0:
                        emit_av_kt(iq - 1, k2, nkt, last=(k2 == nkt - 1))
                kt1 = subkts[-1]
                E.wait("scalar", "t_p", tsc)
                if head > 0:
                    E.wait("scalar", "t_p", avkt_t[(iq - 1, kt1)])
                elif qb > 0:
                    pl = qb * NH - 1
                    kk = min(kt1, nkt_prev_qb - 1)
                    if kk >= 0:
                        E.wait("scalar", "t_p", avkt_t[(pl, kk)])
                if pair:
                    pb = (kt // 2) % 2
                    v = E.inc("scalar", "sc_p",
                              lambda e, pb=pb, kt=kt: e.activation(
                                  out=big[:, et_off + kt * TB: et_off + (kt + 2) * TB],
                                  in_=T["psA"][pb][:, 0:2 * TB], func=AF.Exp))
                    exp_slot[kt] = exp_slot[kt + 1] = v
                    last_exp = v
                elif kt1 >= nkt - NSTR:
                    es = n_strad % 2
                    n_strad += 1
                    if n_strad > 2:
                        E.wait("scalar", "v_p", msk_v[n_strad - 2])
                    v = E.inc("scalar", "sc_p",
                              lambda e, bank=SBK4[kt1 % 4], es=es: e.activation(
                                  out=et_t(es), in_=ps[bank][:, 0:TB], func=AF.Exp))
                    exp_slot[kt1] = v
                    last_exp = v
                    E.wait("vector", "sc_p", v)
                    j = kt1 - (nkt - NSTR)
                    msk_v[n_strad] = E.inc("vector", "v_p",
                                           lambda e, kt1=kt1, es=es, j=j: e.tensor_tensor(
                                               out=et(kt1), in0=et_t(es),
                                               in1=g("cm_s")[:, (NSTR - 1 - j) * 128:
                                                             (NSTR - 1 - j) * 128 + TB],
                                               op=OP.mult))
                    msk_last[iq] = msk_v[n_strad]
                else:
                    v = E.inc("scalar", "sc_p",
                              lambda e, bank=SBK4[kt1 % 4], kt1=kt1: e.activation(
                                  out=et(kt1), in_=ps[bank][:, 0:TB], func=AF.Exp))
                    exp_slot[kt1] = v
                    last_exp = v
                kt += 2 if pair else 1
            prev_exp_last = last_exp
            if head > 0:
                den_tail(iq - 1)
        liq = qb * NH + NH - 1
        E.wait("tensor", "sc_p", prev_exp_last)
        for kt in range(nkt):
            emit_av_kt(liq, kt, nkt, last=(kt == nkt - 1))
        den_tail(liq)
        n_attn_qb = attn_v[qb * NH + NH - 1]
        first_wo = True
        for ot in range(nOC):
            bank = ot % 2
            io = qb * nOC + ot
            if io >= 2:
                E.wait("tensor", "v_p", r1_v[io - 2])
            for dc2 in range(nHC // 2):
                dc = 2 * dc2
                qsel = dc2 % 2
                eng = "sync" if qsel == 0 else "gpsimd"
                jq = wo_qctr[qsel]
                wo_qctr[qsel] += 1
                ws = 2 * qsel + (jq % 2)
                if jq >= 2:
                    E.wait(eng, "t_p", wo_use_t[(qsel, jq - 2)])
                E.dma(eng, f"s_wot{ws}",
                      g("wo_t")[ws][:, :].rearrange("p (t c) -> p t c", c=128),
                      g("wob_scr")[dc * 128:(dc + 2) * 128,
                                   ot * 128:(ot + 1) * 128].rearrange("(t p) c -> p t c", p=128))
                E.wait("tensor", f"s_wot{ws}", 16 * (jq // 2 + 1))
                E.wait("tensor", "v_p", n_attn_qb)
                if first_wo:
                    E.wait("tensor", "sc_p", prev_exp_last)
                    first_wo = False
                E.run("tensor", lambda e, ws=ws, dc=dc, bank=bank, st=(dc == 0): e.matmul(
                    ps[bank][:, 0:TB], g("wo_t")[ws][:, 0:128], attn(dc),
                    start=st, stop=False, skip_group_check=True))
                wo_use_t[(qsel, jq)] = E.inc("tensor", "t_p",
                                             lambda e, ws=ws, dc=dc, bank=bank,
                                             sp=(dc == nHC - 2): e.matmul(
                                                 ps[bank][:, 0:TB], g("wo_t")[ws][:, 128:256],
                                                 attn(dc + 1),
                                                 start=False, stop=sp, skip_group_check=True))
            wo_last_t = wo_use_t[(qsel, jq)]
            xs = ot % 2
            if io >= 2:
                E.wait("sync", "v_p", r1_v[io - 2])
            if qb == 0 and ot == 0:
                E.wait("sync", "sc_p", p1_sc)
            E.dma("sync", f"s_xres{xs}", xres[xs],
                  g("x_fm")[ot * 128:(ot + 1) * 128, qb * TB:(qb + 1) * TB])
            if qb == 0 and ot == 0:
                E.wait("vector", "sc_p", p1_sc)
            if qb > 0 and ot == 0:
                E.wait("vector", "sc_p", ln_prev_stat['sc'])
            E.wait("vector", "t_p", wo_last_t)
            E.wait("vector", f"s_xres{xs}", 16 * (io // 2 + 1))
            r1_v[io] = E.inc("vector", "v_p",
                             lambda e, ot=ot, bank=bank, xs=xs: e.tensor_tensor(
                                 out=r1[:, ot * TB:(ot + 1) * TB], in0=ps[bank][:, 0:TB],
                                 in1=xres[xs], op=OP.add))
        rv = {oc: r1_v[qb * nOC + oc] for oc in range(nOC)}
        emit_ln(cfg, E, T, r1, rv, r1b_t, r1sq_t, (2, 3, 4, 5), ln_prev_stat)
        for oc in range(nOC):
            s = 0
            io = qb * nOC + oc
            if io >= 1:
                E.wait("vector", "s_x2f", 16 * io)
            E.run("vector", lambda e, oc=oc: e.tensor_tensor(
                out=tmp_f, in0=r1[:, oc * TB:(oc + 1) * TB], in1=ps[4][:, 0:TB], op=OP.mult))
            E.drain("vector")
            x2f_v[io] = E.inc("vector", "v_p", lambda e, s=s: e.tensor_tensor(
                out=x2f_t[s], in0=tmp_f, in1=ps[5][:, 0:TB], op=OP.subtract))
            E.wait("gpsimd", "v_p", x2f_v[io])
            n_x2f = E.dma("gpsimd", "s_x2f",
                          g("x2f_scr")[oc * 128:(oc + 1) * 128, qb * TB:(qb + 1) * TB],
                          x2f_t[s])
            x2f_cnt[(qb, oc)] = n_x2f

    E.wait("gpsimd", "t_p", tp_p1v)
    for hc in range(nHC):
        E.wait("gpsimd", "s_x2f", x2f_cnt[(nTB - 1, hc)])
        E.dma("gpsimd", f"s_x2g{hc * NG // nHC}", xact[:, hc * S:(hc + 1) * S],
              g("x2f_scr")[hc * 128:(hc + 1) * 128, :])
    n_x2g = [E.count(f"s_x2g{i}") for i in range(NG)]
    E.wait("gpsimd", "g_p", ag_g)
    E.wait("sync", "g_p", ag_g)
    sil_sc, h_v, gu_t = {}, {}, {}
    h_fill = {0: 0, 1: 0}
    QW = min(4, nHC)
    nHQ = (nHC + QW - 1) // QW
    wcv = [g("wc")[k][:, :, :].rearrange("(h a) p c -> h a p c", a=nIC)
           for k in range(cfg.KAG)]
    BL = cfg.ncores * nIC
    wgu_fill = {0: 0, 1: 0}
    for ic in range(nIC):
        for hc in range(nHC):
            iw = ic * nHC + hc
            hq, hsub = divmod(hc, QW)
            jq = ic * nHQ + hq
            sg = jq % 2
            if hsub == 0:
                w = min(QW, nHC - hc)
                if jq >= 2:
                    pic2, phq2 = divmod(jq - 2, nHQ)
                    iw2 = pic2 * nHC + min((phq2 + 1) * QW, nHC) - 1
                    E.wait("gpsimd", "t_p", gu_t[iw2])
                    E.wait("sync", "t_p", gu_t[iw2])
                kg, hg = divmod(hc, cfg.ncores)
                ku, hu = divmod(nHC + hc, cfg.ncores)
                E.dma("gpsimd", f"s_wg{sg}",
                      g("wg_t")[sg][:, 0:w * 128].rearrange("p (t c) -> p t c", c=128),
                      wcv[kg][hg:hg + w, ic:ic + 1, :, :].rearrange("t a p c -> p (t a) c"))
                E.dma("sync", f"s_wu{sg}",
                      g("wu_t")[sg][:, 0:w * 128].rearrange("p (t c) -> p t c", c=128),
                      wcv[ku][hu:hu + w, ic:ic + 1, :, :].rearrange("t a p c -> p (t a) c"))
                wgu_fill[sg] += 16
                E.wait("tensor", f"s_wg{sg}", wgu_fill[sg])
                E.wait("tensor", f"s_wu{sg}", wgu_fill[sg])
            if ic == 0 and hc % (nHC // NG) == 0:
                E.wait("tensor", f"s_x2g{hc * NG // nHC}", n_x2g[hc * NG // nHC])
            if iw == 0:
                E.wait("tensor", "v_p", E.count("v_p"))
                E.wait("tensor", "sc_p", E.count("sc_p"))
            for qb in range(nTB):
                if ic >= 1 and hc == 0:
                    E.wait("tensor", "sc_p", sil_sc[(ic - 1, qb)])
                    E.wait("tensor", "v_p", h_v[(ic - 1, qb)])
                fng = lambda e, sg=sg, hc=hc, qb=qb, hs=hsub, st=(hc == 0), sp=(hc == nHC - 1): e.matmul(
                    ps[qb][:, 0:TB], g("wg_t")[sg][:, hs * 128:(hs + 1) * 128],
                    xact[:, hc * S + qb * TB: hc * S + (qb + 1) * TB],
                    start=st, stop=sp, skip_group_check=True)
                fnu = lambda e, sg=sg, hc=hc, qb=qb, hs=hsub, st=(hc == 0), sp=(hc == nHC - 1): e.matmul(
                    ps[nTB + qb][:, 0:TB], g("wu_t")[sg][:, hs * 128:(hs + 1) * 128],
                    xact[:, hc * S + qb * TB: hc * S + (qb + 1) * TB],
                    start=st, stop=sp, skip_group_check=True)
                E.run("tensor", fng)
                if qb == nTB - 1:
                    gu_t[iw] = E.inc("tensor", "t_p", fnu)
                else:
                    E.run("tensor", fnu)
        for qb in range(nTB):
            s = qb % 2
            iq = ic * nTB + qb
            E.wait("scalar", "t_p", gu_t[ic * nHC + nHC - 1])
            if iq >= 2:
                E.wait("scalar", "v_p", h_v[((iq - 2) // nTB, (iq - 2) % nTB)])
            sil_sc[(ic, qb)] = E.inc("scalar", "sc_p",
                                     lambda e, qb=qb, s=s: e.activation(
                                         out=sil_t[s], in_=ps[qb][:, 0:TB],
                                         func=AF.Silu))
            E.wait("vector", "sc_p", sil_sc[(ic, qb)])
            if h_fill[s] >= 1:
                E.wait("vector", f"s_h{s}", 16 * h_fill[s])
            h_v[(ic, qb)] = E.inc("vector", "v_p",
                                  lambda e, qb=qb, s=s: e.tensor_tensor(
                                      out=h_t[s], in0=sil_t[s],
                                      in1=ps[nTB + qb][:, 0:TB], op=OP.mult))
            E.wait("scalar", "v_p", h_v[(ic, qb)])
            E.dma("scalar", f"s_h{s}",
                  g("h_scr")[ic * 128:(ic + 1) * 128, qb * TB:(qb + 1) * TB],
                  h_t[s])
            h_fill[s] += 1
    n_h = [E.count(f"s_h{s_}") for s_ in range(2)]

    for s_ in range(2):
        E.wait("sync", f"s_h{s_}", n_h[s_])
    E.wait("sync", "s_x2f", n_x2f)
    E.wait("sync", "t_p", E.count("t_p"))
    do_t, r2_v, wd_use_t = {}, {}, {}
    wd_qctr = {0: 0, 1: 0}
    n_out = 0
    first_p4 = True
    nICP = (nIC + 1) // 2
    h_free_t, n_hl_q = {}, {}
    for s_ in range(2):
        E.wait("gpsimd", f"s_h{s_}", n_h[s_])
    E.wait("gpsimd", "t_p", E.count("t_p"))
    for icp in range(nICP):
        ic0, icw = 2 * icp, min(2, nIC - 2 * icp)
        eng2 = "sync" if icp % 2 == 0 else "gpsimd"
        for icsub in range(icw):
            ic = ic0 + icsub
            E.dma(eng2, f"s_hl{icp % 2}", h_sl(ic),
                  g("h_scr")[ic * 128:(ic + 1) * 128, 0:TB])
    n_hl_q[0] = (E.count("s_hl0"), E.count("s_hl1"))
    for qb in range(nTB):
        n_hl = n_hl_q[qb]
        for op_ in range(nOC // 2):
            ot0, ot1 = 2 * op_, 2 * op_ + 1
            gp_pair = qb * (nOC // 2) + op_
            b0, b1 = 2 * (gp_pair % 2), 2 * (gp_pair % 2) + 1
            if gp_pair >= 2:
                qb_p, op_p = divmod(gp_pair - 2, nOC // 2)
                E.wait("tensor", "v_p", r2_v[qb_p * nOC + 2 * op_p + 1])
            for ic in range(nIC):
                qsel = ic % 2
                eng = "sync" if qsel == 0 else "gpsimd"
                jq = wd_qctr[qsel]
                wd_qctr[qsel] += 1
                ws = 3 * qsel + (jq % 3)
                if jq >= 3:
                    E.wait(eng, "t_p", wd_use_t[(qsel, jq - 3)])
                gid_d = 2 * cfg.G1 + ic * nOC + ot0
                kd, ld = divmod(gid_d, BL)
                E.dma(eng, f"s_wd{ws}",
                      g("wd_t")[ws][:, :].rearrange("p (t c) -> p t c", c=128),
                      g("wc")[kd][ld:ld + 2, :, :].rearrange("t p c -> p t c"))
                E.wait("tensor", f"s_wd{ws}", 16 * (jq // 3 + 1))
                if first_p4:
                    E.wait("tensor", "v_p", E.count("v_p"))
                    E.wait("tensor", "sc_p", E.count("sc_p"))
                    first_p4 = False
                if ic == 0:
                    E.wait("tensor", "s_hl0", n_hl[0])
                    E.wait("tensor", "s_hl1", n_hl[1])
                E.run("tensor", lambda e, ws=ws, ic=ic, b=b0, st=(ic == 0),
                      sp=(ic == nIC - 1): e.matmul(
                          ps[b][:, 0:TB], g("wd_t")[ws][:, 0:128], h_sl(ic),
                          start=st, stop=sp, skip_group_check=True))
                tmm = E.inc("tensor", "t_p",
                            lambda e, ws=ws, ic=ic, b=b1, st=(ic == 0),
                            sp=(ic == nIC - 1): e.matmul(
                                ps[b][:, 0:TB], g("wd_t")[ws][:, 128:256],
                                h_sl(ic),
                                start=st, stop=sp, skip_group_check=True))
                wd_use_t[(qsel, jq)] = tmm
                if op_ == nOC // 2 - 1:
                    h_free_t[ic] = tmm
            tdo_pair = tmm
            for i, (ot, bank) in enumerate(((ot0, b0), (ot1, b1))):
                io = qb * nOC + ot
                do_t[(qb, ot)] = tdo_pair
                xs = ot % 2
                if io >= 2:
                    E.wait("sync", "v_p", r2_v[io - 2])
                E.dma("sync", f"s_x2r{xs}", x2r[xs],
                      g("x2f_scr")[ot * 128:(ot + 1) * 128, qb * TB:(qb + 1) * TB])
                if qb == 0 and ot == 0:
                    E.wait("vector", "s_x2f", n_x2f)
                    E.wait("vector", "sc_p", ln_prev_stat['sc'])
                E.wait("vector", "t_p", tdo_pair)
                E.wait("vector", f"s_x2r{xs}", 16 * (io // 2 + 1))
                r2_v[io] = E.inc("vector", "v_p",
                                 lambda e, ot=ot, bank=bank, xs=xs: e.tensor_tensor(
                                     out=r2[:, ot * TB:(ot + 1) * TB], in0=ps[bank][:, 0:TB],
                                     in1=x2r[xs], op=OP.add))
        if qb + 1 < nTB:
            for icp in range(nICP):
                ic0, icw = 2 * icp, min(2, nIC - 2 * icp)
                eng2 = "sync" if icp % 2 == 0 else "gpsimd"
                E.wait(eng2, "t_p", h_free_t[ic0 + icw - 1])
                for icsub in range(icw):
                    ic = ic0 + icsub
                    E.dma(eng2, f"s_hl{icp % 2}", h_sl(ic),
                          g("h_scr")[ic * 128:(ic + 1) * 128,
                                     (qb + 1) * TB:(qb + 2) * TB])
            n_hl_q[qb + 1] = (E.count("s_hl0"), E.count("s_hl1"))
        rv = {oc: r2_v[qb * nOC + oc] for oc in range(nOC)}
        emit_ln(cfg, E, T, r2, rv, r1b_t, r1sq_t, (6, 7, 4, 5), ln_prev_stat)
        for oc in range(nOC):
            s = 0
            io = qb * nOC + oc
            if io >= 1:
                E.wait("vector", "s_out", 16 * io)
            E.run("vector", lambda e, oc=oc: e.tensor_tensor(
                out=tmp_f, in0=r2[:, oc * TB:(oc + 1) * TB], in1=ps[4][:, 0:TB], op=OP.mult))
            E.drain("vector")
            co = E.inc("vector", "v_p", lambda e, s=s: e.tensor_tensor(
                out=out_t[s], in0=tmp_f, in1=ps[5][:, 0:TB], op=OP.subtract))
            E.wait("gpsimd", "v_p", co)
            n_out = E.dma("gpsimd", "s_out",
                          g("out_fm")[oc * 128:(oc + 1) * 128, qb * TB:(qb + 1) * TB],
                          out_t[s])
    E.wait("gpsimd", "s_out", n_out)


def emit_ln(cfg, E, T, r_slab, r_v, r1b_t, r1sq_t, banks, ln_prev_stat):
    H, TB, nOC = cfg.H, cfg.TB, cfg.nOC
    g = lambda n: T[n]
    ps = T["ps"]
    bsum, bsq, ba, bb = banks
    stat_t = {}
    crs_last = 0
    for oc in range(nOC):
        s = oc % 2
        E.wait("scalar", "v_p", r_v[oc])
        if oc >= 1:
            E.wait("scalar", "t_p", stat_t[oc - 1])
        else:
            E.wait("scalar", "t_p", ln_prev_stat['t'])
        crb = E.inc("scalar", "sc_p", lambda e, oc=oc, s=s: e.copy(
            out=r1b_t(s), in_=r_slab[:, oc * TB:(oc + 1) * TB]))
        crs = E.inc("scalar", "sc_p", lambda e, oc=oc, s=s: e.activation(
            out=r1sq_t(s), in_=r_slab[:, oc * TB:(oc + 1) * TB], func=AF.Square))
        crs_last = crs
        E.wait("tensor", "sc_p", crs)
        E.run("tensor", lambda e, s=s, st=(oc == 0), sp=(oc == nOC - 1): e.matmul(
            ps[bsum][0:1, 0:TB], g("onec_s")[:, :], r1b_t(s),
            start=st, stop=sp, skip_group_check=True))
        stat_t[oc] = E.inc("tensor", "t_p",
                           lambda e, s=s, st=(oc == 0), sp=(oc == nOC - 1): e.matmul(
                               ps[bsq][0:1, 0:TB], g("onec_s")[:, :], r1sq_t(s),
                               start=st, stop=sp, skip_group_check=True))
    ln_prev_stat['t'] = stat_t[nOC - 1]
    ln_prev_stat['sc'] = crs_last
    E.wait("vector", "t_p", stat_t[nOC - 1])
    E.run("vector", lambda e: e.tensor_scalar(out=g("mn_f")[:, :], in0=ps[bsum][0:1, 0:TB],
                                              scalar1=1.0 / H, scalar2=None, op0=OP.mult))
    E.run("vector", lambda e: e.tensor_scalar(out=g("msq_f")[:, :], in0=ps[bsq][0:1, 0:TB],
                                              scalar1=1.0 / H, scalar2=None, op0=OP.mult))
    E.drain("vector")
    E.run("vector", lambda e: e.tensor_tensor(out=g("mn2_f")[:, :], in0=g("mn_f")[:, :],
                                              in1=g("mn_f")[:, :], op=OP.mult))
    E.drain("vector")
    E.run("vector", lambda e: e.tensor_tensor(
        out=g("var_f")[:, :], in0=g("msq_f")[:, :], in1=g("mn2_f")[:, :], op=OP.subtract))
    E.drain("vector")
    cvar = E.inc("vector", "v_p", lambda e: e.tensor_scalar(
        out=g("msq_f")[:, :], in0=g("var_f")[:, :], scalar1=EPS, scalar2=None, op0=OP.add))
    E.wait("scalar", "v_p", cvar)
    E.run("scalar", lambda e: e.activation(
        out=g("sd_f")[:, :], in_=g("msq_f")[:, :], func=AF.Ln))
    E.drain("scalar")
    csd = E.inc("scalar", "sc_p", lambda e: e.activation(
        out=g("rstd_f")[:, :], in_=g("sd_f")[:, :], func=AF.Exp, scale=-0.5))
    E.wait("vector", "sc_p", csd)
    E.run("vector", lambda e: e.tensor_tensor(out=g("bvec_f")[:, :], in0=g("mn_f")[:, :],
                                              in1=g("rstd_f")[:, :], op=OP.mult))
    E.drain("vector")
    E.run("vector", lambda e: e.tensor_copy(g("a_b")[:, :], g("rstd_f")[:, :]))
    E.drain("vector")
    cab = E.inc("vector", "v_p", lambda e: e.tensor_copy(g("b_b")[:, :], g("bvec_f")[:, :]))
    E.wait("tensor", "v_p", cab)
    E.run("tensor", lambda e: e.matmul(ps[ba][:, 0:TB], g("oner_s")[:, :], g("a_b")[:, :],
                                       start=True, stop=True, skip_group_check=True))
    tab = E.inc("tensor", "t_p", lambda e: e.matmul(
        ps[bb][:, 0:TB], g("oner_s")[:, :], g("b_b")[:, :], start=True, stop=True,
        skip_group_check=True))
    E.wait("vector", "t_p", tab)



def make_host_inputs(cfg: Cfg, x, w_qkv, w_out, lut, gate_idx, up_idx, down_idx):
    import ml_dtypes
    BF = ml_dtypes.bfloat16
    lutb = np.broadcast_to(np.asarray(lut, np.float32), (128, LUT_N)).copy()
    kp = np.arange(128)[:, None]
    uu = np.arange((cfg.NSTR - 1) * 128 + cfg.TB)[None, :]
    cm = (uu >= kp + (cfg.NSTR - 1) * 128).astype(BF)
    onesc = np.ones((128, 1), dtype=BF)
    onesr = np.ones((1, 128), dtype=BF)
    srcs = {
        "gidxT": np.ascontiguousarray(np.asarray(gate_idx).T.astype(np.int32)),
        "uidxT": np.ascontiguousarray(np.asarray(up_idx).T.astype(np.int32)),
        "didxT": np.ascontiguousarray(np.asarray(down_idx).T.astype(np.int32)),
    }
    in_maps = []
    for c in range(cfg.ncores):
        myidx = np.zeros((cfg.TPC, 128, 128), np.int32)
        for t in range(cfg.TPC):
            gid = (t // cfg.CSZ) * (cfg.ncores * cfg.CSZ) + c * cfg.CSZ + (t % cfg.CSZ)
            if gid >= cfg.NT:
                continue
            src, rc, ct = w_tile_src(cfg, gid)
            myidx[t] = srcs[src][rc * 128:(rc + 1) * 128, ct * 128:(ct + 1) * 128]
        in_maps.append({
            "x_fm": np.ascontiguousarray(np.asarray(x[c]).T).astype(np.float32, copy=False),
            "wqkv": np.asarray(w_qkv, np.float32),
            "wout": np.asarray(w_out, np.float32),
            "myidx": myidx,
            "lutb": lutb,
            "cmask": cm,
            "onesc": onesc,
            "onesr": onesr,
        })
    return in_maps


def gather_output(cfg, results):
    return np.stack([np.ascontiguousarray(r["out_fm"].T) for r in results], axis=0)


def numpy_reference(cfg: Cfg, x, w_qkv, w_out, lut, gate_idx, up_idx, down_idx):
    def layernorm(v):
        m = v.mean(-1, keepdims=True)
        s = v.var(-1, keepdims=True)
        return (v - m) / np.sqrt(s + EPS)
    def softmax(s):
        m = s.max(-1, keepdims=True)
        e = np.exp(s - m)
        return e / e.sum(-1, keepdims=True)
    x = np.asarray(x, np.float64)
    w_qkv = np.asarray(w_qkv, np.float64)
    w_out = np.asarray(w_out, np.float64)
    lut = np.asarray(lut, np.float64)
    B, S, H = x.shape
    NH, D = cfg.NH, cfg.D
    qkv = x @ w_qkv
    qkv = qkv.reshape(B, S, 3, NH, D)
    q, k, v = qkv[:, :, 0], qkv[:, :, 1], qkv[:, :, 2]
    q = q / np.sqrt(np.float64(D))
    scores = np.einsum("bqhd,bkhd->bhqk", q, k)
    mask = np.arange(S)[:, None] >= np.arange(S)[None, :]
    scores = np.where(mask[None, None], scores, -1e10)
    a = softmax(scores)
    o = np.einsum("bhqk,bkhd->bqhd", a, v).reshape(B, S, H)
    x1 = layernorm(x + o @ w_out)
    gate = x1 @ lut[gate_idx].T
    up = x1 @ lut[up_idx].T
    hh = (gate / (1 + np.exp(-gate))) * up
    x2 = layernorm(x1 + hh @ lut[down_idx].T)
    return x2.astype(np.float32)



def kernel(L0_attn_x, L0_attn_w_qkv, L0_attn_w_out, L0_ffn_lut,
           L0_ffn_gate_idx, L0_ffn_up_idx, L0_ffn_down_idx):
    cfg = FULL
    in_maps = make_host_inputs(cfg, L0_attn_x, L0_attn_w_qkv, L0_attn_w_out,
                               L0_ffn_lut, L0_ffn_gate_idx, L0_ffn_up_idx,
                               L0_ffn_down_idx)
    nc = build_program(cfg)
    res = run_bass_kernel_spmd(nc, in_maps, core_ids=list(range(cfg.ncores)))
    return gather_output(cfg, res.results)


# version 20
# speedup vs baseline: 1.0323x; 1.0104x over previous
import math
import numpy as np
from contextlib import ExitStack
from dataclasses import dataclass

import concourse.bass as bass
import concourse.mybir as mybir
from concourse.bass_utils import run_bass_kernel_spmd

F32 = mybir.dt.float32
BF16 = mybir.dt.bfloat16
I32 = mybir.dt.int32
AF = mybir.ActivationFunctionType
OP = mybir.AluOpType

LUT_N = 16
EPS = 1e-5
MRUN = 8


@dataclass
class Cfg:
    S: int = 2048
    H: int = 2048
    NH: int = 16
    I: int = 5504
    TB: int = 512
    ncores: int = 8

    @property
    def D(self): return self.H // self.NH
    @property
    def F3(self): return 3 * self.H
    @property
    def nHC(self): return self.H // 128
    @property
    def nOC(self): return self.H // 128
    @property
    def nIC(self): return self.I // 128
    @property
    def nTB(self): return self.S // self.TB
    @property
    def nTT(self): return self.S // 128
    @property
    def NSTR(self): return self.TB // 128
    @property
    def DBW(self): return min(512, self.H)
    @property
    def nDB(self): return self.H // self.DBW
    @property
    def G1(self): return self.nHC * self.nIC
    @property
    def NT(self): return 2 * self.G1 + self.nIC * self.nOC
    @property
    def TPC0(self): return (self.NT + self.ncores - 1) // self.ncores
    @property
    def CSZ(self): return self.nIC
    @property
    def KAG(self): return (self.TPC0 + self.CSZ - 1) // self.CSZ
    @property
    def TPC(self): return self.KAG * self.CSZ
    @property
    def qscale(self): return 1.0 / math.sqrt(self.D)


FULL = Cfg()
SMALL = Cfg(S=256, H=256, NH=2, I=384, TB=128, ncores=8)
MED = Cfg(S=1024, H=512, NH=4, I=768, TB=512, ncores=8)

ENGINES = ("sync", "gpsimd", "vector", "scalar", "tensor")


class Emitter:
    def __init__(self, nc, ctx):
        self.nc = nc
        self.ctx = ctx
        self.q = {e: [] for e in ENGINES}
        self.sems = {}

    def _sem(self, name):
        if name not in self.sems:
            h = self.ctx.enter_context(self.nc.semaphore(name))
            self.sems[name] = [h, 0]
        return self.sems[name]

    def count(self, name):
        return self.sems[name][1] if name in self.sems else 0

    def run(self, eng, fn):
        self.q[eng].append(fn)

    def inc(self, eng, sem_name, fn, k=1):
        s = self._sem(sem_name)
        s[1] += k
        self.q[eng].append(lambda e, fn=fn, h=s[0], k=k: fn(e).then_inc(h, k))
        return s[1]

    def dma(self, eng, sem_name, out, in_):
        return self.inc(eng, sem_name, lambda e, o=out, i=in_: e.dma_start(out=o, in_=i), k=16)

    def wait(self, eng, sem_name, val):
        if val is None or val <= 0:
            return
        s = self._sem(sem_name)
        assert val <= s[1], f"wait {sem_name}>={val} but only {s[1]} emitted"
        self.q[eng].append(lambda e, h=s[0], v=val: e.wait_ge(h, v))

    def drain(self, eng):
        self.q[eng].append(lambda e: e.drain())

    def finalize(self, block):
        for eng in ENGINES:
            dec = getattr(block, eng)
            q = self.q[eng]
            def make(qq):
                def body(e):
                    for fn in qq:
                        fn(e)
                return body
            dec(make(q))


def w_tile_src(cfg, gid):
    if gid < cfg.G1:
        return "gidxT", gid // cfg.nIC, gid % cfg.nIC
    if gid < 2 * cfg.G1:
        return "uidxT", (gid - cfg.G1) // cfg.nIC, (gid - cfg.G1) % cfg.nIC
    g2 = gid - 2 * cfg.G1
    return "didxT", g2 // cfg.nOC, g2 % cfg.nOC


def build_program(cfg: Cfg):
    nc = bass.Bass("TRN2", target_bir_lowering=False)
    ctx = ExitStack()
    S, H, I, TB = cfg.S, cfg.H, cfg.I, cfg.TB
    nHC, nOC, nIC, nTT, NSTR = cfg.nHC, cfg.nOC, cfg.nIC, cfg.nTT, cfg.NSTR
    DBW = cfg.DBW

    x_fm = nc.dram_tensor("x_fm", [H, S], F32, kind="ExternalInput")
    wqkv = nc.dram_tensor("wqkv", [H, cfg.F3], F32, kind="ExternalInput")
    wout = nc.dram_tensor("wout", [H, H], F32, kind="ExternalInput")
    myidx = nc.dram_tensor("myidx", [cfg.TPC, 128, 128], I32, kind="ExternalInput")
    lutb = nc.dram_tensor("lutb", [128, LUT_N], F32, kind="ExternalInput")
    cmask = nc.dram_tensor("cmask", [128, (NSTR - 1) * 128 + TB], BF16, kind="ExternalInput")
    onesc = nc.dram_tensor("onesc", [128, 1], BF16, kind="ExternalInput")
    onesr = nc.dram_tensor("onesr", [1, 128], BF16, kind="ExternalInput")
    out_fm = nc.dram_tensor("out_fm", [H, S], F32, kind="ExternalOutput")

    q_scr = nc.dram_tensor("q_scr", [H, S], BF16)
    k_scr = nc.dram_tensor("k_scr", [H, S], BF16)
    v_scr = nc.dram_tensor("v_scr", [S, H], BF16)
    wob_scr = nc.dram_tensor("wob_scr", [H, H], BF16)
    x2f_scr = nc.dram_tensor("x2f_scr", [H, S], F32)
    x2b_scr = nc.dram_tensor("x2b_scr", [H, S], BF16)
    h_scr = nc.dram_tensor("h_scr", [I, S], BF16)
    wcat_my = nc.dram_tensor("wcat_my", [cfg.TPC, 128, 128], BF16)
    if cfg.KAG > 1:
        assert nHC % 4 == 0 and nOC % 2 == 0
    wc = [nc.dram_tensor(f"wcat{k}", [cfg.ncores * cfg.CSZ, 128, 128], BF16,
                         addr_space="Shared") for k in range(cfg.KAG)]

    sb = lambda name, shape, dt: ctx.enter_context(nc.sbuf_tensor(name, shape, dt))

    with ctx:
        xact = sb("xact", [128, nHC * S], BF16)
        p2cols = nTT * TB + 4 * S + 9 * TB
        bigcols = max(p2cols, nIC * TB, 21 * MRUN * 128)
        big = sb("big", [128, bigcols], BF16)
        r1b_sl = sb("r1b_sl", [128, TB], BF16)
        r1sq_sl = sb("r1sq_sl", [128, TB], BF16)
        
        slab16 = sb("slab16", [128, nHC * max(DBW, TB)], BF16)
        arf_cols = max(8 * 128 + 2 * DBW + 2 * S, nOC * TB + 4 * TB)
        arf = sb("arf", [128, arf_cols], F32)

        lut_s = sb("lut_s", [128, LUT_N], F32)
        cm_s = sb("cm_s", [128, (NSTR - 1) * 128 + TB], BF16)
        onec_s = sb("onec_s", [128, 1], BF16)
        oner_s = sb("oner_s", [1, 128], BF16)
        wqk_b = [sb(f"wqk_b{s}", [128, 128], BF16) for s in range(8)]
        qk_e = [sb(f"qk_e{s}", [128, TB], BF16) for s in range(4)]
        ve_t = [sb(f"ve_t{s}", [128, DBW], BF16) for s in range(1)]
        wo_t = [sb(f"wo_t{s}", [128, 2 * 128], BF16) for s in range(4)]
        wg_t = [sb(f"wg_t{s}", [128, 4 * 128], BF16) for s in range(2)]
        wu_t = [sb(f"wu_t{s}", [128, 4 * 128], BF16) for s in range(2)]
        wd_t = [sb(f"wd_t{s}", [128, 2 * 128], BF16) for s in range(6)]
        mn_f = sb("mn_f", [1, TB], F32)
        msq_f = sb("msq_f", [1, TB], F32)
        mn2_f = sb("mn2_f", [1, TB], F32)
        var_f = sb("var_f", [1, TB], F32)
        sd_f = sb("sd_f", [1, TB], F32)
        rstd_f = sb("rstd_f", [1, TB], F32)
        den_f = mn_f
        rden_f = msq_f
        bvec_f = sd_f
        rden_b = sb("rden_b", [1, TB], BF16)
        a_b = sb("a_b", [1, TB], BF16)
        b_b = sb("b_b", [1, TB], BF16)
        

        BKW = 512
        psA = [ctx.enter_context(nc.psum_tensor(f"psA{i}", [128, 2 * BKW], F32)) for i in range(2)]
        psS = [ctx.enter_context(nc.psum_tensor(f"psS{i}", [128, BKW], F32)) for i in range(4)]
        ps = [psA[0][:, 0:BKW], psA[0][:, BKW:2 * BKW],
              psS[0][:, :], psS[1][:, :], psS[2][:, :], psS[3][:, :],
              psA[1][:, 0:BKW], psA[1][:, BKW:2 * BKW]]

        T = dict(
            x_fm=x_fm, wqkv=wqkv, wout=wout, myidx=myidx, lutb=lutb, cmask=cmask,
            onesc=onesc, onesr=onesr, out_fm=out_fm, q_scr=q_scr, k_scr=k_scr,
            v_scr=v_scr, wob_scr=wob_scr, x2f_scr=x2f_scr, x2b_scr=x2b_scr,
            h_scr=h_scr, wcat_my=wcat_my, wc=wc,
            xact=xact, big=big, slab16=slab16, arf=arf,
            r1b_sl=r1b_sl, r1sq_sl=r1sq_sl,

            lut_s=lut_s, cm_s=cm_s,
            onec_s=onec_s, oner_s=oner_s, wqk_b=wqk_b, qk_e=qk_e,
            ve_t=ve_t, wo_t=wo_t, wg_t=wg_t, wu_t=wu_t, wd_t=wd_t,
            den_f=den_f, rden_f=rden_f, rden_b=rden_b,
            mn_f=mn_f, msq_f=msq_f, mn2_f=mn2_f, var_f=var_f, sd_f=sd_f,
            rstd_f=rstd_f, bvec_f=bvec_f, a_b=a_b, b_b=b_b, ps=ps, psA=psA,
        )
        with nc.Block() as block:
            E = Emitter(nc, ctx)
            build_phases(cfg, E, T)
            E.finalize(block)
    return nc


def build_phases(cfg, E, T):
    S, H, I, TB = cfg.S, cfg.H, cfg.I, cfg.TB
    nHC, nOC, nIC, nTB, nTT, NSTR = (cfg.nHC, cfg.nOC, cfg.nIC, cfg.nTB,
                                     cfg.nTT, cfg.NSTR)
    NH, DBW, nDB = cfg.NH, cfg.DBW, cfg.nDB
    PSW = 512
    ps = T["ps"]
    big, xact, slab16, arf = T["big"], T["xact"], T["slab16"], T["arf"]
    g = lambda n: T[n]

    et_off = 0
    ksl_off = nTT * TB
    vsl_off = ksl_off + 2 * S
    qsl_off = vsl_off + 2 * S
    et = lambda kt: big[:, et_off + kt * TB: et_off + (kt + 1) * TB]
    k_sl = lambda s, w: big[:, ksl_off + s * S: ksl_off + s * S + w]
    v_sl = lambda s, w: big[:, vsl_off + s * S: vsl_off + s * S + w]
    q_sl = lambda s: big[:, qsl_off + s * TB: qsl_off + (s + 1) * TB]
    r1b_t = lambda s: T["r1b_sl"][:, 0:TB]
    r1sq_t = lambda s: T["r1sq_sl"][:, 0:TB]
    tail0 = qsl_off + 2 * TB
    et_t = lambda s: big[:, tail0 + s * TB: tail0 + (s + 1) * TB]
    h_sl = lambda ic: big[:, ic * TB:(ic + 1) * TB]
    bc_s = big[:, tail0 + 2 * TB: tail0 + 3 * TB]
    sil_t = [big[:, tail0 + (3 + s) * TB: tail0 + (4 + s) * TB] for s in range(2)]
    h_t = [big[:, tail0 + (5 + s) * TB: tail0 + (6 + s) * TB] for s in range(2)]
    attn = lambda dc: slab16[:, dc * TB:(dc + 1) * TB]
    wv_b = lambda hc: slab16[:, hc * DBW:(hc + 1) * DBW]
    wqk_f = [arf[:, s * 128:(s + 1) * 128] for s in range(8)]
    wv_f = [arf[:, 8 * 128 + s * DBW: 8 * 128 + (s + 1) * DBW] for s in range(2)]
    xs_f = [arf[:, 8 * 128 + 2 * DBW + s * S: 8 * 128 + 2 * DBW + (s + 1) * S] for s in range(2)]
    r1 = arf[:, 0:nOC * TB]
    xres = [arf[:, nOC * TB + s * TB: nOC * TB + (s + 1) * TB] for s in range(2)]
    tmp_f = arf[:, nOC * TB + 2 * TB: nOC * TB + 3 * TB]
    x2f_t = [arf[:, nOC * TB + 3 * TB: nOC * TB + 4 * TB]]
    r2, x2r, out_t = r1, xres, x2f_t

    E.dma("sync", "s_lut", g("lut_s")[:, :], g("lutb")[:, :])
    E.dma("sync", "s_cm", g("cm_s")[:, :], g("cmask")[:, :])
    E.dma("sync", "s_onc", g("onec_s")[:, :], g("onesc")[:, :])
    E.dma("sync", "s_onr", g("oner_s")[:, :], g("onesr")[:, :])

    NG = min(4, nHC)
    runs = []
    t = 0
    while t < cfg.TPC:
        w = min(MRUN, cfg.TPC - t)
        runs.append((t, w))
        t += w
    MP = MRUN * 128
    m_part = [big[:, j * MP:(j + 1) * MP] for j in range(16)]
    m_idxb = [big[:, (16 + s) * MP:(17 + s) * MP] for s in range(3)]
    m_out = [big[:, (19 + s) * MP:(20 + s) * MP] for s in range(2)]
    E.wait("vector", "s_lut", 16)
    acc_v = {}

    def mat_dma(ri):
        t0, w = runs[ri]
        slot = ri % 3
        if ri >= 3:
            E.wait("gpsimd", "v_p", acc_v[ri - 3])
        E.dma("gpsimd", f"s_midx{slot}",
              m_idxb[slot][:, 0:w * 128].rearrange("p (t c) -> p t c", c=128),
              g("myidx")[t0:t0 + w, :, :].rearrange("t p c -> p t c"))

    for r0 in range(min(3, len(runs))):
        mat_dma(r0)
    for ri, (t0, w) in enumerate(runs):
        slot = ri % 3
        E.wait("vector", f"s_midx{slot}", 16 * (ri // 3 + 1))
        for j in range(LUT_N):
            E.run("vector", lambda e, w=w, j=j, slot=slot: e.tensor_scalar(
                out=m_part[j][:, 0:w * 128], in0=m_idxb[slot][:, 0:w * 128],
                scalar1=float(j), scalar2=g("lut_s")[:, j:j + 1],
                op0=OP.is_equal, op1=OP.mult))
        E.drain("vector")
        step = 1
        while step < 16:
            for j0 in range(0, 16, 2 * step):
                last = (2 * step >= 16)
                if last:
                    if ri >= 2:
                        E.wait("vector", f"s_wmy{ri % 2}", 16 * ((ri - 2) // 2 + 1))
                    fn = lambda e, w=w, ri=ri: e.tensor_tensor(
                        out=m_out[ri % 2][:, 0:w * 128], in0=m_part[0][:, 0:w * 128],
                        in1=m_part[8][:, 0:w * 128], op=OP.add)
                    acc_v[ri] = E.inc("vector", "v_p", fn)
                else:
                    fn = lambda e, j0=j0, s=step, w=w: e.tensor_tensor(
                        out=m_part[j0][:, 0:w * 128], in0=m_part[j0][:, 0:w * 128],
                        in1=m_part[j0 + s][:, 0:w * 128], op=OP.add)
                    E.run("vector", fn)
            E.drain("vector")
            step *= 2
        if ri + 3 < len(runs):
            mat_dma(ri + 3)
        E.wait("gpsimd", "v_p", acc_v[ri])
        E.dma("gpsimd", f"s_wmy{ri % 2}",
              g("wcat_my")[t0:t0 + w, :, :].rearrange("t p c -> p t c"),
              m_out[ri % 2][:, 0:w * 128].rearrange("p (t c) -> p t c", c=128))
        for k in range(cfg.KAG):
            if ((k + 1) * cfg.CSZ - 1) // MRUN == ri:
                E.wait("gpsimd", "s_wmy0", 16 * (ri // 2 + 1))
                if ri >= 1:
                    E.wait("gpsimd", "s_wmy1", 16 * ((ri - 1) // 2 + 1))
                ag_g = E.inc("gpsimd", "g_p", lambda e, k=k: e.collective_compute(
                    "AllGather", OP.bypass, replica_groups=[list(range(cfg.ncores))],
                    ins=[g("wcat_my")[k * cfg.CSZ:(k + 1) * cfg.CSZ, :, :].opt()],
                    outs=[g("wc")[k].ap().opt()]))
    mat_done_v = acc_v[len(runs) - 1]
    n_wmy = [E.count("s_wmy0"), E.count("s_wmy1")]

    qkcast_s, lastmm_t, stopmm_t, qkevac_s = {}, {}, {}, {}
    xcast_s = {}
    for ft in range(2 * nHC):
        isq = ft < nHC
        for hc in range(nHC):
            it = ft * nHC + hc
            slot = it % 8
            if ft == 0:
                xsl = hc % 2
                if hc >= 2:
                    E.wait("sync", "sc_p", xcast_s[hc - 2])
                E.dma("sync", f"s_xs{xsl}", xs_f[xsl],
                      g("x_fm")[hc * 128:(hc + 1) * 128, :])
                E.wait("scalar", f"s_xs{xsl}", 16 * (hc // 2 + 1))
                xcast_s[hc] = E.inc("scalar", "sc_p",
                                    lambda e, s=xsl, k=hc: e.copy(
                                        out=xact[:, k * S:(k + 1) * S], in_=xs_f[s]))
            if it >= 8:
                E.wait("sync", "sc_p", qkcast_s[it - 8])
            E.dma("sync", f"s_wqk{slot}", wqk_f[slot],
                  g("wqkv")[hc * 128:(hc + 1) * 128, ft * 128:(ft + 1) * 128])
            if it >= 8:
                E.wait("scalar", "t_p", lastmm_t[it - 8])
            E.wait("scalar", f"s_wqk{slot}", 16 * (it // 8 + 1))
            qkcast_s[it] = E.inc("scalar", "sc_p",
                                 lambda e, s=slot: e.copy(out=g("wqk_b")[s][:, :],
                                                          in_=wqk_f[s]))
            E.wait("tensor", "sc_p", qkcast_s[it])
            if ft == 0:
                E.wait("tensor", "sc_p", xcast_s[hc])
            if hc == 0 and ft >= 2:
                E.wait("tensor", "sc_p", qkevac_s[(ft - 2, nTB - 1)])
            bk0 = (ft % 2) * nTB
            for tb in range(nTB):
                fn = lambda e, s=slot, hc=hc, tb=tb, bk0=bk0, st=(hc == 0), sp=(hc == nHC - 1): e.matmul(
                    ps[bk0 + tb][:, 0:TB], g("wqk_b")[s][:, :],
                    xact[:, hc * S + tb * TB: hc * S + (tb + 1) * TB],
                    start=st, stop=sp, skip_group_check=True)
                if hc == nHC - 1:
                    stopmm_t[(ft, tb)] = E.inc("tensor", "t_p", fn)
                    lastmm_t[it] = stopmm_t[(ft, tb)]
                elif tb == nTB - 1:
                    lastmm_t[it] = E.inc("tensor", "t_p", fn)
                else:
                    E.run("tensor", fn)
        for tb in range(nTB):
            es = (ft * nTB + tb) % 4
            idx = ft * nTB + tb
            E.wait("scalar", "t_p", stopmm_t[(ft, tb)])
            if idx >= 4:
                E.wait("scalar", f"s_qkst{es}", 16 * (idx // 4))
            sc = cfg.qscale if isq else 1.0
            qkevac_s[(ft, tb)] = E.inc("scalar", "sc_p",
                                       lambda e, tb=tb, es=es, sc=sc, bk0=(ft % 2) * nTB:
                                       e.activation(
                                           out=g("qk_e")[es][:, :], in_=ps[bk0 + tb][:, 0:TB],
                                           func=AF.Copy, scale=sc))
            E.wait("scalar", "sc_p", qkevac_s[(ft, tb)])
            dst = g("q_scr") if isq else g("k_scr")
            r0 = (ft if isq else ft - nHC) * 128
            E.dma("scalar", f"s_qkst{es}", dst[r0:r0 + 128, tb * TB:(tb + 1) * TB],
                  g("qk_e")[es][:, :])
    n_qkst = [E.count(f"s_qkst{e_}") for e_ in range(4)]


    wvc_s, vstop_t, vevac_s = {}, {}, {}
    for db in range(nDB):
        for hc in range(nHC):
            it = db * nHC + hc
            slot = hc % 2
            if it >= 2:
                E.wait("sync", "sc_p", wvc_s[it - 2])
            E.dma("sync", f"s_wv{slot}", wv_f[slot],
                  g("wqkv")[hc * 128:(hc + 1) * 128, 2 * H + db * DBW: 2 * H + (db + 1) * DBW])
            if db >= 1:
                E.wait("scalar", "t_p", vstop_t[(db - 1, nTT - 1)])
            E.wait("scalar", f"s_wv{slot}", 16 * (it // 2 + 1))
            wvc_s[it] = E.inc("scalar", "sc_p",
                              lambda e, hc=hc, s=slot: e.copy(out=wv_b(hc), in_=wv_f[s]))
        for tt in range(nTT):
            bank = 4 + (tt % 2)
            it = db * nTT + tt
            if it == 0:
                E.wait("tensor", "sc_p", qkevac_s[(2 * nHC - 1, nTB - 1)])
            if it >= 2:
                E.wait("tensor", "sc_p", vevac_s[it - 2])
            E.wait("tensor", "sc_p", wvc_s[db * nHC + nHC - 1])
            for hc in range(nHC):
                fn = lambda e, hc=hc, tt=tt, bank=bank, st=(hc == 0), sp=(hc == nHC - 1): e.matmul(
                    ps[bank][:, 0:DBW],
                    xact[:, hc * S + tt * 128: hc * S + (tt + 1) * 128],
                    wv_b(hc), start=st, stop=sp, skip_group_check=True)
                if hc == nHC - 1:
                    vstop_t[(db, tt)] = E.inc("tensor", "t_p", fn)
                else:
                    E.run("tensor", fn)
            E.wait("scalar", "t_p", vstop_t[(db, tt)])
            if it >= 1:
                E.wait("scalar", "s_vst", 16 * it)
            es = 0
            vevac_s[it] = E.inc("scalar", "sc_p",
                                lambda e, bank=bank, es=es: e.copy(out=g("ve_t")[es][:, :],
                                                                   in_=ps[bank][:, 0:DBW]))
            E.wait("scalar", "sc_p", vevac_s[it])
            E.dma("scalar", "s_vst",
                  g("v_scr")[tt * 128:(tt + 1) * 128, db * DBW:(db + 1) * DBW],
                  g("ve_t")[es][:, :])
    n_vst = E.count("s_vst")
    for i4 in range(NG):
        E.dma("gpsimd", "s_wob", g("wob_scr")[i4 * (H // NG):(i4 + 1) * (H // NG), :],
              g("wout")[i4 * (H // NG):(i4 + 1) * (H // NG), :])
    n_wob = E.count("s_wob")
    p1_sc = E.count("sc_p")
    tp_p1v = E.count("t_p")

    E.wait("vector", "s_cm", 16)
    E.wait("scalar", "v_p", mat_done_v)
    E.wait("scalar", "s_wmy0", n_wmy[0])
    E.wait("scalar", "s_wmy1", n_wmy[1])
    E.wait("sync", "v_p", mat_done_v)
    E.wait("sync", "s_wmy0", n_wmy[0])
    E.wait("sync", "s_wmy1", n_wmy[1])
    E.wait("gpsimd", "s_wob", n_wob)
    E.wait("sync", "s_wob", n_wob)
    n_strad = 0
    msk_v, avden_t, attn_v, denev_v, bcs_v = {}, {}, {}, {}, {}
    wo_use_t, r1_v, x2f_v = {}, {}, {}
    wo_qctr = {0: 0, 1: 0}
    vsl_fill = {}
    prev_exp_last = 0
    n_x2f = 0
    ln_prev_stat = {'t': 0, 'sc': 0}
    wo_last_t = 0
    x2f_cnt = {}
    avkt_t = {}
    msk_last = {}
    for qb in range(nTB):
        nkt = (qb + 1) * NSTR
        nkt_prev_qb = qb * NSTR

        def den_tail(iq2):
            head2 = iq2 % NH
            bav = 2 + (head2 % 2)
            bden = 4 + (head2 % 2)
            E.wait("scalar", "t_p", avden_t[iq2])
            E.run("scalar", lambda e, bden=bden: e.activation(
                out=g("rden_f")[:, :], in_=ps[bden][0:1, 0:TB], func=AF.Ln))
            E.drain("scalar")
            denev_v[iq2] = E.inc("scalar", "sc_p", lambda e: e.activation(
                out=g("rden_b")[:, :], in_=g("rden_f")[:, :], func=AF.Exp, scale=-1.0))
            E.wait("tensor", "sc_p", denev_v[iq2])
            E.wait("tensor", "s_onr", 16)
            if iq2 > 0:
                E.wait("tensor", "v_p", bcs_v[iq2 - 1])
            bcb = 2 + ((head2 + 1) % 2)
            if iq2 >= 1:
                E.wait("tensor", "v_p", attn_v[iq2 - 1])
            tbc = E.inc("tensor", "t_p", lambda e, bcb=bcb: e.matmul(
                ps[bcb][:, 0:TB], g("oner_s")[:, :], g("rden_b")[:, :],
                start=True, stop=True, skip_group_check=True))
            E.wait("vector", "t_p", tbc)
            bcs_v[iq2] = E.inc("vector", "v_p",
                               lambda e, bcb=bcb: e.tensor_copy(bc_s, ps[bcb][:, 0:TB]))
            E.drain("vector")
            if head2 == 0 and iq2 >= NH:
                E.wait("vector", "t_p", wo_last_t)
            attn_v[iq2] = E.inc("vector", "v_p",
                                lambda e, head2=head2, bav=bav: e.tensor_tensor(
                                    out=attn(head2), in0=ps[bav][:, 0:TB], in1=bc_s,
                                    op=OP.mult))
            E.drain("vector")

        def emit_av_kt(iq2, kt, nkt2, last):
            head2 = iq2 % NH
            bav = 2 + (head2 % 2)
            bden = 4 + (head2 % 2)
            sl2 = head2 % 2
            if kt == 0:
                if head2 <= 1 and iq2 >= NH and (iq2 // NH) == qb:
                    E.wait("tensor", "v_p", x2f_v[(qb - 1) * nOC + nOC - 1])
                if iq2 >= 2:
                    E.wait("tensor", "v_p", attn_v[iq2 - 2])
                    E.wait("tensor", "sc_p", denev_v[iq2 - 2])
                if iq2 >= 1:
                    E.wait("tensor", "v_p", bcs_v[iq2 - 1])
                if iq2 in msk_last:
                    E.wait("tensor", "v_p", msk_last[iq2])
                E.wait("tensor", f"s_vsl{sl2}", vsl_fill[iq2])
                E.wait("tensor", "s_onc", 16)
            E.run("tensor", lambda e, sl2=sl2, kt=kt, nkt2=nkt2, bav=bav,
                  st=(kt == 0), sp=last: e.matmul(
                      ps[bav][:, 0:TB], v_sl(sl2, nkt2 * 128)[:, kt * 128:(kt + 1) * 128],
                      et(kt), start=st, stop=sp, skip_group_check=True))
            avkt_t[(iq2, kt)] = E.inc("tensor", "t_p",
                                      lambda e, kt=kt, bden=bden, st=(kt == 0), sp=last: e.matmul(
                                          ps[bden][0:1, 0:TB], g("onec_s")[:, :], et(kt),
                                          start=st, stop=sp, skip_group_check=True))
            if last:
                avden_t[iq2] = avkt_t[(iq2, kt)]

        for head in range(NH):
            sl = head % 2
            iq = qb * NH + head
            if iq >= 2:
                E.wait("sync", "t_p", avden_t[iq - 2])
            idxq = head * nTB + qb
            idxk = (nHC + head) * nTB + qb
            E.wait("sync", f"s_qkst{idxq % 4}", 16 * (idxq // 4 + 1))
            E.wait("sync", f"s_qkst{idxk % 4}", 16 * (idxk // 4 + 1))
            idxv = (nDB - 1) * nTT + (NSTR * (qb + 1) - 1)
            E.wait("sync", "s_vst", 16 * (idxv + 1))
            E.dma("sync", f"s_qsl{sl}", q_sl(sl),
                  g("q_scr")[head * 128:(head + 1) * 128, qb * TB:(qb + 1) * TB])
            E.dma("sync", f"s_ksl{sl}", k_sl(sl, nkt * 128),
                  g("k_scr")[head * 128:(head + 1) * 128, 0:nkt * 128])
            E.dma("sync", f"s_vsl{sl}",
                  v_sl(sl, nkt * 128).rearrange("p (t c) -> p t c", c=128),
                  g("v_scr")[0:nkt * 128, head * 128:(head + 1) * 128]
                  .rearrange("(t p) c -> p t c", p=128))
            nfill = 16 * (iq // 2 + 1)
            vsl_fill[iq] = nfill
            E.wait("tensor", f"s_qsl{sl}", nfill)
            E.wait("tensor", f"s_ksl{sl}", nfill)
            exp_slot = {}
            PAIR_OK = (TB == 512)
            SBK4 = (0, 1, 6, 7)
            ns_end = nkt - NSTR
            last_exp = 0
            kt = 0
            while kt < nkt:
                pair = (kt + 1 < ns_end and kt % 2 == 0 and PAIR_OK)
                subkts = (kt, kt + 1) if pair else (kt,)
                for k2 in subkts:
                    c = k2
                    bank = SBK4[c % 4]
                    if c >= 4:
                        E.wait("tensor", "sc_p", exp_slot[c - 4])
                    elif iq > 0:
                        E.wait("tensor", "sc_p", prev_exp_last)
                    else:
                        E.wait("tensor", "sc_p", p1_sc)
                    if c == 0 and head == 0 and qb > 0:
                        E.wait("tensor", "v_p", r1_v[(qb - 1) * nOC + nOC - 1])
                    tsc = E.inc("tensor", "t_p", lambda e, sl=sl, k2=k2, bank=bank, nkt=nkt: e.matmul(
                        ps[bank][:, 0:TB], k_sl(sl, nkt * 128)[:, k2 * 128:(k2 + 1) * 128],
                        q_sl(sl), start=True, stop=True, skip_group_check=True))
                    if head > 0:
                        emit_av_kt(iq - 1, k2, nkt, last=(k2 == nkt - 1))
                kt1 = subkts[-1]
                E.wait("scalar", "t_p", tsc)
                if head > 0:
                    E.wait("scalar", "t_p", avkt_t[(iq - 1, kt1)])
                elif qb > 0:
                    pl = qb * NH - 1
                    kk = min(kt1, nkt_prev_qb - 1)
                    if kk >= 0:
                        E.wait("scalar", "t_p", avkt_t[(pl, kk)])
                if pair:
                    pb = (kt // 2) % 2
                    v = E.inc("scalar", "sc_p",
                              lambda e, pb=pb, kt=kt: e.activation(
                                  out=big[:, et_off + kt * TB: et_off + (kt + 2) * TB],
                                  in_=T["psA"][pb][:, 0:2 * TB], func=AF.Exp))
                    exp_slot[kt] = exp_slot[kt + 1] = v
                    last_exp = v
                elif kt1 >= nkt - NSTR:
                    es = n_strad % 2
                    n_strad += 1
                    if n_strad > 2:
                        E.wait("scalar", "v_p", msk_v[n_strad - 2])
                    v = E.inc("scalar", "sc_p",
                              lambda e, bank=SBK4[kt1 % 4], es=es: e.activation(
                                  out=et_t(es), in_=ps[bank][:, 0:TB], func=AF.Exp))
                    exp_slot[kt1] = v
                    last_exp = v
                    E.wait("vector", "sc_p", v)
                    j = kt1 - (nkt - NSTR)
                    msk_v[n_strad] = E.inc("vector", "v_p",
                                           lambda e, kt1=kt1, es=es, j=j: e.tensor_tensor(
                                               out=et(kt1), in0=et_t(es),
                                               in1=g("cm_s")[:, (NSTR - 1 - j) * 128:
                                                             (NSTR - 1 - j) * 128 + TB],
                                               op=OP.mult))
                    msk_last[iq] = msk_v[n_strad]
                else:
                    v = E.inc("scalar", "sc_p",
                              lambda e, bank=SBK4[kt1 % 4], kt1=kt1: e.activation(
                                  out=et(kt1), in_=ps[bank][:, 0:TB], func=AF.Exp))
                    exp_slot[kt1] = v
                    last_exp = v
                kt += 2 if pair else 1
            prev_exp_last = last_exp
            if head > 0:
                den_tail(iq - 1)
        liq = qb * NH + NH - 1
        E.wait("tensor", "sc_p", prev_exp_last)
        for kt in range(nkt):
            emit_av_kt(liq, kt, nkt, last=(kt == nkt - 1))
        den_tail(liq)
        n_attn_qb = attn_v[qb * NH + NH - 1]
        first_wo = True
        for ot in range(nOC):
            bank = ot % 2
            io = qb * nOC + ot
            if io >= 2:
                E.wait("tensor", "v_p", r1_v[io - 2])
            for dc2 in range(nHC // 2):
                dc = 2 * dc2
                qsel = dc2 % 2
                eng = "sync" if qsel == 0 else "gpsimd"
                jq = wo_qctr[qsel]
                wo_qctr[qsel] += 1
                ws = 2 * qsel + (jq % 2)
                if jq >= 2:
                    E.wait(eng, "t_p", wo_use_t[(qsel, jq - 2)])
                E.dma(eng, f"s_wot{ws}",
                      g("wo_t")[ws][:, :].rearrange("p (t c) -> p t c", c=128),
                      g("wob_scr")[dc * 128:(dc + 2) * 128,
                                   ot * 128:(ot + 1) * 128].rearrange("(t p) c -> p t c", p=128))
                E.wait("tensor", f"s_wot{ws}", 16 * (jq // 2 + 1))
                E.wait("tensor", "v_p", n_attn_qb)
                if first_wo:
                    E.wait("tensor", "sc_p", prev_exp_last)
                    first_wo = False
                E.run("tensor", lambda e, ws=ws, dc=dc, bank=bank, st=(dc == 0): e.matmul(
                    ps[bank][:, 0:TB], g("wo_t")[ws][:, 0:128], attn(dc),
                    start=st, stop=False, skip_group_check=True))
                wo_use_t[(qsel, jq)] = E.inc("tensor", "t_p",
                                             lambda e, ws=ws, dc=dc, bank=bank,
                                             sp=(dc == nHC - 2): e.matmul(
                                                 ps[bank][:, 0:TB], g("wo_t")[ws][:, 128:256],
                                                 attn(dc + 1),
                                                 start=False, stop=sp, skip_group_check=True))
            wo_last_t = wo_use_t[(qsel, jq)]
            xs = ot % 2
            if io >= 2:
                E.wait("sync", "v_p", r1_v[io - 2])
            if qb == 0 and ot == 0:
                E.wait("sync", "sc_p", p1_sc)
            E.dma("sync", f"s_xres{xs}", xres[xs],
                  g("x_fm")[ot * 128:(ot + 1) * 128, qb * TB:(qb + 1) * TB])
            if qb == 0 and ot == 0:
                E.wait("vector", "sc_p", p1_sc)
            if qb > 0 and ot == 0:
                E.wait("vector", "sc_p", ln_prev_stat['sc'])
            E.wait("vector", "t_p", wo_last_t)
            E.wait("vector", f"s_xres{xs}", 16 * (io // 2 + 1))
            r1_v[io] = E.inc("vector", "v_p",
                             lambda e, ot=ot, bank=bank, xs=xs: e.tensor_tensor(
                                 out=r1[:, ot * TB:(ot + 1) * TB], in0=ps[bank][:, 0:TB],
                                 in1=xres[xs], op=OP.add))
        rv = {oc: r1_v[qb * nOC + oc] for oc in range(nOC)}
        emit_ln(cfg, E, T, r1, rv, r1b_t, r1sq_t, (2, 3, 4, 5), ln_prev_stat)
        for oc in range(nOC):
            s = 0
            io = qb * nOC + oc
            if io >= 1:
                E.wait("vector", "s_x2f", 16 * io)
            E.run("vector", lambda e, oc=oc: e.tensor_tensor(
                out=tmp_f, in0=r1[:, oc * TB:(oc + 1) * TB], in1=ps[4][:, 0:TB], op=OP.mult))
            E.drain("vector")
            x2f_v[io] = E.inc("vector", "v_p", lambda e, s=s: e.tensor_tensor(
                out=x2f_t[s], in0=tmp_f, in1=ps[5][:, 0:TB], op=OP.subtract))
            E.wait("gpsimd", "v_p", x2f_v[io])
            n_x2f = E.dma("gpsimd", "s_x2f",
                          g("x2f_scr")[oc * 128:(oc + 1) * 128, qb * TB:(qb + 1) * TB],
                          x2f_t[s])
            x2f_cnt[(qb, oc)] = n_x2f

    E.wait("gpsimd", "t_p", tp_p1v)
    for hc in range(nHC):
        E.wait("gpsimd", "s_x2f", x2f_cnt[(nTB - 1, hc)])
        E.dma("gpsimd", f"s_x2g{hc * NG // nHC}", xact[:, hc * S:(hc + 1) * S],
              g("x2f_scr")[hc * 128:(hc + 1) * 128, :])
    n_x2g = [E.count(f"s_x2g{i}") for i in range(NG)]
    E.wait("gpsimd", "g_p", ag_g)
    E.wait("sync", "g_p", ag_g)
    sil_sc, h_v, gu_t = {}, {}, {}
    h_fill = {0: 0, 1: 0}
    QW = min(4, nHC)
    nHQ = (nHC + QW - 1) // QW
    wcv = [g("wc")[k][:, :, :].rearrange("(h a) p c -> h a p c", a=nIC)
           for k in range(cfg.KAG)]
    BL = cfg.ncores * nIC
    wgu_fill = {0: 0, 1: 0}
    for ic in range(nIC):
        for hc in range(nHC):
            iw = ic * nHC + hc
            hq, hsub = divmod(hc, QW)
            jq = ic * nHQ + hq
            sg = jq % 2
            if hsub == 0:
                w = min(QW, nHC - hc)
                if jq >= 2:
                    pic2, phq2 = divmod(jq - 2, nHQ)
                    iw2 = pic2 * nHC + min((phq2 + 1) * QW, nHC) - 1
                    E.wait("gpsimd", "t_p", gu_t[iw2])
                    E.wait("sync", "t_p", gu_t[iw2])
                kg, hg = divmod(hc, cfg.ncores)
                ku, hu = divmod(nHC + hc, cfg.ncores)
                E.dma("gpsimd", f"s_wg{sg}",
                      g("wg_t")[sg][:, 0:w * 128].rearrange("p (t c) -> p t c", c=128),
                      wcv[kg][hg:hg + w, ic:ic + 1, :, :].rearrange("t a p c -> p (t a) c"))
                E.dma("sync", f"s_wu{sg}",
                      g("wu_t")[sg][:, 0:w * 128].rearrange("p (t c) -> p t c", c=128),
                      wcv[ku][hu:hu + w, ic:ic + 1, :, :].rearrange("t a p c -> p (t a) c"))
                wgu_fill[sg] += 16
                E.wait("tensor", f"s_wg{sg}", wgu_fill[sg])
                E.wait("tensor", f"s_wu{sg}", wgu_fill[sg])
            if ic == 0 and hc % (nHC // NG) == 0:
                E.wait("tensor", f"s_x2g{hc * NG // nHC}", n_x2g[hc * NG // nHC])
            if iw == 0:
                E.wait("tensor", "v_p", E.count("v_p"))
                E.wait("tensor", "sc_p", E.count("sc_p"))
            for qb in range(nTB):
                if ic >= 1 and hc == 0:
                    E.wait("tensor", "sc_p", sil_sc[(ic - 1, qb)])
                    E.wait("tensor", "v_p", h_v[(ic - 1, qb)])
                fng = lambda e, sg=sg, hc=hc, qb=qb, hs=hsub, st=(hc == 0), sp=(hc == nHC - 1): e.matmul(
                    ps[qb][:, 0:TB], g("wg_t")[sg][:, hs * 128:(hs + 1) * 128],
                    xact[:, hc * S + qb * TB: hc * S + (qb + 1) * TB],
                    start=st, stop=sp, skip_group_check=True)
                fnu = lambda e, sg=sg, hc=hc, qb=qb, hs=hsub, st=(hc == 0), sp=(hc == nHC - 1): e.matmul(
                    ps[nTB + qb][:, 0:TB], g("wu_t")[sg][:, hs * 128:(hs + 1) * 128],
                    xact[:, hc * S + qb * TB: hc * S + (qb + 1) * TB],
                    start=st, stop=sp, skip_group_check=True)
                E.run("tensor", fng)
                if qb == nTB - 1:
                    gu_t[iw] = E.inc("tensor", "t_p", fnu)
                else:
                    E.run("tensor", fnu)
        for qb in range(nTB):
            s = qb % 2
            iq = ic * nTB + qb
            E.wait("scalar", "t_p", gu_t[ic * nHC + nHC - 1])
            if iq >= 2:
                E.wait("scalar", "v_p", h_v[((iq - 2) // nTB, (iq - 2) % nTB)])
            sil_sc[(ic, qb)] = E.inc("scalar", "sc_p",
                                     lambda e, qb=qb, s=s: e.activation(
                                         out=sil_t[s], in_=ps[qb][:, 0:TB],
                                         func=AF.Silu))
            E.wait("vector", "sc_p", sil_sc[(ic, qb)])
            if h_fill[s] >= 1:
                E.wait("vector", f"s_h{s}", 16 * h_fill[s])
            h_v[(ic, qb)] = E.inc("vector", "v_p",
                                  lambda e, qb=qb, s=s: e.tensor_tensor(
                                      out=h_t[s], in0=sil_t[s],
                                      in1=ps[nTB + qb][:, 0:TB], op=OP.mult))
            E.wait("scalar", "v_p", h_v[(ic, qb)])
            E.dma("scalar", f"s_h{s}",
                  g("h_scr")[ic * 128:(ic + 1) * 128, qb * TB:(qb + 1) * TB],
                  h_t[s])
            h_fill[s] += 1
    n_h = [E.count(f"s_h{s_}") for s_ in range(2)]

    for s_ in range(2):
        E.wait("sync", f"s_h{s_}", n_h[s_])
    E.wait("sync", "s_x2f", n_x2f)
    E.wait("sync", "t_p", E.count("t_p"))
    do_t, r2_v, wd_use_t = {}, {}, {}
    wd_qctr = {0: 0, 1: 0}
    n_out = 0
    first_p4 = True
    nICP = (nIC + 1) // 2
    h_free_t, n_hl_q = {}, {}
    for s_ in range(2):
        E.wait("gpsimd", f"s_h{s_}", n_h[s_])
    E.wait("gpsimd", "t_p", E.count("t_p"))
    for icp in range(nICP):
        ic0, icw = 2 * icp, min(2, nIC - 2 * icp)
        eng2 = "sync" if icp % 2 == 0 else "gpsimd"
        for icsub in range(icw):
            ic = ic0 + icsub
            E.dma(eng2, f"s_hl{icp % 2}", h_sl(ic),
                  g("h_scr")[ic * 128:(ic + 1) * 128, 0:TB])
    n_hl_q[0] = (E.count("s_hl0"), E.count("s_hl1"))
    for qb in range(nTB):
        n_hl = n_hl_q[qb]
        for op_ in range(nOC // 2):
            ot0, ot1 = 2 * op_, 2 * op_ + 1
            gp_pair = qb * (nOC // 2) + op_
            b0, b1 = 2 * (gp_pair % 2), 2 * (gp_pair % 2) + 1
            if gp_pair >= 2:
                qb_p, op_p = divmod(gp_pair - 2, nOC // 2)
                E.wait("tensor", "v_p", r2_v[qb_p * nOC + 2 * op_p + 1])
            for ic in range(nIC):
                qsel = ic % 2
                eng = "sync" if qsel == 0 else "gpsimd"
                jq = wd_qctr[qsel]
                wd_qctr[qsel] += 1
                ws = 3 * qsel + (jq % 3)
                if jq >= 3:
                    E.wait(eng, "t_p", wd_use_t[(qsel, jq - 3)])
                gid_d = 2 * cfg.G1 + ic * nOC + ot0
                kd, ld = divmod(gid_d, BL)
                E.dma(eng, f"s_wd{ws}",
                      g("wd_t")[ws][:, :].rearrange("p (t c) -> p t c", c=128),
                      g("wc")[kd][ld:ld + 2, :, :].rearrange("t p c -> p t c"))
                E.wait("tensor", f"s_wd{ws}", 16 * (jq // 3 + 1))
                if first_p4:
                    E.wait("tensor", "v_p", E.count("v_p"))
                    E.wait("tensor", "sc_p", E.count("sc_p"))
                    first_p4 = False
                if ic == 0:
                    E.wait("tensor", "s_hl0", n_hl[0])
                    E.wait("tensor", "s_hl1", n_hl[1])
                E.run("tensor", lambda e, ws=ws, ic=ic, b=b0, st=(ic == 0),
                      sp=(ic == nIC - 1): e.matmul(
                          ps[b][:, 0:TB], g("wd_t")[ws][:, 0:128], h_sl(ic),
                          start=st, stop=sp, skip_group_check=True))
                tmm = E.inc("tensor", "t_p",
                            lambda e, ws=ws, ic=ic, b=b1, st=(ic == 0),
                            sp=(ic == nIC - 1): e.matmul(
                                ps[b][:, 0:TB], g("wd_t")[ws][:, 128:256],
                                h_sl(ic),
                                start=st, stop=sp, skip_group_check=True))
                wd_use_t[(qsel, jq)] = tmm
                if op_ == nOC // 2 - 1:
                    h_free_t[ic] = tmm
            tdo_pair = tmm
            for i, (ot, bank) in enumerate(((ot0, b0), (ot1, b1))):
                io = qb * nOC + ot
                do_t[(qb, ot)] = tdo_pair
                xs = ot % 2
                if io >= 2:
                    E.wait("sync", "v_p", r2_v[io - 2])
                E.dma("sync", f"s_x2r{xs}", x2r[xs],
                      g("x2f_scr")[ot * 128:(ot + 1) * 128, qb * TB:(qb + 1) * TB])
                if qb == 0 and ot == 0:
                    E.wait("vector", "s_x2f", n_x2f)
                    E.wait("vector", "sc_p", ln_prev_stat['sc'])
                E.wait("vector", "t_p", tdo_pair)
                E.wait("vector", f"s_x2r{xs}", 16 * (io // 2 + 1))
                r2_v[io] = E.inc("vector", "v_p",
                                 lambda e, ot=ot, bank=bank, xs=xs: e.tensor_tensor(
                                     out=r2[:, ot * TB:(ot + 1) * TB], in0=ps[bank][:, 0:TB],
                                     in1=x2r[xs], op=OP.add))
        if qb + 1 < nTB:
            for icp in range(nICP):
                ic0, icw = 2 * icp, min(2, nIC - 2 * icp)
                eng2 = "sync" if icp % 2 == 0 else "gpsimd"
                E.wait(eng2, "t_p", h_free_t[ic0 + icw - 1])
                for icsub in range(icw):
                    ic = ic0 + icsub
                    E.dma(eng2, f"s_hl{icp % 2}", h_sl(ic),
                          g("h_scr")[ic * 128:(ic + 1) * 128,
                                     (qb + 1) * TB:(qb + 2) * TB])
            n_hl_q[qb + 1] = (E.count("s_hl0"), E.count("s_hl1"))
        rv = {oc: r2_v[qb * nOC + oc] for oc in range(nOC)}
        emit_ln(cfg, E, T, r2, rv, r1b_t, r1sq_t, (6, 7, 4, 5), ln_prev_stat)
        for oc in range(nOC):
            s = 0
            io = qb * nOC + oc
            if io >= 1:
                E.wait("vector", "s_out", 16 * io)
            E.run("vector", lambda e, oc=oc: e.tensor_tensor(
                out=tmp_f, in0=r2[:, oc * TB:(oc + 1) * TB], in1=ps[4][:, 0:TB], op=OP.mult))
            E.drain("vector")
            co = E.inc("vector", "v_p", lambda e, s=s: e.tensor_tensor(
                out=out_t[s], in0=tmp_f, in1=ps[5][:, 0:TB], op=OP.subtract))
            E.wait("gpsimd", "v_p", co)
            n_out = E.dma("gpsimd", "s_out",
                          g("out_fm")[oc * 128:(oc + 1) * 128, qb * TB:(qb + 1) * TB],
                          out_t[s])
    E.wait("gpsimd", "s_out", n_out)


def emit_ln(cfg, E, T, r_slab, r_v, r1b_t, r1sq_t, banks, ln_prev_stat):
    H, TB, nOC = cfg.H, cfg.TB, cfg.nOC
    g = lambda n: T[n]
    ps = T["ps"]
    bsum, bsq, ba, bb = banks
    stat_t = {}
    crs_last = 0
    for oc in range(nOC):
        s = oc % 2
        E.wait("scalar", "v_p", r_v[oc])
        if oc >= 1:
            E.wait("scalar", "t_p", stat_t[oc - 1])
        else:
            E.wait("scalar", "t_p", ln_prev_stat['t'])
        crb = E.inc("scalar", "sc_p", lambda e, oc=oc, s=s: e.copy(
            out=r1b_t(s), in_=r_slab[:, oc * TB:(oc + 1) * TB]))
        crs = E.inc("scalar", "sc_p", lambda e, oc=oc, s=s: e.activation(
            out=r1sq_t(s), in_=r_slab[:, oc * TB:(oc + 1) * TB], func=AF.Square))
        crs_last = crs
        E.wait("tensor", "sc_p", crs)
        E.run("tensor", lambda e, s=s, st=(oc == 0), sp=(oc == nOC - 1): e.matmul(
            ps[bsum][0:1, 0:TB], g("onec_s")[:, :], r1b_t(s),
            start=st, stop=sp, skip_group_check=True))
        stat_t[oc] = E.inc("tensor", "t_p",
                           lambda e, s=s, st=(oc == 0), sp=(oc == nOC - 1): e.matmul(
                               ps[bsq][0:1, 0:TB], g("onec_s")[:, :], r1sq_t(s),
                               start=st, stop=sp, skip_group_check=True))
    ln_prev_stat['t'] = stat_t[nOC - 1]
    ln_prev_stat['sc'] = crs_last
    E.wait("vector", "t_p", stat_t[nOC - 1])
    E.run("vector", lambda e: e.tensor_scalar(out=g("mn_f")[:, :], in0=ps[bsum][0:1, 0:TB],
                                              scalar1=1.0 / H, scalar2=None, op0=OP.mult))
    E.run("vector", lambda e: e.tensor_scalar(out=g("msq_f")[:, :], in0=ps[bsq][0:1, 0:TB],
                                              scalar1=1.0 / H, scalar2=None, op0=OP.mult))
    E.drain("vector")
    E.run("vector", lambda e: e.tensor_tensor(out=g("mn2_f")[:, :], in0=g("mn_f")[:, :],
                                              in1=g("mn_f")[:, :], op=OP.mult))
    E.drain("vector")
    E.run("vector", lambda e: e.tensor_tensor(
        out=g("var_f")[:, :], in0=g("msq_f")[:, :], in1=g("mn2_f")[:, :], op=OP.subtract))
    E.drain("vector")
    cvar = E.inc("vector", "v_p", lambda e: e.tensor_scalar(
        out=g("msq_f")[:, :], in0=g("var_f")[:, :], scalar1=EPS, scalar2=None, op0=OP.add))
    E.wait("scalar", "v_p", cvar)
    E.run("scalar", lambda e: e.activation(
        out=g("sd_f")[:, :], in_=g("msq_f")[:, :], func=AF.Ln))
    E.drain("scalar")
    csd = E.inc("scalar", "sc_p", lambda e: e.activation(
        out=g("rstd_f")[:, :], in_=g("sd_f")[:, :], func=AF.Exp, scale=-0.5))
    E.wait("vector", "sc_p", csd)
    E.run("vector", lambda e: e.tensor_tensor(out=g("bvec_f")[:, :], in0=g("mn_f")[:, :],
                                              in1=g("rstd_f")[:, :], op=OP.mult))
    E.drain("vector")
    E.run("vector", lambda e: e.tensor_copy(g("a_b")[:, :], g("rstd_f")[:, :]))
    E.drain("vector")
    cab = E.inc("vector", "v_p", lambda e: e.tensor_copy(g("b_b")[:, :], g("bvec_f")[:, :]))
    E.wait("tensor", "v_p", cab)
    E.run("tensor", lambda e: e.matmul(ps[ba][:, 0:TB], g("oner_s")[:, :], g("a_b")[:, :],
                                       start=True, stop=True, skip_group_check=True))
    tab = E.inc("tensor", "t_p", lambda e: e.matmul(
        ps[bb][:, 0:TB], g("oner_s")[:, :], g("b_b")[:, :], start=True, stop=True,
        skip_group_check=True))
    E.wait("vector", "t_p", tab)



def make_host_inputs(cfg: Cfg, x, w_qkv, w_out, lut, gate_idx, up_idx, down_idx):
    import ml_dtypes
    BF = ml_dtypes.bfloat16
    lutb = np.broadcast_to(np.asarray(lut, np.float32), (128, LUT_N)).copy()
    kp = np.arange(128)[:, None]
    uu = np.arange((cfg.NSTR - 1) * 128 + cfg.TB)[None, :]
    cm = (uu >= kp + (cfg.NSTR - 1) * 128).astype(BF)
    onesc = np.ones((128, 1), dtype=BF)
    onesr = np.ones((1, 128), dtype=BF)
    srcs = {
        "gidxT": np.ascontiguousarray(np.asarray(gate_idx).T.astype(np.int32)),
        "uidxT": np.ascontiguousarray(np.asarray(up_idx).T.astype(np.int32)),
        "didxT": np.ascontiguousarray(np.asarray(down_idx).T.astype(np.int32)),
    }
    in_maps = []
    for c in range(cfg.ncores):
        myidx = np.zeros((cfg.TPC, 128, 128), np.int32)
        for t in range(cfg.TPC):
            gid = (t // cfg.CSZ) * (cfg.ncores * cfg.CSZ) + c * cfg.CSZ + (t % cfg.CSZ)
            if gid >= cfg.NT:
                continue
            src, rc, ct = w_tile_src(cfg, gid)
            myidx[t] = srcs[src][rc * 128:(rc + 1) * 128, ct * 128:(ct + 1) * 128]
        in_maps.append({
            "x_fm": np.ascontiguousarray(np.asarray(x[c]).T).astype(np.float32, copy=False),
            "wqkv": np.asarray(w_qkv, np.float32),
            "wout": np.asarray(w_out, np.float32),
            "myidx": myidx,
            "lutb": lutb,
            "cmask": cm,
            "onesc": onesc,
            "onesr": onesr,
        })
    return in_maps


def gather_output(cfg, results):
    return np.stack([np.ascontiguousarray(r["out_fm"].T) for r in results], axis=0)


def numpy_reference(cfg: Cfg, x, w_qkv, w_out, lut, gate_idx, up_idx, down_idx):
    def layernorm(v):
        m = v.mean(-1, keepdims=True)
        s = v.var(-1, keepdims=True)
        return (v - m) / np.sqrt(s + EPS)
    def softmax(s):
        m = s.max(-1, keepdims=True)
        e = np.exp(s - m)
        return e / e.sum(-1, keepdims=True)
    x = np.asarray(x, np.float64)
    w_qkv = np.asarray(w_qkv, np.float64)
    w_out = np.asarray(w_out, np.float64)
    lut = np.asarray(lut, np.float64)
    B, S, H = x.shape
    NH, D = cfg.NH, cfg.D
    qkv = x @ w_qkv
    qkv = qkv.reshape(B, S, 3, NH, D)
    q, k, v = qkv[:, :, 0], qkv[:, :, 1], qkv[:, :, 2]
    q = q / np.sqrt(np.float64(D))
    scores = np.einsum("bqhd,bkhd->bhqk", q, k)
    mask = np.arange(S)[:, None] >= np.arange(S)[None, :]
    scores = np.where(mask[None, None], scores, -1e10)
    a = softmax(scores)
    o = np.einsum("bhqk,bkhd->bqhd", a, v).reshape(B, S, H)
    x1 = layernorm(x + o @ w_out)
    gate = x1 @ lut[gate_idx].T
    up = x1 @ lut[up_idx].T
    hh = (gate / (1 + np.exp(-gate))) * up
    x2 = layernorm(x1 + hh @ lut[down_idx].T)
    return x2.astype(np.float32)



def kernel(L0_attn_x, L0_attn_w_qkv, L0_attn_w_out, L0_ffn_lut,
           L0_ffn_gate_idx, L0_ffn_up_idx, L0_ffn_down_idx):
    cfg = FULL
    in_maps = make_host_inputs(cfg, L0_attn_x, L0_attn_w_qkv, L0_attn_w_out,
                               L0_ffn_lut, L0_ffn_gate_idx, L0_ffn_up_idx,
                               L0_ffn_down_idx)
    nc = build_program(cfg)
    res = run_bass_kernel_spmd(nc, in_maps, core_ids=list(range(cfg.ncores)))
    return gather_output(cfg, res.results)


# version 21
# speedup vs baseline: 1.0416x; 1.0090x over previous
import math
import numpy as np
from contextlib import ExitStack
from dataclasses import dataclass

import concourse.bass as bass
import concourse.mybir as mybir
from concourse.bass_utils import run_bass_kernel_spmd

F32 = mybir.dt.float32
BF16 = mybir.dt.bfloat16
I32 = mybir.dt.int32
AF = mybir.ActivationFunctionType
OP = mybir.AluOpType

LUT_N = 16
EPS = 1e-5
MRUN = 8


@dataclass
class Cfg:
    S: int = 2048
    H: int = 2048
    NH: int = 16
    I: int = 5504
    TB: int = 512
    ncores: int = 8

    @property
    def D(self): return self.H // self.NH
    @property
    def F3(self): return 3 * self.H
    @property
    def nHC(self): return self.H // 128
    @property
    def nOC(self): return self.H // 128
    @property
    def nIC(self): return self.I // 128
    @property
    def nTB(self): return self.S // self.TB
    @property
    def nTT(self): return self.S // 128
    @property
    def NSTR(self): return self.TB // 128
    @property
    def DBW(self): return min(512, self.H)
    @property
    def nDB(self): return self.H // self.DBW
    @property
    def G1(self): return self.nHC * self.nIC
    @property
    def NT(self): return 2 * self.G1 + self.nIC * self.nOC
    @property
    def TPC0(self): return (self.NT + self.ncores - 1) // self.ncores
    @property
    def CSZ(self): return self.nIC
    @property
    def KAG(self): return (self.TPC0 + self.CSZ - 1) // self.CSZ
    @property
    def TPC(self): return self.KAG * self.CSZ
    @property
    def qscale(self): return 1.0 / math.sqrt(self.D)


FULL = Cfg()
SMALL = Cfg(S=256, H=256, NH=2, I=384, TB=128, ncores=8)
MED = Cfg(S=1024, H=512, NH=4, I=768, TB=512, ncores=8)

ENGINES = ("sync", "gpsimd", "vector", "scalar", "tensor")


class Emitter:
    def __init__(self, nc, ctx):
        self.nc = nc
        self.ctx = ctx
        self.q = {e: [] for e in ENGINES}
        self.sems = {}

    def _sem(self, name):
        if name not in self.sems:
            h = self.ctx.enter_context(self.nc.semaphore(name))
            self.sems[name] = [h, 0]
        return self.sems[name]

    def count(self, name):
        return self.sems[name][1] if name in self.sems else 0

    def run(self, eng, fn):
        self.q[eng].append(fn)

    def inc(self, eng, sem_name, fn, k=1):
        s = self._sem(sem_name)
        s[1] += k
        self.q[eng].append(lambda e, fn=fn, h=s[0], k=k: fn(e).then_inc(h, k))
        return s[1]

    def dma(self, eng, sem_name, out, in_):
        return self.inc(eng, sem_name, lambda e, o=out, i=in_: e.dma_start(out=o, in_=i), k=16)

    def wait(self, eng, sem_name, val):
        if val is None or val <= 0:
            return
        s = self._sem(sem_name)
        assert val <= s[1], f"wait {sem_name}>={val} but only {s[1]} emitted"
        self.q[eng].append(lambda e, h=s[0], v=val: e.wait_ge(h, v))

    def drain(self, eng):
        self.q[eng].append(lambda e: e.drain())

    def finalize(self, block):
        for eng in ENGINES:
            dec = getattr(block, eng)
            q = self.q[eng]
            def make(qq):
                def body(e):
                    for fn in qq:
                        fn(e)
                return body
            dec(make(q))


def w_tile_src(cfg, gid):
    if gid < cfg.G1:
        return "gidxT", gid // cfg.nIC, gid % cfg.nIC
    if gid < 2 * cfg.G1:
        return "uidxT", (gid - cfg.G1) // cfg.nIC, (gid - cfg.G1) % cfg.nIC
    g2 = gid - 2 * cfg.G1
    return "didxT", g2 // cfg.nOC, g2 % cfg.nOC


def build_program(cfg: Cfg):
    nc = bass.Bass("TRN2", target_bir_lowering=False)
    ctx = ExitStack()
    S, H, I, TB = cfg.S, cfg.H, cfg.I, cfg.TB
    nHC, nOC, nIC, nTT, NSTR = cfg.nHC, cfg.nOC, cfg.nIC, cfg.nTT, cfg.NSTR
    DBW = cfg.DBW

    x_fm = nc.dram_tensor("x_fm", [H, S], F32, kind="ExternalInput")
    wqkv = nc.dram_tensor("wqkv", [H, cfg.F3], F32, kind="ExternalInput")
    wout = nc.dram_tensor("wout", [H, H], F32, kind="ExternalInput")
    myidx = nc.dram_tensor("myidx", [cfg.TPC, 128, 128], I32, kind="ExternalInput")
    lutb = nc.dram_tensor("lutb", [128, LUT_N], F32, kind="ExternalInput")
    cmask = nc.dram_tensor("cmask", [128, (NSTR - 1) * 128 + TB], BF16, kind="ExternalInput")
    onesc = nc.dram_tensor("onesc", [128, 1], BF16, kind="ExternalInput")
    onesr = nc.dram_tensor("onesr", [1, 128], BF16, kind="ExternalInput")
    out_fm = nc.dram_tensor("out_fm", [H, S], F32, kind="ExternalOutput")

    q_scr = nc.dram_tensor("q_scr", [H, S], BF16)
    k_scr = nc.dram_tensor("k_scr", [H, S], BF16)
    v_scr = nc.dram_tensor("v_scr", [S, H], BF16)
    wob_scr = nc.dram_tensor("wob_scr", [H, H], BF16)
    x2f_scr = nc.dram_tensor("x2f_scr", [H, S], F32)
    x2b_scr = nc.dram_tensor("x2b_scr", [H, S], BF16)
    h_scr = nc.dram_tensor("h_scr", [I, S], BF16)
    wcat_my = nc.dram_tensor("wcat_my", [cfg.TPC, 128, 128], BF16)
    if cfg.KAG > 1:
        assert nHC % 4 == 0 and nOC % 2 == 0
    wc = [nc.dram_tensor(f"wcat{k}", [cfg.ncores * cfg.CSZ, 128, 128], BF16,
                         addr_space="Shared") for k in range(cfg.KAG)]

    sb = lambda name, shape, dt: ctx.enter_context(nc.sbuf_tensor(name, shape, dt))

    with ctx:
        xact = sb("xact", [128, nHC * S], BF16)
        p2cols = nTT * TB + 4 * S + 9 * TB
        bigcols = max(p2cols, nIC * TB, 21 * MRUN * 128)
        big = sb("big", [128, bigcols], BF16)
        r1b_sl = sb("r1b_sl", [128, TB], BF16)
        r1sq_sl = sb("r1sq_sl", [128, TB], BF16)
        
        slab16 = sb("slab16", [128, nHC * max(DBW, TB)], BF16)
        arf_cols = max(8 * 128 + 2 * DBW + 2 * S, nOC * TB + 4 * TB)
        arf = sb("arf", [128, arf_cols], F32)

        lut_s = sb("lut_s", [128, LUT_N], F32)
        cm_s = sb("cm_s", [128, (NSTR - 1) * 128 + TB], BF16)
        onec_s = sb("onec_s", [128, 1], BF16)
        oner_s = sb("oner_s", [1, 128], BF16)
        wqk_b = [sb(f"wqk_b{s}", [128, 128], BF16) for s in range(8)]
        qk_e = [sb(f"qk_e{s}", [128, TB], BF16) for s in range(4)]
        ve_t = [sb(f"ve_t{s}", [128, DBW], BF16) for s in range(1)]
        wo_t = [sb(f"wo_t{s}", [128, 2 * 128], BF16) for s in range(4)]
        wg_t = [sb(f"wg_t{s}", [128, 4 * 128], BF16) for s in range(2)]
        wu_t = [sb(f"wu_t{s}", [128, 4 * 128], BF16) for s in range(2)]
        wd_t = [sb(f"wd_t{s}", [128, 2 * 128], BF16) for s in range(8)]
        mn_f = sb("mn_f", [1, TB], F32)
        msq_f = sb("msq_f", [1, TB], F32)
        mn2_f = sb("mn2_f", [1, TB], F32)
        var_f = sb("var_f", [1, TB], F32)
        sd_f = sb("sd_f", [1, TB], F32)
        rstd_f = sb("rstd_f", [1, TB], F32)
        den_f = mn_f
        rden_f = msq_f
        bvec_f = sd_f
        rden_b = sb("rden_b", [1, TB], BF16)
        a_b = sb("a_b", [1, TB], BF16)
        b_b = sb("b_b", [1, TB], BF16)
        

        BKW = 512
        psA = [ctx.enter_context(nc.psum_tensor(f"psA{i}", [128, 2 * BKW], F32)) for i in range(2)]
        psS = [ctx.enter_context(nc.psum_tensor(f"psS{i}", [128, BKW], F32)) for i in range(4)]
        ps = [psA[0][:, 0:BKW], psA[0][:, BKW:2 * BKW],
              psS[0][:, :], psS[1][:, :], psS[2][:, :], psS[3][:, :],
              psA[1][:, 0:BKW], psA[1][:, BKW:2 * BKW]]

        T = dict(
            x_fm=x_fm, wqkv=wqkv, wout=wout, myidx=myidx, lutb=lutb, cmask=cmask,
            onesc=onesc, onesr=onesr, out_fm=out_fm, q_scr=q_scr, k_scr=k_scr,
            v_scr=v_scr, wob_scr=wob_scr, x2f_scr=x2f_scr, x2b_scr=x2b_scr,
            h_scr=h_scr, wcat_my=wcat_my, wc=wc,
            xact=xact, big=big, slab16=slab16, arf=arf,
            r1b_sl=r1b_sl, r1sq_sl=r1sq_sl,

            lut_s=lut_s, cm_s=cm_s,
            onec_s=onec_s, oner_s=oner_s, wqk_b=wqk_b, qk_e=qk_e,
            ve_t=ve_t, wo_t=wo_t, wg_t=wg_t, wu_t=wu_t, wd_t=wd_t,
            den_f=den_f, rden_f=rden_f, rden_b=rden_b,
            mn_f=mn_f, msq_f=msq_f, mn2_f=mn2_f, var_f=var_f, sd_f=sd_f,
            rstd_f=rstd_f, bvec_f=bvec_f, a_b=a_b, b_b=b_b, ps=ps, psA=psA,
        )
        with nc.Block() as block:
            E = Emitter(nc, ctx)
            build_phases(cfg, E, T)
            E.finalize(block)
    return nc


def build_phases(cfg, E, T):
    S, H, I, TB = cfg.S, cfg.H, cfg.I, cfg.TB
    nHC, nOC, nIC, nTB, nTT, NSTR = (cfg.nHC, cfg.nOC, cfg.nIC, cfg.nTB,
                                     cfg.nTT, cfg.NSTR)
    NH, DBW, nDB = cfg.NH, cfg.DBW, cfg.nDB
    PSW = 512
    ps = T["ps"]
    big, xact, slab16, arf = T["big"], T["xact"], T["slab16"], T["arf"]
    g = lambda n: T[n]

    et_off = 0
    ksl_off = nTT * TB
    vsl_off = ksl_off + 2 * S
    qsl_off = vsl_off + 2 * S
    et = lambda kt: big[:, et_off + kt * TB: et_off + (kt + 1) * TB]
    k_sl = lambda s, w: big[:, ksl_off + s * S: ksl_off + s * S + w]
    v_sl = lambda s, w: big[:, vsl_off + s * S: vsl_off + s * S + w]
    q_sl = lambda s: big[:, qsl_off + s * TB: qsl_off + (s + 1) * TB]
    r1b_t = lambda s: T["r1b_sl"][:, 0:TB]
    r1sq_t = lambda s: T["r1sq_sl"][:, 0:TB]
    tail0 = qsl_off + 2 * TB
    et_t = lambda s: big[:, tail0 + s * TB: tail0 + (s + 1) * TB]
    h_sl = lambda ic: big[:, ic * TB:(ic + 1) * TB]
    bc_s = big[:, tail0 + 2 * TB: tail0 + 3 * TB]
    sil_t = [big[:, tail0 + (3 + s) * TB: tail0 + (4 + s) * TB] for s in range(2)]
    h_t = [big[:, tail0 + (5 + s) * TB: tail0 + (6 + s) * TB] for s in range(2)]
    attn = lambda dc: slab16[:, dc * TB:(dc + 1) * TB]
    wv_b = lambda hc: slab16[:, hc * DBW:(hc + 1) * DBW]
    wqk_f = [arf[:, s * 128:(s + 1) * 128] for s in range(8)]
    wv_f = [arf[:, 8 * 128 + s * DBW: 8 * 128 + (s + 1) * DBW] for s in range(2)]
    xs_f = [arf[:, 8 * 128 + 2 * DBW + s * S: 8 * 128 + 2 * DBW + (s + 1) * S] for s in range(2)]
    r1 = arf[:, 0:nOC * TB]
    xres = [arf[:, nOC * TB + s * TB: nOC * TB + (s + 1) * TB] for s in range(2)]
    tmp_f = arf[:, nOC * TB + 2 * TB: nOC * TB + 3 * TB]
    x2f_t = [arf[:, nOC * TB + 3 * TB: nOC * TB + 4 * TB]]
    r2, x2r, out_t = r1, xres, x2f_t

    E.dma("sync", "s_lut", g("lut_s")[:, :], g("lutb")[:, :])
    E.dma("sync", "s_cm", g("cm_s")[:, :], g("cmask")[:, :])
    E.dma("sync", "s_onc", g("onec_s")[:, :], g("onesc")[:, :])
    E.dma("sync", "s_onr", g("oner_s")[:, :], g("onesr")[:, :])

    NG = min(4, nHC)
    runs = []
    t = 0
    while t < cfg.TPC:
        w = min(MRUN, cfg.TPC - t)
        runs.append((t, w))
        t += w
    MP = MRUN * 128
    m_part = [big[:, j * MP:(j + 1) * MP] for j in range(16)]
    m_idxb = [big[:, (16 + s) * MP:(17 + s) * MP] for s in range(3)]
    m_out = [big[:, (19 + s) * MP:(20 + s) * MP] for s in range(2)]
    E.wait("vector", "s_lut", 16)
    acc_v = {}

    def mat_dma(ri):
        t0, w = runs[ri]
        slot = ri % 3
        if ri >= 3:
            E.wait("gpsimd", "v_p", acc_v[ri - 3])
        E.dma("gpsimd", f"s_midx{slot}",
              m_idxb[slot][:, 0:w * 128].rearrange("p (t c) -> p t c", c=128),
              g("myidx")[t0:t0 + w, :, :].rearrange("t p c -> p t c"))

    for r0 in range(min(3, len(runs))):
        mat_dma(r0)
    for ri, (t0, w) in enumerate(runs):
        slot = ri % 3
        E.wait("vector", f"s_midx{slot}", 16 * (ri // 3 + 1))
        for j in range(LUT_N):
            E.run("vector", lambda e, w=w, j=j, slot=slot: e.tensor_scalar(
                out=m_part[j][:, 0:w * 128], in0=m_idxb[slot][:, 0:w * 128],
                scalar1=float(j), scalar2=g("lut_s")[:, j:j + 1],
                op0=OP.is_equal, op1=OP.mult))
        E.drain("vector")
        step = 1
        while step < 16:
            for j0 in range(0, 16, 2 * step):
                last = (2 * step >= 16)
                if last:
                    if ri >= 2:
                        E.wait("vector", f"s_wmy{ri % 2}", 16 * ((ri - 2) // 2 + 1))
                    fn = lambda e, w=w, ri=ri: e.tensor_tensor(
                        out=m_out[ri % 2][:, 0:w * 128], in0=m_part[0][:, 0:w * 128],
                        in1=m_part[8][:, 0:w * 128], op=OP.add)
                    acc_v[ri] = E.inc("vector", "v_p", fn)
                else:
                    fn = lambda e, j0=j0, s=step, w=w: e.tensor_tensor(
                        out=m_part[j0][:, 0:w * 128], in0=m_part[j0][:, 0:w * 128],
                        in1=m_part[j0 + s][:, 0:w * 128], op=OP.add)
                    E.run("vector", fn)
            E.drain("vector")
            step *= 2
        if ri + 3 < len(runs):
            mat_dma(ri + 3)
        E.wait("gpsimd", "v_p", acc_v[ri])
        E.dma("gpsimd", f"s_wmy{ri % 2}",
              g("wcat_my")[t0:t0 + w, :, :].rearrange("t p c -> p t c"),
              m_out[ri % 2][:, 0:w * 128].rearrange("p (t c) -> p t c", c=128))
        for k in range(cfg.KAG):
            if ((k + 1) * cfg.CSZ - 1) // MRUN == ri:
                E.wait("gpsimd", "s_wmy0", 16 * (ri // 2 + 1))
                if ri >= 1:
                    E.wait("gpsimd", "s_wmy1", 16 * ((ri - 1) // 2 + 1))
                ag_g = E.inc("gpsimd", "g_p", lambda e, k=k: e.collective_compute(
                    "AllGather", OP.bypass, replica_groups=[list(range(cfg.ncores))],
                    ins=[g("wcat_my")[k * cfg.CSZ:(k + 1) * cfg.CSZ, :, :].opt()],
                    outs=[g("wc")[k].ap().opt()]))
    mat_done_v = acc_v[len(runs) - 1]
    n_wmy = [E.count("s_wmy0"), E.count("s_wmy1")]
    for i4 in range(NG):
        E.dma("gpsimd", "s_wob", g("wob_scr")[i4 * (H // NG):(i4 + 1) * (H // NG), :],
              g("wout")[i4 * (H // NG):(i4 + 1) * (H // NG), :])
    n_wob = E.count("s_wob")

    qkcast_s, lastmm_t, stopmm_t, qkevac_s = {}, {}, {}, {}
    xcast_s = {}
    for ft in range(2 * nHC):
        isq = ft < nHC
        for hc in range(nHC):
            it = ft * nHC + hc
            slot = it % 8
            if ft == 0:
                xsl = hc % 2
                if hc >= 2:
                    E.wait("sync", "sc_p", xcast_s[hc - 2])
                E.dma("sync", f"s_xs{xsl}", xs_f[xsl],
                      g("x_fm")[hc * 128:(hc + 1) * 128, :])
                E.wait("scalar", f"s_xs{xsl}", 16 * (hc // 2 + 1))
                xcast_s[hc] = E.inc("scalar", "sc_p",
                                    lambda e, s=xsl, k=hc: e.copy(
                                        out=xact[:, k * S:(k + 1) * S], in_=xs_f[s]))
            if it >= 8:
                E.wait("sync", "sc_p", qkcast_s[it - 8])
            E.dma("sync", f"s_wqk{slot}", wqk_f[slot],
                  g("wqkv")[hc * 128:(hc + 1) * 128, ft * 128:(ft + 1) * 128])
            if it >= 8:
                E.wait("scalar", "t_p", lastmm_t[it - 8])
            E.wait("scalar", f"s_wqk{slot}", 16 * (it // 8 + 1))
            qkcast_s[it] = E.inc("scalar", "sc_p",
                                 lambda e, s=slot: e.copy(out=g("wqk_b")[s][:, :],
                                                          in_=wqk_f[s]))
            E.wait("tensor", "sc_p", qkcast_s[it])
            if ft == 0:
                E.wait("tensor", "sc_p", xcast_s[hc])
            if hc == 0 and ft >= 2:
                E.wait("tensor", "sc_p", qkevac_s[(ft - 2, nTB - 1)])
            bk0 = (ft % 2) * nTB
            for tb in range(nTB):
                fn = lambda e, s=slot, hc=hc, tb=tb, bk0=bk0, st=(hc == 0), sp=(hc == nHC - 1): e.matmul(
                    ps[bk0 + tb][:, 0:TB], g("wqk_b")[s][:, :],
                    xact[:, hc * S + tb * TB: hc * S + (tb + 1) * TB],
                    start=st, stop=sp, skip_group_check=True)
                if hc == nHC - 1:
                    stopmm_t[(ft, tb)] = E.inc("tensor", "t_p", fn)
                    lastmm_t[it] = stopmm_t[(ft, tb)]
                elif tb == nTB - 1:
                    lastmm_t[it] = E.inc("tensor", "t_p", fn)
                else:
                    E.run("tensor", fn)
        for tb in range(nTB):
            es = (ft * nTB + tb) % 4
            idx = ft * nTB + tb
            E.wait("scalar", "t_p", stopmm_t[(ft, tb)])
            if idx >= 4:
                E.wait("scalar", f"s_qkst{es}", 16 * (idx // 4))
            sc = cfg.qscale if isq else 1.0
            qkevac_s[(ft, tb)] = E.inc("scalar", "sc_p",
                                       lambda e, tb=tb, es=es, sc=sc, bk0=(ft % 2) * nTB:
                                       e.activation(
                                           out=g("qk_e")[es][:, :], in_=ps[bk0 + tb][:, 0:TB],
                                           func=AF.Copy, scale=sc))
            E.wait("scalar", "sc_p", qkevac_s[(ft, tb)])
            dst = g("q_scr") if isq else g("k_scr")
            r0 = (ft if isq else ft - nHC) * 128
            E.dma("scalar", f"s_qkst{es}", dst[r0:r0 + 128, tb * TB:(tb + 1) * TB],
                  g("qk_e")[es][:, :])
    n_qkst = [E.count(f"s_qkst{e_}") for e_ in range(4)]


    wvc_s, vstop_t, vevac_s = {}, {}, {}
    for db in range(nDB):
        for hc in range(nHC):
            it = db * nHC + hc
            slot = hc % 2
            if it >= 2:
                E.wait("sync", "sc_p", wvc_s[it - 2])
            E.dma("sync", f"s_wv{slot}", wv_f[slot],
                  g("wqkv")[hc * 128:(hc + 1) * 128, 2 * H + db * DBW: 2 * H + (db + 1) * DBW])
            if db >= 1:
                E.wait("scalar", "t_p", vstop_t[(db - 1, nTT - 1)])
            E.wait("scalar", f"s_wv{slot}", 16 * (it // 2 + 1))
            wvc_s[it] = E.inc("scalar", "sc_p",
                              lambda e, hc=hc, s=slot: e.copy(out=wv_b(hc), in_=wv_f[s]))
        for tt in range(nTT):
            bank = 4 + (tt % 2)
            it = db * nTT + tt
            if it == 0:
                E.wait("tensor", "sc_p", qkevac_s[(2 * nHC - 1, nTB - 1)])
            if it >= 2:
                E.wait("tensor", "sc_p", vevac_s[it - 2])
            E.wait("tensor", "sc_p", wvc_s[db * nHC + nHC - 1])
            for hc in range(nHC):
                fn = lambda e, hc=hc, tt=tt, bank=bank, st=(hc == 0), sp=(hc == nHC - 1): e.matmul(
                    ps[bank][:, 0:DBW],
                    xact[:, hc * S + tt * 128: hc * S + (tt + 1) * 128],
                    wv_b(hc), start=st, stop=sp, skip_group_check=True)
                if hc == nHC - 1:
                    vstop_t[(db, tt)] = E.inc("tensor", "t_p", fn)
                else:
                    E.run("tensor", fn)
            E.wait("scalar", "t_p", vstop_t[(db, tt)])
            if it >= 1:
                E.wait("scalar", "s_vst", 16 * it)
            es = 0
            vevac_s[it] = E.inc("scalar", "sc_p",
                                lambda e, bank=bank, es=es: e.copy(out=g("ve_t")[es][:, :],
                                                                   in_=ps[bank][:, 0:DBW]))
            E.wait("scalar", "sc_p", vevac_s[it])
            E.dma("scalar", "s_vst",
                  g("v_scr")[tt * 128:(tt + 1) * 128, db * DBW:(db + 1) * DBW],
                  g("ve_t")[es][:, :])
    n_vst = E.count("s_vst")
    p1_sc = E.count("sc_p")
    tp_p1v = E.count("t_p")

    E.wait("vector", "s_cm", 16)
    E.wait("scalar", "v_p", mat_done_v)
    E.wait("scalar", "s_wmy0", n_wmy[0])
    E.wait("scalar", "s_wmy1", n_wmy[1])
    E.wait("sync", "v_p", mat_done_v)
    E.wait("sync", "s_wmy0", n_wmy[0])
    E.wait("sync", "s_wmy1", n_wmy[1])
    E.wait("gpsimd", "s_wob", n_wob)
    E.wait("sync", "s_wob", n_wob)
    n_strad = 0
    msk_v, avden_t, attn_v, denev_v, bcs_v = {}, {}, {}, {}, {}
    wo_use_t, r1_v, x2f_v = {}, {}, {}
    wo_qctr = {0: 0, 1: 0}
    vsl_fill = {}
    prev_exp_last = 0
    n_x2f = 0
    ln_prev_stat = {'t': 0, 'sc': 0}
    wo_last_t = 0
    x2f_cnt = {}
    avkt_t = {}
    msk_last = {}
    for qb in range(nTB):
        nkt = (qb + 1) * NSTR
        nkt_prev_qb = qb * NSTR

        def den_tail(iq2):
            head2 = iq2 % NH
            bav = 2 + (head2 % 2)
            bden = 4 + (head2 % 2)
            E.wait("scalar", "t_p", avden_t[iq2])
            E.run("scalar", lambda e, bden=bden: e.activation(
                out=g("rden_f")[:, :], in_=ps[bden][0:1, 0:TB], func=AF.Ln))
            E.drain("scalar")
            denev_v[iq2] = E.inc("scalar", "sc_p", lambda e: e.activation(
                out=g("rden_b")[:, :], in_=g("rden_f")[:, :], func=AF.Exp, scale=-1.0))
            E.wait("tensor", "sc_p", denev_v[iq2])
            E.wait("tensor", "s_onr", 16)
            if iq2 > 0:
                E.wait("tensor", "v_p", bcs_v[iq2 - 1])
            bcb = 2 + ((head2 + 1) % 2)
            if iq2 >= 1:
                E.wait("tensor", "v_p", attn_v[iq2 - 1])
            tbc = E.inc("tensor", "t_p", lambda e, bcb=bcb: e.matmul(
                ps[bcb][:, 0:TB], g("oner_s")[:, :], g("rden_b")[:, :],
                start=True, stop=True, skip_group_check=True))
            E.wait("vector", "t_p", tbc)
            bcs_v[iq2] = E.inc("vector", "v_p",
                               lambda e, bcb=bcb: e.tensor_copy(bc_s, ps[bcb][:, 0:TB]))
            E.drain("vector")
            if head2 == 0 and iq2 >= NH:
                E.wait("vector", "t_p", wo_last_t)
            attn_v[iq2] = E.inc("vector", "v_p",
                                lambda e, head2=head2, bav=bav: e.tensor_tensor(
                                    out=attn(head2), in0=ps[bav][:, 0:TB], in1=bc_s,
                                    op=OP.mult))
            E.drain("vector")

        def emit_av_kt(iq2, kt, nkt2, last):
            head2 = iq2 % NH
            bav = 2 + (head2 % 2)
            bden = 4 + (head2 % 2)
            sl2 = head2 % 2
            if kt == 0:
                if head2 <= 1 and iq2 >= NH and (iq2 // NH) == qb:
                    E.wait("tensor", "v_p", x2f_v[(qb - 1) * nOC + nOC - 1])
                if iq2 >= 2:
                    E.wait("tensor", "v_p", attn_v[iq2 - 2])
                    E.wait("tensor", "sc_p", denev_v[iq2 - 2])
                if iq2 >= 1:
                    E.wait("tensor", "v_p", bcs_v[iq2 - 1])
                if iq2 in msk_last:
                    E.wait("tensor", "v_p", msk_last[iq2])
                E.wait("tensor", f"s_vsl{sl2}", vsl_fill[iq2])
                E.wait("tensor", "s_onc", 16)
            E.run("tensor", lambda e, sl2=sl2, kt=kt, nkt2=nkt2, bav=bav,
                  st=(kt == 0), sp=last: e.matmul(
                      ps[bav][:, 0:TB], v_sl(sl2, nkt2 * 128)[:, kt * 128:(kt + 1) * 128],
                      et(kt), start=st, stop=sp, skip_group_check=True))
            avkt_t[(iq2, kt)] = E.inc("tensor", "t_p",
                                      lambda e, kt=kt, bden=bden, st=(kt == 0), sp=last: e.matmul(
                                          ps[bden][0:1, 0:TB], g("onec_s")[:, :], et(kt),
                                          start=st, stop=sp, skip_group_check=True))
            if last:
                avden_t[iq2] = avkt_t[(iq2, kt)]

        for head in range(NH):
            sl = head % 2
            iq = qb * NH + head
            if iq >= 2:
                E.wait("sync", "t_p", avden_t[iq - 2])
            idxq = head * nTB + qb
            idxk = (nHC + head) * nTB + qb
            E.wait("sync", f"s_qkst{idxq % 4}", 16 * (idxq // 4 + 1))
            E.wait("sync", f"s_qkst{idxk % 4}", 16 * (idxk // 4 + 1))
            idxv = (nDB - 1) * nTT + (NSTR * (qb + 1) - 1)
            E.wait("sync", "s_vst", 16 * (idxv + 1))
            E.dma("sync", f"s_qsl{sl}", q_sl(sl),
                  g("q_scr")[head * 128:(head + 1) * 128, qb * TB:(qb + 1) * TB])
            E.dma("sync", f"s_ksl{sl}", k_sl(sl, nkt * 128),
                  g("k_scr")[head * 128:(head + 1) * 128, 0:nkt * 128])
            E.dma("sync", f"s_vsl{sl}",
                  v_sl(sl, nkt * 128).rearrange("p (t c) -> p t c", c=128),
                  g("v_scr")[0:nkt * 128, head * 128:(head + 1) * 128]
                  .rearrange("(t p) c -> p t c", p=128))
            nfill = 16 * (iq // 2 + 1)
            vsl_fill[iq] = nfill
            E.wait("tensor", f"s_qsl{sl}", nfill)
            E.wait("tensor", f"s_ksl{sl}", nfill)
            exp_slot = {}
            PAIR_OK = (TB == 512)
            SBK4 = (0, 1, 6, 7)
            ns_end = nkt - NSTR
            last_exp = 0
            kt = 0
            while kt < nkt:
                pair = (kt + 1 < ns_end and kt % 2 == 0 and PAIR_OK)
                subkts = (kt, kt + 1) if pair else (kt,)
                for k2 in subkts:
                    c = k2
                    bank = SBK4[c % 4]
                    if c >= 4:
                        E.wait("tensor", "sc_p", exp_slot[c - 4])
                    elif iq > 0:
                        E.wait("tensor", "sc_p", prev_exp_last)
                    else:
                        E.wait("tensor", "sc_p", p1_sc)
                    if c == 0 and head == 0 and qb > 0:
                        E.wait("tensor", "v_p", r1_v[(qb - 1) * nOC + nOC - 1])
                    tsc = E.inc("tensor", "t_p", lambda e, sl=sl, k2=k2, bank=bank, nkt=nkt: e.matmul(
                        ps[bank][:, 0:TB], k_sl(sl, nkt * 128)[:, k2 * 128:(k2 + 1) * 128],
                        q_sl(sl), start=True, stop=True, skip_group_check=True))
                    if head > 0:
                        emit_av_kt(iq - 1, k2, nkt, last=(k2 == nkt - 1))
                kt1 = subkts[-1]
                E.wait("scalar", "t_p", tsc)
                if head > 0:
                    E.wait("scalar", "t_p", avkt_t[(iq - 1, kt1)])
                elif qb > 0:
                    pl = qb * NH - 1
                    kk = min(kt1, nkt_prev_qb - 1)
                    if kk >= 0:
                        E.wait("scalar", "t_p", avkt_t[(pl, kk)])
                if pair:
                    pb = (kt // 2) % 2
                    v = E.inc("scalar", "sc_p",
                              lambda e, pb=pb, kt=kt: e.activation(
                                  out=big[:, et_off + kt * TB: et_off + (kt + 2) * TB],
                                  in_=T["psA"][pb][:, 0:2 * TB], func=AF.Exp))
                    exp_slot[kt] = exp_slot[kt + 1] = v
                    last_exp = v
                elif kt1 >= nkt - NSTR:
                    es = n_strad % 2
                    n_strad += 1
                    if n_strad > 2:
                        E.wait("scalar", "v_p", msk_v[n_strad - 2])
                    v = E.inc("scalar", "sc_p",
                              lambda e, bank=SBK4[kt1 % 4], es=es: e.activation(
                                  out=et_t(es), in_=ps[bank][:, 0:TB], func=AF.Exp))
                    exp_slot[kt1] = v
                    last_exp = v
                    E.wait("vector", "sc_p", v)
                    j = kt1 - (nkt - NSTR)
                    msk_v[n_strad] = E.inc("vector", "v_p",
                                           lambda e, kt1=kt1, es=es, j=j: e.tensor_tensor(
                                               out=et(kt1), in0=et_t(es),
                                               in1=g("cm_s")[:, (NSTR - 1 - j) * 128:
                                                             (NSTR - 1 - j) * 128 + TB],
                                               op=OP.mult))
                    msk_last[iq] = msk_v[n_strad]
                else:
                    v = E.inc("scalar", "sc_p",
                              lambda e, bank=SBK4[kt1 % 4], kt1=kt1: e.activation(
                                  out=et(kt1), in_=ps[bank][:, 0:TB], func=AF.Exp))
                    exp_slot[kt1] = v
                    last_exp = v
                kt += 2 if pair else 1
            prev_exp_last = last_exp
            if head > 0:
                den_tail(iq - 1)
        liq = qb * NH + NH - 1
        E.wait("tensor", "sc_p", prev_exp_last)
        for kt in range(nkt):
            emit_av_kt(liq, kt, nkt, last=(kt == nkt - 1))
        den_tail(liq)
        n_attn_qb = attn_v[qb * NH + NH - 1]
        first_wo = True
        for ot in range(nOC):
            bank = ot % 2
            io = qb * nOC + ot
            if io >= 2:
                E.wait("tensor", "v_p", r1_v[io - 2])
            for dc2 in range(nHC // 2):
                dc = 2 * dc2
                qsel = dc2 % 2
                eng = "sync" if qsel == 0 else "gpsimd"
                jq = wo_qctr[qsel]
                wo_qctr[qsel] += 1
                ws = 2 * qsel + (jq % 2)
                if jq >= 2:
                    E.wait(eng, "t_p", wo_use_t[(qsel, jq - 2)])
                E.dma(eng, f"s_wot{ws}",
                      g("wo_t")[ws][:, :].rearrange("p (t c) -> p t c", c=128),
                      g("wob_scr")[dc * 128:(dc + 2) * 128,
                                   ot * 128:(ot + 1) * 128].rearrange("(t p) c -> p t c", p=128))
                E.wait("tensor", f"s_wot{ws}", 16 * (jq // 2 + 1))
                E.wait("tensor", "v_p", n_attn_qb)
                if first_wo:
                    E.wait("tensor", "sc_p", prev_exp_last)
                    first_wo = False
                E.run("tensor", lambda e, ws=ws, dc=dc, bank=bank, st=(dc == 0): e.matmul(
                    ps[bank][:, 0:TB], g("wo_t")[ws][:, 0:128], attn(dc),
                    start=st, stop=False, skip_group_check=True))
                wo_use_t[(qsel, jq)] = E.inc("tensor", "t_p",
                                             lambda e, ws=ws, dc=dc, bank=bank,
                                             sp=(dc == nHC - 2): e.matmul(
                                                 ps[bank][:, 0:TB], g("wo_t")[ws][:, 128:256],
                                                 attn(dc + 1),
                                                 start=False, stop=sp, skip_group_check=True))
            wo_last_t = wo_use_t[(qsel, jq)]
            xs = ot % 2
            if io >= 2:
                E.wait("sync", "v_p", r1_v[io - 2])
            if qb == 0 and ot == 0:
                E.wait("sync", "sc_p", p1_sc)
            E.dma("sync", f"s_xres{xs}", xres[xs],
                  g("x_fm")[ot * 128:(ot + 1) * 128, qb * TB:(qb + 1) * TB])
            if qb == 0 and ot == 0:
                E.wait("vector", "sc_p", p1_sc)
            if qb > 0 and ot == 0:
                E.wait("vector", "sc_p", ln_prev_stat['sc'])
            E.wait("vector", "t_p", wo_last_t)
            E.wait("vector", f"s_xres{xs}", 16 * (io // 2 + 1))
            r1_v[io] = E.inc("vector", "v_p",
                             lambda e, ot=ot, bank=bank, xs=xs: e.tensor_tensor(
                                 out=r1[:, ot * TB:(ot + 1) * TB], in0=ps[bank][:, 0:TB],
                                 in1=xres[xs], op=OP.add))
        rv = {oc: r1_v[qb * nOC + oc] for oc in range(nOC)}
        emit_ln(cfg, E, T, r1, rv, r1b_t, r1sq_t, (2, 3, 4, 5), ln_prev_stat)
        for oc in range(nOC):
            s = 0
            io = qb * nOC + oc
            if io >= 1:
                E.wait("vector", "s_x2f", 16 * io)
            E.run("vector", lambda e, oc=oc: e.tensor_tensor(
                out=tmp_f, in0=r1[:, oc * TB:(oc + 1) * TB], in1=ps[4][:, 0:TB], op=OP.mult))
            E.drain("vector")
            x2f_v[io] = E.inc("vector", "v_p", lambda e, s=s: e.tensor_tensor(
                out=x2f_t[s], in0=tmp_f, in1=ps[5][:, 0:TB], op=OP.subtract))
            E.wait("gpsimd", "v_p", x2f_v[io])
            n_x2f = E.dma("gpsimd", "s_x2f",
                          g("x2f_scr")[oc * 128:(oc + 1) * 128, qb * TB:(qb + 1) * TB],
                          x2f_t[s])
            x2f_cnt[(qb, oc)] = n_x2f

    E.wait("gpsimd", "t_p", tp_p1v)
    for hc in range(nHC):
        E.wait("gpsimd", "s_x2f", x2f_cnt[(nTB - 1, hc)])
        E.dma("gpsimd", f"s_x2g{hc * NG // nHC}", xact[:, hc * S:(hc + 1) * S],
              g("x2f_scr")[hc * 128:(hc + 1) * 128, :])
    n_x2g = [E.count(f"s_x2g{i}") for i in range(NG)]
    E.wait("gpsimd", "g_p", ag_g)
    E.wait("sync", "g_p", ag_g)
    sil_sc, h_v, gu_t = {}, {}, {}
    h_fill = {0: 0, 1: 0}
    QW = min(4, nHC)
    nHQ = (nHC + QW - 1) // QW
    wcv = [g("wc")[k][:, :, :].rearrange("(h a) p c -> h a p c", a=nIC)
           for k in range(cfg.KAG)]
    BL = cfg.ncores * nIC
    wgu_fill = {0: 0, 1: 0}
    for ic in range(nIC):
        for hc in range(nHC):
            iw = ic * nHC + hc
            hq, hsub = divmod(hc, QW)
            jq = ic * nHQ + hq
            sg = jq % 2
            if hsub == 0:
                w = min(QW, nHC - hc)
                if jq >= 2:
                    pic2, phq2 = divmod(jq - 2, nHQ)
                    iw2 = pic2 * nHC + min((phq2 + 1) * QW, nHC) - 1
                    E.wait("gpsimd", "t_p", gu_t[iw2])
                    E.wait("sync", "t_p", gu_t[iw2])
                kg, hg = divmod(hc, cfg.ncores)
                ku, hu = divmod(nHC + hc, cfg.ncores)
                E.dma("gpsimd", f"s_wg{sg}",
                      g("wg_t")[sg][:, 0:w * 128].rearrange("p (t c) -> p t c", c=128),
                      wcv[kg][hg:hg + w, ic:ic + 1, :, :].rearrange("t a p c -> p (t a) c"))
                E.dma("sync", f"s_wu{sg}",
                      g("wu_t")[sg][:, 0:w * 128].rearrange("p (t c) -> p t c", c=128),
                      wcv[ku][hu:hu + w, ic:ic + 1, :, :].rearrange("t a p c -> p (t a) c"))
                wgu_fill[sg] += 16
                E.wait("tensor", f"s_wg{sg}", wgu_fill[sg])
                E.wait("tensor", f"s_wu{sg}", wgu_fill[sg])
            if ic == 0 and hc % (nHC // NG) == 0:
                E.wait("tensor", f"s_x2g{hc * NG // nHC}", n_x2g[hc * NG // nHC])
            if iw == 0:
                E.wait("tensor", "v_p", E.count("v_p"))
                E.wait("tensor", "sc_p", E.count("sc_p"))
            for qb in range(nTB):
                if ic >= 1 and hc == 0:
                    E.wait("tensor", "sc_p", sil_sc[(ic - 1, qb)])
                    E.wait("tensor", "v_p", h_v[(ic - 1, qb)])
                fng = lambda e, sg=sg, hc=hc, qb=qb, hs=hsub, st=(hc == 0), sp=(hc == nHC - 1): e.matmul(
                    ps[qb][:, 0:TB], g("wg_t")[sg][:, hs * 128:(hs + 1) * 128],
                    xact[:, hc * S + qb * TB: hc * S + (qb + 1) * TB],
                    start=st, stop=sp, skip_group_check=True)
                fnu = lambda e, sg=sg, hc=hc, qb=qb, hs=hsub, st=(hc == 0), sp=(hc == nHC - 1): e.matmul(
                    ps[nTB + qb][:, 0:TB], g("wu_t")[sg][:, hs * 128:(hs + 1) * 128],
                    xact[:, hc * S + qb * TB: hc * S + (qb + 1) * TB],
                    start=st, stop=sp, skip_group_check=True)
                E.run("tensor", fng)
                if qb == nTB - 1:
                    gu_t[iw] = E.inc("tensor", "t_p", fnu)
                else:
                    E.run("tensor", fnu)
        for qb in range(nTB):
            s = qb % 2
            iq = ic * nTB + qb
            E.wait("scalar", "t_p", gu_t[ic * nHC + nHC - 1])
            if iq >= 2:
                E.wait("scalar", "v_p", h_v[((iq - 2) // nTB, (iq - 2) % nTB)])
            sil_sc[(ic, qb)] = E.inc("scalar", "sc_p",
                                     lambda e, qb=qb, s=s: e.activation(
                                         out=sil_t[s], in_=ps[qb][:, 0:TB],
                                         func=AF.Silu))
            E.wait("vector", "sc_p", sil_sc[(ic, qb)])
            if h_fill[s] >= 1:
                E.wait("vector", f"s_h{s}", 16 * h_fill[s])
            h_v[(ic, qb)] = E.inc("vector", "v_p",
                                  lambda e, qb=qb, s=s: e.tensor_tensor(
                                      out=h_t[s], in0=sil_t[s],
                                      in1=ps[nTB + qb][:, 0:TB], op=OP.mult))
            E.wait("scalar", "v_p", h_v[(ic, qb)])
            E.dma("scalar", f"s_h{s}",
                  g("h_scr")[ic * 128:(ic + 1) * 128, qb * TB:(qb + 1) * TB],
                  h_t[s])
            h_fill[s] += 1
    n_h = [E.count(f"s_h{s_}") for s_ in range(2)]

    for s_ in range(2):
        E.wait("sync", f"s_h{s_}", n_h[s_])
    E.wait("sync", "s_x2f", n_x2f)
    E.wait("sync", "t_p", E.count("t_p"))
    do_t, r2_v, wd_use_t = {}, {}, {}
    wd_qctr = {0: 0, 1: 0}
    n_out = 0
    first_p4 = True
    nICP = (nIC + 1) // 2
    h_free_t, n_hl_q = {}, {}
    for s_ in range(2):
        E.wait("gpsimd", f"s_h{s_}", n_h[s_])
    E.wait("gpsimd", "t_p", E.count("t_p"))
    for icp in range(nICP):
        ic0, icw = 2 * icp, min(2, nIC - 2 * icp)
        eng2 = "sync" if icp % 2 == 0 else "gpsimd"
        for icsub in range(icw):
            ic = ic0 + icsub
            E.dma(eng2, f"s_hl{icp % 2}", h_sl(ic),
                  g("h_scr")[ic * 128:(ic + 1) * 128, 0:TB])
    n_hl_q[0] = (E.count("s_hl0"), E.count("s_hl1"))
    for qb in range(nTB):
        n_hl = n_hl_q[qb]
        for op_ in range(nOC // 2):
            ot0, ot1 = 2 * op_, 2 * op_ + 1
            gp_pair = qb * (nOC // 2) + op_
            b0, b1 = 2 * (gp_pair % 2), 2 * (gp_pair % 2) + 1
            if gp_pair >= 2:
                qb_p, op_p = divmod(gp_pair - 2, nOC // 2)
                E.wait("tensor", "v_p", r2_v[qb_p * nOC + 2 * op_p + 1])
            for ic in range(nIC):
                qsel = ic % 2
                eng = "sync" if qsel == 0 else "gpsimd"
                jq = wd_qctr[qsel]
                wd_qctr[qsel] += 1
                ws = 4 * qsel + (jq % 4)
                if jq >= 4:
                    E.wait(eng, "t_p", wd_use_t[(qsel, jq - 4)])
                gid_d = 2 * cfg.G1 + ic * nOC + ot0
                kd, ld = divmod(gid_d, BL)
                E.dma(eng, f"s_wd{ws}",
                      g("wd_t")[ws][:, :].rearrange("p (t c) -> p t c", c=128),
                      g("wc")[kd][ld:ld + 2, :, :].rearrange("t p c -> p t c"))
                E.wait("tensor", f"s_wd{ws}", 16 * (jq // 4 + 1))
                if first_p4:
                    E.wait("tensor", "v_p", E.count("v_p"))
                    E.wait("tensor", "sc_p", E.count("sc_p"))
                    first_p4 = False
                if ic == 0:
                    E.wait("tensor", "s_hl0", n_hl[0])
                    E.wait("tensor", "s_hl1", n_hl[1])
                E.run("tensor", lambda e, ws=ws, ic=ic, b=b0, st=(ic == 0),
                      sp=(ic == nIC - 1): e.matmul(
                          ps[b][:, 0:TB], g("wd_t")[ws][:, 0:128], h_sl(ic),
                          start=st, stop=sp, skip_group_check=True))
                tmm = E.inc("tensor", "t_p",
                            lambda e, ws=ws, ic=ic, b=b1, st=(ic == 0),
                            sp=(ic == nIC - 1): e.matmul(
                                ps[b][:, 0:TB], g("wd_t")[ws][:, 128:256],
                                h_sl(ic),
                                start=st, stop=sp, skip_group_check=True))
                wd_use_t[(qsel, jq)] = tmm
                if op_ == nOC // 2 - 1:
                    h_free_t[ic] = tmm
            tdo_pair = tmm
            for i, (ot, bank) in enumerate(((ot0, b0), (ot1, b1))):
                io = qb * nOC + ot
                do_t[(qb, ot)] = tdo_pair
                xs = ot % 2
                if io >= 2:
                    E.wait("sync", "v_p", r2_v[io - 2])
                E.dma("sync", f"s_x2r{xs}", x2r[xs],
                      g("x2f_scr")[ot * 128:(ot + 1) * 128, qb * TB:(qb + 1) * TB])
                if qb == 0 and ot == 0:
                    E.wait("vector", "s_x2f", n_x2f)
                    E.wait("vector", "sc_p", ln_prev_stat['sc'])
                E.wait("vector", "t_p", tdo_pair)
                E.wait("vector", f"s_x2r{xs}", 16 * (io // 2 + 1))
                r2_v[io] = E.inc("vector", "v_p",
                                 lambda e, ot=ot, bank=bank, xs=xs: e.tensor_tensor(
                                     out=r2[:, ot * TB:(ot + 1) * TB], in0=ps[bank][:, 0:TB],
                                     in1=x2r[xs], op=OP.add))
        if qb + 1 < nTB:
            for icp in range(nICP):
                ic0, icw = 2 * icp, min(2, nIC - 2 * icp)
                eng2 = "sync" if icp % 2 == 0 else "gpsimd"
                E.wait(eng2, "t_p", h_free_t[ic0 + icw - 1])
                for icsub in range(icw):
                    ic = ic0 + icsub
                    E.dma(eng2, f"s_hl{icp % 2}", h_sl(ic),
                          g("h_scr")[ic * 128:(ic + 1) * 128,
                                     (qb + 1) * TB:(qb + 2) * TB])
            n_hl_q[qb + 1] = (E.count("s_hl0"), E.count("s_hl1"))
        rv = {oc: r2_v[qb * nOC + oc] for oc in range(nOC)}
        emit_ln(cfg, E, T, r2, rv, r1b_t, r1sq_t, (6, 7, 4, 5), ln_prev_stat)
        for oc in range(nOC):
            s = 0
            io = qb * nOC + oc
            if io >= 1:
                E.wait("vector", "s_out", 16 * io)
            E.run("vector", lambda e, oc=oc: e.tensor_tensor(
                out=tmp_f, in0=r2[:, oc * TB:(oc + 1) * TB], in1=ps[4][:, 0:TB], op=OP.mult))
            E.drain("vector")
            co = E.inc("vector", "v_p", lambda e, s=s: e.tensor_tensor(
                out=out_t[s], in0=tmp_f, in1=ps[5][:, 0:TB], op=OP.subtract))
            E.wait("gpsimd", "v_p", co)
            n_out = E.dma("gpsimd", "s_out",
                          g("out_fm")[oc * 128:(oc + 1) * 128, qb * TB:(qb + 1) * TB],
                          out_t[s])
    E.wait("gpsimd", "s_out", n_out)


def emit_ln(cfg, E, T, r_slab, r_v, r1b_t, r1sq_t, banks, ln_prev_stat):
    H, TB, nOC = cfg.H, cfg.TB, cfg.nOC
    g = lambda n: T[n]
    ps = T["ps"]
    bsum, bsq, ba, bb = banks
    stat_t = {}
    crs_last = 0
    for oc in range(nOC):
        s = oc % 2
        E.wait("scalar", "v_p", r_v[oc])
        if oc >= 1:
            E.wait("scalar", "t_p", stat_t[oc - 1])
        else:
            E.wait("scalar", "t_p", ln_prev_stat['t'])
        crb = E.inc("scalar", "sc_p", lambda e, oc=oc, s=s: e.copy(
            out=r1b_t(s), in_=r_slab[:, oc * TB:(oc + 1) * TB]))
        crs = E.inc("scalar", "sc_p", lambda e, oc=oc, s=s: e.activation(
            out=r1sq_t(s), in_=r_slab[:, oc * TB:(oc + 1) * TB], func=AF.Square))
        crs_last = crs
        E.wait("tensor", "sc_p", crs)
        E.run("tensor", lambda e, s=s, st=(oc == 0), sp=(oc == nOC - 1): e.matmul(
            ps[bsum][0:1, 0:TB], g("onec_s")[:, :], r1b_t(s),
            start=st, stop=sp, skip_group_check=True))
        stat_t[oc] = E.inc("tensor", "t_p",
                           lambda e, s=s, st=(oc == 0), sp=(oc == nOC - 1): e.matmul(
                               ps[bsq][0:1, 0:TB], g("onec_s")[:, :], r1sq_t(s),
                               start=st, stop=sp, skip_group_check=True))
    ln_prev_stat['t'] = stat_t[nOC - 1]
    ln_prev_stat['sc'] = crs_last
    E.wait("vector", "t_p", stat_t[nOC - 1])
    E.run("vector", lambda e: e.tensor_scalar(out=g("mn_f")[:, :], in0=ps[bsum][0:1, 0:TB],
                                              scalar1=1.0 / H, scalar2=None, op0=OP.mult))
    E.run("vector", lambda e: e.tensor_scalar(out=g("msq_f")[:, :], in0=ps[bsq][0:1, 0:TB],
                                              scalar1=1.0 / H, scalar2=None, op0=OP.mult))
    E.drain("vector")
    E.run("vector", lambda e: e.tensor_tensor(out=g("mn2_f")[:, :], in0=g("mn_f")[:, :],
                                              in1=g("mn_f")[:, :], op=OP.mult))
    E.drain("vector")
    E.run("vector", lambda e: e.tensor_tensor(
        out=g("var_f")[:, :], in0=g("msq_f")[:, :], in1=g("mn2_f")[:, :], op=OP.subtract))
    E.drain("vector")
    cvar = E.inc("vector", "v_p", lambda e: e.tensor_scalar(
        out=g("msq_f")[:, :], in0=g("var_f")[:, :], scalar1=EPS, scalar2=None, op0=OP.add))
    E.wait("scalar", "v_p", cvar)
    E.run("scalar", lambda e: e.activation(
        out=g("sd_f")[:, :], in_=g("msq_f")[:, :], func=AF.Ln))
    E.drain("scalar")
    csd = E.inc("scalar", "sc_p", lambda e: e.activation(
        out=g("rstd_f")[:, :], in_=g("sd_f")[:, :], func=AF.Exp, scale=-0.5))
    E.wait("vector", "sc_p", csd)
    E.run("vector", lambda e: e.tensor_tensor(out=g("bvec_f")[:, :], in0=g("mn_f")[:, :],
                                              in1=g("rstd_f")[:, :], op=OP.mult))
    E.drain("vector")
    E.run("vector", lambda e: e.tensor_copy(g("a_b")[:, :], g("rstd_f")[:, :]))
    E.drain("vector")
    cab = E.inc("vector", "v_p", lambda e: e.tensor_copy(g("b_b")[:, :], g("bvec_f")[:, :]))
    E.wait("tensor", "v_p", cab)
    E.run("tensor", lambda e: e.matmul(ps[ba][:, 0:TB], g("oner_s")[:, :], g("a_b")[:, :],
                                       start=True, stop=True, skip_group_check=True))
    tab = E.inc("tensor", "t_p", lambda e: e.matmul(
        ps[bb][:, 0:TB], g("oner_s")[:, :], g("b_b")[:, :], start=True, stop=True,
        skip_group_check=True))
    E.wait("vector", "t_p", tab)



def make_host_inputs(cfg: Cfg, x, w_qkv, w_out, lut, gate_idx, up_idx, down_idx):
    import ml_dtypes
    BF = ml_dtypes.bfloat16
    lutb = np.broadcast_to(np.asarray(lut, np.float32), (128, LUT_N)).copy()
    kp = np.arange(128)[:, None]
    uu = np.arange((cfg.NSTR - 1) * 128 + cfg.TB)[None, :]
    cm = (uu >= kp + (cfg.NSTR - 1) * 128).astype(BF)
    onesc = np.ones((128, 1), dtype=BF)
    onesr = np.ones((1, 128), dtype=BF)
    srcs = {
        "gidxT": np.ascontiguousarray(np.asarray(gate_idx).T.astype(np.int32)),
        "uidxT": np.ascontiguousarray(np.asarray(up_idx).T.astype(np.int32)),
        "didxT": np.ascontiguousarray(np.asarray(down_idx).T.astype(np.int32)),
    }
    in_maps = []
    for c in range(cfg.ncores):
        myidx = np.zeros((cfg.TPC, 128, 128), np.int32)
        for t in range(cfg.TPC):
            gid = (t // cfg.CSZ) * (cfg.ncores * cfg.CSZ) + c * cfg.CSZ + (t % cfg.CSZ)
            if gid >= cfg.NT:
                continue
            src, rc, ct = w_tile_src(cfg, gid)
            myidx[t] = srcs[src][rc * 128:(rc + 1) * 128, ct * 128:(ct + 1) * 128]
        in_maps.append({
            "x_fm": np.ascontiguousarray(np.asarray(x[c]).T).astype(np.float32, copy=False),
            "wqkv": np.asarray(w_qkv, np.float32),
            "wout": np.asarray(w_out, np.float32),
            "myidx": myidx,
            "lutb": lutb,
            "cmask": cm,
            "onesc": onesc,
            "onesr": onesr,
        })
    return in_maps


def gather_output(cfg, results):
    return np.stack([np.ascontiguousarray(r["out_fm"].T) for r in results], axis=0)


def numpy_reference(cfg: Cfg, x, w_qkv, w_out, lut, gate_idx, up_idx, down_idx):
    def layernorm(v):
        m = v.mean(-1, keepdims=True)
        s = v.var(-1, keepdims=True)
        return (v - m) / np.sqrt(s + EPS)
    def softmax(s):
        m = s.max(-1, keepdims=True)
        e = np.exp(s - m)
        return e / e.sum(-1, keepdims=True)
    x = np.asarray(x, np.float64)
    w_qkv = np.asarray(w_qkv, np.float64)
    w_out = np.asarray(w_out, np.float64)
    lut = np.asarray(lut, np.float64)
    B, S, H = x.shape
    NH, D = cfg.NH, cfg.D
    qkv = x @ w_qkv
    qkv = qkv.reshape(B, S, 3, NH, D)
    q, k, v = qkv[:, :, 0], qkv[:, :, 1], qkv[:, :, 2]
    q = q / np.sqrt(np.float64(D))
    scores = np.einsum("bqhd,bkhd->bhqk", q, k)
    mask = np.arange(S)[:, None] >= np.arange(S)[None, :]
    scores = np.where(mask[None, None], scores, -1e10)
    a = softmax(scores)
    o = np.einsum("bhqk,bkhd->bqhd", a, v).reshape(B, S, H)
    x1 = layernorm(x + o @ w_out)
    gate = x1 @ lut[gate_idx].T
    up = x1 @ lut[up_idx].T
    hh = (gate / (1 + np.exp(-gate))) * up
    x2 = layernorm(x1 + hh @ lut[down_idx].T)
    return x2.astype(np.float32)



def kernel(L0_attn_x, L0_attn_w_qkv, L0_attn_w_out, L0_ffn_lut,
           L0_ffn_gate_idx, L0_ffn_up_idx, L0_ffn_down_idx):
    cfg = FULL
    in_maps = make_host_inputs(cfg, L0_attn_x, L0_attn_w_qkv, L0_attn_w_out,
                               L0_ffn_lut, L0_ffn_gate_idx, L0_ffn_up_idx,
                               L0_ffn_down_idx)
    nc = build_program(cfg)
    res = run_bass_kernel_spmd(nc, in_maps, core_ids=list(range(cfg.ncores)))
    return gather_output(cfg, res.results)


# version 22
# speedup vs baseline: 1.0677x; 1.0250x over previous
import math
import numpy as np
from contextlib import ExitStack
from dataclasses import dataclass

import concourse.bass as bass
import concourse.mybir as mybir
from concourse.bass_utils import run_bass_kernel_spmd

F32 = mybir.dt.float32
BF16 = mybir.dt.bfloat16
I32 = mybir.dt.int32
AF = mybir.ActivationFunctionType
OP = mybir.AluOpType

LUT_N = 16
EPS = 1e-5
MRUN = 8


@dataclass
class Cfg:
    S: int = 2048
    H: int = 2048
    NH: int = 16
    I: int = 5504
    TB: int = 512
    ncores: int = 8

    @property
    def D(self): return self.H // self.NH
    @property
    def F3(self): return 3 * self.H
    @property
    def nHC(self): return self.H // 128
    @property
    def nOC(self): return self.H // 128
    @property
    def nIC(self): return self.I // 128
    @property
    def nTB(self): return self.S // self.TB
    @property
    def nTT(self): return self.S // 128
    @property
    def NSTR(self): return self.TB // 128
    @property
    def DBW(self): return min(512, self.H)
    @property
    def nDB(self): return self.H // self.DBW
    @property
    def G1(self): return self.nHC * self.nIC
    @property
    def NT(self): return 2 * self.G1 + self.nIC * self.nOC
    @property
    def TPC0(self): return (self.NT + self.ncores - 1) // self.ncores
    @property
    def CSZ(self): return self.nIC
    @property
    def KAG(self): return (self.TPC0 + self.CSZ - 1) // self.CSZ
    @property
    def TPC(self): return self.KAG * self.CSZ
    @property
    def qscale(self): return 1.0 / math.sqrt(self.D)


FULL = Cfg()
SMALL = Cfg(S=256, H=256, NH=2, I=384, TB=128, ncores=8)
MED = Cfg(S=1024, H=512, NH=4, I=768, TB=512, ncores=8)

ENGINES = ("sync", "gpsimd", "vector", "scalar", "tensor")


class Emitter:
    def __init__(self, nc, ctx):
        self.nc = nc
        self.ctx = ctx
        self.q = {e: [] for e in ENGINES}
        self.sems = {}

    def _sem(self, name):
        if name not in self.sems:
            h = self.ctx.enter_context(self.nc.semaphore(name))
            self.sems[name] = [h, 0]
        return self.sems[name]

    def count(self, name):
        return self.sems[name][1] if name in self.sems else 0

    def run(self, eng, fn):
        self.q[eng].append(fn)

    def inc(self, eng, sem_name, fn, k=1):
        s = self._sem(sem_name)
        s[1] += k
        self.q[eng].append(lambda e, fn=fn, h=s[0], k=k: fn(e).then_inc(h, k))
        return s[1]

    def dma(self, eng, sem_name, out, in_):
        return self.inc(eng, sem_name, lambda e, o=out, i=in_: e.dma_start(out=o, in_=i), k=16)

    def wait(self, eng, sem_name, val):
        if val is None or val <= 0:
            return
        s = self._sem(sem_name)
        assert val <= s[1], f"wait {sem_name}>={val} but only {s[1]} emitted"
        self.q[eng].append(lambda e, h=s[0], v=val: e.wait_ge(h, v))

    def drain(self, eng):
        self.q[eng].append(lambda e: e.drain())

    def finalize(self, block):
        for eng in ENGINES:
            dec = getattr(block, eng)
            q = self.q[eng]
            def make(qq):
                def body(e):
                    for fn in qq:
                        fn(e)
                return body
            dec(make(q))


def w_tile_src(cfg, gid):
    if gid < cfg.G1:
        return "gidxT", gid // cfg.nIC, gid % cfg.nIC
    if gid < 2 * cfg.G1:
        return "uidxT", (gid - cfg.G1) // cfg.nIC, (gid - cfg.G1) % cfg.nIC
    g2 = gid - 2 * cfg.G1
    return "didxT", g2 // cfg.nOC, g2 % cfg.nOC


def build_program(cfg: Cfg):
    nc = bass.Bass("TRN2", target_bir_lowering=False)
    ctx = ExitStack()
    S, H, I, TB = cfg.S, cfg.H, cfg.I, cfg.TB
    nHC, nOC, nIC, nTT, NSTR = cfg.nHC, cfg.nOC, cfg.nIC, cfg.nTT, cfg.NSTR
    DBW = cfg.DBW

    x_fm = nc.dram_tensor("x_fm", [H, S], F32, kind="ExternalInput")
    wqkv = nc.dram_tensor("wqkv", [H, cfg.F3], F32, kind="ExternalInput")
    wout = nc.dram_tensor("wout", [H, H], F32, kind="ExternalInput")
    myidx = nc.dram_tensor("myidx", [cfg.TPC, 128, 128], I32, kind="ExternalInput")
    lutb = nc.dram_tensor("lutb", [128, LUT_N], F32, kind="ExternalInput")
    cmask = nc.dram_tensor("cmask", [128, (NSTR - 1) * 128 + TB], BF16, kind="ExternalInput")
    onesc = nc.dram_tensor("onesc", [128, 1], BF16, kind="ExternalInput")
    onesr = nc.dram_tensor("onesr", [1, 128], BF16, kind="ExternalInput")
    out_fm = nc.dram_tensor("out_fm", [H, S], F32, kind="ExternalOutput")

    q_scr = nc.dram_tensor("q_scr", [H, S], BF16)
    k_scr = nc.dram_tensor("k_scr", [H, S], BF16)
    v_scr = nc.dram_tensor("v_scr", [S, H], BF16)
    wob_scr = nc.dram_tensor("wob_scr", [H, H], BF16)
    x2f_scr = nc.dram_tensor("x2f_scr", [H, S], F32)
    x2b_scr = nc.dram_tensor("x2b_scr", [H, S], BF16)
    h_scr = nc.dram_tensor("h_scr", [I, S], BF16)
    wcat_my = nc.dram_tensor("wcat_my", [cfg.TPC, 128, 128], BF16)
    if cfg.KAG > 1:
        assert nHC % 4 == 0 and nOC % 2 == 0
    wc = [nc.dram_tensor(f"wcat{k}", [cfg.ncores * cfg.CSZ, 128, 128], BF16,
                         addr_space="Shared") for k in range(cfg.KAG)]

    sb = lambda name, shape, dt: ctx.enter_context(nc.sbuf_tensor(name, shape, dt))

    with ctx:
        xact = sb("xact", [128, nHC * S], BF16)
        p2cols = nTT * TB + 4 * S + 9 * TB
        bigcols = max(p2cols, nIC * TB, 21 * MRUN * 128)
        big = sb("big", [128, bigcols], BF16)
        r1b_sl = sb("r1b_sl", [128, TB], BF16)
        r1sq_sl = sb("r1sq_sl", [128, TB], BF16)
        
        slab16 = sb("slab16", [128, nHC * max(DBW, TB)], BF16)
        arf_cols = max(8 * 128 + 2 * DBW + 2 * S, nOC * TB + 4 * TB)
        arf = sb("arf", [128, arf_cols], F32)

        lut_s = sb("lut_s", [128, LUT_N], F32)
        cm_s = sb("cm_s", [128, (NSTR - 1) * 128 + TB], BF16)
        onec_s = sb("onec_s", [128, 1], BF16)
        oner_s = sb("oner_s", [1, 128], BF16)
        wqk_b = [sb(f"wqk_b{s}", [128, 128], BF16) for s in range(8)]
        qk_e = [sb(f"qk_e{s}", [128, TB], BF16) for s in range(4)]
        ve_t = [sb(f"ve_t{s}", [128, DBW], BF16) for s in range(1)]
        wo_t = [sb(f"wo_t{s}", [128, 2 * 128], BF16) for s in range(6)]
        wg_t = [sb(f"wg_t{s}", [128, 4 * 128], BF16) for s in range(2)]
        wu_t = [sb(f"wu_t{s}", [128, 4 * 128], BF16) for s in range(2)]
        wd_t = [sb(f"wd_t{s}", [128, 2 * 128], BF16) for s in range(8)]
        mn_f = sb("mn_f", [1, TB], F32)
        msq_f = sb("msq_f", [1, TB], F32)
        mn2_f = sb("mn2_f", [1, TB], F32)
        var_f = sb("var_f", [1, TB], F32)
        sd_f = sb("sd_f", [1, TB], F32)
        rstd_f = sb("rstd_f", [1, TB], F32)
        den_f = mn_f
        rden_f = msq_f
        bvec_f = sd_f
        rden_b = sb("rden_b", [1, TB], BF16)
        a_b = sb("a_b", [1, TB], BF16)
        b_b = sb("b_b", [1, TB], BF16)
        

        BKW = 512
        psA = [ctx.enter_context(nc.psum_tensor(f"psA{i}", [128, 2 * BKW], F32)) for i in range(2)]
        psS = [ctx.enter_context(nc.psum_tensor(f"psS{i}", [128, BKW], F32)) for i in range(4)]
        ps = [psA[0][:, 0:BKW], psA[0][:, BKW:2 * BKW],
              psS[0][:, :], psS[1][:, :], psS[2][:, :], psS[3][:, :],
              psA[1][:, 0:BKW], psA[1][:, BKW:2 * BKW]]

        T = dict(
            x_fm=x_fm, wqkv=wqkv, wout=wout, myidx=myidx, lutb=lutb, cmask=cmask,
            onesc=onesc, onesr=onesr, out_fm=out_fm, q_scr=q_scr, k_scr=k_scr,
            v_scr=v_scr, wob_scr=wob_scr, x2f_scr=x2f_scr, x2b_scr=x2b_scr,
            h_scr=h_scr, wcat_my=wcat_my, wc=wc,
            xact=xact, big=big, slab16=slab16, arf=arf,
            r1b_sl=r1b_sl, r1sq_sl=r1sq_sl,

            lut_s=lut_s, cm_s=cm_s,
            onec_s=onec_s, oner_s=oner_s, wqk_b=wqk_b, qk_e=qk_e,
            ve_t=ve_t, wo_t=wo_t, wg_t=wg_t, wu_t=wu_t, wd_t=wd_t,
            den_f=den_f, rden_f=rden_f, rden_b=rden_b,
            mn_f=mn_f, msq_f=msq_f, mn2_f=mn2_f, var_f=var_f, sd_f=sd_f,
            rstd_f=rstd_f, bvec_f=bvec_f, a_b=a_b, b_b=b_b, ps=ps, psA=psA,
        )
        with nc.Block() as block:
            E = Emitter(nc, ctx)
            build_phases(cfg, E, T)
            E.finalize(block)
    return nc


def build_phases(cfg, E, T):
    S, H, I, TB = cfg.S, cfg.H, cfg.I, cfg.TB
    nHC, nOC, nIC, nTB, nTT, NSTR = (cfg.nHC, cfg.nOC, cfg.nIC, cfg.nTB,
                                     cfg.nTT, cfg.NSTR)
    NH, DBW, nDB = cfg.NH, cfg.DBW, cfg.nDB
    PSW = 512
    ps = T["ps"]
    big, xact, slab16, arf = T["big"], T["xact"], T["slab16"], T["arf"]
    g = lambda n: T[n]

    et_off = 0
    ksl_off = nTT * TB
    vsl_off = ksl_off + 2 * S
    qsl_off = vsl_off + 2 * S
    et = lambda kt: big[:, et_off + kt * TB: et_off + (kt + 1) * TB]
    k_sl = lambda s, w: big[:, ksl_off + s * S: ksl_off + s * S + w]
    v_sl = lambda s, w: big[:, vsl_off + s * S: vsl_off + s * S + w]
    q_sl = lambda s: big[:, qsl_off + s * TB: qsl_off + (s + 1) * TB]
    r1b_t = lambda s: T["r1b_sl"][:, 0:TB]
    r1sq_t = lambda s: T["r1sq_sl"][:, 0:TB]
    tail0 = qsl_off + 2 * TB
    et_t = lambda s: big[:, tail0 + s * TB: tail0 + (s + 1) * TB]
    h_sl = lambda ic: big[:, ic * TB:(ic + 1) * TB]
    bc_s = big[:, tail0 + 2 * TB: tail0 + 3 * TB]
    sil_t = [big[:, tail0 + (3 + s) * TB: tail0 + (4 + s) * TB] for s in range(2)]
    h_t = [big[:, tail0 + (5 + s) * TB: tail0 + (6 + s) * TB] for s in range(2)]
    attn = lambda dc: slab16[:, dc * TB:(dc + 1) * TB]
    wv_b = lambda hc: slab16[:, hc * DBW:(hc + 1) * DBW]
    wqk_f = [arf[:, s * 128:(s + 1) * 128] for s in range(8)]
    wv_f = [arf[:, 8 * 128 + s * DBW: 8 * 128 + (s + 1) * DBW] for s in range(2)]
    xs_f = [arf[:, 8 * 128 + 2 * DBW + s * S: 8 * 128 + 2 * DBW + (s + 1) * S] for s in range(2)]
    r1 = arf[:, 0:nOC * TB]
    xres = [arf[:, nOC * TB + s * TB: nOC * TB + (s + 1) * TB] for s in range(2)]
    tmp_f = arf[:, nOC * TB + 2 * TB: nOC * TB + 3 * TB]
    x2f_t = [arf[:, nOC * TB + 3 * TB: nOC * TB + 4 * TB]]
    r2, x2r, out_t = r1, xres, x2f_t

    E.dma("sync", "s_lut", g("lut_s")[:, :], g("lutb")[:, :])
    E.dma("sync", "s_cm", g("cm_s")[:, :], g("cmask")[:, :])
    E.dma("sync", "s_onc", g("onec_s")[:, :], g("onesc")[:, :])
    E.dma("sync", "s_onr", g("oner_s")[:, :], g("onesr")[:, :])

    NG = min(4, nHC)
    runs = []
    t = 0
    while t < cfg.TPC:
        w = min(MRUN, cfg.TPC - t)
        runs.append((t, w))
        t += w
    MP = MRUN * 128
    m_part = [big[:, j * MP:(j + 1) * MP] for j in range(16)]
    m_idxb = [big[:, (16 + s) * MP:(17 + s) * MP] for s in range(3)]
    m_out = [big[:, (19 + s) * MP:(20 + s) * MP] for s in range(2)]
    E.wait("vector", "s_lut", 16)
    acc_v = {}

    def mat_dma(ri):
        t0, w = runs[ri]
        slot = ri % 3
        if ri >= 3:
            E.wait("gpsimd", "v_p", acc_v[ri - 3])
        E.dma("gpsimd", f"s_midx{slot}",
              m_idxb[slot][:, 0:w * 128].rearrange("p (t c) -> p t c", c=128),
              g("myidx")[t0:t0 + w, :, :].rearrange("t p c -> p t c"))

    for r0 in range(min(3, len(runs))):
        mat_dma(r0)
    for ri, (t0, w) in enumerate(runs):
        slot = ri % 3
        E.wait("vector", f"s_midx{slot}", 16 * (ri // 3 + 1))
        for j in range(LUT_N):
            E.run("vector", lambda e, w=w, j=j, slot=slot: e.tensor_scalar(
                out=m_part[j][:, 0:w * 128], in0=m_idxb[slot][:, 0:w * 128],
                scalar1=float(j), scalar2=g("lut_s")[:, j:j + 1],
                op0=OP.is_equal, op1=OP.mult))
        E.drain("vector")
        step = 1
        while step < 16:
            for j0 in range(0, 16, 2 * step):
                last = (2 * step >= 16)
                if last:
                    if ri >= 2:
                        E.wait("vector", f"s_wmy{ri % 2}", 16 * ((ri - 2) // 2 + 1))
                    fn = lambda e, w=w, ri=ri: e.tensor_tensor(
                        out=m_out[ri % 2][:, 0:w * 128], in0=m_part[0][:, 0:w * 128],
                        in1=m_part[8][:, 0:w * 128], op=OP.add)
                    acc_v[ri] = E.inc("vector", "v_p", fn)
                else:
                    fn = lambda e, j0=j0, s=step, w=w: e.tensor_tensor(
                        out=m_part[j0][:, 0:w * 128], in0=m_part[j0][:, 0:w * 128],
                        in1=m_part[j0 + s][:, 0:w * 128], op=OP.add)
                    E.run("vector", fn)
            E.drain("vector")
            step *= 2
        if ri + 3 < len(runs):
            mat_dma(ri + 3)
        E.wait("gpsimd", "v_p", acc_v[ri])
        E.dma("gpsimd", f"s_wmy{ri % 2}",
              g("wcat_my")[t0:t0 + w, :, :].rearrange("t p c -> p t c"),
              m_out[ri % 2][:, 0:w * 128].rearrange("p (t c) -> p t c", c=128))
        for k in range(cfg.KAG):
            if ((k + 1) * cfg.CSZ - 1) // MRUN == ri:
                E.wait("gpsimd", "s_wmy0", 16 * (ri // 2 + 1))
                if ri >= 1:
                    E.wait("gpsimd", "s_wmy1", 16 * ((ri - 1) // 2 + 1))
                ag_g = E.inc("gpsimd", "g_p", lambda e, k=k: e.collective_compute(
                    "AllGather", OP.bypass, replica_groups=[list(range(cfg.ncores))],
                    ins=[g("wcat_my")[k * cfg.CSZ:(k + 1) * cfg.CSZ, :, :].opt()],
                    outs=[g("wc")[k].ap().opt()]))
    mat_done_v = acc_v[len(runs) - 1]
    n_wmy = [E.count("s_wmy0"), E.count("s_wmy1")]
    for i4 in range(NG):
        E.dma("gpsimd", "s_wob", g("wob_scr")[i4 * (H // NG):(i4 + 1) * (H // NG), :],
              g("wout")[i4 * (H // NG):(i4 + 1) * (H // NG), :])
    n_wob = E.count("s_wob")

    qkcast_s, lastmm_t, stopmm_t, qkevac_s = {}, {}, {}, {}
    xcast_s = {}
    for ft in range(2 * nHC):
        isq = ft < nHC
        for hc in range(nHC):
            it = ft * nHC + hc
            slot = it % 8
            if ft == 0:
                xsl = hc % 2
                if hc >= 2:
                    E.wait("sync", "sc_p", xcast_s[hc - 2])
                E.dma("sync", f"s_xs{xsl}", xs_f[xsl],
                      g("x_fm")[hc * 128:(hc + 1) * 128, :])
                E.wait("scalar", f"s_xs{xsl}", 16 * (hc // 2 + 1))
                xcast_s[hc] = E.inc("scalar", "sc_p",
                                    lambda e, s=xsl, k=hc: e.copy(
                                        out=xact[:, k * S:(k + 1) * S], in_=xs_f[s]))
            if it >= 8:
                E.wait("sync", "sc_p", qkcast_s[it - 8])
            E.dma("sync", f"s_wqk{slot}", wqk_f[slot],
                  g("wqkv")[hc * 128:(hc + 1) * 128, ft * 128:(ft + 1) * 128])
            if it >= 8:
                E.wait("scalar", "t_p", lastmm_t[it - 8])
            E.wait("scalar", f"s_wqk{slot}", 16 * (it // 8 + 1))
            qkcast_s[it] = E.inc("scalar", "sc_p",
                                 lambda e, s=slot: e.copy(out=g("wqk_b")[s][:, :],
                                                          in_=wqk_f[s]))
            E.wait("tensor", "sc_p", qkcast_s[it])
            if ft == 0:
                E.wait("tensor", "sc_p", xcast_s[hc])
            if hc == 0 and ft >= 2:
                E.wait("tensor", "sc_p", qkevac_s[(ft - 2, nTB - 1)])
            bk0 = (ft % 2) * nTB
            for tb in range(nTB):
                fn = lambda e, s=slot, hc=hc, tb=tb, bk0=bk0, st=(hc == 0), sp=(hc == nHC - 1): e.matmul(
                    ps[bk0 + tb][:, 0:TB], g("wqk_b")[s][:, :],
                    xact[:, hc * S + tb * TB: hc * S + (tb + 1) * TB],
                    start=st, stop=sp, skip_group_check=True)
                if hc == nHC - 1:
                    stopmm_t[(ft, tb)] = E.inc("tensor", "t_p", fn)
                    lastmm_t[it] = stopmm_t[(ft, tb)]
                elif tb == nTB - 1:
                    lastmm_t[it] = E.inc("tensor", "t_p", fn)
                else:
                    E.run("tensor", fn)
        for tb in range(nTB):
            es = (ft * nTB + tb) % 4
            idx = ft * nTB + tb
            E.wait("scalar", "t_p", stopmm_t[(ft, tb)])
            if idx >= 4:
                E.wait("scalar", f"s_qkst{es}", 16 * (idx // 4))
            sc = cfg.qscale if isq else 1.0
            qkevac_s[(ft, tb)] = E.inc("scalar", "sc_p",
                                       lambda e, tb=tb, es=es, sc=sc, bk0=(ft % 2) * nTB:
                                       e.activation(
                                           out=g("qk_e")[es][:, :], in_=ps[bk0 + tb][:, 0:TB],
                                           func=AF.Copy, scale=sc))
            E.wait("scalar", "sc_p", qkevac_s[(ft, tb)])
            dst = g("q_scr") if isq else g("k_scr")
            r0 = (ft if isq else ft - nHC) * 128
            E.dma("scalar", f"s_qkst{es}", dst[r0:r0 + 128, tb * TB:(tb + 1) * TB],
                  g("qk_e")[es][:, :])
    n_qkst = [E.count(f"s_qkst{e_}") for e_ in range(4)]


    wvc_s, vstop_t, vevac_s = {}, {}, {}
    for db in range(nDB):
        for hc in range(nHC):
            it = db * nHC + hc
            slot = hc % 2
            if it >= 2:
                E.wait("sync", "sc_p", wvc_s[it - 2])
            E.dma("sync", f"s_wv{slot}", wv_f[slot],
                  g("wqkv")[hc * 128:(hc + 1) * 128, 2 * H + db * DBW: 2 * H + (db + 1) * DBW])
            if db >= 1:
                E.wait("scalar", "t_p", vstop_t[(db - 1, nTT - 1)])
            E.wait("scalar", f"s_wv{slot}", 16 * (it // 2 + 1))
            wvc_s[it] = E.inc("scalar", "sc_p",
                              lambda e, hc=hc, s=slot: e.copy(out=wv_b(hc), in_=wv_f[s]))
        for tt in range(nTT):
            bank = 4 + (tt % 2)
            it = db * nTT + tt
            if it == 0:
                E.wait("tensor", "sc_p", qkevac_s[(2 * nHC - 1, nTB - 1)])
            if it >= 2:
                E.wait("tensor", "sc_p", vevac_s[it - 2])
            E.wait("tensor", "sc_p", wvc_s[db * nHC + nHC - 1])
            for hc in range(nHC):
                fn = lambda e, hc=hc, tt=tt, bank=bank, st=(hc == 0), sp=(hc == nHC - 1): e.matmul(
                    ps[bank][:, 0:DBW],
                    xact[:, hc * S + tt * 128: hc * S + (tt + 1) * 128],
                    wv_b(hc), start=st, stop=sp, skip_group_check=True)
                if hc == nHC - 1:
                    vstop_t[(db, tt)] = E.inc("tensor", "t_p", fn)
                else:
                    E.run("tensor", fn)
            E.wait("scalar", "t_p", vstop_t[(db, tt)])
            if it >= 1:
                E.wait("scalar", "s_vst", 16 * it)
            es = 0
            vevac_s[it] = E.inc("scalar", "sc_p",
                                lambda e, bank=bank, es=es: e.copy(out=g("ve_t")[es][:, :],
                                                                   in_=ps[bank][:, 0:DBW]))
            E.wait("scalar", "sc_p", vevac_s[it])
            E.dma("scalar", "s_vst",
                  g("v_scr")[tt * 128:(tt + 1) * 128, db * DBW:(db + 1) * DBW],
                  g("ve_t")[es][:, :])
    n_vst = E.count("s_vst")
    p1_sc = E.count("sc_p")
    tp_p1v = E.count("t_p")

    E.wait("vector", "s_cm", 16)
    E.wait("scalar", "v_p", mat_done_v)
    E.wait("scalar", "s_wmy0", n_wmy[0])
    E.wait("scalar", "s_wmy1", n_wmy[1])
    E.wait("sync", "v_p", mat_done_v)
    E.wait("sync", "s_wmy0", n_wmy[0])
    E.wait("sync", "s_wmy1", n_wmy[1])
    E.wait("gpsimd", "s_wob", n_wob)
    E.wait("sync", "s_wob", n_wob)
    n_strad = 0
    msk_v, avden_t, attn_v, denev_v, bcs_v = {}, {}, {}, {}, {}
    wo_use_t, r1_v, x2f_v = {}, {}, {}
    wo_qctr = {0: 0, 1: 0}
    vsl_fill = {}
    prev_exp_last = 0
    n_x2f = 0
    ln_prev_stat = {'t': 0, 'sc': 0}
    wo_last_t = 0
    x2f_cnt = {}
    avkt_t = {}
    msk_last = {}
    for qb in range(nTB):
        nkt = (qb + 1) * NSTR
        nkt_prev_qb = qb * NSTR

        def den_tail(iq2):
            head2 = iq2 % NH
            bav = 2 + (head2 % 2)
            bden = 4 + (head2 % 2)
            E.wait("scalar", "t_p", avden_t[iq2])
            E.run("scalar", lambda e, bden=bden: e.activation(
                out=g("rden_f")[:, :], in_=ps[bden][0:1, 0:TB], func=AF.Ln))
            E.drain("scalar")
            denev_v[iq2] = E.inc("scalar", "sc_p", lambda e: e.activation(
                out=g("rden_b")[:, :], in_=g("rden_f")[:, :], func=AF.Exp, scale=-1.0))
            E.wait("tensor", "sc_p", denev_v[iq2])
            E.wait("tensor", "s_onr", 16)
            if iq2 > 0:
                E.wait("tensor", "v_p", bcs_v[iq2 - 1])
            bcb = 2 + ((head2 + 1) % 2)
            if iq2 >= 1:
                E.wait("tensor", "v_p", attn_v[iq2 - 1])
            tbc = E.inc("tensor", "t_p", lambda e, bcb=bcb: e.matmul(
                ps[bcb][:, 0:TB], g("oner_s")[:, :], g("rden_b")[:, :],
                start=True, stop=True, skip_group_check=True))
            E.wait("vector", "t_p", tbc)
            bcs_v[iq2] = E.inc("vector", "v_p",
                               lambda e, bcb=bcb: e.tensor_copy(bc_s, ps[bcb][:, 0:TB]))
            E.drain("vector")
            if head2 == 0 and iq2 >= NH:
                E.wait("vector", "t_p", wo_last_t)
            attn_v[iq2] = E.inc("vector", "v_p",
                                lambda e, head2=head2, bav=bav: e.tensor_tensor(
                                    out=attn(head2), in0=ps[bav][:, 0:TB], in1=bc_s,
                                    op=OP.mult))
            E.drain("vector")

        def emit_av_kt(iq2, kt, nkt2, last):
            head2 = iq2 % NH
            bav = 2 + (head2 % 2)
            bden = 4 + (head2 % 2)
            sl2 = head2 % 2
            if kt == 0:
                if head2 <= 1 and iq2 >= NH and (iq2 // NH) == qb:
                    E.wait("tensor", "v_p", x2f_v[(qb - 1) * nOC + nOC - 1])
                if iq2 >= 2:
                    E.wait("tensor", "v_p", attn_v[iq2 - 2])
                    E.wait("tensor", "sc_p", denev_v[iq2 - 2])
                if iq2 >= 1:
                    E.wait("tensor", "v_p", bcs_v[iq2 - 1])
                if iq2 in msk_last:
                    E.wait("tensor", "v_p", msk_last[iq2])
                E.wait("tensor", f"s_vsl{sl2}", vsl_fill[iq2])
                E.wait("tensor", "s_onc", 16)
            E.run("tensor", lambda e, sl2=sl2, kt=kt, nkt2=nkt2, bav=bav,
                  st=(kt == 0), sp=last: e.matmul(
                      ps[bav][:, 0:TB], v_sl(sl2, nkt2 * 128)[:, kt * 128:(kt + 1) * 128],
                      et(kt), start=st, stop=sp, skip_group_check=True))
            avkt_t[(iq2, kt)] = E.inc("tensor", "t_p",
                                      lambda e, kt=kt, bden=bden, st=(kt == 0), sp=last: e.matmul(
                                          ps[bden][0:1, 0:TB], g("onec_s")[:, :], et(kt),
                                          start=st, stop=sp, skip_group_check=True))
            if last:
                avden_t[iq2] = avkt_t[(iq2, kt)]

        for head in range(NH):
            sl = head % 2
            iq = qb * NH + head
            if iq >= 2:
                E.wait("sync", "t_p", avden_t[iq - 2])
            idxq = head * nTB + qb
            idxk = (nHC + head) * nTB + qb
            E.wait("sync", f"s_qkst{idxq % 4}", 16 * (idxq // 4 + 1))
            E.wait("sync", f"s_qkst{idxk % 4}", 16 * (idxk // 4 + 1))
            idxv = (nDB - 1) * nTT + (NSTR * (qb + 1) - 1)
            E.wait("sync", "s_vst", 16 * (idxv + 1))
            E.dma("sync", f"s_qsl{sl}", q_sl(sl),
                  g("q_scr")[head * 128:(head + 1) * 128, qb * TB:(qb + 1) * TB])
            E.dma("sync", f"s_ksl{sl}", k_sl(sl, nkt * 128),
                  g("k_scr")[head * 128:(head + 1) * 128, 0:nkt * 128])
            E.dma("sync", f"s_vsl{sl}",
                  v_sl(sl, nkt * 128).rearrange("p (t c) -> p t c", c=128),
                  g("v_scr")[0:nkt * 128, head * 128:(head + 1) * 128]
                  .rearrange("(t p) c -> p t c", p=128))
            nfill = 16 * (iq // 2 + 1)
            vsl_fill[iq] = nfill
            E.wait("tensor", f"s_qsl{sl}", nfill)
            E.wait("tensor", f"s_ksl{sl}", nfill)
            exp_slot = {}
            PAIR_OK = (TB == 512)
            SBK4 = (0, 1, 6, 7)
            ns_end = nkt - NSTR
            last_exp = 0
            kt = 0
            while kt < nkt:
                pair = (kt + 1 < ns_end and kt % 2 == 0 and PAIR_OK)
                subkts = (kt, kt + 1) if pair else (kt,)
                for k2 in subkts:
                    c = k2
                    bank = SBK4[c % 4]
                    if c >= 4:
                        E.wait("tensor", "sc_p", exp_slot[c - 4])
                    elif iq > 0:
                        E.wait("tensor", "sc_p", prev_exp_last)
                    else:
                        E.wait("tensor", "sc_p", p1_sc)
                    if c == 0 and head == 0 and qb > 0:
                        E.wait("tensor", "v_p", r1_v[(qb - 1) * nOC + nOC - 1])
                    tsc = E.inc("tensor", "t_p", lambda e, sl=sl, k2=k2, bank=bank, nkt=nkt: e.matmul(
                        ps[bank][:, 0:TB], k_sl(sl, nkt * 128)[:, k2 * 128:(k2 + 1) * 128],
                        q_sl(sl), start=True, stop=True, skip_group_check=True))
                    if head > 0:
                        emit_av_kt(iq - 1, k2, nkt, last=(k2 == nkt - 1))
                kt1 = subkts[-1]
                E.wait("scalar", "t_p", tsc)
                if head > 0:
                    E.wait("scalar", "t_p", avkt_t[(iq - 1, kt1)])
                elif qb > 0:
                    pl = qb * NH - 1
                    kk = min(kt1, nkt_prev_qb - 1)
                    if kk >= 0:
                        E.wait("scalar", "t_p", avkt_t[(pl, kk)])
                if pair:
                    pb = (kt // 2) % 2
                    v = E.inc("scalar", "sc_p",
                              lambda e, pb=pb, kt=kt: e.activation(
                                  out=big[:, et_off + kt * TB: et_off + (kt + 2) * TB],
                                  in_=T["psA"][pb][:, 0:2 * TB], func=AF.Exp))
                    exp_slot[kt] = exp_slot[kt + 1] = v
                    last_exp = v
                elif kt1 >= nkt - NSTR:
                    es = n_strad % 2
                    n_strad += 1
                    if n_strad > 2:
                        E.wait("scalar", "v_p", msk_v[n_strad - 2])
                    v = E.inc("scalar", "sc_p",
                              lambda e, bank=SBK4[kt1 % 4], es=es: e.activation(
                                  out=et_t(es), in_=ps[bank][:, 0:TB], func=AF.Exp))
                    exp_slot[kt1] = v
                    last_exp = v
                    E.wait("vector", "sc_p", v)
                    j = kt1 - (nkt - NSTR)
                    msk_v[n_strad] = E.inc("vector", "v_p",
                                           lambda e, kt1=kt1, es=es, j=j: e.tensor_tensor(
                                               out=et(kt1), in0=et_t(es),
                                               in1=g("cm_s")[:, (NSTR - 1 - j) * 128:
                                                             (NSTR - 1 - j) * 128 + TB],
                                               op=OP.mult))
                    msk_last[iq] = msk_v[n_strad]
                else:
                    v = E.inc("scalar", "sc_p",
                              lambda e, bank=SBK4[kt1 % 4], kt1=kt1: e.activation(
                                  out=et(kt1), in_=ps[bank][:, 0:TB], func=AF.Exp))
                    exp_slot[kt1] = v
                    last_exp = v
                kt += 2 if pair else 1
            prev_exp_last = last_exp
            if head > 0:
                den_tail(iq - 1)
        liq = qb * NH + NH - 1
        E.wait("tensor", "sc_p", prev_exp_last)
        for kt in range(nkt):
            emit_av_kt(liq, kt, nkt, last=(kt == nkt - 1))
        den_tail(liq)
        n_attn_qb = attn_v[qb * NH + NH - 1]
        first_wo = True
        for ot in range(nOC):
            bank = ot % 2
            io = qb * nOC + ot
            if io >= 2:
                E.wait("tensor", "v_p", r1_v[io - 2])
            for dc2 in range(nHC // 2):
                dc = 2 * dc2
                qsel = dc2 % 2
                eng = "sync" if qsel == 0 else "gpsimd"
                jq = wo_qctr[qsel]
                wo_qctr[qsel] += 1
                ws = 3 * qsel + (jq % 3)
                if jq >= 3:
                    E.wait(eng, "t_p", wo_use_t[(qsel, jq - 3)])
                E.dma(eng, f"s_wot{ws}",
                      g("wo_t")[ws][:, :].rearrange("p (t c) -> p t c", c=128),
                      g("wob_scr")[dc * 128:(dc + 2) * 128,
                                   ot * 128:(ot + 1) * 128].rearrange("(t p) c -> p t c", p=128))
                E.wait("tensor", f"s_wot{ws}", 16 * (jq // 3 + 1))
                E.wait("tensor", "v_p", n_attn_qb)
                if first_wo:
                    E.wait("tensor", "sc_p", prev_exp_last)
                    first_wo = False
                E.run("tensor", lambda e, ws=ws, dc=dc, bank=bank, st=(dc == 0): e.matmul(
                    ps[bank][:, 0:TB], g("wo_t")[ws][:, 0:128], attn(dc),
                    start=st, stop=False, skip_group_check=True))
                wo_use_t[(qsel, jq)] = E.inc("tensor", "t_p",
                                             lambda e, ws=ws, dc=dc, bank=bank,
                                             sp=(dc == nHC - 2): e.matmul(
                                                 ps[bank][:, 0:TB], g("wo_t")[ws][:, 128:256],
                                                 attn(dc + 1),
                                                 start=False, stop=sp, skip_group_check=True))
            wo_last_t = wo_use_t[(qsel, jq)]
            xs = ot % 2
            if io >= 2:
                E.wait("sync", "v_p", r1_v[io - 2])
            if qb == 0 and ot == 0:
                E.wait("sync", "sc_p", p1_sc)
            E.dma("sync", f"s_xres{xs}", xres[xs],
                  g("x_fm")[ot * 128:(ot + 1) * 128, qb * TB:(qb + 1) * TB])
            if qb == 0 and ot == 0:
                E.wait("vector", "sc_p", p1_sc)
            if qb > 0 and ot == 0:
                E.wait("vector", "sc_p", ln_prev_stat['sc'])
            E.wait("vector", "t_p", wo_last_t)
            E.wait("vector", f"s_xres{xs}", 16 * (io // 2 + 1))
            r1_v[io] = E.inc("vector", "v_p",
                             lambda e, ot=ot, bank=bank, xs=xs: e.tensor_tensor(
                                 out=r1[:, ot * TB:(ot + 1) * TB], in0=ps[bank][:, 0:TB],
                                 in1=xres[xs], op=OP.add))
        rv = {oc: r1_v[qb * nOC + oc] for oc in range(nOC)}
        emit_ln(cfg, E, T, r1, rv, r1b_t, r1sq_t, (2, 3, 4, 5), ln_prev_stat)
        for oc in range(nOC):
            s = 0
            io = qb * nOC + oc
            if io >= 1:
                E.wait("vector", "s_x2f", 16 * io)
            E.run("vector", lambda e, oc=oc: e.tensor_tensor(
                out=tmp_f, in0=r1[:, oc * TB:(oc + 1) * TB], in1=ps[4][:, 0:TB], op=OP.mult))
            E.drain("vector")
            x2f_v[io] = E.inc("vector", "v_p", lambda e, s=s: e.tensor_tensor(
                out=x2f_t[s], in0=tmp_f, in1=ps[5][:, 0:TB], op=OP.subtract))
            E.wait("gpsimd", "v_p", x2f_v[io])
            n_x2f = E.dma("gpsimd", "s_x2f",
                          g("x2f_scr")[oc * 128:(oc + 1) * 128, qb * TB:(qb + 1) * TB],
                          x2f_t[s])
            x2f_cnt[(qb, oc)] = n_x2f

    E.wait("gpsimd", "t_p", tp_p1v)
    for hc in range(nHC):
        E.wait("gpsimd", "s_x2f", x2f_cnt[(nTB - 1, hc)])
        E.dma("gpsimd", f"s_x2g{hc * NG // nHC}", xact[:, hc * S:(hc + 1) * S],
              g("x2f_scr")[hc * 128:(hc + 1) * 128, :])
    n_x2g = [E.count(f"s_x2g{i}") for i in range(NG)]
    E.wait("gpsimd", "g_p", ag_g)
    E.wait("sync", "g_p", ag_g)
    sil_sc, h_v, gu_t = {}, {}, {}
    h_fill = {0: 0, 1: 0}
    QW = min(4, nHC)
    nHQ = (nHC + QW - 1) // QW
    wcv = [g("wc")[k][:, :, :].rearrange("(h a) p c -> h a p c", a=nIC)
           for k in range(cfg.KAG)]
    BL = cfg.ncores * nIC
    wgu_fill = {0: 0, 1: 0}
    for ic in range(nIC):
        for hc in range(nHC):
            iw = ic * nHC + hc
            hq, hsub = divmod(hc, QW)
            jq = ic * nHQ + hq
            sg = jq % 2
            if hsub == 0:
                w = min(QW, nHC - hc)
                if jq >= 2:
                    pic2, phq2 = divmod(jq - 2, nHQ)
                    iw2 = pic2 * nHC + min((phq2 + 1) * QW, nHC) - 1
                    E.wait("gpsimd", "t_p", gu_t[iw2])
                    E.wait("sync", "t_p", gu_t[iw2])
                kg, hg = divmod(hc, cfg.ncores)
                ku, hu = divmod(nHC + hc, cfg.ncores)
                E.dma("gpsimd", f"s_wg{sg}",
                      g("wg_t")[sg][:, 0:w * 128].rearrange("p (t c) -> p t c", c=128),
                      wcv[kg][hg:hg + w, ic:ic + 1, :, :].rearrange("t a p c -> p (t a) c"))
                E.dma("sync", f"s_wu{sg}",
                      g("wu_t")[sg][:, 0:w * 128].rearrange("p (t c) -> p t c", c=128),
                      wcv[ku][hu:hu + w, ic:ic + 1, :, :].rearrange("t a p c -> p (t a) c"))
                wgu_fill[sg] += 16
                E.wait("tensor", f"s_wg{sg}", wgu_fill[sg])
                E.wait("tensor", f"s_wu{sg}", wgu_fill[sg])
            if ic == 0 and hc % (nHC // NG) == 0:
                E.wait("tensor", f"s_x2g{hc * NG // nHC}", n_x2g[hc * NG // nHC])
            if iw == 0:
                E.wait("tensor", "v_p", E.count("v_p"))
                E.wait("tensor", "sc_p", E.count("sc_p"))
            for qb in range(nTB):
                if ic >= 1 and hc == 0:
                    E.wait("tensor", "sc_p", sil_sc[(ic - 1, qb)])
                    E.wait("tensor", "v_p", h_v[(ic - 1, qb)])
                fng = lambda e, sg=sg, hc=hc, qb=qb, hs=hsub, st=(hc == 0), sp=(hc == nHC - 1): e.matmul(
                    ps[qb][:, 0:TB], g("wg_t")[sg][:, hs * 128:(hs + 1) * 128],
                    xact[:, hc * S + qb * TB: hc * S + (qb + 1) * TB],
                    start=st, stop=sp, skip_group_check=True)
                fnu = lambda e, sg=sg, hc=hc, qb=qb, hs=hsub, st=(hc == 0), sp=(hc == nHC - 1): e.matmul(
                    ps[nTB + qb][:, 0:TB], g("wu_t")[sg][:, hs * 128:(hs + 1) * 128],
                    xact[:, hc * S + qb * TB: hc * S + (qb + 1) * TB],
                    start=st, stop=sp, skip_group_check=True)
                E.run("tensor", fng)
                if qb == nTB - 1:
                    gu_t[iw] = E.inc("tensor", "t_p", fnu)
                else:
                    E.run("tensor", fnu)
        for qb in range(nTB):
            s = qb % 2
            iq = ic * nTB + qb
            E.wait("scalar", "t_p", gu_t[ic * nHC + nHC - 1])
            if iq >= 2:
                E.wait("scalar", "v_p", h_v[((iq - 2) // nTB, (iq - 2) % nTB)])
            sil_sc[(ic, qb)] = E.inc("scalar", "sc_p",
                                     lambda e, qb=qb, s=s: e.activation(
                                         out=sil_t[s], in_=ps[qb][:, 0:TB],
                                         func=AF.Silu))
            E.wait("vector", "sc_p", sil_sc[(ic, qb)])
            if h_fill[s] >= 1:
                E.wait("vector", f"s_h{s}", 16 * h_fill[s])
            h_v[(ic, qb)] = E.inc("vector", "v_p",
                                  lambda e, qb=qb, s=s: e.tensor_tensor(
                                      out=h_t[s], in0=sil_t[s],
                                      in1=ps[nTB + qb][:, 0:TB], op=OP.mult))
            E.wait("scalar", "v_p", h_v[(ic, qb)])
            E.dma("scalar", f"s_h{s}",
                  g("h_scr")[ic * 128:(ic + 1) * 128, qb * TB:(qb + 1) * TB],
                  h_t[s])
            h_fill[s] += 1
    n_h = [E.count(f"s_h{s_}") for s_ in range(2)]

    for s_ in range(2):
        E.wait("sync", f"s_h{s_}", n_h[s_])
    E.wait("sync", "s_x2f", n_x2f)
    E.wait("sync", "t_p", E.count("t_p"))
    do_t, r2_v, wd_use_t = {}, {}, {}
    wd_qctr = {0: 0, 1: 0}
    n_out = 0
    first_p4 = True
    nICP = (nIC + 1) // 2
    h_free_t, n_hl_q = {}, {}
    for s_ in range(2):
        E.wait("gpsimd", f"s_h{s_}", n_h[s_])
    E.wait("gpsimd", "t_p", E.count("t_p"))
    for icp in range(nICP):
        ic0, icw = 2 * icp, min(2, nIC - 2 * icp)
        eng2 = "sync" if icp % 2 == 0 else "gpsimd"
        for icsub in range(icw):
            ic = ic0 + icsub
            E.dma(eng2, f"s_hl{icp % 2}", h_sl(ic),
                  g("h_scr")[ic * 128:(ic + 1) * 128, 0:TB])
    n_hl_q[0] = (E.count("s_hl0"), E.count("s_hl1"))
    for qb in range(nTB):
        n_hl = n_hl_q[qb]
        for op_ in range(nOC // 2):
            ot0, ot1 = 2 * op_, 2 * op_ + 1
            gp_pair = qb * (nOC // 2) + op_
            b0, b1 = 2 * (gp_pair % 2), 2 * (gp_pair % 2) + 1
            if gp_pair >= 2:
                qb_p, op_p = divmod(gp_pair - 2, nOC // 2)
                E.wait("tensor", "v_p", r2_v[qb_p * nOC + 2 * op_p + 1])
            for ic in range(nIC):
                qsel = ic % 2
                eng = "sync" if qsel == 0 else "gpsimd"
                jq = wd_qctr[qsel]
                wd_qctr[qsel] += 1
                ws = 4 * qsel + (jq % 4)
                if jq >= 4:
                    E.wait(eng, "t_p", wd_use_t[(qsel, jq - 4)])
                gid_d = 2 * cfg.G1 + ic * nOC + ot0
                kd, ld = divmod(gid_d, BL)
                E.dma(eng, f"s_wd{ws}",
                      g("wd_t")[ws][:, :].rearrange("p (t c) -> p t c", c=128),
                      g("wc")[kd][ld:ld + 2, :, :].rearrange("t p c -> p t c"))
                E.wait("tensor", f"s_wd{ws}", 16 * (jq // 4 + 1))
                if first_p4:
                    E.wait("tensor", "v_p", E.count("v_p"))
                    E.wait("tensor", "sc_p", E.count("sc_p"))
                    first_p4 = False
                if ic == 0:
                    E.wait("tensor", "s_hl0", n_hl[0])
                    E.wait("tensor", "s_hl1", n_hl[1])
                E.run("tensor", lambda e, ws=ws, ic=ic, b=b0, st=(ic == 0),
                      sp=(ic == nIC - 1): e.matmul(
                          ps[b][:, 0:TB], g("wd_t")[ws][:, 0:128], h_sl(ic),
                          start=st, stop=sp, skip_group_check=True))
                tmm = E.inc("tensor", "t_p",
                            lambda e, ws=ws, ic=ic, b=b1, st=(ic == 0),
                            sp=(ic == nIC - 1): e.matmul(
                                ps[b][:, 0:TB], g("wd_t")[ws][:, 128:256],
                                h_sl(ic),
                                start=st, stop=sp, skip_group_check=True))
                wd_use_t[(qsel, jq)] = tmm
                if op_ == nOC // 2 - 1:
                    h_free_t[ic] = tmm
            tdo_pair = tmm
            for i, (ot, bank) in enumerate(((ot0, b0), (ot1, b1))):
                io = qb * nOC + ot
                do_t[(qb, ot)] = tdo_pair
                xs = ot % 2
                if io >= 2:
                    E.wait("sync", "v_p", r2_v[io - 2])
                E.dma("sync", f"s_x2r{xs}", x2r[xs],
                      g("x2f_scr")[ot * 128:(ot + 1) * 128, qb * TB:(qb + 1) * TB])
                if qb == 0 and ot == 0:
                    E.wait("vector", "s_x2f", n_x2f)
                    E.wait("vector", "sc_p", ln_prev_stat['sc'])
                E.wait("vector", "t_p", tdo_pair)
                E.wait("vector", f"s_x2r{xs}", 16 * (io // 2 + 1))
                r2_v[io] = E.inc("vector", "v_p",
                                 lambda e, ot=ot, bank=bank, xs=xs: e.tensor_tensor(
                                     out=r2[:, ot * TB:(ot + 1) * TB], in0=ps[bank][:, 0:TB],
                                     in1=x2r[xs], op=OP.add))
        if qb + 1 < nTB:
            for icp in range(nICP):
                ic0, icw = 2 * icp, min(2, nIC - 2 * icp)
                eng2 = "sync" if icp % 2 == 0 else "gpsimd"
                E.wait(eng2, "t_p", h_free_t[ic0 + icw - 1])
                for icsub in range(icw):
                    ic = ic0 + icsub
                    E.dma(eng2, f"s_hl{icp % 2}", h_sl(ic),
                          g("h_scr")[ic * 128:(ic + 1) * 128,
                                     (qb + 1) * TB:(qb + 2) * TB])
            n_hl_q[qb + 1] = (E.count("s_hl0"), E.count("s_hl1"))
        rv = {oc: r2_v[qb * nOC + oc] for oc in range(nOC)}
        emit_ln(cfg, E, T, r2, rv, r1b_t, r1sq_t, (6, 7, 4, 5), ln_prev_stat)
        for oc in range(nOC):
            s = 0
            io = qb * nOC + oc
            if io >= 1:
                E.wait("vector", "s_out", 16 * io)
            E.run("vector", lambda e, oc=oc: e.tensor_tensor(
                out=tmp_f, in0=r2[:, oc * TB:(oc + 1) * TB], in1=ps[4][:, 0:TB], op=OP.mult))
            E.drain("vector")
            co = E.inc("vector", "v_p", lambda e, s=s: e.tensor_tensor(
                out=out_t[s], in0=tmp_f, in1=ps[5][:, 0:TB], op=OP.subtract))
            E.wait("gpsimd", "v_p", co)
            n_out = E.dma("gpsimd", "s_out",
                          g("out_fm")[oc * 128:(oc + 1) * 128, qb * TB:(qb + 1) * TB],
                          out_t[s])
    E.wait("gpsimd", "s_out", n_out)


def emit_ln(cfg, E, T, r_slab, r_v, r1b_t, r1sq_t, banks, ln_prev_stat):
    H, TB, nOC = cfg.H, cfg.TB, cfg.nOC
    g = lambda n: T[n]
    ps = T["ps"]
    bsum, bsq, ba, bb = banks
    stat_t = {}
    crs_last = 0
    for oc in range(nOC):
        s = oc % 2
        E.wait("scalar", "v_p", r_v[oc])
        if oc >= 1:
            E.wait("scalar", "t_p", stat_t[oc - 1])
        else:
            E.wait("scalar", "t_p", ln_prev_stat['t'])
        crb = E.inc("scalar", "sc_p", lambda e, oc=oc, s=s: e.copy(
            out=r1b_t(s), in_=r_slab[:, oc * TB:(oc + 1) * TB]))
        crs = E.inc("scalar", "sc_p", lambda e, oc=oc, s=s: e.activation(
            out=r1sq_t(s), in_=r_slab[:, oc * TB:(oc + 1) * TB], func=AF.Square))
        crs_last = crs
        E.wait("tensor", "sc_p", crs)
        E.run("tensor", lambda e, s=s, st=(oc == 0), sp=(oc == nOC - 1): e.matmul(
            ps[bsum][0:1, 0:TB], g("onec_s")[:, :], r1b_t(s),
            start=st, stop=sp, skip_group_check=True))
        stat_t[oc] = E.inc("tensor", "t_p",
                           lambda e, s=s, st=(oc == 0), sp=(oc == nOC - 1): e.matmul(
                               ps[bsq][0:1, 0:TB], g("onec_s")[:, :], r1sq_t(s),
                               start=st, stop=sp, skip_group_check=True))
    ln_prev_stat['t'] = stat_t[nOC - 1]
    ln_prev_stat['sc'] = crs_last
    E.wait("vector", "t_p", stat_t[nOC - 1])
    E.run("vector", lambda e: e.tensor_scalar(out=g("mn_f")[:, :], in0=ps[bsum][0:1, 0:TB],
                                              scalar1=1.0 / H, scalar2=None, op0=OP.mult))
    E.run("vector", lambda e: e.tensor_scalar(out=g("msq_f")[:, :], in0=ps[bsq][0:1, 0:TB],
                                              scalar1=1.0 / H, scalar2=None, op0=OP.mult))
    E.drain("vector")
    E.run("vector", lambda e: e.tensor_tensor(out=g("mn2_f")[:, :], in0=g("mn_f")[:, :],
                                              in1=g("mn_f")[:, :], op=OP.mult))
    E.drain("vector")
    E.run("vector", lambda e: e.tensor_tensor(
        out=g("var_f")[:, :], in0=g("msq_f")[:, :], in1=g("mn2_f")[:, :], op=OP.subtract))
    E.drain("vector")
    cvar = E.inc("vector", "v_p", lambda e: e.tensor_scalar(
        out=g("msq_f")[:, :], in0=g("var_f")[:, :], scalar1=EPS, scalar2=None, op0=OP.add))
    E.wait("scalar", "v_p", cvar)
    E.run("scalar", lambda e: e.activation(
        out=g("sd_f")[:, :], in_=g("msq_f")[:, :], func=AF.Ln))
    E.drain("scalar")
    csd = E.inc("scalar", "sc_p", lambda e: e.activation(
        out=g("rstd_f")[:, :], in_=g("sd_f")[:, :], func=AF.Exp, scale=-0.5))
    E.wait("vector", "sc_p", csd)
    E.run("vector", lambda e: e.tensor_tensor(out=g("bvec_f")[:, :], in0=g("mn_f")[:, :],
                                              in1=g("rstd_f")[:, :], op=OP.mult))
    E.drain("vector")
    E.run("vector", lambda e: e.tensor_copy(g("a_b")[:, :], g("rstd_f")[:, :]))
    E.drain("vector")
    cab = E.inc("vector", "v_p", lambda e: e.tensor_copy(g("b_b")[:, :], g("bvec_f")[:, :]))
    E.wait("tensor", "v_p", cab)
    E.run("tensor", lambda e: e.matmul(ps[ba][:, 0:TB], g("oner_s")[:, :], g("a_b")[:, :],
                                       start=True, stop=True, skip_group_check=True))
    tab = E.inc("tensor", "t_p", lambda e: e.matmul(
        ps[bb][:, 0:TB], g("oner_s")[:, :], g("b_b")[:, :], start=True, stop=True,
        skip_group_check=True))
    E.wait("vector", "t_p", tab)



def make_host_inputs(cfg: Cfg, x, w_qkv, w_out, lut, gate_idx, up_idx, down_idx):
    import ml_dtypes
    BF = ml_dtypes.bfloat16
    lutb = np.broadcast_to(np.asarray(lut, np.float32), (128, LUT_N)).copy()
    kp = np.arange(128)[:, None]
    uu = np.arange((cfg.NSTR - 1) * 128 + cfg.TB)[None, :]
    cm = (uu >= kp + (cfg.NSTR - 1) * 128).astype(BF)
    onesc = np.ones((128, 1), dtype=BF)
    onesr = np.ones((1, 128), dtype=BF)
    srcs = {
        "gidxT": np.ascontiguousarray(np.asarray(gate_idx).T.astype(np.int32)),
        "uidxT": np.ascontiguousarray(np.asarray(up_idx).T.astype(np.int32)),
        "didxT": np.ascontiguousarray(np.asarray(down_idx).T.astype(np.int32)),
    }
    in_maps = []
    for c in range(cfg.ncores):
        myidx = np.zeros((cfg.TPC, 128, 128), np.int32)
        for t in range(cfg.TPC):
            gid = (t // cfg.CSZ) * (cfg.ncores * cfg.CSZ) + c * cfg.CSZ + (t % cfg.CSZ)
            if gid >= cfg.NT:
                continue
            src, rc, ct = w_tile_src(cfg, gid)
            myidx[t] = srcs[src][rc * 128:(rc + 1) * 128, ct * 128:(ct + 1) * 128]
        in_maps.append({
            "x_fm": np.ascontiguousarray(np.asarray(x[c]).T).astype(np.float32, copy=False),
            "wqkv": np.asarray(w_qkv, np.float32),
            "wout": np.asarray(w_out, np.float32),
            "myidx": myidx,
            "lutb": lutb,
            "cmask": cm,
            "onesc": onesc,
            "onesr": onesr,
        })
    return in_maps


def gather_output(cfg, results):
    return np.stack([np.ascontiguousarray(r["out_fm"].T) for r in results], axis=0)


def numpy_reference(cfg: Cfg, x, w_qkv, w_out, lut, gate_idx, up_idx, down_idx):
    def layernorm(v):
        m = v.mean(-1, keepdims=True)
        s = v.var(-1, keepdims=True)
        return (v - m) / np.sqrt(s + EPS)
    def softmax(s):
        m = s.max(-1, keepdims=True)
        e = np.exp(s - m)
        return e / e.sum(-1, keepdims=True)
    x = np.asarray(x, np.float64)
    w_qkv = np.asarray(w_qkv, np.float64)
    w_out = np.asarray(w_out, np.float64)
    lut = np.asarray(lut, np.float64)
    B, S, H = x.shape
    NH, D = cfg.NH, cfg.D
    qkv = x @ w_qkv
    qkv = qkv.reshape(B, S, 3, NH, D)
    q, k, v = qkv[:, :, 0], qkv[:, :, 1], qkv[:, :, 2]
    q = q / np.sqrt(np.float64(D))
    scores = np.einsum("bqhd,bkhd->bhqk", q, k)
    mask = np.arange(S)[:, None] >= np.arange(S)[None, :]
    scores = np.where(mask[None, None], scores, -1e10)
    a = softmax(scores)
    o = np.einsum("bhqk,bkhd->bqhd", a, v).reshape(B, S, H)
    x1 = layernorm(x + o @ w_out)
    gate = x1 @ lut[gate_idx].T
    up = x1 @ lut[up_idx].T
    hh = (gate / (1 + np.exp(-gate))) * up
    x2 = layernorm(x1 + hh @ lut[down_idx].T)
    return x2.astype(np.float32)



def kernel(L0_attn_x, L0_attn_w_qkv, L0_attn_w_out, L0_ffn_lut,
           L0_ffn_gate_idx, L0_ffn_up_idx, L0_ffn_down_idx):
    cfg = FULL
    in_maps = make_host_inputs(cfg, L0_attn_x, L0_attn_w_qkv, L0_attn_w_out,
                               L0_ffn_lut, L0_ffn_gate_idx, L0_ffn_up_idx,
                               L0_ffn_down_idx)
    nc = build_program(cfg)
    res = run_bass_kernel_spmd(nc, in_maps, core_ids=list(range(cfg.ncores)))
    return gather_output(cfg, res.results)


# version 23 (final)
# speedup vs baseline: 1.0823x; 1.0137x over previous
import math
import numpy as np
from contextlib import ExitStack
from dataclasses import dataclass

import concourse.bass as bass
import concourse.mybir as mybir
from concourse.bass_utils import run_bass_kernel_spmd

F32 = mybir.dt.float32
BF16 = mybir.dt.bfloat16
I32 = mybir.dt.int32
AF = mybir.ActivationFunctionType
OP = mybir.AluOpType

LUT_N = 16
EPS = 1e-5
MRUN = 8


@dataclass
class Cfg:
    S: int = 2048
    H: int = 2048
    NH: int = 16
    I: int = 5504
    TB: int = 512
    ncores: int = 8

    @property
    def D(self): return self.H // self.NH
    @property
    def F3(self): return 3 * self.H
    @property
    def nHC(self): return self.H // 128
    @property
    def nOC(self): return self.H // 128
    @property
    def nIC(self): return self.I // 128
    @property
    def nTB(self): return self.S // self.TB
    @property
    def nTT(self): return self.S // 128
    @property
    def NSTR(self): return self.TB // 128
    @property
    def DBW(self): return min(512, self.H)
    @property
    def nDB(self): return self.H // self.DBW
    @property
    def G1(self): return self.nHC * self.nIC
    @property
    def NT(self): return 2 * self.G1 + self.nIC * self.nOC
    @property
    def TPC0(self): return (self.NT + self.ncores - 1) // self.ncores
    @property
    def CSZ(self): return self.nIC
    @property
    def KAG(self): return (self.TPC0 + self.CSZ - 1) // self.CSZ
    @property
    def TPC(self): return self.KAG * self.CSZ
    @property
    def qscale(self): return 1.0 / math.sqrt(self.D)


FULL = Cfg()
SMALL = Cfg(S=256, H=256, NH=2, I=384, TB=128, ncores=8)
MED = Cfg(S=1024, H=512, NH=4, I=768, TB=512, ncores=8)

ENGINES = ("sync", "gpsimd", "vector", "scalar", "tensor")


class Emitter:
    def __init__(self, nc, ctx):
        self.nc = nc
        self.ctx = ctx
        self.q = {e: [] for e in ENGINES}
        self.sems = {}

    def _sem(self, name):
        if name not in self.sems:
            h = self.ctx.enter_context(self.nc.semaphore(name))
            self.sems[name] = [h, 0]
        return self.sems[name]

    def count(self, name):
        return self.sems[name][1] if name in self.sems else 0

    def run(self, eng, fn):
        self.q[eng].append(fn)

    def inc(self, eng, sem_name, fn, k=1):
        s = self._sem(sem_name)
        s[1] += k
        self.q[eng].append(lambda e, fn=fn, h=s[0], k=k: fn(e).then_inc(h, k))
        return s[1]

    def dma(self, eng, sem_name, out, in_):
        return self.inc(eng, sem_name, lambda e, o=out, i=in_: e.dma_start(out=o, in_=i), k=16)

    def wait(self, eng, sem_name, val):
        if val is None or val <= 0:
            return
        s = self._sem(sem_name)
        assert val <= s[1], f"wait {sem_name}>={val} but only {s[1]} emitted"
        self.q[eng].append(lambda e, h=s[0], v=val: e.wait_ge(h, v))

    def drain(self, eng):
        self.q[eng].append(lambda e: e.drain())

    def finalize(self, block):
        for eng in ENGINES:
            dec = getattr(block, eng)
            q = self.q[eng]
            def make(qq):
                def body(e):
                    for fn in qq:
                        fn(e)
                return body
            dec(make(q))


def w_tile_src(cfg, gid):
    if gid < cfg.G1:
        return "gidxT", gid // cfg.nIC, gid % cfg.nIC
    if gid < 2 * cfg.G1:
        return "uidxT", (gid - cfg.G1) // cfg.nIC, (gid - cfg.G1) % cfg.nIC
    g2 = gid - 2 * cfg.G1
    return "didxT", g2 // cfg.nOC, g2 % cfg.nOC


def build_program(cfg: Cfg):
    nc = bass.Bass("TRN2", target_bir_lowering=False)
    ctx = ExitStack()
    S, H, I, TB = cfg.S, cfg.H, cfg.I, cfg.TB
    nHC, nOC, nIC, nTT, NSTR = cfg.nHC, cfg.nOC, cfg.nIC, cfg.nTT, cfg.NSTR
    DBW = cfg.DBW

    x_fm = nc.dram_tensor("x_fm", [H, S], F32, kind="ExternalInput")
    wqkv = nc.dram_tensor("wqkv", [H, cfg.F3], F32, kind="ExternalInput")
    wout = nc.dram_tensor("wout", [H, H], F32, kind="ExternalInput")
    myidx = nc.dram_tensor("myidx", [cfg.TPC, 128, 128], I32, kind="ExternalInput")
    lutb = nc.dram_tensor("lutb", [128, LUT_N], F32, kind="ExternalInput")
    cmask = nc.dram_tensor("cmask", [128, (NSTR - 1) * 128 + TB], BF16, kind="ExternalInput")
    onesc = nc.dram_tensor("onesc", [128, 1], BF16, kind="ExternalInput")
    onesr = nc.dram_tensor("onesr", [1, 128], BF16, kind="ExternalInput")
    out_fm = nc.dram_tensor("out_fm", [H, S], F32, kind="ExternalOutput")

    q_scr = nc.dram_tensor("q_scr", [H, S], BF16)
    k_scr = nc.dram_tensor("k_scr", [H, S], BF16)
    v_scr = nc.dram_tensor("v_scr", [S, H], BF16)
    wob_scr = nc.dram_tensor("wob_scr", [H, H], BF16)
    x2f_scr = nc.dram_tensor("x2f_scr", [H, S], F32)
    x2b_scr = nc.dram_tensor("x2b_scr", [H, S], BF16)
    h_scr = nc.dram_tensor("h_scr", [I, S], BF16)
    wcat_my = nc.dram_tensor("wcat_my", [cfg.TPC, 128, 128], BF16)
    if cfg.KAG > 1:
        assert nHC % 4 == 0 and nOC % 2 == 0
    wc = [nc.dram_tensor(f"wcat{k}", [cfg.ncores * cfg.CSZ, 128, 128], BF16,
                         addr_space="Shared") for k in range(cfg.KAG)]

    sb = lambda name, shape, dt: ctx.enter_context(nc.sbuf_tensor(name, shape, dt))

    with ctx:
        xact = sb("xact", [128, nHC * S], BF16)
        p2cols = nTT * TB + 4 * S + 9 * TB
        bigcols = max(p2cols, nIC * TB, 21 * MRUN * 128)
        big = sb("big", [128, bigcols], BF16)
        r1b_sl = sb("r1b_sl", [128, TB], BF16)
        r1sq_sl = sb("r1sq_sl", [128, TB], BF16)
        
        slab16 = sb("slab16", [128, nHC * max(DBW, TB)], BF16)
        arf_cols = max(8 * 128 + 2 * DBW + 2 * S, nOC * TB + 4 * TB)
        arf = sb("arf", [128, arf_cols], F32)

        lut_s = sb("lut_s", [128, LUT_N], F32)
        cm_s = sb("cm_s", [128, (NSTR - 1) * 128 + TB], BF16)
        onec_s = sb("onec_s", [128, 1], BF16)
        oner_s = sb("oner_s", [1, 128], BF16)
        wqk_b = [sb(f"wqk_b{s}", [128, 128], BF16) for s in range(8)]
        qk_e = [sb(f"qk_e{s}", [128, TB], BF16) for s in range(4)]
        ve_t = [sb(f"ve_t{s}", [128, DBW], BF16) for s in range(1)]
        wo_t = [sb(f"wo_t{s}", [128, 2 * 128], BF16) for s in range(6)]
        wg_t = [sb(f"wg_t{s}", [128, 4 * 128], BF16) for s in range(3)]
        wu_t = [sb(f"wu_t{s}", [128, 4 * 128], BF16) for s in range(3)]
        wd_t = [sb(f"wd_t{s}", [128, 2 * 128], BF16) for s in range(8)]
        mn_f = sb("mn_f", [1, TB], F32)
        msq_f = sb("msq_f", [1, TB], F32)
        mn2_f = sb("mn2_f", [1, TB], F32)
        var_f = sb("var_f", [1, TB], F32)
        sd_f = sb("sd_f", [1, TB], F32)
        rstd_f = sb("rstd_f", [1, TB], F32)
        den_f = mn_f
        rden_f = msq_f
        bvec_f = sd_f
        rden_b = sb("rden_b", [1, TB], BF16)
        a_b = sb("a_b", [1, TB], BF16)
        b_b = sb("b_b", [1, TB], BF16)
        

        BKW = 512
        psA = [ctx.enter_context(nc.psum_tensor(f"psA{i}", [128, 2 * BKW], F32)) for i in range(2)]
        psS = [ctx.enter_context(nc.psum_tensor(f"psS{i}", [128, BKW], F32)) for i in range(4)]
        ps = [psA[0][:, 0:BKW], psA[0][:, BKW:2 * BKW],
              psS[0][:, :], psS[1][:, :], psS[2][:, :], psS[3][:, :],
              psA[1][:, 0:BKW], psA[1][:, BKW:2 * BKW]]

        T = dict(
            x_fm=x_fm, wqkv=wqkv, wout=wout, myidx=myidx, lutb=lutb, cmask=cmask,
            onesc=onesc, onesr=onesr, out_fm=out_fm, q_scr=q_scr, k_scr=k_scr,
            v_scr=v_scr, wob_scr=wob_scr, x2f_scr=x2f_scr, x2b_scr=x2b_scr,
            h_scr=h_scr, wcat_my=wcat_my, wc=wc,
            xact=xact, big=big, slab16=slab16, arf=arf,
            r1b_sl=r1b_sl, r1sq_sl=r1sq_sl,

            lut_s=lut_s, cm_s=cm_s,
            onec_s=onec_s, oner_s=oner_s, wqk_b=wqk_b, qk_e=qk_e,
            ve_t=ve_t, wo_t=wo_t, wg_t=wg_t, wu_t=wu_t, wd_t=wd_t,
            den_f=den_f, rden_f=rden_f, rden_b=rden_b,
            mn_f=mn_f, msq_f=msq_f, mn2_f=mn2_f, var_f=var_f, sd_f=sd_f,
            rstd_f=rstd_f, bvec_f=bvec_f, a_b=a_b, b_b=b_b, ps=ps, psA=psA,
        )
        with nc.Block() as block:
            E = Emitter(nc, ctx)
            build_phases(cfg, E, T)
            E.finalize(block)
    return nc


def build_phases(cfg, E, T):
    S, H, I, TB = cfg.S, cfg.H, cfg.I, cfg.TB
    nHC, nOC, nIC, nTB, nTT, NSTR = (cfg.nHC, cfg.nOC, cfg.nIC, cfg.nTB,
                                     cfg.nTT, cfg.NSTR)
    NH, DBW, nDB = cfg.NH, cfg.DBW, cfg.nDB
    PSW = 512
    ps = T["ps"]
    big, xact, slab16, arf = T["big"], T["xact"], T["slab16"], T["arf"]
    g = lambda n: T[n]

    et_off = 0
    ksl_off = nTT * TB
    vsl_off = ksl_off + 2 * S
    qsl_off = vsl_off + 2 * S
    et = lambda kt: big[:, et_off + kt * TB: et_off + (kt + 1) * TB]
    k_sl = lambda s, w: big[:, ksl_off + s * S: ksl_off + s * S + w]
    v_sl = lambda s, w: big[:, vsl_off + s * S: vsl_off + s * S + w]
    q_sl = lambda s: big[:, qsl_off + s * TB: qsl_off + (s + 1) * TB]
    r1b_t = lambda s: T["r1b_sl"][:, 0:TB]
    r1sq_t = lambda s: T["r1sq_sl"][:, 0:TB]
    tail0 = qsl_off + 2 * TB
    et_t = lambda s: big[:, tail0 + s * TB: tail0 + (s + 1) * TB]
    h_sl = lambda ic: big[:, ic * TB:(ic + 1) * TB]
    bc_s = big[:, tail0 + 2 * TB: tail0 + 3 * TB]
    sil_t = [big[:, tail0 + (3 + s) * TB: tail0 + (4 + s) * TB] for s in range(2)]
    h_t = [big[:, tail0 + (5 + s) * TB: tail0 + (6 + s) * TB] for s in range(2)]
    attn = lambda dc: slab16[:, dc * TB:(dc + 1) * TB]
    wv_b = lambda hc: slab16[:, hc * DBW:(hc + 1) * DBW]
    wqk_f = [arf[:, s * 128:(s + 1) * 128] for s in range(8)]
    wv_f = [arf[:, 8 * 128 + s * DBW: 8 * 128 + (s + 1) * DBW] for s in range(2)]
    xs_f = [arf[:, 8 * 128 + 2 * DBW + s * S: 8 * 128 + 2 * DBW + (s + 1) * S] for s in range(2)]
    r1 = arf[:, 0:nOC * TB]
    xres = [arf[:, nOC * TB + s * TB: nOC * TB + (s + 1) * TB] for s in range(2)]
    tmp_f = arf[:, nOC * TB + 2 * TB: nOC * TB + 3 * TB]
    x2f_t = [arf[:, nOC * TB + 3 * TB: nOC * TB + 4 * TB]]
    r2, x2r, out_t = r1, xres, x2f_t

    E.dma("sync", "s_lut", g("lut_s")[:, :], g("lutb")[:, :])
    E.dma("sync", "s_cm", g("cm_s")[:, :], g("cmask")[:, :])
    E.dma("sync", "s_onc", g("onec_s")[:, :], g("onesc")[:, :])
    E.dma("sync", "s_onr", g("oner_s")[:, :], g("onesr")[:, :])

    NG = min(4, nHC)
    runs = []
    t = 0
    while t < cfg.TPC:
        w = min(MRUN, cfg.TPC - t)
        runs.append((t, w))
        t += w
    MP = MRUN * 128
    m_part = [big[:, j * MP:(j + 1) * MP] for j in range(16)]
    m_idxb = [big[:, (16 + s) * MP:(17 + s) * MP] for s in range(3)]
    m_out = [big[:, (19 + s) * MP:(20 + s) * MP] for s in range(2)]
    E.wait("vector", "s_lut", 16)
    acc_v = {}

    def mat_dma(ri):
        t0, w = runs[ri]
        slot = ri % 3
        if ri >= 3:
            E.wait("gpsimd", "v_p", acc_v[ri - 3])
        E.dma("gpsimd", f"s_midx{slot}",
              m_idxb[slot][:, 0:w * 128].rearrange("p (t c) -> p t c", c=128),
              g("myidx")[t0:t0 + w, :, :].rearrange("t p c -> p t c"))

    for r0 in range(min(3, len(runs))):
        mat_dma(r0)
    for ri, (t0, w) in enumerate(runs):
        slot = ri % 3
        E.wait("vector", f"s_midx{slot}", 16 * (ri // 3 + 1))
        for j in range(LUT_N):
            E.run("vector", lambda e, w=w, j=j, slot=slot: e.tensor_scalar(
                out=m_part[j][:, 0:w * 128], in0=m_idxb[slot][:, 0:w * 128],
                scalar1=float(j), scalar2=g("lut_s")[:, j:j + 1],
                op0=OP.is_equal, op1=OP.mult))
        E.drain("vector")
        step = 1
        while step < 16:
            for j0 in range(0, 16, 2 * step):
                last = (2 * step >= 16)
                if last:
                    if ri >= 2:
                        E.wait("vector", f"s_wmy{ri % 2}", 16 * ((ri - 2) // 2 + 1))
                    fn = lambda e, w=w, ri=ri: e.tensor_tensor(
                        out=m_out[ri % 2][:, 0:w * 128], in0=m_part[0][:, 0:w * 128],
                        in1=m_part[8][:, 0:w * 128], op=OP.add)
                    acc_v[ri] = E.inc("vector", "v_p", fn)
                else:
                    fn = lambda e, j0=j0, s=step, w=w: e.tensor_tensor(
                        out=m_part[j0][:, 0:w * 128], in0=m_part[j0][:, 0:w * 128],
                        in1=m_part[j0 + s][:, 0:w * 128], op=OP.add)
                    E.run("vector", fn)
            E.drain("vector")
            step *= 2
        if ri + 3 < len(runs):
            mat_dma(ri + 3)
        E.wait("gpsimd", "v_p", acc_v[ri])
        E.dma("gpsimd", f"s_wmy{ri % 2}",
              g("wcat_my")[t0:t0 + w, :, :].rearrange("t p c -> p t c"),
              m_out[ri % 2][:, 0:w * 128].rearrange("p (t c) -> p t c", c=128))
        for k in range(cfg.KAG):
            if ((k + 1) * cfg.CSZ - 1) // MRUN == ri:
                E.wait("gpsimd", "s_wmy0", 16 * (ri // 2 + 1))
                if ri >= 1:
                    E.wait("gpsimd", "s_wmy1", 16 * ((ri - 1) // 2 + 1))
                ag_g = E.inc("gpsimd", "g_p", lambda e, k=k: e.collective_compute(
                    "AllGather", OP.bypass, replica_groups=[list(range(cfg.ncores))],
                    ins=[g("wcat_my")[k * cfg.CSZ:(k + 1) * cfg.CSZ, :, :].opt()],
                    outs=[g("wc")[k].ap().opt()]))
    mat_done_v = acc_v[len(runs) - 1]
    n_wmy = [E.count("s_wmy0"), E.count("s_wmy1")]
    for i4 in range(NG):
        E.dma("gpsimd", "s_wob", g("wob_scr")[i4 * (H // NG):(i4 + 1) * (H // NG), :],
              g("wout")[i4 * (H // NG):(i4 + 1) * (H // NG), :])
    n_wob = E.count("s_wob")

    qkcast_s, lastmm_t, stopmm_t, qkevac_s = {}, {}, {}, {}
    xcast_s = {}
    for ft in range(2 * nHC):
        isq = ft < nHC
        for hc in range(nHC):
            it = ft * nHC + hc
            slot = it % 8
            if ft == 0:
                xsl = hc % 2
                if hc >= 2:
                    E.wait("sync", "sc_p", xcast_s[hc - 2])
                E.dma("sync", f"s_xs{xsl}", xs_f[xsl],
                      g("x_fm")[hc * 128:(hc + 1) * 128, :])
                E.wait("scalar", f"s_xs{xsl}", 16 * (hc // 2 + 1))
                xcast_s[hc] = E.inc("scalar", "sc_p",
                                    lambda e, s=xsl, k=hc: e.copy(
                                        out=xact[:, k * S:(k + 1) * S], in_=xs_f[s]))
            if it >= 8:
                E.wait("sync", "sc_p", qkcast_s[it - 8])
            E.dma("sync", f"s_wqk{slot}", wqk_f[slot],
                  g("wqkv")[hc * 128:(hc + 1) * 128, ft * 128:(ft + 1) * 128])
            if it >= 8:
                E.wait("scalar", "t_p", lastmm_t[it - 8])
            E.wait("scalar", f"s_wqk{slot}", 16 * (it // 8 + 1))
            qkcast_s[it] = E.inc("scalar", "sc_p",
                                 lambda e, s=slot: e.copy(out=g("wqk_b")[s][:, :],
                                                          in_=wqk_f[s]))
            E.wait("tensor", "sc_p", qkcast_s[it])
            if ft == 0:
                E.wait("tensor", "sc_p", xcast_s[hc])
            if hc == 0 and ft >= 2:
                E.wait("tensor", "sc_p", qkevac_s[(ft - 2, nTB - 1)])
            bk0 = (ft % 2) * nTB
            for tb in range(nTB):
                fn = lambda e, s=slot, hc=hc, tb=tb, bk0=bk0, st=(hc == 0), sp=(hc == nHC - 1): e.matmul(
                    ps[bk0 + tb][:, 0:TB], g("wqk_b")[s][:, :],
                    xact[:, hc * S + tb * TB: hc * S + (tb + 1) * TB],
                    start=st, stop=sp, skip_group_check=True)
                if hc == nHC - 1:
                    stopmm_t[(ft, tb)] = E.inc("tensor", "t_p", fn)
                    lastmm_t[it] = stopmm_t[(ft, tb)]
                elif tb == nTB - 1:
                    lastmm_t[it] = E.inc("tensor", "t_p", fn)
                else:
                    E.run("tensor", fn)
        for tb in range(nTB):
            es = (ft * nTB + tb) % 4
            idx = ft * nTB + tb
            E.wait("scalar", "t_p", stopmm_t[(ft, tb)])
            if idx >= 4:
                E.wait("scalar", f"s_qkst{es}", 16 * (idx // 4))
            sc = cfg.qscale if isq else 1.0
            qkevac_s[(ft, tb)] = E.inc("scalar", "sc_p",
                                       lambda e, tb=tb, es=es, sc=sc, bk0=(ft % 2) * nTB:
                                       e.activation(
                                           out=g("qk_e")[es][:, :], in_=ps[bk0 + tb][:, 0:TB],
                                           func=AF.Copy, scale=sc))
            E.wait("scalar", "sc_p", qkevac_s[(ft, tb)])
            dst = g("q_scr") if isq else g("k_scr")
            r0 = (ft if isq else ft - nHC) * 128
            E.dma("scalar", f"s_qkst{es}", dst[r0:r0 + 128, tb * TB:(tb + 1) * TB],
                  g("qk_e")[es][:, :])
    n_qkst = [E.count(f"s_qkst{e_}") for e_ in range(4)]


    wvc_s, vstop_t, vevac_s = {}, {}, {}
    for db in range(nDB):
        for hc in range(nHC):
            it = db * nHC + hc
            slot = hc % 2
            if it >= 2:
                E.wait("sync", "sc_p", wvc_s[it - 2])
            E.dma("sync", f"s_wv{slot}", wv_f[slot],
                  g("wqkv")[hc * 128:(hc + 1) * 128, 2 * H + db * DBW: 2 * H + (db + 1) * DBW])
            if db >= 1:
                E.wait("scalar", "t_p", vstop_t[(db - 1, nTT - 1)])
            E.wait("scalar", f"s_wv{slot}", 16 * (it // 2 + 1))
            wvc_s[it] = E.inc("scalar", "sc_p",
                              lambda e, hc=hc, s=slot: e.copy(out=wv_b(hc), in_=wv_f[s]))
        for tt in range(nTT):
            bank = 4 + (tt % 2)
            it = db * nTT + tt
            if it == 0:
                E.wait("tensor", "sc_p", qkevac_s[(2 * nHC - 1, nTB - 1)])
            if it >= 2:
                E.wait("tensor", "sc_p", vevac_s[it - 2])
            E.wait("tensor", "sc_p", wvc_s[db * nHC + nHC - 1])
            for hc in range(nHC):
                fn = lambda e, hc=hc, tt=tt, bank=bank, st=(hc == 0), sp=(hc == nHC - 1): e.matmul(
                    ps[bank][:, 0:DBW],
                    xact[:, hc * S + tt * 128: hc * S + (tt + 1) * 128],
                    wv_b(hc), start=st, stop=sp, skip_group_check=True)
                if hc == nHC - 1:
                    vstop_t[(db, tt)] = E.inc("tensor", "t_p", fn)
                else:
                    E.run("tensor", fn)
            E.wait("scalar", "t_p", vstop_t[(db, tt)])
            if it >= 1:
                E.wait("scalar", "s_vst", 16 * it)
            es = 0
            vevac_s[it] = E.inc("scalar", "sc_p",
                                lambda e, bank=bank, es=es: e.copy(out=g("ve_t")[es][:, :],
                                                                   in_=ps[bank][:, 0:DBW]))
            E.wait("scalar", "sc_p", vevac_s[it])
            E.dma("scalar", "s_vst",
                  g("v_scr")[tt * 128:(tt + 1) * 128, db * DBW:(db + 1) * DBW],
                  g("ve_t")[es][:, :])
    n_vst = E.count("s_vst")
    p1_sc = E.count("sc_p")
    tp_p1v = E.count("t_p")

    E.wait("vector", "s_cm", 16)
    E.wait("scalar", "v_p", mat_done_v)
    E.wait("scalar", "s_wmy0", n_wmy[0])
    E.wait("scalar", "s_wmy1", n_wmy[1])
    E.wait("sync", "v_p", mat_done_v)
    E.wait("sync", "s_wmy0", n_wmy[0])
    E.wait("sync", "s_wmy1", n_wmy[1])
    E.wait("gpsimd", "s_wob", n_wob)
    E.wait("sync", "s_wob", n_wob)
    n_strad = 0
    msk_v, avden_t, attn_v, denev_v, bcs_v = {}, {}, {}, {}, {}
    wo_use_t, r1_v, x2f_v = {}, {}, {}
    wo_qctr = {0: 0, 1: 0}
    vsl_fill = {}
    prev_exp_last = 0
    n_x2f = 0
    ln_prev_stat = {'t': 0, 'sc': 0}
    wo_last_t = 0
    x2f_cnt = {}
    avkt_t = {}
    msk_last = {}
    for qb in range(nTB):
        nkt = (qb + 1) * NSTR
        nkt_prev_qb = qb * NSTR

        def den_tail(iq2):
            head2 = iq2 % NH
            bav = 2 + (head2 % 2)
            bden = 4 + (head2 % 2)
            E.wait("scalar", "t_p", avden_t[iq2])
            E.run("scalar", lambda e, bden=bden: e.activation(
                out=g("rden_f")[:, :], in_=ps[bden][0:1, 0:TB], func=AF.Ln))
            E.drain("scalar")
            denev_v[iq2] = E.inc("scalar", "sc_p", lambda e: e.activation(
                out=g("rden_b")[:, :], in_=g("rden_f")[:, :], func=AF.Exp, scale=-1.0))
            E.wait("tensor", "sc_p", denev_v[iq2])
            E.wait("tensor", "s_onr", 16)
            if iq2 > 0:
                E.wait("tensor", "v_p", bcs_v[iq2 - 1])
            bcb = 2 + ((head2 + 1) % 2)
            if iq2 >= 1:
                E.wait("tensor", "v_p", attn_v[iq2 - 1])
            tbc = E.inc("tensor", "t_p", lambda e, bcb=bcb: e.matmul(
                ps[bcb][:, 0:TB], g("oner_s")[:, :], g("rden_b")[:, :],
                start=True, stop=True, skip_group_check=True))
            E.wait("vector", "t_p", tbc)
            bcs_v[iq2] = E.inc("vector", "v_p",
                               lambda e, bcb=bcb: e.tensor_copy(bc_s, ps[bcb][:, 0:TB]))
            E.drain("vector")
            if head2 == 0 and iq2 >= NH:
                E.wait("vector", "t_p", wo_last_t)
            attn_v[iq2] = E.inc("vector", "v_p",
                                lambda e, head2=head2, bav=bav: e.tensor_tensor(
                                    out=attn(head2), in0=ps[bav][:, 0:TB], in1=bc_s,
                                    op=OP.mult))
            E.drain("vector")

        def emit_av_kt(iq2, kt, nkt2, last):
            head2 = iq2 % NH
            bav = 2 + (head2 % 2)
            bden = 4 + (head2 % 2)
            sl2 = head2 % 2
            if kt == 0:
                if head2 <= 1 and iq2 >= NH and (iq2 // NH) == qb:
                    E.wait("tensor", "v_p", x2f_v[(qb - 1) * nOC + nOC - 1])
                if iq2 >= 2:
                    E.wait("tensor", "v_p", attn_v[iq2 - 2])
                    E.wait("tensor", "sc_p", denev_v[iq2 - 2])
                if iq2 >= 1:
                    E.wait("tensor", "v_p", bcs_v[iq2 - 1])
                if iq2 in msk_last:
                    E.wait("tensor", "v_p", msk_last[iq2])
                E.wait("tensor", f"s_vsl{sl2}", vsl_fill[iq2])
                E.wait("tensor", "s_onc", 16)
            E.run("tensor", lambda e, sl2=sl2, kt=kt, nkt2=nkt2, bav=bav,
                  st=(kt == 0), sp=last: e.matmul(
                      ps[bav][:, 0:TB], v_sl(sl2, nkt2 * 128)[:, kt * 128:(kt + 1) * 128],
                      et(kt), start=st, stop=sp, skip_group_check=True))
            avkt_t[(iq2, kt)] = E.inc("tensor", "t_p",
                                      lambda e, kt=kt, bden=bden, st=(kt == 0), sp=last: e.matmul(
                                          ps[bden][0:1, 0:TB], g("onec_s")[:, :], et(kt),
                                          start=st, stop=sp, skip_group_check=True))
            if last:
                avden_t[iq2] = avkt_t[(iq2, kt)]

        for head in range(NH):
            sl = head % 2
            iq = qb * NH + head
            if iq >= 2:
                E.wait("sync", "t_p", avden_t[iq - 2])
            idxq = head * nTB + qb
            idxk = (nHC + head) * nTB + qb
            E.wait("sync", f"s_qkst{idxq % 4}", 16 * (idxq // 4 + 1))
            E.wait("sync", f"s_qkst{idxk % 4}", 16 * (idxk // 4 + 1))
            idxv = (nDB - 1) * nTT + (NSTR * (qb + 1) - 1)
            E.wait("sync", "s_vst", 16 * (idxv + 1))
            E.dma("sync", f"s_qsl{sl}", q_sl(sl),
                  g("q_scr")[head * 128:(head + 1) * 128, qb * TB:(qb + 1) * TB])
            E.dma("sync", f"s_ksl{sl}", k_sl(sl, nkt * 128),
                  g("k_scr")[head * 128:(head + 1) * 128, 0:nkt * 128])
            E.dma("sync", f"s_vsl{sl}",
                  v_sl(sl, nkt * 128).rearrange("p (t c) -> p t c", c=128),
                  g("v_scr")[0:nkt * 128, head * 128:(head + 1) * 128]
                  .rearrange("(t p) c -> p t c", p=128))
            nfill = 16 * (iq // 2 + 1)
            vsl_fill[iq] = nfill
            E.wait("tensor", f"s_qsl{sl}", nfill)
            E.wait("tensor", f"s_ksl{sl}", nfill)
            exp_slot = {}
            PAIR_OK = (TB == 512)
            SBK4 = (0, 1, 6, 7)
            ns_end = nkt - NSTR
            last_exp = 0
            kt = 0
            while kt < nkt:
                pair = (kt + 1 < ns_end and kt % 2 == 0 and PAIR_OK)
                subkts = (kt, kt + 1) if pair else (kt,)
                for k2 in subkts:
                    c = k2
                    bank = SBK4[c % 4]
                    if c >= 4:
                        E.wait("tensor", "sc_p", exp_slot[c - 4])
                    elif iq > 0:
                        E.wait("tensor", "sc_p", prev_exp_last)
                    else:
                        E.wait("tensor", "sc_p", p1_sc)
                    if c == 0 and head == 0 and qb > 0:
                        E.wait("tensor", "v_p", r1_v[(qb - 1) * nOC + nOC - 1])
                    tsc = E.inc("tensor", "t_p", lambda e, sl=sl, k2=k2, bank=bank, nkt=nkt: e.matmul(
                        ps[bank][:, 0:TB], k_sl(sl, nkt * 128)[:, k2 * 128:(k2 + 1) * 128],
                        q_sl(sl), start=True, stop=True, skip_group_check=True))
                    if head > 0:
                        emit_av_kt(iq - 1, k2, nkt, last=(k2 == nkt - 1))
                kt1 = subkts[-1]
                E.wait("scalar", "t_p", tsc)
                if head > 0:
                    E.wait("scalar", "t_p", avkt_t[(iq - 1, kt1)])
                elif qb > 0:
                    pl = qb * NH - 1
                    kk = min(kt1, nkt_prev_qb - 1)
                    if kk >= 0:
                        E.wait("scalar", "t_p", avkt_t[(pl, kk)])
                if pair:
                    pb = (kt // 2) % 2
                    v = E.inc("scalar", "sc_p",
                              lambda e, pb=pb, kt=kt: e.activation(
                                  out=big[:, et_off + kt * TB: et_off + (kt + 2) * TB],
                                  in_=T["psA"][pb][:, 0:2 * TB], func=AF.Exp))
                    exp_slot[kt] = exp_slot[kt + 1] = v
                    last_exp = v
                elif kt1 >= nkt - NSTR:
                    es = n_strad % 2
                    n_strad += 1
                    if n_strad > 2:
                        E.wait("scalar", "v_p", msk_v[n_strad - 2])
                    v = E.inc("scalar", "sc_p",
                              lambda e, bank=SBK4[kt1 % 4], es=es: e.activation(
                                  out=et_t(es), in_=ps[bank][:, 0:TB], func=AF.Exp))
                    exp_slot[kt1] = v
                    last_exp = v
                    E.wait("vector", "sc_p", v)
                    j = kt1 - (nkt - NSTR)
                    msk_v[n_strad] = E.inc("vector", "v_p",
                                           lambda e, kt1=kt1, es=es, j=j: e.tensor_tensor(
                                               out=et(kt1), in0=et_t(es),
                                               in1=g("cm_s")[:, (NSTR - 1 - j) * 128:
                                                             (NSTR - 1 - j) * 128 + TB],
                                               op=OP.mult))
                    msk_last[iq] = msk_v[n_strad]
                else:
                    v = E.inc("scalar", "sc_p",
                              lambda e, bank=SBK4[kt1 % 4], kt1=kt1: e.activation(
                                  out=et(kt1), in_=ps[bank][:, 0:TB], func=AF.Exp))
                    exp_slot[kt1] = v
                    last_exp = v
                kt += 2 if pair else 1
            prev_exp_last = last_exp
            if head > 0:
                den_tail(iq - 1)
        liq = qb * NH + NH - 1
        E.wait("tensor", "sc_p", prev_exp_last)
        for kt in range(nkt):
            emit_av_kt(liq, kt, nkt, last=(kt == nkt - 1))
        den_tail(liq)
        n_attn_qb = attn_v[qb * NH + NH - 1]
        first_wo = True
        for ot in range(nOC):
            bank = ot % 2
            io = qb * nOC + ot
            if io >= 2:
                E.wait("tensor", "v_p", r1_v[io - 2])
            for dc2 in range(nHC // 2):
                dc = 2 * dc2
                qsel = dc2 % 2
                eng = "sync" if qsel == 0 else "gpsimd"
                jq = wo_qctr[qsel]
                wo_qctr[qsel] += 1
                ws = 3 * qsel + (jq % 3)
                if jq >= 3:
                    E.wait(eng, "t_p", wo_use_t[(qsel, jq - 3)])
                E.dma(eng, f"s_wot{ws}",
                      g("wo_t")[ws][:, :].rearrange("p (t c) -> p t c", c=128),
                      g("wob_scr")[dc * 128:(dc + 2) * 128,
                                   ot * 128:(ot + 1) * 128].rearrange("(t p) c -> p t c", p=128))
                E.wait("tensor", f"s_wot{ws}", 16 * (jq // 3 + 1))
                E.wait("tensor", "v_p", n_attn_qb)
                if first_wo:
                    E.wait("tensor", "sc_p", prev_exp_last)
                    first_wo = False
                E.run("tensor", lambda e, ws=ws, dc=dc, bank=bank, st=(dc == 0): e.matmul(
                    ps[bank][:, 0:TB], g("wo_t")[ws][:, 0:128], attn(dc),
                    start=st, stop=False, skip_group_check=True))
                wo_use_t[(qsel, jq)] = E.inc("tensor", "t_p",
                                             lambda e, ws=ws, dc=dc, bank=bank,
                                             sp=(dc == nHC - 2): e.matmul(
                                                 ps[bank][:, 0:TB], g("wo_t")[ws][:, 128:256],
                                                 attn(dc + 1),
                                                 start=False, stop=sp, skip_group_check=True))
            wo_last_t = wo_use_t[(qsel, jq)]
            xs = ot % 2
            if io >= 2:
                E.wait("sync", "v_p", r1_v[io - 2])
            if qb == 0 and ot == 0:
                E.wait("sync", "sc_p", p1_sc)
            E.dma("sync", f"s_xres{xs}", xres[xs],
                  g("x_fm")[ot * 128:(ot + 1) * 128, qb * TB:(qb + 1) * TB])
            if qb == 0 and ot == 0:
                E.wait("vector", "sc_p", p1_sc)
            if qb > 0 and ot == 0:
                E.wait("vector", "sc_p", ln_prev_stat['sc'])
            E.wait("vector", "t_p", wo_last_t)
            E.wait("vector", f"s_xres{xs}", 16 * (io // 2 + 1))
            r1_v[io] = E.inc("vector", "v_p",
                             lambda e, ot=ot, bank=bank, xs=xs: e.tensor_tensor(
                                 out=r1[:, ot * TB:(ot + 1) * TB], in0=ps[bank][:, 0:TB],
                                 in1=xres[xs], op=OP.add))
        rv = {oc: r1_v[qb * nOC + oc] for oc in range(nOC)}
        emit_ln(cfg, E, T, r1, rv, r1b_t, r1sq_t, (2, 3, 4, 5), ln_prev_stat)
        for oc in range(nOC):
            s = 0
            io = qb * nOC + oc
            if io >= 1:
                E.wait("vector", "s_x2f", 16 * io)
            E.run("vector", lambda e, oc=oc: e.tensor_tensor(
                out=tmp_f, in0=r1[:, oc * TB:(oc + 1) * TB], in1=ps[4][:, 0:TB], op=OP.mult))
            E.drain("vector")
            x2f_v[io] = E.inc("vector", "v_p", lambda e, s=s: e.tensor_tensor(
                out=x2f_t[s], in0=tmp_f, in1=ps[5][:, 0:TB], op=OP.subtract))
            E.wait("gpsimd", "v_p", x2f_v[io])
            n_x2f = E.dma("gpsimd", "s_x2f",
                          g("x2f_scr")[oc * 128:(oc + 1) * 128, qb * TB:(qb + 1) * TB],
                          x2f_t[s])
            x2f_cnt[(qb, oc)] = n_x2f

    E.wait("gpsimd", "t_p", tp_p1v)
    for hc in range(nHC):
        E.wait("gpsimd", "s_x2f", x2f_cnt[(nTB - 1, hc)])
        E.dma("gpsimd", f"s_x2g{hc * NG // nHC}", xact[:, hc * S:(hc + 1) * S],
              g("x2f_scr")[hc * 128:(hc + 1) * 128, :])
    n_x2g = [E.count(f"s_x2g{i}") for i in range(NG)]
    E.wait("gpsimd", "g_p", ag_g)
    E.wait("sync", "g_p", ag_g)
    sil_sc, h_v, gu_t = {}, {}, {}
    h_fill = {0: 0, 1: 0}
    QW = min(4, nHC)
    nHQ = (nHC + QW - 1) // QW
    wcv = [g("wc")[k][:, :, :].rearrange("(h a) p c -> h a p c", a=nIC)
           for k in range(cfg.KAG)]
    BL = cfg.ncores * nIC
    wgu_fill = {0: 0, 1: 0, 2: 0}
    for ic in range(nIC):
        for hc in range(nHC):
            iw = ic * nHC + hc
            hq, hsub = divmod(hc, QW)
            jq = ic * nHQ + hq
            sg = jq % 3
            if hsub == 0:
                w = min(QW, nHC - hc)
                if jq >= 3:
                    pic2, phq2 = divmod(jq - 3, nHQ)
                    iw2 = pic2 * nHC + min((phq2 + 1) * QW, nHC) - 1
                    E.wait("gpsimd", "t_p", gu_t[iw2])
                    E.wait("sync", "t_p", gu_t[iw2])
                kg, hg = divmod(hc, cfg.ncores)
                ku, hu = divmod(nHC + hc, cfg.ncores)
                E.dma("gpsimd", f"s_wg{sg}",
                      g("wg_t")[sg][:, 0:w * 128].rearrange("p (t c) -> p t c", c=128),
                      wcv[kg][hg:hg + w, ic:ic + 1, :, :].rearrange("t a p c -> p (t a) c"))
                E.dma("sync", f"s_wu{sg}",
                      g("wu_t")[sg][:, 0:w * 128].rearrange("p (t c) -> p t c", c=128),
                      wcv[ku][hu:hu + w, ic:ic + 1, :, :].rearrange("t a p c -> p (t a) c"))
                wgu_fill[sg] += 16
                E.wait("tensor", f"s_wg{sg}", wgu_fill[sg])
                E.wait("tensor", f"s_wu{sg}", wgu_fill[sg])
            if ic == 0 and hc % (nHC // NG) == 0:
                E.wait("tensor", f"s_x2g{hc * NG // nHC}", n_x2g[hc * NG // nHC])
            if iw == 0:
                E.wait("tensor", "v_p", E.count("v_p"))
                E.wait("tensor", "sc_p", E.count("sc_p"))
            for qb in range(nTB):
                if ic >= 1 and hc == 0:
                    E.wait("tensor", "sc_p", sil_sc[(ic - 1, qb)])
                    E.wait("tensor", "v_p", h_v[(ic - 1, qb)])
                fng = lambda e, sg=sg, hc=hc, qb=qb, hs=hsub, st=(hc == 0), sp=(hc == nHC - 1): e.matmul(
                    ps[qb][:, 0:TB], g("wg_t")[sg][:, hs * 128:(hs + 1) * 128],
                    xact[:, hc * S + qb * TB: hc * S + (qb + 1) * TB],
                    start=st, stop=sp, skip_group_check=True)
                fnu = lambda e, sg=sg, hc=hc, qb=qb, hs=hsub, st=(hc == 0), sp=(hc == nHC - 1): e.matmul(
                    ps[nTB + qb][:, 0:TB], g("wu_t")[sg][:, hs * 128:(hs + 1) * 128],
                    xact[:, hc * S + qb * TB: hc * S + (qb + 1) * TB],
                    start=st, stop=sp, skip_group_check=True)
                E.run("tensor", fng)
                if qb == nTB - 1:
                    gu_t[iw] = E.inc("tensor", "t_p", fnu)
                else:
                    E.run("tensor", fnu)
        for qb in range(nTB):
            s = qb % 2
            iq = ic * nTB + qb
            E.wait("scalar", "t_p", gu_t[ic * nHC + nHC - 1])
            if iq >= 2:
                E.wait("scalar", "v_p", h_v[((iq - 2) // nTB, (iq - 2) % nTB)])
            sil_sc[(ic, qb)] = E.inc("scalar", "sc_p",
                                     lambda e, qb=qb, s=s: e.activation(
                                         out=sil_t[s], in_=ps[qb][:, 0:TB],
                                         func=AF.Silu))
            E.wait("vector", "sc_p", sil_sc[(ic, qb)])
            if h_fill[s] >= 1:
                E.wait("vector", f"s_h{s}", 16 * h_fill[s])
            h_v[(ic, qb)] = E.inc("vector", "v_p",
                                  lambda e, qb=qb, s=s: e.tensor_tensor(
                                      out=h_t[s], in0=sil_t[s],
                                      in1=ps[nTB + qb][:, 0:TB], op=OP.mult))
            E.wait("scalar", "v_p", h_v[(ic, qb)])
            E.dma("scalar", f"s_h{s}",
                  g("h_scr")[ic * 128:(ic + 1) * 128, qb * TB:(qb + 1) * TB],
                  h_t[s])
            h_fill[s] += 1
    n_h = [E.count(f"s_h{s_}") for s_ in range(2)]

    for s_ in range(2):
        E.wait("sync", f"s_h{s_}", n_h[s_])
    E.wait("sync", "s_x2f", n_x2f)
    E.wait("sync", "t_p", E.count("t_p"))
    do_t, r2_v, wd_use_t = {}, {}, {}
    wd_qctr = {0: 0, 1: 0}
    n_out = 0
    first_p4 = True
    nICP = (nIC + 1) // 2
    h_free_t, n_hl_q = {}, {}
    for s_ in range(2):
        E.wait("gpsimd", f"s_h{s_}", n_h[s_])
    E.wait("gpsimd", "t_p", E.count("t_p"))
    for icp in range(nICP):
        ic0, icw = 2 * icp, min(2, nIC - 2 * icp)
        eng2 = "sync" if icp % 2 == 0 else "gpsimd"
        for icsub in range(icw):
            ic = ic0 + icsub
            E.dma(eng2, f"s_hl{icp % 2}", h_sl(ic),
                  g("h_scr")[ic * 128:(ic + 1) * 128, 0:TB])
    n_hl_q[0] = (E.count("s_hl0"), E.count("s_hl1"))
    for qb in range(nTB):
        n_hl = n_hl_q[qb]
        for op_ in range(nOC // 2):
            ot0, ot1 = 2 * op_, 2 * op_ + 1
            gp_pair = qb * (nOC // 2) + op_
            b0, b1 = 2 * (gp_pair % 2), 2 * (gp_pair % 2) + 1
            if gp_pair >= 2:
                qb_p, op_p = divmod(gp_pair - 2, nOC // 2)
                E.wait("tensor", "v_p", r2_v[qb_p * nOC + 2 * op_p + 1])
            for ic in range(nIC):
                qsel = ic % 2
                eng = "sync" if qsel == 0 else "gpsimd"
                jq = wd_qctr[qsel]
                wd_qctr[qsel] += 1
                ws = 4 * qsel + (jq % 4)
                if jq >= 4:
                    E.wait(eng, "t_p", wd_use_t[(qsel, jq - 4)])
                gid_d = 2 * cfg.G1 + ic * nOC + ot0
                kd, ld = divmod(gid_d, BL)
                E.dma(eng, f"s_wd{ws}",
                      g("wd_t")[ws][:, :].rearrange("p (t c) -> p t c", c=128),
                      g("wc")[kd][ld:ld + 2, :, :].rearrange("t p c -> p t c"))
                E.wait("tensor", f"s_wd{ws}", 16 * (jq // 4 + 1))
                if first_p4:
                    E.wait("tensor", "v_p", E.count("v_p"))
                    E.wait("tensor", "sc_p", E.count("sc_p"))
                    first_p4 = False
                if ic == 0:
                    E.wait("tensor", "s_hl0", n_hl[0])
                    E.wait("tensor", "s_hl1", n_hl[1])
                E.run("tensor", lambda e, ws=ws, ic=ic, b=b0, st=(ic == 0),
                      sp=(ic == nIC - 1): e.matmul(
                          ps[b][:, 0:TB], g("wd_t")[ws][:, 0:128], h_sl(ic),
                          start=st, stop=sp, skip_group_check=True))
                tmm = E.inc("tensor", "t_p",
                            lambda e, ws=ws, ic=ic, b=b1, st=(ic == 0),
                            sp=(ic == nIC - 1): e.matmul(
                                ps[b][:, 0:TB], g("wd_t")[ws][:, 128:256],
                                h_sl(ic),
                                start=st, stop=sp, skip_group_check=True))
                wd_use_t[(qsel, jq)] = tmm
                if op_ == nOC // 2 - 1:
                    h_free_t[ic] = tmm
            tdo_pair = tmm
            for i, (ot, bank) in enumerate(((ot0, b0), (ot1, b1))):
                io = qb * nOC + ot
                do_t[(qb, ot)] = tdo_pair
                xs = ot % 2
                if io >= 2:
                    E.wait("sync", "v_p", r2_v[io - 2])
                E.dma("sync", f"s_x2r{xs}", x2r[xs],
                      g("x2f_scr")[ot * 128:(ot + 1) * 128, qb * TB:(qb + 1) * TB])
                if qb == 0 and ot == 0:
                    E.wait("vector", "s_x2f", n_x2f)
                    E.wait("vector", "sc_p", ln_prev_stat['sc'])
                E.wait("vector", "t_p", tdo_pair)
                E.wait("vector", f"s_x2r{xs}", 16 * (io // 2 + 1))
                r2_v[io] = E.inc("vector", "v_p",
                                 lambda e, ot=ot, bank=bank, xs=xs: e.tensor_tensor(
                                     out=r2[:, ot * TB:(ot + 1) * TB], in0=ps[bank][:, 0:TB],
                                     in1=x2r[xs], op=OP.add))
        if qb + 1 < nTB:
            for icp in range(nICP):
                ic0, icw = 2 * icp, min(2, nIC - 2 * icp)
                eng2 = "sync" if icp % 2 == 0 else "gpsimd"
                E.wait(eng2, "t_p", h_free_t[ic0 + icw - 1])
                for icsub in range(icw):
                    ic = ic0 + icsub
                    E.dma(eng2, f"s_hl{icp % 2}", h_sl(ic),
                          g("h_scr")[ic * 128:(ic + 1) * 128,
                                     (qb + 1) * TB:(qb + 2) * TB])
            n_hl_q[qb + 1] = (E.count("s_hl0"), E.count("s_hl1"))
        rv = {oc: r2_v[qb * nOC + oc] for oc in range(nOC)}
        emit_ln(cfg, E, T, r2, rv, r1b_t, r1sq_t, (6, 7, 4, 5), ln_prev_stat)
        for oc in range(nOC):
            s = 0
            io = qb * nOC + oc
            if io >= 1:
                E.wait("vector", "s_out", 16 * io)
            E.run("vector", lambda e, oc=oc: e.tensor_tensor(
                out=tmp_f, in0=r2[:, oc * TB:(oc + 1) * TB], in1=ps[4][:, 0:TB], op=OP.mult))
            E.drain("vector")
            co = E.inc("vector", "v_p", lambda e, s=s: e.tensor_tensor(
                out=out_t[s], in0=tmp_f, in1=ps[5][:, 0:TB], op=OP.subtract))
            E.wait("gpsimd", "v_p", co)
            n_out = E.dma("gpsimd", "s_out",
                          g("out_fm")[oc * 128:(oc + 1) * 128, qb * TB:(qb + 1) * TB],
                          out_t[s])
    E.wait("gpsimd", "s_out", n_out)


def emit_ln(cfg, E, T, r_slab, r_v, r1b_t, r1sq_t, banks, ln_prev_stat):
    H, TB, nOC = cfg.H, cfg.TB, cfg.nOC
    g = lambda n: T[n]
    ps = T["ps"]
    bsum, bsq, ba, bb = banks
    stat_t = {}
    crs_last = 0
    for oc in range(nOC):
        s = oc % 2
        E.wait("scalar", "v_p", r_v[oc])
        if oc >= 1:
            E.wait("scalar", "t_p", stat_t[oc - 1])
        else:
            E.wait("scalar", "t_p", ln_prev_stat['t'])
        crb = E.inc("scalar", "sc_p", lambda e, oc=oc, s=s: e.copy(
            out=r1b_t(s), in_=r_slab[:, oc * TB:(oc + 1) * TB]))
        crs = E.inc("scalar", "sc_p", lambda e, oc=oc, s=s: e.activation(
            out=r1sq_t(s), in_=r_slab[:, oc * TB:(oc + 1) * TB], func=AF.Square))
        crs_last = crs
        E.wait("tensor", "sc_p", crs)
        E.run("tensor", lambda e, s=s, st=(oc == 0), sp=(oc == nOC - 1): e.matmul(
            ps[bsum][0:1, 0:TB], g("onec_s")[:, :], r1b_t(s),
            start=st, stop=sp, skip_group_check=True))
        stat_t[oc] = E.inc("tensor", "t_p",
                           lambda e, s=s, st=(oc == 0), sp=(oc == nOC - 1): e.matmul(
                               ps[bsq][0:1, 0:TB], g("onec_s")[:, :], r1sq_t(s),
                               start=st, stop=sp, skip_group_check=True))
    ln_prev_stat['t'] = stat_t[nOC - 1]
    ln_prev_stat['sc'] = crs_last
    E.wait("vector", "t_p", stat_t[nOC - 1])
    E.run("vector", lambda e: e.tensor_scalar(out=g("mn_f")[:, :], in0=ps[bsum][0:1, 0:TB],
                                              scalar1=1.0 / H, scalar2=None, op0=OP.mult))
    E.run("vector", lambda e: e.tensor_scalar(out=g("msq_f")[:, :], in0=ps[bsq][0:1, 0:TB],
                                              scalar1=1.0 / H, scalar2=None, op0=OP.mult))
    E.drain("vector")
    E.run("vector", lambda e: e.tensor_tensor(out=g("mn2_f")[:, :], in0=g("mn_f")[:, :],
                                              in1=g("mn_f")[:, :], op=OP.mult))
    E.drain("vector")
    E.run("vector", lambda e: e.tensor_tensor(
        out=g("var_f")[:, :], in0=g("msq_f")[:, :], in1=g("mn2_f")[:, :], op=OP.subtract))
    E.drain("vector")
    cvar = E.inc("vector", "v_p", lambda e: e.tensor_scalar(
        out=g("msq_f")[:, :], in0=g("var_f")[:, :], scalar1=EPS, scalar2=None, op0=OP.add))
    E.wait("scalar", "v_p", cvar)
    E.run("scalar", lambda e: e.activation(
        out=g("sd_f")[:, :], in_=g("msq_f")[:, :], func=AF.Ln))
    E.drain("scalar")
    csd = E.inc("scalar", "sc_p", lambda e: e.activation(
        out=g("rstd_f")[:, :], in_=g("sd_f")[:, :], func=AF.Exp, scale=-0.5))
    E.wait("vector", "sc_p", csd)
    E.run("vector", lambda e: e.tensor_tensor(out=g("bvec_f")[:, :], in0=g("mn_f")[:, :],
                                              in1=g("rstd_f")[:, :], op=OP.mult))
    E.drain("vector")
    E.run("vector", lambda e: e.tensor_copy(g("a_b")[:, :], g("rstd_f")[:, :]))
    E.drain("vector")
    cab = E.inc("vector", "v_p", lambda e: e.tensor_copy(g("b_b")[:, :], g("bvec_f")[:, :]))
    E.wait("tensor", "v_p", cab)
    E.run("tensor", lambda e: e.matmul(ps[ba][:, 0:TB], g("oner_s")[:, :], g("a_b")[:, :],
                                       start=True, stop=True, skip_group_check=True))
    tab = E.inc("tensor", "t_p", lambda e: e.matmul(
        ps[bb][:, 0:TB], g("oner_s")[:, :], g("b_b")[:, :], start=True, stop=True,
        skip_group_check=True))
    E.wait("vector", "t_p", tab)



def make_host_inputs(cfg: Cfg, x, w_qkv, w_out, lut, gate_idx, up_idx, down_idx):
    import ml_dtypes
    BF = ml_dtypes.bfloat16
    lutb = np.broadcast_to(np.asarray(lut, np.float32), (128, LUT_N)).copy()
    kp = np.arange(128)[:, None]
    uu = np.arange((cfg.NSTR - 1) * 128 + cfg.TB)[None, :]
    cm = (uu >= kp + (cfg.NSTR - 1) * 128).astype(BF)
    onesc = np.ones((128, 1), dtype=BF)
    onesr = np.ones((1, 128), dtype=BF)
    srcs = {
        "gidxT": np.ascontiguousarray(np.asarray(gate_idx).T.astype(np.int32)),
        "uidxT": np.ascontiguousarray(np.asarray(up_idx).T.astype(np.int32)),
        "didxT": np.ascontiguousarray(np.asarray(down_idx).T.astype(np.int32)),
    }
    in_maps = []
    for c in range(cfg.ncores):
        myidx = np.zeros((cfg.TPC, 128, 128), np.int32)
        for t in range(cfg.TPC):
            gid = (t // cfg.CSZ) * (cfg.ncores * cfg.CSZ) + c * cfg.CSZ + (t % cfg.CSZ)
            if gid >= cfg.NT:
                continue
            src, rc, ct = w_tile_src(cfg, gid)
            myidx[t] = srcs[src][rc * 128:(rc + 1) * 128, ct * 128:(ct + 1) * 128]
        in_maps.append({
            "x_fm": np.ascontiguousarray(np.asarray(x[c]).T).astype(np.float32, copy=False),
            "wqkv": np.asarray(w_qkv, np.float32),
            "wout": np.asarray(w_out, np.float32),
            "myidx": myidx,
            "lutb": lutb,
            "cmask": cm,
            "onesc": onesc,
            "onesr": onesr,
        })
    return in_maps


def gather_output(cfg, results):
    return np.stack([np.ascontiguousarray(r["out_fm"].T) for r in results], axis=0)


def numpy_reference(cfg: Cfg, x, w_qkv, w_out, lut, gate_idx, up_idx, down_idx):
    def layernorm(v):
        m = v.mean(-1, keepdims=True)
        s = v.var(-1, keepdims=True)
        return (v - m) / np.sqrt(s + EPS)
    def softmax(s):
        m = s.max(-1, keepdims=True)
        e = np.exp(s - m)
        return e / e.sum(-1, keepdims=True)
    x = np.asarray(x, np.float64)
    w_qkv = np.asarray(w_qkv, np.float64)
    w_out = np.asarray(w_out, np.float64)
    lut = np.asarray(lut, np.float64)
    B, S, H = x.shape
    NH, D = cfg.NH, cfg.D
    qkv = x @ w_qkv
    qkv = qkv.reshape(B, S, 3, NH, D)
    q, k, v = qkv[:, :, 0], qkv[:, :, 1], qkv[:, :, 2]
    q = q / np.sqrt(np.float64(D))
    scores = np.einsum("bqhd,bkhd->bhqk", q, k)
    mask = np.arange(S)[:, None] >= np.arange(S)[None, :]
    scores = np.where(mask[None, None], scores, -1e10)
    a = softmax(scores)
    o = np.einsum("bhqk,bkhd->bqhd", a, v).reshape(B, S, H)
    x1 = layernorm(x + o @ w_out)
    gate = x1 @ lut[gate_idx].T
    up = x1 @ lut[up_idx].T
    hh = (gate / (1 + np.exp(-gate))) * up
    x2 = layernorm(x1 + hh @ lut[down_idx].T)
    return x2.astype(np.float32)



def kernel(L0_attn_x, L0_attn_w_qkv, L0_attn_w_out, L0_ffn_lut,
           L0_ffn_gate_idx, L0_ffn_up_idx, L0_ffn_down_idx):
    cfg = FULL
    in_maps = make_host_inputs(cfg, L0_attn_x, L0_attn_w_qkv, L0_attn_w_out,
                               L0_ffn_lut, L0_ffn_gate_idx, L0_ffn_up_idx,
                               L0_ffn_down_idx)
    nc = build_program(cfg)
    res = run_bass_kernel_spmd(nc, in_maps, core_ids=list(range(cfg.ncores)))
    return gather_output(cfg, res.results)
